# version 02
# baseline (speedup 1.0000x reference)
import contextlib
import numpy as np
import ml_dtypes
import concourse.bass as bass
import concourse.mybir as mybir
from concourse.bass_utils import run_bass_kernel_spmd
from concourse.bass_core_rust import IndirectOffsetOnAxis

F32 = mybir.dt.float32
BF16 = mybir.dt.bfloat16
I32 = mybir.dt.int32
AF = mybir.ActivationFunctionType
ALU = mybir.AluOpType
AX = mybir.AxisListType
BF = ml_dtypes.bfloat16

NCORES = 8
_DBG_HEADS = None
_DBG_L0 = False
_DBG_L2F = False
_DBG_L3 = False
_DBG_L2 = None
T = 2048
C = 1024
CH = C // 128
NT = T // 128
ROFF = 65536.0



_ALL_STREAMS = ("tensor", "vector", "scalar", "gpsimd", "sync")


class _Op:
    __slots__ = ("fn", "waits", "inc_sem", "inc_val", "tag")

    def __init__(self, fn, waits, inc_sem, inc_val, tag):
        self.fn = fn
        self.waits = waits
        self.inc_sem = inc_sem
        self.inc_val = inc_val
        self.tag = tag


class Prog:
    def __init__(self, name="prog"):
        self.name = name
        self.nc = bass.Bass("TRN2", target_bir_lowering=False)
        self.stack = contextlib.ExitStack()
        self.ops = {e: [] for e in _ALL_STREAMS}
        self.sem_count = {}
        self.last_writer = {}
        self.readers = {}
        self.hiwater = {e: {} for e in _ALL_STREAMS}
        self.finalized = False

    def dram(self, name, shape, dtype, kind="ExternalInput"):
        return self.nc.dram_tensor(name, list(shape), dtype, kind=kind).ap()

    def sb(self, name, shape, dtype):
        return self.stack.enter_context(self.nc.sbuf_tensor(name, list(shape), dtype))

    def ps(self, name, shape, dtype=F32):
        return self.stack.enter_context(self.nc.psum_tensor(name, list(shape), dtype))

    DMA_POOL = 8

    def op(self, stream, fn, reads=(), writes=(), dma=False, tag=""):
        assert not self.finalized
        if dma:
            idx = getattr(self, "dma_idx", None)
            if idx is None:
                idx = self.dma_idx = {}
            i = idx.get(stream, 0)
            idx[stream] = i + 1
            semkey = f"d{stream}{i % self.DMA_POOL}"
        else:
            semkey = stream
        inc = 16 if dma else 1
        cur = self.sem_count.get(semkey, 0)
        new = cur + inc
        self.sem_count[semkey] = new

        deps = {}

        def add_dep(sk, v):
            if deps.get(sk, 0) < v:
                deps[sk] = v

        if dma and cur > 0:
            add_dep(semkey, cur)

        for k in reads:
            for sk, v in self.last_writer.get(k, {}).items():
                add_dep(sk, v)
        for k in writes:
            for sk, v in self.last_writer.get(k, {}).items():
                add_dep(sk, v)
            for sk, v in self.readers.get(k, {}).items():
                add_dep(sk, v)

        hw = self.hiwater[stream]
        waits = []
        for sk, v in sorted(deps.items()):
            if sk == semkey and v >= new:
                raise RuntimeError(f"self dependency on {semkey} at {v} (op {tag})")
            if hw.get(sk, 0) >= v:
                continue
            waits.append((sk, v))
            hw[sk] = v

        self.ops[stream].append(_Op(fn, waits, semkey, inc, tag))

        for k in reads:
            self.readers.setdefault(k, {})[semkey] = new
        for k in writes:
            self.last_writer.setdefault(k, {})[semkey] = new
            self.readers[k] = {}

    def build(self):
        assert not self.finalized
        self.finalized = True
        nc = self.nc
        sems = {}
        with self.stack:
            for key in sorted(self.sem_count):
                sems[key] = self.stack.enter_context(nc.semaphore("s_" + key))
            final_deps = dict(self.sem_count)
            with nc.Block() as block:
                def make_runner(stream):
                    def run(eng):
                        for o in self.ops[stream]:
                            for sk, v in o.waits:
                                eng.wait_ge(sems[sk], v)
                            inst = o.fn(eng)
                            inst.then_inc(sems[o.inc_sem], o.inc_val)
                        if stream == "sync":
                            for sk, v in sorted(final_deps.items()):
                                eng.wait_ge(sems[sk], v)
                    return run
                block.tensor(make_runner("tensor"))
                block.vector(make_runner("vector"))
                block.scalar(make_runner("scalar"))
                block.gpsimd(make_runner("gpsimd"))
                block.sync(make_runner("sync"))
        return nc


def k_(name, i=None):
    return name if i is None else f"{name}:{i}"



class Ctx:

    def __init__(self, name):
        self.p = Prog(name)
        self.b = {}
        p, b = self.p, self.b
        b["ones_f"] = p.sb("ones_f", [128, 1], F32)
        b["ones_k1"] = p.sb("ones_k1", [1, 128], F32)
        b["ones_k1b"] = p.sb("ones_k1b", [1, 128], BF16)
        b["cb"] = p.sb("cb", [128, 3], F32)
        p.op("vector", lambda e: e.memset(b["cb"][:, 0:1], 1e-5), writes=["cb"])
        p.op("vector", lambda e: e.memset(b["cb"][:, 1:2], 1.0), writes=["cb"])
        p.op("vector", lambda e: e.memset(b["cb"][:, 2:3], 1e-6), writes=["cb"])
        p.op("vector", lambda e: e.memset(b["ones_f"][:, :], 1.0), writes=["ones_f"])
        p.op("vector", lambda e: e.memset(b["ones_k1"][:, :], 1.0), writes=["ones_k1"])
        p.op("vector", lambda e: e.memset(b["ones_k1b"][:, :], 1.0), writes=["ones_k1b"])

    def resid_in(self, name="xin"):
        p, b = self.p, self.b
        xin = p.dram(name, [C, T], F32)
        xc = []
        for c in range(CH):
            t = p.sb(f"x_{c}", [128, T], F32)
            xc.append(t)
            p.op("sync",
                 lambda e, c=c, t=t: e.dma_start(out=t[:, :], in_=xin[c * 128:(c + 1) * 128, :]),
                 writes=[k_("x", c)], dma=True, tag="ldx")
        b["x"] = xc
        return xc

    def resid_out(self, name="xout"):
        p, b = self.p, self.b
        xout = p.dram(name, [C, T], F32, kind="ExternalOutput")
        for c in range(CH):
            p.op("sync",
                 lambda e, c=c: e.dma_start(out=xout[c * 128:(c + 1) * 128, :], in_=b["x"][c][:, :]),
                 reads=[k_("x", c)], dma=True, tag="stx")

def emit_ln(ctx, src, dstbuf, dstkey):
    p, b = ctx.p, ctx.b
    x = b[src]
    xsq, scr = b["xsq"], b["scr"]
    rowA, rowB, rrow_bf = b["rowA"], b["rowB"], b["rrow_bf"]
    ps_sum, ps_sq, ps_bM, ps_bR = b["ps_sum"], b["ps_sq"], b["ps_bM"], b["ps_bR"]
    ones_f, ones_k1, ones_k1b = b["ones_f"], b["ones_k1"], b["ones_k1b"]
    cb = b["cb"]

    for s in range(4):
        sl = slice(s * 512, (s + 1) * 512)
        for c in range(CH):
            p.op("tensor",
                 lambda e, c=c, sl=sl: e.matmul(
                     ps_sum[0:1, :], ones_f[:, 0:1], x[c][:, sl],
                     start=(c == 0), stop=(c == CH - 1)),
                 reads=[k_(src, c), "ones_f"], writes=["ps_sum"], tag="ln_mm_s")
        for c in range(CH):
            p.op("scalar",
                 lambda e, c=c, sl=sl: e.activation(xsq[:, :], x[c][:, sl], AF.Square),
                 reads=[k_(src, c)], writes=["xsq"], tag="ln_sq")
            p.op("tensor",
                 lambda e, c=c: e.matmul(
                     ps_sq[0:1, :], ones_f[:, 0:1], xsq[:, :],
                     start=(c == 0), stop=(c == CH - 1)),
                 reads=["xsq", "ones_f"], writes=["ps_sq"], tag="ln_mm_q")
        p.op("vector",
             lambda e, sl=sl: e.tensor_scalar_mul(rowA[0:1, sl], ps_sum[0:1, :], 1.0 / C),
             reads=["ps_sum"], writes=["rowA"], tag="ln_mean")
        p.op("vector",
             lambda e, sl=sl: e.tensor_scalar_mul(rowB[0:1, sl], ps_sq[0:1, :], 1.0 / C),
             reads=["ps_sq"], writes=["rowB"], tag="ln_msq")

    for s in range(4):
        sl = slice(s * 512, (s + 1) * 512)
        p.op("vector",
             lambda e, sl=sl: e.tensor_mul(scr[0:1, :], rowA[0:1, sl], rowA[0:1, sl]),
             reads=["rowA"], writes=["scr"], tag="ln_m2")
        p.op("vector",
             lambda e, sl=sl: e.tensor_sub(rowB[0:1, sl], rowB[0:1, sl], scr[0:1, :]),
             reads=["rowB", "scr"], writes=["rowB"], tag="ln_var")
        p.op("scalar",
             lambda e, sl=sl: e.activation(scr[0:1, :], rowB[0:1, sl], AF.Ln, bias=cb[0:1, 0:1]),
             reads=["rowB", "cb"], writes=["scr"], tag="ln_ln")
        p.op("scalar",
             lambda e, sl=sl: e.activation(rowB[0:1, sl], scr[0:1, :], AF.Exp, scale=-0.5),
             reads=["scr"], writes=["rowB"], tag="ln_exp")
    p.op("vector",
         lambda e: e.tensor_copy(rrow_bf[0:1, :], rowB[0:1, :]),
         reads=["rowB"], writes=["rrow_bf"], tag="ln_rbf")

    for s in range(4):
        sl = slice(s * 512, (s + 1) * 512)
        p.op("tensor",
             lambda e, sl=sl: e.matmul(ps_bM[:, :], ones_k1[:, :], rowA[0:1, sl],
                                       start=True, stop=True),
             reads=["rowA", "ones_k1"], writes=["ps_bM"], tag="ln_bM")
        p.op("tensor",
             lambda e, sl=sl: e.matmul(ps_bR[:, :], ones_k1b[:, :], rrow_bf[0:1, sl],
                                       start=True, stop=True),
             reads=["rrow_bf", "ones_k1b"], writes=["ps_bR"], tag="ln_bR")
        for c in range(CH):
            p.op("vector",
                 lambda e, c=c, sl=sl: e.tensor_sub(scr[:, :], x[c][:, sl], ps_bM[:, :]),
                 reads=[k_(src, c), "ps_bM"], writes=["scr"], tag="ln_sub")
            p.op("vector",
                 lambda e, c=c, sl=sl: e.tensor_mul(dstbuf[c][:, sl], scr[:, :], ps_bR[:, :]),
                 reads=["scr", "ps_bR"], writes=[k_(dstkey, c)], tag="ln_nrm")


def load_weight_bf16(ctx, dram_ap, sbuf_t, key, rows, cols, tag="w"):
    p = ctx.p
    for c in range(rows // 128):
        p.op("gpsimd",
             lambda e, c=c: e.dma_start(
                 out=sbuf_t[:, c * cols:(c + 1) * cols],
                 in_=dram_ap[c * 128:(c + 1) * 128, :]),
             writes=[k_(key, c)], dma=True, tag=tag)



def build_stage_l0ffn():
    ctx = Ctx("l0ffn")
    p, b = ctx.p, ctx.b
    ctx.resid_in()
    ctx.ln_scratch()

    wrT = p.dram("W_rT", [C, C], F32)
    wkT = p.dram("W_kT", [C, C], F32)
    wvT = p.dram("W_vT", [C, C], F32)
    mur = p.dram("mu_r", [C], F32)
    muk = p.dram("mu_k", [C], F32)

    nbuf = [p.sb(f"nbuf_{c}", [128, T], BF16) for c in range(CH)]
    mix = [p.sb(f"mix_{c}", [128, T], BF16) for c in range(CH)]
    r_t = [p.sb(f"r_{c}", [128, T], BF16) for c in range(CH)]
    wbuf = p.sb("wbuf", [128, CH * C], BF16)
    scr = b["scr"]
    scr2 = p.sb("scr2", [128, 512], F32)
    mixscr = p.sb("mixscr", [128, 512], BF16)
    ps_mm = [p.ps(f"ps_mm{i}", [128, 512], F32) for i in range(4)]

    emit_ln(ctx, "x", nbuf, "nbuf")



    mur_t = p.sb("mur_t", [128, CH], F32)
    muk_t = p.sb("muk_t", [128, CH], F32)
    mu1_t = p.sb("mu1_t", [128, 2 * CH], F32)
    for c in range(CH):
        p.op("gpsimd",
             lambda e, c=c: e.dma_start(out=mur_t[:, c:c + 1],
                                        in_=mur[c * 128:(c + 1) * 128].unsqueeze(1)),
             writes=["mur_t"], dma=True)
        p.op("gpsimd",
             lambda e, c=c: e.dma_start(out=muk_t[:, c:c + 1],
                                        in_=muk[c * 128:(c + 1) * 128].unsqueeze(1)),
             writes=["muk_t"], dma=True)
    p.op("vector", lambda e: e.tensor_scalar(mu1_t[:, 0:CH], mur_t[:, :], -1.0, 1.0, ALU.mult, ALU.add),
         reads=["mur_t"], writes=["mu1_t"])
    p.op("vector", lambda e: e.tensor_scalar(mu1_t[:, CH:2 * CH], muk_t[:, :], -1.0, 1.0, ALU.mult, ALU.add),
         reads=["muk_t"], writes=["mu1_t"])

    def emit_mix(mu_tile, mukey, mu1col, c):
        for s in range(4):
            lo = max(1, s * 512)
            hi = (s + 1) * 512
            n = hi - lo
            p.op("vector",
                 lambda e, c=c, lo=lo, hi=hi, n=n: e.tensor_scalar_mul(
                     mixscr[:, 0:n], nbuf[c][:, lo:hi], mu_tile[:, c:c + 1]),
                 reads=[k_("nbuf", c), mukey], writes=["mixscr"], tag="mixa")
            p.op("vector",
                 lambda e, c=c, lo=lo, hi=hi, n=n: e.scalar_tensor_tensor(
                     mix[c][:, lo:hi], nbuf[c][:, lo - 1:hi - 1],
                     mu1_t[:, mu1col + c:mu1col + c + 1],
                     mixscr[:, 0:n], ALU.mult, ALU.add),
                 reads=[k_("nbuf", c), "mu1_t", "mixscr"], writes=[k_("mix", c)], tag="mixb")
        p.op("vector",
             lambda e, c=c: e.tensor_scalar_mul(mix[c][:, 0:1], nbuf[c][:, 0:1], mu_tile[:, c:c + 1]),
             reads=[k_("nbuf", c), mukey], writes=[k_("mix", c)], tag="mixc")

    def emit_proj(w_tile, w_key, src_key, post, tag):
        for oc in range(CH):
            for ts in range(4):
                bank = ps_mm[ts % 4]
                bkey = f"ps_mm{ts % 4}"
                sl = slice(ts * 512, (ts + 1) * 512)
                for kc in range(CH):
                    p.op("tensor",
                         lambda e, oc=oc, kc=kc, sl=sl, bank=bank, w_tile=w_tile: e.matmul(
                             bank[:, :],
                             w_tile[:, kc * C + oc * 128: kc * C + (oc + 1) * 128],
                             (mix if src_key == "mix" else nbuf)[kc][:, sl],
                             start=(kc == 0), stop=(kc == CH - 1)),
                         reads=[k_(w_key, kc), k_(src_key, kc)],
                         writes=[bkey], tag="mm_" + tag)
                post(oc, ts, sl, bank, bkey)

    def post_r(oc, ts, sl, bank, bkey):
        p.op("scalar",
             lambda e, bank=bank: e.activation(scr[:, :], bank[:, :], AF.Exp, scale=-1.0),
             reads=[bkey], writes=["scr"], tag="sig_e")
        p.op("scalar",
             lambda e: e.activation(scr2[:, :], scr[:, :], AF.Ln, bias=b["cb"][:, 1:2]),
             reads=["scr", "cb"], writes=["scr2"], tag="sig_l")
        p.op("scalar",
             lambda e, oc=oc, sl=sl: e.activation(r_t[oc][:, sl], scr2[:, :], AF.Exp, scale=-1.0),
             reads=["scr2"], writes=[k_("r", oc)], tag="sig_s")

    def post_k(oc, ts, sl, bank, bkey):
        p.op("scalar",
             lambda e, bank=bank: e.activation(scr[:, :], bank[:, :], AF.Relu),
             reads=[bkey], writes=["scr"], tag="k_relu")
        p.op("vector",
             lambda e, oc=oc, sl=sl: e.tensor_mul(nbuf[oc][:, sl], scr[:, :], scr[:, :]),
             reads=["scr"], writes=[k_("nbuf", oc)], tag="k_sq")

    load_weight_bf16(ctx, wrT, wbuf, "wbuf", C, C, tag="wr")
    for c in range(CH):
        emit_mix(mur_t, "mur_t", 0, c)
    emit_proj(wbuf, "wbuf", "mix", post_r, "r")
    load_weight_bf16(ctx, wkT, wbuf, "wbuf", C, C, tag="wk")
    for c in range(CH):
        emit_mix(muk_t, "muk_t", CH, c)
    emit_proj(wbuf, "wbuf", "mix", post_k, "k")
    load_weight_bf16(ctx, wvT, wbuf, "wbuf", C, C, tag="wv")
    for oc in range(CH):
        for ts in range(4):
            bank = ps_mm[ts % 4]
            bkey = f"ps_mm{ts % 4}"
            sl = slice(ts * 512, (ts + 1) * 512)
            for kc in range(CH):
                p.op("tensor",
                     lambda e, oc=oc, kc=kc, sl=sl, bank=bank: e.matmul(
                         bank[:, :],
                         wbuf[:, kc * C + oc * 128: kc * C + (oc + 1) * 128],
                         nbuf[kc][:, sl],
                         start=(kc == 0), stop=(kc == CH - 1)),
                     reads=[k_("wbuf", kc), k_("nbuf", kc)],
                     writes=[bkey], tag="mm_v")
            p.op("vector",
                 lambda e, oc=oc, sl=sl, bank=bank: e.tensor_mul(
                     scr2[:, :], r_t[oc][:, sl], bank[:, :]),
                 reads=[k_("r", oc), bkey], writes=["scr2"], tag="v_mul")
            p.op("vector",
                 lambda e, oc=oc, sl=sl: e.tensor_add(
                     b["x"][oc][:, sl], b["x"][oc][:, sl], scr2[:, :]),
                 reads=["scr2", k_("x", oc)], writes=[k_("x", oc)], tag="v_add")

    ctx.resid_out()
    return p.build()


def _ctx_ln_scratch(self):
    p, b = self.p, self.b
    b["xsq"] = p.sb("xsq", [128, 512], F32)
    b["rowA"] = p.sb("rowA", [1, T], F32)
    b["rowB"] = p.sb("rowB", [1, T], F32)
    b["rrow_bf"] = p.sb("rrow_bf", [1, T], BF16)
    b["scr"] = p.sb("scr", [128, 512], F32)
    b["ps_sum"] = p.ps("ps_sum", [128, 512], F32)
    b["ps_sq"] = p.ps("ps_sq", [128, 512], F32)
    b["ps_bM"] = p.ps("ps_bM", [128, 512], F32)
    b["ps_bR"] = p.ps("ps_bR", [128, 512], F32)


Ctx.ln_scratch = _ctx_ln_scratch




def build_stage_l1ffn():
    INNER = 4096
    ctx = Ctx("l1ffn")
    p, b = ctx.p, ctx.b
    cb = b["cb"]
    ones_f, ones_k1 = b["ones_f"], b["ones_k1"]

    xin = p.dram("xin", [T, C], F32)
    xout = p.dram("xout", [T, C], F32, kind="ExternalOutput")
    wgT = p.dram("w_gateT", [C, INNER], F32)
    wuT = p.dram("w_upT", [C, INNER], F32)
    wdT = p.dram("w_downT", [INNER, C], F32)
    ident = p.dram("ident", [128, 128], BF16)
    hspill = p.dram("hspill", [T, INNER], F32, kind="Internal")

    idt = p.sb("idt", [128, 128], BF16)
    p.op("sync", lambda e: e.dma_start(out=idt[:, :], in_=ident[:, :]),
         writes=["idt"], dma=True)

    wdq = p.sb("wdq", [128, 32 * C], BF16)
    xqf = [p.sb(f"xqf_{c}", [128, T], BF16) for c in range(CH)]
    h32 = [p.sb("h32_0", [128, INNER], F32)]
    wst = [p.sb(f"wst_{i}", [128, 2048], F32) for i in range(2)]
    wqg = p.sb("wqg", [128, CH * 512], BF16)
    wqu = p.sb("wqu", [128, CH * 512], BF16)
    hfeat = p.sb("hfeat", [128, 32 * 128], BF16)
    xres = [p.sb(f"xres_{i}", [128, C], F32) for i in range(2)]
    xtok = [p.sb(f"xtok_{i}", [128, C], F32) for i in range(2)]
    xnf = p.sb("xnf", [128, C], F32)
    xqi = p.sb("xqi", [128, C], I32)
    xqb = p.sb("xqb", [128, C], BF16)
    wqi = p.sb("wqi", [128, 1024], I32)
    rscr = p.sb("rscr", [128, 1536], F32)
    g32 = p.sb("g32", [128, 512], F32)
    u32 = p.sb("u32", [128, 512], F32)
    sg32 = p.sb("sg32", [128, 512], F32)
    hcur = [p.sb(f"hcur_{i}", [128, 512], F32) for i in range(2)]
    hqb = p.sb("hqb", [128, INNER], BF16)
    rows = p.sb("rows", [128, 24], F32)
    hmax = p.sb("hmax", [128, NT], F32)
    rst_all = p.sb("rst_all", [128, NT], F32)
    mrow = p.sb("mrow", [128, 4], F32)
    tiny = p.sb("tiny", [1, 8], F32)
    wsum = p.sb("wsum", [128, 4], F32)

    ps_mm = [p.ps(f"ps_mm{i}", [128, 512], F32) for i in range(4)]
    ps_tr = [p.ps("ps_tr0", [128, 512], BF16), p.ps("ps_tr1", [128, 512], BF16)]
    ps_sc = p.ps("ps_sc", [128, 512], F32)

    def prepass2(dram, rows_, cols_, col, subw):
        p.op("vector", lambda e, col=col: e.memset(wsum[:, col:col + 1], 0.0),
             writes=["wsum"], tag="wz")
        nch = rows_ // 128
        nsub = cols_ // subw
        idx = 0
        for cc in range(nch):
            for s in range(nsub):
                st = wst[idx % 2]
                stk = f"wst:{idx % 2}"
                idx += 1
                p.op("gpsimd",
                     lambda e, cc=cc, s=s, st=st, dram=dram, subw=subw: e.dma_start(
                         out=st[:, 0:subw],
                         in_=dram[cc * 128:(cc + 1) * 128, s * subw:(s + 1) * subw]),
                     writes=[stk], dma=True, tag="wpre_ld")
                p.op("vector",
                     lambda e, st=st, subw=subw: e.tensor_reduce(
                         wsum[:, 3:4], st[:, 0:subw], AX.X, ALU.add,
                         apply_absolute_value=True),
                     reads=[stk], writes=["wsum"], tag="wpre_red")
                p.op("vector",
                     lambda e, col=col: e.tensor_add(wsum[:, col:col + 1],
                                                     wsum[:, col:col + 1], wsum[:, 3:4]),
                     reads=["wsum"], writes=["wsum"], tag="wpre_acc")
        p.op("tensor",
             lambda e, col=col: e.matmul(ps_sc[0:1, 0:1], wsum[:, col:col + 1],
                                         ones_f[:, 0:1], start=True, stop=True),
             reads=["wsum", "ones_f"], writes=["ps_sc"], tag="wpre_tot")
        p.op("vector",
             lambda e, col=col, rows_=rows_, cols_=cols_: e.tensor_scalar(
                 tiny[0:1, col:col + 1], ps_sc[0:1, 0:1], 1.0 / (rows_ * cols_), 1e-5,
                 ALU.mult, ALU.max),
             reads=["ps_sc"], writes=["tiny"], tag="wpre_mean")

    prepass2(wgT, C, INNER, 0, 2048)
    prepass2(wuT, C, INNER, 1, 2048)
    prepass2(wdT, INNER, C, 2, 1024)
    p.op("vector", lambda e: e.reciprocal(tiny[0:1, 4:7], tiny[0:1, 0:3]),
         reads=["tiny"], writes=["tiny"], tag="wpre_recip")
    for j in range(3):
        p.op("tensor",
             lambda e, j=j: e.matmul(ps_sc[:, 0:1], ones_k1[:, :], tiny[0:1, j:j + 1],
                                     start=True, stop=True),
             reads=["tiny", "ones_k1"], writes=["ps_sc"], tag="minv_b")
        p.op("vector",
             lambda e, j=j: e.tensor_copy(mrow[:, j:j + 1], ps_sc[:, 0:1]),
             reads=["ps_sc"], writes=["mrow"], tag="minv_c")

    p.op("tensor",
         lambda e: e.matmul(ps_sc[:, 0:1], ones_k1[:, :], tiny[0:1, 6:7],
                            start=True, stop=True),
         reads=["tiny", "ones_k1"], writes=["ps_sc"], tag="wsd_b")
    p.op("vector",
         lambda e: e.tensor_copy(mrow[:, 3:4], ps_sc[:, 0:1]),
         reads=["ps_sc"], writes=["mrow"], tag="wsd_c")
    for cc in range(32):
        st = wst[cc % 2]
        stk = f"wst:{cc % 2}"
        p.op("gpsimd",
             lambda e, cc=cc, st=st: e.dma_start(
                 out=st[:, 0:1024], in_=wdT[cc * 128:(cc + 1) * 128, :]),
             writes=[stk], dma=True, tag="wd_ld")
        p.op("vector",
             lambda e, st=st: e.tensor_scalar(st[:, 1024:2048], st[:, 0:1024],
                                              mrow[:, 3:4], 0.5, ALU.mult, ALU.is_ge),
             reads=[stk, "mrow"], writes=[stk], tag="wd_q1")
        p.op("vector",
             lambda e, st=st: e.tensor_scalar(st[:, 0:1024], st[:, 0:1024],
                                              mrow[:, 3:4], -0.5, ALU.mult, ALU.is_le),
             reads=[stk, "mrow"], writes=[stk], tag="wd_q2")
        p.op("vector",
             lambda e, cc=cc, st=st: e.tensor_sub(
                 wdq[:, cc * C:(cc + 1) * C], st[:, 1024:2048], st[:, 0:1024]),
             reads=[stk], writes=[k_("wdq", cc)], tag="wd_q3")

    for t in range(NT):
        xt = xtok[t % 2]
        xtk = f"xtok:{t % 2}"
        p.op("sync",
             lambda e, t=t, xt=xt: e.dma_start(out=xt[:, :], in_=xin[t * 128:(t + 1) * 128, :]),
             writes=[xtk], dma=True, tag="ln_ldx")
        p.op("vector",
             lambda e, xt=xt: e.tensor_reduce(rows[:, 0:1], xt[:, :], AX.X, ALU.add),
             reads=[xtk], writes=["rows"], tag="ln_sum")
        p.op("scalar",
             lambda e, xt=xt: e.activation(xnf[:, :], xt[:, :], AF.Square),
             reads=[xtk], writes=["xnf"], tag="ln_sq")
        p.op("vector",
             lambda e: e.tensor_reduce(rows[:, 1:2], xnf[:, :], AX.X, ALU.add),
             reads=["xnf"], writes=["rows"], tag="ln_sumsq")
        p.op("vector",
             lambda e: e.tensor_scalar_mul(rows[:, 2:3], rows[:, 0:1], 1.0 / C),
             reads=["rows"], writes=["rows"], tag="ln_mean")
        p.op("vector",
             lambda e: e.tensor_mul(rows[:, 3:4], rows[:, 2:3], rows[:, 2:3]),
             reads=["rows"], writes=["rows"], tag="ln_m2")
        p.op("vector",
             lambda e: e.scalar_tensor_tensor(rows[:, 3:4], rows[:, 1:2], 1.0 / C,
                                              rows[:, 3:4], ALU.mult, ALU.subtract),
             reads=["rows"], writes=["rows"], tag="ln_var")
        p.op("scalar",
             lambda e: e.activation(rows[:, 4:5], rows[:, 3:4], AF.Ln, bias=cb[:, 0:1]),
             reads=["rows", "cb"], writes=["rows"], tag="ln_ln")
        p.op("scalar",
             lambda e: e.activation(rows[:, 4:5], rows[:, 4:5], AF.Exp, scale=-0.5),
             reads=["rows"], writes=["rows"], tag="ln_exp")
        p.op("vector",
             lambda e, xt=xt: e.tensor_scalar(xnf[:, :], xt[:, :], rows[:, 2:3],
                                              rows[:, 4:5], ALU.subtract, ALU.mult),
             reads=[xtk, "rows"], writes=["xnf"], tag="ln_nrm")
        p.op("vector",
             lambda e: e.tensor_reduce(rows[:, 5:6], xnf[:, :], AX.X, ALU.max,
                                       apply_absolute_value=True),
             reads=["xnf"], writes=["rows"], tag="aq_max")
        p.op("vector",
             lambda e: e.tensor_scalar_max(rows[:, 5:6], rows[:, 5:6], 1e-5),
             reads=["rows"], writes=["rows"], tag="aq_clip")
        p.op("vector",
             lambda e: e.reciprocal(rows[:, 6:7], rows[:, 5:6]),
             reads=["rows"], writes=["rows"], tag="aq_rec")
        p.op("vector",
             lambda e: e.tensor_scalar_mul(rows[:, 6:7], rows[:, 6:7], 127.0),
             reads=["rows"], writes=["rows"], tag="aq_st")
        p.op("vector",
             lambda e, t=t: e.tensor_scalar_mul(rst_all[:, t:t + 1], rows[:, 5:6], 1.0 / 127.0),
             reads=["rows"], writes=["rst_all"], tag="aq_rst")
        p.op("vector",
             lambda e: e.tensor_scalar_mul(xqi[:, :], xnf[:, :], rows[:, 6:7]),
             reads=["xnf", "rows"], writes=["xqi"], tag="aq_q")
        for qs in range(2):
            qsl = slice(qs * 512, (qs + 1) * 512)
            p.op("vector",
                 lambda e, qsl=qsl: e.scalar_tensor_tensor(
                     rscr[:, 0:512], xnf[:, qsl], rows[:, 6:7], xqi[:, qsl],
                     ALU.mult, ALU.subtract),
                 reads=["xnf", "rows", "xqi"], writes=["rscr"], tag="aq_d")
            p.op("vector",
                 lambda e: e.tensor_single_scalar(rscr[:, 512:1024], rscr[:, 0:512],
                                                  0.5, ALU.is_ge),
                 reads=["rscr"], writes=["rscr"], tag="aq_g1")
            p.op("gpsimd",
                 lambda e: e.tensor_single_scalar(rscr[:, 1024:1536], rscr[:, 0:512],
                                                  -0.5, ALU.is_le),
                 reads=["rscr"], writes=["rscr"], tag="aq_g2")
            p.op("vector",
                 lambda e: e.tensor_sub(rscr[:, 512:1024], rscr[:, 512:1024],
                                        rscr[:, 1024:1536]),
                 reads=["rscr"], writes=["rscr"], tag="aq_gg")
            p.op("vector",
                 lambda e, qsl=qsl: e.tensor_add(xqb[:, qsl], xqi[:, qsl],
                                                 rscr[:, 512:1024]),
                 reads=["xqi", "rscr"], writes=["xqb"], tag="aq_bf")
        for cc in range(CH):
            half = cc % 2
            p.op("tensor",
                 lambda e, cc=cc, half=half: e.transpose(
                     ps_tr[half][:, 0:128],
                     xqb[:, cc * 128:(cc + 1) * 128], idt[:, :]),
                 reads=["xqb", "idt"], writes=[f"ps_tr:{half}"], tag="tr_x")
            p.op("vector",
                 lambda e, cc=cc, t=t, half=half: e.tensor_copy(
                     xqf[cc][:, t * 128:(t + 1) * 128],
                     ps_tr[half][:, 0:128]),
                 reads=[f"ps_tr:{half}"], writes=[k_("xqf", cc)], tag="tr_ev")

    NO = INNER // 512
    for o in range(NO):
        for (dram, dstt, dstk, mcol) in ((wgT, wqg, "wqg", 0), (wuT, wqu, "wqu", 1)):
            p.op("tensor",
                 lambda e, mcol=mcol: e.matmul(ps_sc[:, 0:1], ones_k1[:, :],
                                               tiny[0:1, 4 + mcol:5 + mcol],
                                               start=True, stop=True),
                 reads=["tiny", "ones_k1"], writes=["ps_sc"], tag="ws_b")
            p.op("vector",
                 lambda e: e.tensor_copy(mrow[:, 3:4], ps_sc[:, 0:1]),
                 reads=["ps_sc"], writes=["mrow"], tag="ws_c")
            for cc in range(CH):
                st = wst[cc % 2]
                stk = f"wst:{cc % 2}"
                p.op("gpsimd",
                     lambda e, cc=cc, o=o, st=st, dram=dram: e.dma_start(
                         out=st[:, 0:512],
                         in_=dram[cc * 128:(cc + 1) * 128, o * 512:(o + 1) * 512]),
                     writes=[stk], dma=True, tag="wgu_ld")
                p.op("vector",
                     lambda e, st=st: e.tensor_scalar(st[:, 512:1024], st[:, 0:512],
                                                      mrow[:, 3:4], 0.5, ALU.mult, ALU.is_ge),
                     reads=[stk, "mrow"], writes=[stk], tag="wgu_q1")
                p.op("vector",
                     lambda e, st=st: e.tensor_scalar(
                         st[:, 0:512], st[:, 0:512],
                         mrow[:, 3:4], -0.5, ALU.mult, ALU.is_le),
                     reads=[stk, "mrow"], writes=[stk], tag="wgu_q2")
                p.op("vector",
                     lambda e, cc=cc, st=st, dstt=dstt, dstk=dstk: e.tensor_sub(
                         dstt[:, cc * 512:(cc + 1) * 512], st[:, 512:1024],
                         st[:, 0:512]),
                     reads=[stk], writes=[k_(dstk, cc)], tag="wgu_q3")
        for t in range(NT):
            bg = ps_mm[0]
            bu = ps_mm[1]
            for cc in range(CH):
                p.op("tensor",
                     lambda e, cc=cc, t=t, bg=bg: e.matmul(
                         bg[:, :], xqf[cc][:, t * 128:(t + 1) * 128],
                         wqg[:, cc * 512:(cc + 1) * 512],
                         start=(cc == 0), stop=(cc == CH - 1)),
                     reads=[k_("xqf", cc), k_("wqg", cc)], writes=["ps_mm0"], tag="mm_g")
            for cc in range(CH):
                p.op("tensor",
                     lambda e, cc=cc, t=t, bu=bu: e.matmul(
                         bu[:, :], xqf[cc][:, t * 128:(t + 1) * 128],
                         wqu[:, cc * 512:(cc + 1) * 512],
                         start=(cc == 0), stop=(cc == CH - 1)),
                     reads=[k_("xqf", cc), k_("wqu", cc)], writes=["ps_mm1"], tag="mm_u")
            p.op("vector",
                 lambda e, t=t: e.tensor_mul(rows[:, 8:9], rst_all[:, t:t + 1], mrow[:, 0:1]),
                 reads=["rst_all", "mrow"], writes=["rows"], tag="dsc_g")
            p.op("vector",
                 lambda e, t=t: e.tensor_mul(rows[:, 9:10], rst_all[:, t:t + 1], mrow[:, 1:2]),
                 reads=["rst_all", "mrow"], writes=["rows"], tag="dsc_u")
            p.op("vector",
                 lambda e: e.tensor_scalar_mul(g32[:, :], ps_mm[0][:, :], rows[:, 8:9]),
                 reads=["ps_mm0", "rows"], writes=["g32"], tag="deq_g")
            p.op("vector",
                 lambda e: e.tensor_scalar_mul(u32[:, :], ps_mm[1][:, :], rows[:, 9:10]),
                 reads=["ps_mm1", "rows"], writes=["u32"], tag="deq_u")
            p.op("scalar",
                 lambda e: e.activation(sg32[:, :], g32[:, :], AF.Exp, scale=-1.0),
                 reads=["g32"], writes=["sg32"], tag="sil_e")
            p.op("scalar",
                 lambda e: e.activation(sg32[:, :], sg32[:, :], AF.Ln, bias=cb[:, 1:2]),
                 reads=["sg32", "cb"], writes=["sg32"], tag="sil_l")
            p.op("scalar",
                 lambda e: e.activation(sg32[:, :], sg32[:, :], AF.Exp, scale=-1.0),
                 reads=["sg32"], writes=["sg32"], tag="sil_s")
            hc = hcur[t % 2]
            hck = f"hcur:{t % 2}"
            p.op("vector",
                 lambda e, hc=hc: e.tensor_mul(hc[:, :], g32[:, :], sg32[:, :]),
                 reads=["g32", "sg32"], writes=[hck], tag="h_m1")
            p.op("vector",
                 lambda e, hc=hc: e.tensor_mul(hc[:, :], hc[:, :], u32[:, :]),
                 reads=["u32", hck], writes=[hck], tag="h_m2")
            p.op("vector",
                 lambda e, hc=hc: e.tensor_reduce(rows[:, 10:11], hc[:, :], AX.X,
                                                  ALU.max, apply_absolute_value=True),
                 reads=[hck], writes=["rows"], tag="h_max")
            if o == 0:
                p.op("vector",
                     lambda e, t=t: e.tensor_copy(hmax[:, t:t + 1], rows[:, 10:11]),
                     reads=["rows"], writes=["hmax"], tag="h_max0")
            else:
                p.op("vector",
                     lambda e, t=t: e.tensor_max(hmax[:, t:t + 1], hmax[:, t:t + 1],
                                                 rows[:, 10:11]),
                     reads=["rows", "hmax"], writes=["hmax"], tag="h_maxa")
            p.op("sync",
                 lambda e, t=t, o=o, hc=hc: e.dma_start(
                     out=hspill[t * 128:(t + 1) * 128, o * 512:(o + 1) * 512],
                     in_=hc[:, :]),
                 reads=[hck], dma=True, tag="h_sp")

    for t in range(NT):
        ht = h32[0]
        htk = "h32:0"
        p.op("sync",
             lambda e, t=t, ht=ht: e.dma_start(
                 out=ht[:, :], in_=hspill[t * 128:(t + 1) * 128, :]),
             writes=[htk], dma=True, tag="h_rl")
        xr = xres[t % 2]
        xrk = f"xres:{t % 2}"
        p.op("sync",
             lambda e, t=t, xr=xr: e.dma_start(out=xr[:, :], in_=xin[t * 128:(t + 1) * 128, :]),
             writes=[xrk], dma=True, tag="x_rl")
        p.op("vector",
             lambda e, t=t: e.tensor_scalar_max(rows[:, 11:12], hmax[:, t:t + 1], 1e-5),
             reads=["hmax"], writes=["rows"], tag="h_clip")
        p.op("vector",
             lambda e: e.reciprocal(rows[:, 12:13], rows[:, 11:12]),
             reads=["rows"], writes=["rows"], tag="h_rec")
        p.op("vector",
             lambda e: e.tensor_scalar_mul(rows[:, 12:13], rows[:, 12:13], 127.0),
             reads=["rows"], writes=["rows"], tag="h_s2")
        p.op("vector",
             lambda e: e.scalar_tensor_tensor(rows[:, 13:14], rows[:, 11:12], 1.0 / 127.0,
                                              mrow[:, 2:3], ALU.mult, ALU.mult),
             reads=["rows", "mrow"], writes=["rows"], tag="h_dscd")
        for hs in range(INNER // 512):
            hsl = slice(hs * 512, (hs + 1) * 512)
            p.op("vector",
                 lambda e, ht=ht, hsl=hsl: e.tensor_scalar_mul(
                     wqi[:, 0:512], ht[:, hsl], rows[:, 12:13]),
                 reads=[htk, "rows"], writes=["wqi"], tag="h_q")
            p.op("vector",
                 lambda e, ht=ht, hsl=hsl: e.scalar_tensor_tensor(
                     rscr[:, 0:512], ht[:, hsl], rows[:, 12:13], wqi[:, 0:512],
                     ALU.mult, ALU.subtract),
                 reads=[htk, "rows", "wqi"], writes=["rscr"], tag="h_d")
            p.op("vector",
                 lambda e: e.tensor_single_scalar(rscr[:, 512:1024], rscr[:, 0:512],
                                                  0.5, ALU.is_ge),
                 reads=["rscr"], writes=["rscr"], tag="h_g1")
            p.op("gpsimd",
                 lambda e: e.tensor_single_scalar(rscr[:, 1024:1536], rscr[:, 0:512],
                                                  -0.5, ALU.is_le),
                 reads=["rscr"], writes=["rscr"], tag="h_g2")
            p.op("vector",
                 lambda e: e.tensor_sub(rscr[:, 512:1024], rscr[:, 512:1024],
                                        rscr[:, 1024:1536]),
                 reads=["rscr"], writes=["rscr"], tag="h_gg")
            p.op("vector",
                 lambda e, hsl=hsl: e.tensor_add(hqb[:, hsl], wqi[:, 0:512],
                                                 rscr[:, 512:1024]),
                 reads=["wqi", "rscr"], writes=["hqb"], tag="h_bf")
        for cc in range(32):
            half = cc % 2
            p.op("tensor",
                 lambda e, cc=cc, half=half: e.transpose(
                     ps_tr[half][:, 0:128],
                     hqb[:, cc * 128:(cc + 1) * 128], idt[:, :]),
                 reads=["hqb", "idt"], writes=[f"ps_tr:{half}"], tag="tr_h")
            p.op("vector",
                 lambda e, cc=cc, half=half: e.tensor_copy(
                     hfeat[:, cc * 128:(cc + 1) * 128],
                     ps_tr[half][:, 0:128]),
                 reads=[f"ps_tr:{half}"], writes=[k_("hfeat", cc)], tag="tr_hev")
        for o2 in range(2):
            bank = ps_mm[2 + o2]
            bkey = f"ps_mm{2 + o2}"
            for cc in range(32):
                p.op("tensor",
                     lambda e, cc=cc, o2=o2, bank=bank: e.matmul(
                         bank[:, :], hfeat[:, cc * 128:(cc + 1) * 128],
                         wdq[:, cc * C + o2 * 512: cc * C + (o2 + 1) * 512],
                         start=(cc == 0), stop=(cc == 31)),
                     reads=[k_("hfeat", cc), k_("wdq", cc)], writes=[bkey], tag="mm_d")
            p.op("vector",
                 lambda e, o2=o2, xr=xr, bank=bank: e.scalar_tensor_tensor(
                     xr[:, o2 * 512:(o2 + 1) * 512], bank[:, :], rows[:, 13:14],
                     xr[:, o2 * 512:(o2 + 1) * 512], ALU.mult, ALU.add),
                 reads=[bkey, "rows", xrk], writes=[xrk], tag="d_add")
        p.op("sync",
             lambda e, t=t, xr=xr: e.dma_start(
                 out=xout[t * 128:(t + 1) * 128, :], in_=xr[:, :]),
             reads=[xrk], dma=True, tag="d_st")

    return p.build()




def build_stage_l2attn():
    H, D = 16, 64
    ctx = Ctx("l2attn")
    p, b = ctx.p, ctx.b
    cb = b["cb"]
    ones_f, ones_k1, ones_k1b = b["ones_f"], b["ones_k1"], b["ones_k1b"]

    xin = p.dram("xin", [C, T], F32)
    xout = p.dram("xout", [C, T], F32, kind="ExternalOutput")
    qkvT = p.dram("qkvT", [C, 3 * C], F32)
    outT = p.dram("outT", [C, C], F32)
    qb2 = p.dram("qb2", [64, H], F32)
    kb2 = p.dram("kb2", [64, H], F32)
    vb2 = p.dram("vb2", [128, H], F32)
    ob2 = p.dram("ob2", [128, CH], F32)
    qac = p.dram("qac", [H, 6, T], BF16)
    kacA = p.dram("kacA", [H, 7, T], BF16)
    kacB = p.dram("kacB", [H, 7, T], BF16)
    diagp = p.dram("diagp", [4, 128, 512], F32)

    xrot = [p.sb(f"xrot_{i}", [128, T], F32) for i in range(2)]
    xn = [p.sb(f"xn_{c}", [128, T], BF16) for c in range(CH)]
    wq = p.sb("wq", [128, CH * 3072], BF16)
    wo = wq
    v65 = [p.sb(f"v65_{t}", [128, H * 65], BF16) for t in range(NT)]
    qaug = p.sb("qaug", [71, T], BF16)
    kaugA = p.sb("kaugA", [71, T], BF16)
    kaugB = p.sb("kaugB", [71, T], BF16)
    pt = p.sb("pt", [128, T], BF16)
    attnres = [p.sb(f"attnres_{i}", [128, T], BF16) for i in range(H // 2)]
    qsq = p.sb("qsq", [64, 512], BF16)
    brow = p.sb("brow", [1, T], F32)
    browb = p.sb("browb", [1, T], BF16)
    nrow = p.sb("nrow", [128, 512], F32)
    Bsb = p.sb("Bsb", [128, 512], F32)
    bias_q = p.sb("bias_q", [64, H], F32)
    bias_k = p.sb("bias_k", [64, H], F32)
    bias_v = p.sb("bias_v", [128, H], F32)
    bias_o = p.sb("bias_o", [128, CH], F32)
    diag_t = p.sb("diag_t", [128, 4 * 512], F32)
    scr = p.sb("scr", [128, 512], F32)
    scr2 = p.sb("scr2", [128, 512], F32)
    tinyr = p.sb("tinyr", [1, 8], F32)

    ps_s = [p.ps(f"ps_s{i}", [128, 512], F32) for i in range(4)]
    ps_a = [p.ps(f"ps_a{i}", [128, 512], F32) for i in range(4)]
    if _DBG_L2 is not None:
        dbg_dram = p.dram("dbg_sc", [128, 512], F32, kind="ExternalOutput")
        dbg2_dram = p.dram("dbg_sc2", [128, 512], F32, kind="ExternalOutput")
    ones64b = p.sb("ones64b", [64, 1], BF16)
    p.op("vector", lambda e: e.memset(ones64b[:, :], 1.0), writes=["ones64b"])
    onesr = p.sb("onesr", [128, 128], F32)
    p.op("vector", lambda e: e.memset(onesr[:, :], 1.0), writes=["onesr"])

    p.op("sync", lambda e: e.dma_start(out=bias_q[:, :], in_=qb2[:, :]), writes=["bias_q"], dma=True)
    p.op("sync", lambda e: e.dma_start(out=bias_k[:, :], in_=kb2[:, :]), writes=["bias_k"], dma=True)
    p.op("sync", lambda e: e.dma_start(out=bias_v[:, :], in_=vb2[:, :]), writes=["bias_v"], dma=True)
    p.op("sync", lambda e: e.dma_start(out=bias_o[:, :], in_=ob2[:, :]), writes=["bias_o"], dma=True)
    for r in range(4):
        p.op("sync", lambda e, r=r: e.dma_start(out=diag_t[:, r * 512:(r + 1) * 512], in_=diagp[r, :, :]),
             writes=["diag_t"], dma=True)
    for c in range(CH):
        p.op("gpsimd",
             lambda e, c=c: e.dma_start(out=wq[:, c * 3072:(c + 1) * 3072],
                                        in_=qkvT[c * 128:(c + 1) * 128, :]),
             writes=[k_("wq", c)], dma=True, tag="wqkv")


    for c in range(CH):
        xr = xrot[c % 2]
        xk = f"xrot:{c % 2}"
        p.op("sync", lambda e, c=c, xr=xr: e.dma_start(out=xr[:, :], in_=xin[c * 128:(c + 1) * 128, :]),
             writes=[xk], dma=True, tag="lnx")
        for s in range(4):
            sl = slice(s * 512, (s + 1) * 512)
            p.op("tensor",
                 lambda e, s=s, sl=sl, c=c, xr=xr: e.matmul(
                     ps_s[s][0:1, :], ones_f[:, 0:1], xr[:, sl],
                     start=(c == 0), stop=(c == CH - 1)),
                 reads=[xk, "ones_f"], writes=[f"ps_s{s}"], tag="ln_mm_s")
            p.op("scalar", lambda e, sl=sl, xr=xr: e.activation(scr[:, :], xr[:, sl], AF.Square),
                 reads=[xk], writes=["scr"], tag="ln_sq")
            p.op("tensor",
                 lambda e, s=s, c=c: e.matmul(
                     ps_s[s][32:33, :], ones_f[:, 0:1], scr[:, :],
                     start=(c == 0), stop=(c == CH - 1)),
                 reads=["scr", "ones_f"], writes=[f"ps_s{s}"], tag="ln_mm_q")
    for s in range(4):
        sl = slice(s * 512, (s + 1) * 512)
        p.op("vector", lambda e, s=s, sl=sl: e.tensor_scalar_mul(brow[0:1, sl], ps_s[s][0:1, :], 1.0 / C),
             reads=[f"ps_s{s}"], writes=["brow"], tag="ln_mean")
        p.op("vector", lambda e, sl=sl: e.tensor_mul(scr2[0:1, :], brow[0:1, sl], brow[0:1, sl]),
             reads=["brow"], writes=["scr2"], tag="ln_m2")
        p.op("vector",
             lambda e, s=s: e.scalar_tensor_tensor(scr[0:1, :], ps_s[s][32:33, :], 1.0 / C,
                                                   scr2[0:1, :], ALU.mult, ALU.subtract),
             reads=[f"ps_s{s}", "scr2"], writes=["scr"], tag="ln_var")
        p.op("scalar", lambda e: e.activation(scr[0:1, :], scr[0:1, :], AF.Ln, bias=cb[0:1, 0:1]),
             reads=["scr", "cb"], writes=["scr"], tag="ln_ln")
        p.op("scalar", lambda e: e.activation(scr[0:1, :], scr[0:1, :], AF.Exp, scale=-0.5),
             reads=["scr"], writes=["scr"], tag="ln_exp")
        p.op("vector", lambda e, sl=sl: e.tensor_copy(browb[0:1, sl], scr[0:1, :]),
             reads=["scr"], writes=["browb"], tag="ln_rbf")
    for c in range(CH):
        xr = xrot[c % 2]
        xk = f"xrot:{c % 2}"
        p.op("sync", lambda e, c=c, xr=xr: e.dma_start(out=xr[:, :], in_=xin[c * 128:(c + 1) * 128, :]),
             writes=[xk], dma=True, tag="lnx2")
        for s in range(4):
            sl = slice(s * 512, (s + 1) * 512)
            p.op("tensor",
                 lambda e, sl=sl: e.matmul(ps_s[2][:, :], ones_k1[:, :], brow[0:1, sl],
                                           start=True, stop=True),
                 reads=["brow", "ones_k1"], writes=["ps_s2"], tag="ln_bM")
            p.op("tensor",
                 lambda e, sl=sl: e.matmul(ps_s[3][:, :], ones_k1b[:, :], browb[0:1, sl],
                                           start=True, stop=True),
                 reads=["browb", "ones_k1b"], writes=["ps_s3"], tag="ln_bR")
            p.op("vector",
                 lambda e, sl=sl, xr=xr: e.tensor_sub(scr[:, :], xr[:, sl], ps_s[2][:, :]),
                 reads=[xk, "ps_s2"], writes=["scr"], tag="ln_sub")
            p.op("vector",
                 lambda e, c=c, sl=sl: e.tensor_mul(xn[c][:, sl], scr[:, :], ps_s[3][:, :]),
                 reads=["scr", "ps_s3"], writes=[k_("xn", c)], tag="ln_nrm")

    for t in range(NT):
        p.op("vector",
             lambda e, t=t: e.memset(
                 v65[t][:, :].rearrange("p (h u) -> p h u", u=65)[:, :, 64:65], 1.0),
             writes=[k_("v65", t)], tag="v_ones")
        for g in range(2):
            bank = ps_a[g]
            bkey = f"ps_a{g}"
            for c in range(CH):
                p.op("tensor",
                     lambda e, t=t, g=g, c=c, bank=bank: e.matmul(
                         bank[:, :],
                         xn[c][:, t * 128:(t + 1) * 128],
                         wq[:, c * 3072:(c + 1) * 3072].rearrange(
                             "p (h u) -> p h u", u=192)[:, g * 8:(g + 1) * 8, 128:192],
                         start=(c == 0), stop=(c == CH - 1)),
                     reads=[k_("xn", c), k_("wq", c)], writes=[bkey], tag="v_mm")
            p.op("vector",
                 lambda e, t=t, g=g, bank=bank: e.tensor_copy(
                     v65[t][:, :].rearrange("p (h u) -> p h u", u=65)[:, g * 8:(g + 1) * 8, 0:64],
                     bank[:, :].rearrange("p (h u) -> p h u", u=64)),
                 reads=[bkey], writes=[k_("v65", t)], tag="v_ev")

    for i in range(H // 2):
        p.op("vector", lambda e, i=i: e.memset(attnres[i][:, :], 0.0),
             writes=[k_("attnres", i)], tag="at_z")
    for h in (range(H) if _DBG_HEADS is None else _DBG_HEADS):
        slope = 0.5 ** h
        hq = h * 192
        par = h % 2
        at = attnres[h // 2]
        atk = k_("attnres", h // 2)

        for s in range(4):
            sl = slice(s * 512, (s + 1) * 512)
            for c in range(CH):
                p.op("tensor",
                     lambda e, s=s, c=c, sl=sl, hq=hq: e.matmul(
                         ps_a[0][0:64, :], wq[:, c * 3072 + hq: c * 3072 + hq + 64],
                         xn[c][:, sl], start=(c == 0), stop=(c == CH - 1)),
                     reads=[k_("wq", c), k_("xn", c)], writes=["ps_a0"], tag="q_mm")
            p.op("vector",
                 lambda e, s=s, sl=sl, h=h: e.tensor_scalar(
                     qaug[0:64, sl], ps_a[0][0:64, :], bias_q[:, h:h + 1], 0.125,
                     ALU.add, ALU.mult),
                 reads=["ps_a0", "bias_q"], writes=["qaug"], tag="q_ev")
            for c in range(CH):
                p.op("tensor",
                     lambda e, s=s, c=c, sl=sl, hq=hq: e.matmul(
                         ps_a[1][0:64, :], wq[:, c * 3072 + hq + 64: c * 3072 + hq + 128],
                         xn[c][:, sl], start=(c == 0), stop=(c == CH - 1)),
                     reads=[k_("wq", c), k_("xn", c)], writes=["ps_a1"], tag="k_mm")
            p.op("vector",
                 lambda e, s=s, sl=sl, h=h: e.tensor_scalar_add(
                     kaugA[0:64, sl], ps_a[1][0:64, :], bias_k[:, h:h + 1]),
                 reads=["ps_a1", "bias_k"], writes=["kaugA"], tag="k_evA")
            p.op("vector",
                 lambda e, sl=sl: e.tensor_copy(kaugB[0:64, sl], kaugA[0:64, sl]),
                 reads=["kaugA"], writes=["kaugB"], tag="k_evB")

        for (srct, srck) in ((kaugA, "kaugA"), (qaug, "qaug")):
            for s in range(4):
                sl = slice(s * 512, (s + 1) * 512)
                p.op("scalar",
                     lambda e, sl=sl, srct=srct: e.activation(qsq[0:64, 0:512], srct[0:64, sl], AF.Square),
                     reads=[srck], writes=["qsq"], tag="nsq")
                p.op("tensor",
                     lambda e: e.matmul(ps_a[2][0:1, :], ones64b[:, 0:1],
                                        qsq[0:64, 0:512], start=True, stop=True),
                     reads=["qsq", "ones64b"], writes=["ps_a2"], tag="nmm")
                p.op("vector",
                     lambda e, sl=sl: e.tensor_copy(brow[0:1, sl], ps_a[2][0:1, :]),
                     reads=["ps_a2"], writes=["brow"], tag="nev")
                if _DBG_L2 is not None and h == _DBG_L2[0] and s == 0 and srck == "kaugA":
                    p.op("vector",
                         lambda e: e.tensor_copy(Bsb[0:64, :], qsq[0:64, 0:512]),
                         reads=["qsq"], writes=["Bsb"], tag="dbg_qsq")
                    p.op("sync",
                         lambda e: e.dma_start(out=dbg_dram[64:128, 0:512], in_=Bsb[0:64, :]),
                         reads=["Bsb"], writes=["dbgout"], dma=True, tag="dbg_qsqd")
                    p.op("vector",
                         lambda e: e.tensor_copy(scr2[0:1, :], ps_a[2][0:1, :]),
                         reads=["ps_a2"], writes=["scr2"], tag="dbg_ps")
                    p.op("sync",
                         lambda e: e.dma_start(out=dbg_dram[6:7, 0:512], in_=scr2[0:1, :]),
                         reads=["scr2"], writes=["dbgout"], dma=True, tag="dbg_psd")
                    p.op("vector",
                         lambda e: e.tensor_copy(scr[0:32, 0:64], kaugA[0:32, 0:64]),
                         reads=["kaugA"], writes=["scr"], tag="dbg_kacp")
                    p.op("gpsimd",
                         lambda e: e.dma_start(out=dbg_dram[8:40, 0:64], in_=scr[0:32, 0:64]),
                         reads=["scr"], writes=["dbgout"], dma=True, tag="dbg_ka")
            if srck == "kaugA":
                p.op("vector",
                     lambda e: e.tensor_reduce(tinyr[0:1, 0:1], brow[0:1, :], AX.X, ALU.max),
                     reads=["brow"], writes=["tinyr"], tag="nk2max")
                if _DBG_L2 is not None and h == _DBG_L2[0]:
                    for ss in range(4):
                        p.op("sync",
                             lambda e, ss=ss: e.dma_start(
                                 out=dbg_dram[2 + ss:3 + ss, 0:512],
                                 in_=brow[0:1, ss * 512:(ss + 1) * 512]),
                             reads=["brow"], writes=["dbgout"], dma=True, tag="dbg_nk")
                    p.op("sync",
                         lambda e: e.dma_start(out=dbg_dram[1:2, 0:1], in_=tinyr[0:1, 0:1]),
                         reads=["tinyr"], writes=["dbgout"], dma=True, tag="dbg_ti")
        p.op("vector",
             lambda e: e.tensor_scalar_mul(brow[0:1, :], brow[0:1, :], tinyr[0:1, 0:1]),
             reads=["brow", "tinyr"], writes=["brow"], tag="bq_t")
        p.op("scalar",
             lambda e: e.activation(brow[0:1, :], brow[0:1, :], AF.Ln, bias=cb[0:1, 0:1]),
             reads=["brow", "cb"], writes=["brow"], tag="bq_ln")
        p.op("scalar",
             lambda e: e.activation(brow[0:1, :], brow[0:1, :], AF.Exp, scale=0.5),
             reads=["brow"], writes=["brow"], tag="bq_exp")
        p.op("vector",
             lambda e: e.tensor_scalar_mul(browb[0:1, :], brow[0:1, :], -1.0),
             reads=["brow"], writes=["browb"], tag="bq_neg")
        p.op("gpsimd",
             lambda e: e.dma_start(out=qaug[64:65, :], in_=browb[0:1, :]),
             reads=["browb"], writes=["qaug"], dma=True, tag="aug_b")
        p.op("sync",
             lambda e, h=h: e.dma_start(out=qaug[65:71, :], in_=qac[h, :, :]),
             writes=["qaug"], dma=True, tag="aug_q")
        p.op("sync",
             lambda e, h=h: e.dma_start(out=kaugA[64:71, :], in_=kacA[h, :, :]),
             writes=["kaugA"], dma=True, tag="aug_kA")
        p.op("sync",
             lambda e, h=h: e.dma_start(out=kaugB[64:71, :], in_=kacB[h, :, :]),
             writes=["kaugB"], dma=True, tag="aug_kB")

        for kt in range(NT):
            for g in range(4):
                if kt < 4 * g:
                    kv, kvk = kaugA, "kaugA"
                elif kt >= 4 * (g + 1):
                    kv, kvk = kaugB, "kaugB"
                else:
                    kv, kvk = kaugA, "kaugA"
                p.op("tensor",
                     lambda e, kt=kt, g=g, kv=kv: e.matmul(
                         ps_s[g][:, :], kv[0:71, kt * 128:(kt + 1) * 128],
                         qaug[0:71, g * 512:(g + 1) * 512], start=True, stop=True),
                     reads=[kvk, "qaug"], writes=[f"ps_s{g}"], tag="sc_mm")
                if 4 * g <= kt < 4 * (g + 1):
                    ri = kt - 4 * g
                    p.op("vector",
                         lambda e, g=g, ri=ri, slope=slope: e.scalar_tensor_tensor(
                             ps_s[g][:, :], diag_t[:, ri * 512:(ri + 1) * 512],
                             -2.0 * slope, ps_s[g][:, :], ALU.mult, ALU.add),
                         reads=["diag_t", f"ps_s{g}"], writes=[f"ps_s{g}"], tag="sc_diag")
                p.op("scalar",
                     lambda e, kt=kt, g=g: e.activation(
                         pt[:, g * 512:(g + 1) * 512], ps_s[g][:, :], AF.Exp),
                     reads=[f"ps_s{g}"], writes=["pt"], tag="sc_exp")
                if _DBG_L2 is not None and (h, kt, g) == _DBG_L2:
                    p.op("vector",
                         lambda e, g=g: e.tensor_copy(Bsb[:, :], ps_s[g][:, :]),
                         reads=[f"ps_s{g}"], writes=["Bsb"], tag="dbg_cp")
                    p.op("sync",
                         lambda e: e.dma_start(out=dbg2_dram[:, :], in_=Bsb[:, :]),
                         reads=["Bsb"], writes=["dbgout2"], dma=True, tag="dbg_st")
                if par == 0:
                    p.op("tensor",
                         lambda e, kt=kt, g=g, h=h: e.matmul(
                             ps_a[g][0:65, :], v65[kt][:, h * 65:(h + 1) * 65],
                             pt[:, g * 512:(g + 1) * 512],
                             start=(kt == 0), stop=(kt == NT - 1)),
                         reads=[k_("v65", kt), "pt"], writes=[f"ps_a{g}"], tag="av_mm")
                else:
                    p.op("tensor",
                         lambda e, kt=kt, g=g, h=h: e.matmul(
                             ps_a[g][64:128, :], v65[kt][:, h * 65:h * 65 + 64],
                             pt[:, g * 512:(g + 1) * 512],
                             start=(kt == 0), stop=(kt == NT - 1)),
                         reads=[k_("v65", kt), "pt"], writes=[f"ps_a{g}"], tag="av_mmo")
                    p.op("tensor",
                         lambda e, kt=kt, g=g, h=h: e.matmul(
                             ps_a[g][0:1, :], v65[kt][:, h * 65 + 64:h * 65 + 65],
                             pt[:, g * 512:(g + 1) * 512],
                             start=(kt == 0), stop=(kt == NT - 1)),
                         reads=[k_("v65", kt), "pt"], writes=[f"ps_a{g}"], tag="av_den")

        for g in range(4):
            gsl = slice(g * 512, (g + 1) * 512)
            if par == 0:
                p.op("vector",
                     lambda e, g=g: e.reciprocal(nrow[64:65, :], ps_a[g][64:65, :]),
                     reads=[f"ps_a{g}"], writes=["nrow"], tag="nm_rc")
                p.op("tensor",
                     lambda e: e.matmul(ps_s[0][0:64, :], onesr[64:65, 0:64], nrow[64:65, :],
                                        start=True, stop=True),
                     reads=["nrow", "onesr"], writes=["ps_s0"], tag="nm_bc")
                p.op("vector",
                     lambda e: e.tensor_copy(Bsb[0:64, :], ps_s[0][0:64, :]),
                     reads=["ps_s0"], writes=["Bsb"], tag="nm_ev")
                p.op("vector",
                     lambda e, g=g: e.tensor_mul(scr[0:64, :], ps_a[g][0:64, :], Bsb[0:64, :]),
                     reads=[f"ps_a{g}", "Bsb"], writes=["scr"], tag="nm_mul")
                p.op("vector",
                     lambda e, g=g, gsl=gsl, h=h, at=at: e.tensor_scalar_add(
                         at[0:64, gsl], scr[0:64, :], bias_v[0:64, h:h + 1]),
                     reads=["scr", "bias_v"], writes=[atk], tag="nm_out")
            else:
                p.op("vector",
                     lambda e, g=g: e.reciprocal(nrow[0:1, :], ps_a[g][0:1, :]),
                     reads=[f"ps_a{g}"], writes=["nrow"], tag="nm_rc")
                p.op("tensor",
                     lambda e: e.matmul(ps_s[0][64:128, :], onesr[0:1, 0:64], nrow[0:1, :],
                                        start=True, stop=True),
                     reads=["nrow", "onesr"], writes=["ps_s0"], tag="nm_bc")
                p.op("vector",
                     lambda e: e.tensor_copy(Bsb[64:128, :], ps_s[0][64:128, :]),
                     reads=["ps_s0"], writes=["Bsb"], tag="nm_ev")
                p.op("vector",
                     lambda e, g=g: e.tensor_mul(scr[64:128, :], ps_a[g][64:128, :], Bsb[64:128, :]),
                     reads=[f"ps_a{g}", "Bsb"], writes=["scr"], tag="nm_mul")
                p.op("vector",
                     lambda e, g=g, gsl=gsl, h=h, at=at: e.tensor_scalar_add(
                         at[64:128, gsl], scr[64:128, :], bias_v[64:128, h:h + 1]),
                     reads=["scr", "bias_v"], writes=[atk], tag="nm_out")

    for c in range(CH):
        wqk = k_("wq", (c * 1024) // 3072)
        wqk2 = k_("wq", ((c + 1) * 1024 - 1) // 3072)
        p.op("gpsimd",
             lambda e, c=c: e.dma_start(out=wo[:, c * 1024:(c + 1) * 1024],
                                        in_=outT[c * 128:(c + 1) * 128, :]),
             writes=list({wqk, wqk2, k_("wo", c)}), dma=True, tag="wout")
    for oc in range(CH):
        xr = xrot[oc % 2]
        xk = f"xrot:{oc % 2}"
        p.op("sync", lambda e, oc=oc, xr=xr: e.dma_start(out=xr[:, :], in_=xin[oc * 128:(oc + 1) * 128, :]),
             writes=[xk], dma=True, tag="ox_ld")
        p.op("vector", lambda e, xr=xr: e.tensor_copy(scr[:, 1:2], xr[:, 0:1]),
             reads=[xk], writes=["scr"], tag="gz_sv")
        for s in range(4):
            sl = slice(s * 512, (s + 1) * 512)
            bank = ps_a[s % 4]
            bkey = f"ps_a{s % 4}"
            for i in range(CH):
                p.op("tensor",
                     lambda e, oc=oc, i=i, sl=sl, bank=bank: e.matmul(
                         bank[:, :], wo[:, i * 1024 + oc * 128: i * 1024 + (oc + 1) * 128],
                         attnres[i][:, sl], start=(i == 0), stop=(i == CH - 1)),
                     reads=[k_("wo", i), k_("attnres", i)], writes=[bkey], tag="o_mm")
            p.op("vector",
                 lambda e, oc=oc, sl=sl, bank=bank, xr=xr: e.scalar_tensor_tensor(
                     xr[:, sl], bank[:, :], bias_o[:, oc:oc + 1], xr[:, sl],
                     ALU.add, ALU.add),
                 reads=[bkey, "bias_o", xk], writes=[xk], tag="o_add")
        p.op("sync",
             lambda e, oc=oc, xr=xr: e.dma_start(out=xout[oc * 128:(oc + 1) * 128, :], in_=xr[:, :]),
             reads=[xk], dma=True, tag="ox_st")

    return p.build()




def build_stage_l1attn():
    H, D = 16, 64
    ctx = Ctx("l1attn")
    p, b = ctx.p, ctx.b
    cb = b["cb"]
    ones_f, ones_k1 = b["ones_f"], b["ones_k1"]

    xin = p.dram("xin", [C, T], F32)
    xout = p.dram("xout", [C, T], F32, kind="ExternalOutput")
    wqd = p.dram("wq", [C, C], F32)
    wkd = p.dram("wk", [C, C], F32)
    wvd = p.dram("wv", [C, C], F32)
    wod = p.dram("wo", [C, C], F32)
    qb2 = p.dram("qb2", [64, H], F32)
    kb2 = p.dram("kb2", [64, H], F32)
    vb2 = p.dram("vb2", [128, H], F32)
    ob2 = p.dram("ob2", [128, CH], F32)
    masks = p.dram("masks", [6, 128, 512], BF16)

    xrot = [p.sb(f"xrot_{i}", [128, T], F32) for i in range(2)]
    xn = [p.sb(f"xn_{c}", [128, T], BF16) for c in range(CH)]
    wqt = p.sb("wqt", [128, CH * C], BF16)
    wkt = p.sb("wkt", [128, CH * C], BF16)
    wvt = p.sb("wvt", [128, CH * C], BF16)
    wot = wqt
    v65 = [p.sb(f"v65_{t}", [128, H * 65], BF16) for t in range(NT)]
    qaug = p.sb("qaug", [65, T], BF16)
    kaug = p.sb("kaug", [65, T], BF16)
    pt = p.sb("pt", [128, T], BF16)
    attnres = [p.sb(f"attnres_{i}", [128, T], BF16) for i in range(H // 2)]
    qsq = p.sb("qsq", [64, 512], BF16)
    brow = p.sb("brow", [1, T], F32)
    browb = p.sb("browb", [1, T], BF16)
    nrow = p.sb("nrow", [128, 512], F32)
    Bsb = p.sb("Bsb", [128, 512], F32)
    bias_q = p.sb("bias_q", [64, H], F32)
    bias_k = p.sb("bias_k", [64, H], F32)
    bias_v = p.sb("bias_v", [128, H], F32)
    bias_o = p.sb("bias_o", [128, CH], F32)
    mask_t = p.sb("mask_t", [128, 6 * 512], BF16)
    scr = p.sb("scr", [128, 512], F32)
    scr2 = p.sb("scr2", [128, 512], F32)
    tinyr = p.sb("tinyr", [1, 8], F32)

    ps_s = [p.ps(f"ps_s{i}", [128, 512], F32) for i in range(4)]
    ps_a = [p.ps(f"ps_a{i}", [128, 512], F32) for i in range(4)]
    ones64b = p.sb("ones64b", [64, 1], BF16)
    p.op("vector", lambda e: e.memset(ones64b[:, :], 1.0), writes=["ones64b"])
    onesr = p.sb("onesr", [128, 128], F32)
    p.op("vector", lambda e: e.memset(onesr[:, :], 1.0), writes=["onesr"])

    p.op("sync", lambda e: e.dma_start(out=bias_q[:, :], in_=qb2[:, :]), writes=["bias_q"], dma=True)
    p.op("sync", lambda e: e.dma_start(out=bias_k[:, :], in_=kb2[:, :]), writes=["bias_k"], dma=True)
    p.op("sync", lambda e: e.dma_start(out=bias_v[:, :], in_=vb2[:, :]), writes=["bias_v"], dma=True)
    p.op("sync", lambda e: e.dma_start(out=bias_o[:, :], in_=ob2[:, :]), writes=["bias_o"], dma=True)
    for r in range(6):
        p.op("sync", lambda e, r=r: e.dma_start(out=mask_t[:, r * 512:(r + 1) * 512], in_=masks[r, :, :]),
             writes=["mask_t"], dma=True)
    for c in range(CH):
        p.op("gpsimd",
             lambda e, c=c: e.dma_start(out=wqt[:, c * C:(c + 1) * C], in_=wqd[c * 128:(c + 1) * 128, :]),
             writes=[k_("wqt", c)], dma=True, tag="wld")
        p.op("gpsimd",
             lambda e, c=c: e.dma_start(out=wkt[:, c * C:(c + 1) * C], in_=wkd[c * 128:(c + 1) * 128, :]),
             writes=[k_("wkt", c)], dma=True, tag="wld")
        p.op("gpsimd",
             lambda e, c=c: e.dma_start(out=wvt[:, c * C:(c + 1) * C], in_=wvd[c * 128:(c + 1) * 128, :]),
             writes=[k_("wvt", c)], dma=True, tag="wld")

    for c in range(CH):
        xr = xrot[c % 2]
        xk = f"xrot:{c % 2}"
        p.op("sync", lambda e, c=c, xr=xr: e.dma_start(out=xr[:, :], in_=xin[c * 128:(c + 1) * 128, :]),
             writes=[xk], dma=True, tag="lnx")
        for s in range(4):
            sl = slice(s * 512, (s + 1) * 512)
            p.op("tensor",
                 lambda e, s=s, sl=sl, c=c, xr=xr: e.matmul(
                     ps_s[s][0:1, :], ones_f[:, 0:1], xr[:, sl],
                     start=(c == 0), stop=(c == CH - 1)),
                 reads=[xk, "ones_f"], writes=[f"ps_s{s}"], tag="ln_mm_s")
            p.op("scalar", lambda e, sl=sl, xr=xr: e.activation(scr[:, :], xr[:, sl], AF.Square),
                 reads=[xk], writes=["scr"], tag="ln_sq")
            p.op("tensor",
                 lambda e, s=s, c=c: e.matmul(
                     ps_s[s][32:33, :], ones_f[:, 0:1], scr[:, :],
                     start=(c == 0), stop=(c == CH - 1)),
                 reads=["scr", "ones_f"], writes=[f"ps_s{s}"], tag="ln_mm_q")
    for s in range(4):
        sl = slice(s * 512, (s + 1) * 512)
        p.op("vector", lambda e, s=s, sl=sl: e.tensor_scalar_mul(brow[0:1, sl], ps_s[s][0:1, :], 1.0 / C),
             reads=[f"ps_s{s}"], writes=["brow"], tag="ln_mean")
        p.op("vector", lambda e, sl=sl: e.tensor_mul(scr2[0:1, :], brow[0:1, sl], brow[0:1, sl]),
             reads=["brow"], writes=["scr2"], tag="ln_m2")
        p.op("vector",
             lambda e, s=s: e.scalar_tensor_tensor(scr[0:1, :], ps_s[s][32:33, :], 1.0 / C,
                                                   scr2[0:1, :], ALU.mult, ALU.subtract),
             reads=[f"ps_s{s}", "scr2"], writes=["scr"], tag="ln_var")
        p.op("scalar", lambda e: e.activation(scr[0:1, :], scr[0:1, :], AF.Ln, bias=cb[0:1, 0:1]),
             reads=["scr", "cb"], writes=["scr"], tag="ln_ln")
        p.op("scalar", lambda e: e.activation(scr[0:1, :], scr[0:1, :], AF.Exp, scale=-0.5),
             reads=["scr"], writes=["scr"], tag="ln_exp")
        p.op("vector", lambda e, sl=sl: e.tensor_copy(browb[0:1, sl], scr[0:1, :]),
             reads=["scr"], writes=["browb"], tag="ln_rbf")
    for c in range(CH):
        xr = xrot[c % 2]
        xk = f"xrot:{c % 2}"
        p.op("sync", lambda e, c=c, xr=xr: e.dma_start(out=xr[:, :], in_=xin[c * 128:(c + 1) * 128, :]),
             writes=[xk], dma=True, tag="lnx2")
        for s in range(4):
            sl = slice(s * 512, (s + 1) * 512)
            p.op("tensor",
                 lambda e, sl=sl: e.matmul(ps_s[2][:, :], ones_k1[:, :], brow[0:1, sl],
                                           start=True, stop=True),
                 reads=["brow", "ones_k1"], writes=["ps_s2"], tag="ln_bM")
            p.op("tensor",
                 lambda e, sl=sl: e.matmul(ps_s[3][:, :], b["ones_k1b"][:, :], browb[0:1, sl],
                                           start=True, stop=True),
                 reads=["browb", "ones_k1b"], writes=["ps_s3"], tag="ln_bR")
            p.op("vector",
                 lambda e, sl=sl, xr=xr: e.tensor_sub(scr[:, :], xr[:, sl], ps_s[2][:, :]),
                 reads=[xk, "ps_s2"], writes=["scr"], tag="ln_sub")
            p.op("vector",
                 lambda e, c=c, sl=sl: e.tensor_mul(xn[c][:, sl], scr[:, :], ps_s[3][:, :]),
                 reads=["scr", "ps_s3"], writes=[k_("xn", c)], tag="ln_nrm")

    for t in range(NT):
        p.op("vector",
             lambda e, t=t: e.memset(
                 v65[t][:, :].rearrange("p (h u) -> p h u", u=65)[:, :, 64:65], 1.0),
             writes=[k_("v65", t)], tag="v_ones")
        for g in range(2):
            bank = ps_a[g]
            bkey = f"ps_a{g}"
            for c in range(CH):
                p.op("tensor",
                     lambda e, t=t, g=g, c=c, bank=bank: e.matmul(
                         bank[:, :], xn[c][:, t * 128:(t + 1) * 128],
                         wvt[:, c * C + g * 512:(c * C) + (g + 1) * 512],
                         start=(c == 0), stop=(c == CH - 1)),
                     reads=[k_("xn", c), k_("wvt", c)], writes=[bkey], tag="v_mm")
            p.op("vector",
                 lambda e, t=t, g=g, bank=bank: e.tensor_copy(
                     v65[t][:, :].rearrange("p (h u) -> p h u", u=65)[:, g * 8:(g + 1) * 8, 0:64],
                     bank[:, :].rearrange("p (h u) -> p h u", u=64)),
                 reads=[bkey], writes=[k_("v65", t)], tag="v_ev")

    for i in range(H // 2):
        p.op("vector", lambda e, i=i: e.memset(attnres[i][:, :], 0.0),
             writes=[k_("attnres", i)], tag="at_z")
    p.op("vector", lambda e: e.memset(kaug[64:65, :], 1.0), writes=["kaug"], tag="k_one")
    for h in (range(H) if _DBG_HEADS is None else _DBG_HEADS):
        hq = h * 64
        par = h % 2
        at = attnres[h // 2]
        atk = k_("attnres", h // 2)
        for s in range(4):
            sl = slice(s * 512, (s + 1) * 512)
            for c in range(CH):
                p.op("tensor",
                     lambda e, c=c, sl=sl, hq=hq: e.matmul(
                         ps_a[0][0:64, :], wqt[:, c * C + hq: c * C + hq + 64],
                         xn[c][:, sl], start=(c == 0), stop=(c == CH - 1)),
                     reads=[k_("wqt", c), k_("xn", c)], writes=["ps_a0"], tag="q_mm")
            p.op("vector",
                 lambda e, sl=sl, h=h: e.tensor_scalar(
                     qaug[0:64, sl], ps_a[0][0:64, :], bias_q[:, h:h + 1], 0.125,
                     ALU.add, ALU.mult),
                 reads=["ps_a0", "bias_q"], writes=["qaug"], tag="q_ev")
            for c in range(CH):
                p.op("tensor",
                     lambda e, c=c, sl=sl, hq=hq: e.matmul(
                         ps_a[1][0:64, :], wkt[:, c * C + hq: c * C + hq + 64],
                         xn[c][:, sl], start=(c == 0), stop=(c == CH - 1)),
                     reads=[k_("wkt", c), k_("xn", c)], writes=["ps_a1"], tag="k_mm")
            p.op("vector",
                 lambda e, sl=sl, h=h: e.tensor_scalar_add(
                     kaug[0:64, sl], ps_a[1][0:64, :], bias_k[:, h:h + 1]),
                 reads=["ps_a1", "bias_k"], writes=["kaug"], tag="k_ev")

        for (srck, tname) in (("kaug", kaug), ("qaug", qaug)):
            for s in range(4):
                sl = slice(s * 512, (s + 1) * 512)
                p.op("scalar",
                     lambda e, sl=sl, tname=tname: e.activation(qsq[0:64, 0:512], tname[0:64, sl], AF.Square),
                     reads=[srck], writes=["qsq"], tag="nsq")
                p.op("tensor",
                     lambda e: e.matmul(ps_a[2][0:1, :], ones64b[:, 0:1],
                                        qsq[0:64, 0:512], start=True, stop=True),
                     reads=["qsq", "ones64b"], writes=["ps_a2"], tag="nmm")
                p.op("vector",
                     lambda e, sl=sl: e.tensor_copy(brow[0:1, sl], ps_a[2][0:1, :]),
                     reads=["ps_a2"], writes=["brow"], tag="nev")
            if srck == "kaug":
                p.op("vector",
                     lambda e: e.tensor_reduce(tinyr[0:1, 0:1], brow[0:1, :], AX.X, ALU.max),
                     reads=["brow"], writes=["tinyr"], tag="nk2max")
        p.op("vector",
             lambda e: e.tensor_scalar_mul(brow[0:1, :], brow[0:1, :], tinyr[0:1, 0:1]),
             reads=["brow", "tinyr"], writes=["brow"], tag="bq_t")
        p.op("scalar",
             lambda e: e.activation(brow[0:1, :], brow[0:1, :], AF.Ln, bias=cb[0:1, 0:1]),
             reads=["brow", "cb"], writes=["brow"], tag="bq_ln")
        p.op("scalar",
             lambda e: e.activation(brow[0:1, :], brow[0:1, :], AF.Exp, scale=0.5),
             reads=["brow"], writes=["brow"], tag="bq_exp")
        p.op("vector",
             lambda e: e.tensor_scalar_mul(browb[0:1, :], brow[0:1, :], -1.0),
             reads=["brow"], writes=["browb"], tag="bq_neg")
        p.op("gpsimd",
             lambda e: e.dma_start(out=qaug[64:65, :], in_=browb[0:1, :]),
             reads=["browb"], writes=["qaug"], dma=True, tag="aug_b")

        RVALS = (-256, -128, 0, 128, 256, 384)
        for g in range(4):
            kts = [kt for kt in range(NT) if 4 * g - 2 <= kt <= 4 * g + 3]
            for idx, kt in enumerate(kts):
                r = kt * 128 - g * 512
                ri = RVALS.index(r)
                p.op("tensor",
                     lambda e, kt=kt, g=g: e.matmul(
                         ps_s[g][:, :], kaug[0:65, kt * 128:(kt + 1) * 128],
                         qaug[0:65, g * 512:(g + 1) * 512], start=True, stop=True),
                     reads=["kaug", "qaug"], writes=[f"ps_s{g}"], tag="sc_mm")
                p.op("scalar",
                     lambda e, g=g: e.activation(
                         pt[:, g * 512:(g + 1) * 512], ps_s[g][:, :], AF.Exp),
                     reads=[f"ps_s{g}"], writes=["pt"], tag="sc_exp")
                p.op("vector",
                     lambda e, g=g, ri=ri: e.tensor_mul(
                         pt[:, g * 512:(g + 1) * 512], pt[:, g * 512:(g + 1) * 512],
                         mask_t[:, ri * 512:(ri + 1) * 512]),
                     reads=["pt", "mask_t"], writes=["pt"], tag="sc_msk")
                if par == 0:
                    p.op("tensor",
                         lambda e, kt=kt, g=g, h=h, idx=idx, last=(idx == len(kts) - 1): e.matmul(
                             ps_a[g][0:65, :], v65[kt][:, h * 65:(h + 1) * 65],
                             pt[:, g * 512:(g + 1) * 512],
                             start=(idx == 0), stop=last),
                         reads=[k_("v65", kt), "pt"], writes=[f"ps_a{g}"], tag="av_mm")
                else:
                    p.op("tensor",
                         lambda e, kt=kt, g=g, h=h, idx=idx, last=(idx == len(kts) - 1): e.matmul(
                             ps_a[g][64:128, :], v65[kt][:, h * 65:h * 65 + 64],
                             pt[:, g * 512:(g + 1) * 512],
                             start=(idx == 0), stop=last),
                         reads=[k_("v65", kt), "pt"], writes=[f"ps_a{g}"], tag="av_mmo")
                    p.op("tensor",
                         lambda e, kt=kt, g=g, h=h, idx=idx, last=(idx == len(kts) - 1): e.matmul(
                             ps_a[g][0:1, :], v65[kt][:, h * 65 + 64:h * 65 + 65],
                             pt[:, g * 512:(g + 1) * 512],
                             start=(idx == 0), stop=last),
                         reads=[k_("v65", kt), "pt"], writes=[f"ps_a{g}"], tag="av_den")

        for g in range(4):
            gsl = slice(g * 512, (g + 1) * 512)
            if par == 0:
                p.op("vector",
                     lambda e, g=g: e.reciprocal(nrow[64:65, :], ps_a[g][64:65, :]),
                     reads=[f"ps_a{g}"], writes=["nrow"], tag="nm_rc")
                p.op("tensor",
                     lambda e: e.matmul(ps_s[0][0:64, :], onesr[64:65, 0:64], nrow[64:65, :],
                                        start=True, stop=True),
                     reads=["nrow", "onesr"], writes=["ps_s0"], tag="nm_bc")
                p.op("vector",
                     lambda e: e.tensor_copy(Bsb[0:64, :], ps_s[0][0:64, :]),
                     reads=["ps_s0"], writes=["Bsb"], tag="nm_ev")
                p.op("vector",
                     lambda e, g=g: e.tensor_mul(scr[0:64, :], ps_a[g][0:64, :], Bsb[0:64, :]),
                     reads=[f"ps_a{g}", "Bsb"], writes=["scr"], tag="nm_mul")
                p.op("vector",
                     lambda e, gsl=gsl, h=h, at=at: e.tensor_scalar_add(
                         at[0:64, gsl], scr[0:64, :], bias_v[0:64, h:h + 1]),
                     reads=["scr", "bias_v"], writes=[atk], tag="nm_out")
            else:
                p.op("vector",
                     lambda e, g=g: e.reciprocal(nrow[0:1, :], ps_a[g][0:1, :]),
                     reads=[f"ps_a{g}"], writes=["nrow"], tag="nm_rc")
                p.op("tensor",
                     lambda e: e.matmul(ps_s[0][64:128, :], onesr[0:1, 0:64], nrow[0:1, :],
                                        start=True, stop=True),
                     reads=["nrow", "onesr"], writes=["ps_s0"], tag="nm_bc")
                p.op("vector",
                     lambda e: e.tensor_copy(Bsb[64:128, :], ps_s[0][64:128, :]),
                     reads=["ps_s0"], writes=["Bsb"], tag="nm_ev")
                p.op("vector",
                     lambda e, g=g: e.tensor_mul(scr[64:128, :], ps_a[g][64:128, :], Bsb[64:128, :]),
                     reads=[f"ps_a{g}", "Bsb"], writes=["scr"], tag="nm_mul")
                p.op("vector",
                     lambda e, gsl=gsl, h=h, at=at: e.tensor_scalar_add(
                         at[64:128, gsl], scr[64:128, :], bias_v[64:128, h:h + 1]),
                     reads=["scr", "bias_v"], writes=[atk], tag="nm_out")

    for c in range(CH):
        p.op("gpsimd",
             lambda e, c=c: e.dma_start(out=wot[:, c * C:(c + 1) * C],
                                        in_=wod[c * 128:(c + 1) * 128, :]),
             writes=[k_("wqt", c), k_("wo", c)], dma=True, tag="wout")
    for oc in range(CH):
        xr = xrot[oc % 2]
        xk = f"xrot:{oc % 2}"
        p.op("sync", lambda e, oc=oc, xr=xr: e.dma_start(out=xr[:, :], in_=xin[oc * 128:(oc + 1) * 128, :]),
             writes=[xk], dma=True, tag="ox_ld")
        p.op("vector", lambda e, xr=xr: e.tensor_copy(scr[:, 1:2], xr[:, 0:1]),
             reads=[xk], writes=["scr"], tag="gz_sv")
        for s in range(4):
            sl = slice(s * 512, (s + 1) * 512)
            bank = ps_a[s % 4]
            bkey = f"ps_a{s % 4}"
            for i in range(CH):
                p.op("tensor",
                     lambda e, oc=oc, i=i, sl=sl, bank=bank: e.matmul(
                         bank[:, :], wot[:, i * C + oc * 128: i * C + (oc + 1) * 128],
                         attnres[i][:, sl], start=(i == 0), stop=(i == CH - 1)),
                     reads=[k_("wo", i), k_("attnres", i)], writes=[bkey], tag="o_mm")
            p.op("vector",
                 lambda e, oc=oc, sl=sl, bank=bank, xr=xr, xk=xk: e.scalar_tensor_tensor(
                     xr[:, sl], bank[:, :], bias_o[:, oc:oc + 1], xr[:, sl],
                     ALU.add, ALU.add),
                 reads=[bkey, "bias_o", xk], writes=[xk], tag="o_add")
        p.op("sync",
             lambda e, oc=oc, xr=xr, xk=xk: e.dma_start(out=xout[oc * 128:(oc + 1) * 128, :], in_=xr[:, :]),
             reads=[xk], dma=True, tag="ox_st")

    return p.build()




def build_stage_l0attn():
    H, D = 16, 64
    ctx = Ctx("l0attn")
    p, b = ctx.p, ctx.b
    cb = b["cb"]
    ones_f, ones_k1 = b["ones_f"], b["ones_k1"]

    xin = p.dram("xin", [C, T], F32)
    xout = p.dram("xout", [C, T], F32, kind="ExternalOutput")
    wqd = p.dram("Wq", [C, C], F32)
    wkd = p.dram("Wk", [C, C], F32)
    wvd = p.dram("Wv", [C, C], F32)
    wod = p.dram("Wo", [C, C], F32)
    ob2 = p.dram("ob2", [128, CH], F32)
    identd = p.dram("ident", [128, 128], BF16)

    xrot = p.sb("xrot", [128, T], F32)
    xn = [p.sb(f"xn_{c}", [128, T], BF16) for c in range(CH)]
    attnres = [p.sb(f"attnres_{i}", [128, T], BF16) for i in range(CH)]
    wq = p.sb("wq", [128, CH * C], BF16)
    wk = p.sb("wk", [128, CH * C], BF16)
    wv = p.sb("wv", [128, CH * C], BF16)
    f1 = p.sb("f1", [128, C], F32)
    f2 = p.sb("f2", [128, C], F32)
    kk = p.sb("kk", [128, C], BF16)
    vv = p.sb("vv", [128, C], BF16)
    vem = p.sb("vem", [128, C], BF16)
    wt = p.sb("wt", [128, C], BF16)
    Dten = p.sb("Dten", [128, C], F32)
    rD = p.sb("rD", [128, C], F32)
    call = p.sb("call", [128, H * H], F32)
    cprod = p.sb("cprod", [128, C], BF16)
    yprod = p.sb("yprod", [128, C], BF16)
    atttok = p.sb("atttok", [128, C], BF16)
    segr = p.sb("segr", [128, 3 * H], F32)
    scr = p.sb("scr", [128, 512], F32)
    scr2 = p.sb("scr2", [128, 512], F32)
    brow = p.sb("brow", [1, T], F32)
    browb = p.sb("browb", [1, T], BF16)
    identb = p.sb("identb", [128, 128], BF16)
    bias_o = p.sb("bias_o", [128, CH], F32)

    psA = p.ps("psA", [128, 512], F32)
    psB = p.ps("psB", [128, 512], F32)
    psD = p.ps("psD", [128, 1024], F32)
    psT = [p.ps("psT0", [128, 128], BF16), p.ps("psT1", [128, 128], BF16)]

    p.op("sync", lambda e: e.dma_start(out=bias_o[:, :], in_=ob2[:, :]), writes=["bias_o"], dma=True)
    p.op("sync", lambda e: e.dma_start(out=identb[:, :], in_=identd[:, :]), writes=["identb"], dma=True)
    for c in range(CH):
        p.op("gpsimd",
             lambda e, c=c: e.dma_start(out=wq[:, c * C:(c + 1) * C], in_=wqd[c * 128:(c + 1) * 128, :]),
             writes=[k_("wq", c)], dma=True, tag="wqld")
        p.op("gpsimd",
             lambda e, c=c: e.dma_start(out=wk[:, c * C:(c + 1) * C], in_=wkd[c * 128:(c + 1) * 128, :]),
             writes=[k_("wk", c)], dma=True, tag="wkld")
        p.op("gpsimd",
             lambda e, c=c: e.dma_start(out=wv[:, c * C:(c + 1) * C], in_=wvd[c * 128:(c + 1) * 128, :]),
             writes=[k_("wv", c)], dma=True, tag="wvld")

    _slots = [(psA, "psA", 0), (psA, "psA", 32), (psA, "psA", 64),
              (psB, "psB", 0), (psB, "psB", 32), (psB, "psB", 64),
              (psD, "psD", 0), (psD, "psD", 32)]

    def _slot(s, kind):
        return _slots[s * 2 + kind]

    for c in range(CH):
        xk = "xrot"
        p.op("sync", lambda e, c=c: e.dma_start(out=xrot[:, :], in_=xin[c * 128:(c + 1) * 128, :]),
             writes=[xk], dma=True, tag="lnx")
        for s in range(4):
            sl = slice(s * 512, (s + 1) * 512)
            bank, bkey, row = _slot(s, 0)
            p.op("tensor",
                 lambda e, sl=sl, c=c, bank=bank, row=row: e.matmul(
                     bank[row:row + 1, 0:512], ones_f[:, 0:1], xrot[:, sl],
                     start=(c == 0), stop=(c == CH - 1)),
                 reads=[xk, "ones_f"], writes=[bkey], tag="ln_mm_s")
            p.op("scalar", lambda e, sl=sl: e.activation(scr[:, :], xrot[:, sl], AF.Square),
                 reads=[xk], writes=["scr"], tag="ln_sq")
            bank, bkey, row = _slot(s, 1)
            p.op("tensor",
                 lambda e, c=c, bank=bank, row=row: e.matmul(
                     bank[row:row + 1, 0:512], ones_f[:, 0:1], scr[:, :],
                     start=(c == 0), stop=(c == CH - 1)),
                 reads=["scr", "ones_f"], writes=[bkey], tag="ln_mm_q")
    for s in range(4):
        sl = slice(s * 512, (s + 1) * 512)
        bank, bkey, row = _slot(s, 0)
        p.op("vector", lambda e, sl=sl, bank=bank, row=row: e.tensor_scalar_mul(
                 brow[0:1, sl], bank[row:row + 1, 0:512], 1.0 / C),
             reads=[bkey], writes=["brow"], tag="ln_mean")
        p.op("vector", lambda e, sl=sl: e.tensor_mul(scr2[0:1, :], brow[0:1, sl], brow[0:1, sl]),
             reads=["brow"], writes=["scr2"], tag="ln_m2")
        bank, bkey, row = _slot(s, 1)
        p.op("vector",
             lambda e, bank=bank, row=row: e.scalar_tensor_tensor(
                 scr[0:1, :], bank[row:row + 1, 0:512], 1.0 / C,
                 scr2[0:1, :], ALU.mult, ALU.subtract),
             reads=[bkey, "scr2"], writes=["scr"], tag="ln_var")
        p.op("scalar", lambda e: e.activation(scr[0:1, :], scr[0:1, :], AF.Ln, bias=cb[0:1, 0:1]),
             reads=["scr", "cb"], writes=["scr"], tag="ln_ln")
        p.op("scalar", lambda e: e.activation(scr[0:1, :], scr[0:1, :], AF.Exp, scale=-0.5),
             reads=["scr"], writes=["scr"], tag="ln_exp")
        p.op("vector", lambda e, sl=sl: e.tensor_copy(browb[0:1, sl], scr[0:1, :]),
             reads=["scr"], writes=["browb"], tag="ln_rbf")
    for c in range(CH):
        xk = "xrot"
        p.op("sync", lambda e, c=c: e.dma_start(out=xrot[:, :], in_=xin[c * 128:(c + 1) * 128, :]),
             writes=[xk], dma=True, tag="lnx2")
        for s in range(4):
            sl = slice(s * 512, (s + 1) * 512)
            p.op("tensor",
                 lambda e, sl=sl: e.matmul(psA[:, :], ones_k1[:, :], brow[0:1, sl],
                                           start=True, stop=True),
                 reads=["brow", "ones_k1"], writes=["psA"], tag="ln_bM")
            p.op("tensor",
                 lambda e, sl=sl: e.matmul(psB[:, :], b["ones_k1b"][:, :], browb[0:1, sl],
                                           start=True, stop=True),
                 reads=["browb", "ones_k1b"], writes=["psB"], tag="ln_bR")
            p.op("vector",
                 lambda e, sl=sl: e.tensor_sub(scr[:, :], xrot[:, sl], psA[:, :]),
                 reads=[xk, "psA"], writes=["scr"], tag="ln_sub")
            p.op("vector",
                 lambda e, c=c, sl=sl: e.tensor_mul(xn[c][:, sl], scr[:, :], psB[:, :]),
                 reads=["scr", "psB"], writes=[k_("xn", c)], tag="ln_nrm")

    segv = segr[:, :].rearrange("p (a h) -> p a h", h=H)
    kkv = kk[:, :].rearrange("p (h d) -> p h d", d=D)
    vemv = vem[:, :].rearrange("p (e h) -> p e h", h=H)

    for t in range(NT):
        tsl = slice(t * 128, (t + 1) * 128)
        for g in range(2):
            bank = psA if g == 0 else psB
            bkey = "psA" if g == 0 else "psB"
            gsl = slice(g * 512, (g + 1) * 512)
            for c in range(CH):
                p.op("tensor",
                     lambda e, g=g, c=c, bank=bank, tsl=tsl: e.matmul(
                         bank[:, :], xn[c][:, tsl],
                         wq[:, c * C + g * 512: c * C + (g + 1) * 512],
                         start=(c == 0), stop=(c == CH - 1)),
                     reads=[k_("xn", c), k_("wq", c)], writes=[bkey], tag="q_mm")
            p.op("scalar",
                 lambda e, gsl=gsl, bank=bank: e.activation(f1[:, gsl], bank[:, :], AF.Copy),
                 reads=[bkey], writes=["f1"], tag="q_ev")
            for c in range(CH):
                p.op("tensor",
                     lambda e, g=g, c=c, bank=bank, tsl=tsl: e.matmul(
                         bank[:, :], xn[c][:, tsl],
                         wk[:, c * C + g * 512: c * C + (g + 1) * 512],
                         start=(c == 0), stop=(c == CH - 1)),
                     reads=[k_("xn", c), k_("wk", c)], writes=[bkey], tag="k_mm")
            p.op("vector",
                 lambda e, bank=bank: e.tensor_scalar_min(scr[:, :], bank[:, :], 0.0),
                 reads=[bkey], writes=["scr"], tag="k_min")
            p.op("scalar",
                 lambda e: e.activation(scr2[:, :], scr[:, :], AF.Exp),
                 reads=["scr"], writes=["scr2"], tag="k_exp")
            p.op("vector",
                 lambda e, gsl=gsl, bank=bank: e.scalar_tensor_tensor(
                     kk[:, gsl], bank[:, :], 0.0, scr2[:, :], ALU.max, ALU.add),
                 reads=[bkey, "scr2"], writes=["kk"], tag="k_elu")
            for c in range(CH):
                p.op("tensor",
                     lambda e, g=g, c=c, bank=bank, tsl=tsl: e.matmul(
                         bank[:, :], xn[c][:, tsl],
                         wv[:, c * C + g * 512: c * C + (g + 1) * 512],
                         start=(c == 0), stop=(c == CH - 1)),
                     reads=[k_("xn", c), k_("wv", c)], writes=[bkey], tag="v_mm")
            p.op("scalar",
                 lambda e, gsl=gsl, bank=bank: e.activation(vv[:, gsl], bank[:, :], AF.Copy),
                 reads=[bkey], writes=["vv"], tag="v_ev")
        p.op("vector",
             lambda e: e.tensor_copy(vemv, vv[:, :].rearrange("p (h e) -> p e h", e=D)),
             reads=["vv"], writes=["vem"], tag="vem")
        p.op("vector", lambda e: e.tensor_copy(Dten[:, 0:D], kk[:, 0:D]),
             reads=["kk"], writes=["Dten"], tag="dc0")
        for h in range(1, H):
            p.op("vector",
                 lambda e, h=h: e.tensor_add(Dten[:, h * D:(h + 1) * D],
                                             Dten[:, (h - 1) * D:h * D], kk[:, h * D:(h + 1) * D]),
                 reads=["Dten", "kk"], writes=["Dten"], tag="dcs")
        p.op("scalar", lambda e: e.activation(rD[:, :], Dten[:, :], AF.Ln, bias=cb[:, 2:3]),
             reads=["Dten", "cb"], writes=["rD"], tag="rd_ln")
        p.op("scalar", lambda e: e.activation(rD[:, :], rD[:, :], AF.Exp, scale=-1.0),
             reads=["rD"], writes=["rD"], tag="rd_exp")
        p.op("vector",
             lambda e: e.tensor_reduce(segv[:, 0, :], f1[:, :].rearrange("p (h u) -> p h u", u=D),
                                       AX.X, ALU.max),
             reads=["f1"], writes=["segr"], tag="q_smax")
        p.op("vector",
             lambda e: e.tensor_tensor(
                 f2[:, :].rearrange("p (h u) -> p h u", u=D),
                 f1[:, :].rearrange("p (h u) -> p h u", u=D),
                 segv[:, 0, :].unsqueeze(2).broadcast_to([128, H, D]),
                 ALU.subtract),
             reads=["f1", "segr"], writes=["f2"], tag="q_sub")
        p.op("scalar", lambda e: e.activation(f2[:, :], f2[:, :], AF.Exp),
             reads=["f2"], writes=["f2"], tag="q_exp")
        p.op("vector",
             lambda e: e.tensor_reduce(segv[:, 1, :], f2[:, :].rearrange("p (h u) -> p h u", u=D),
                                       AX.X, ALU.add),
             reads=["f2"], writes=["segr"], tag="q_ssum")
        p.op("vector",
             lambda e: e.reciprocal(segv[:, 2, :], segv[:, 1, :]),
             reads=["segr"], writes=["segr"], tag="q_rseg")
        p.op("vector",
             lambda e: e.tensor_mul(rD[:, :], f2[:, :], rD[:, :]),
             reads=["f2", "rD"], writes=["rD"], tag="w_m1")
        p.op("vector",
             lambda e: e.tensor_tensor(
                 wt[:, :].rearrange("p (h u) -> p h u", u=D),
                 rD[:, :].rearrange("p (h u) -> p h u", u=D),
                 segv[:, 2, :].unsqueeze(2).broadcast_to([128, H, D]),
                 ALU.mult),
             reads=["rD", "segr"], writes=["wt"], tag="w_m2")
        for h in range(H):
            n = h + 1
            p.op("vector",
                 lambda e, h=h, n=n: e.tensor_tensor(
                     cprod[:, 0:n * D].rearrange("p (a d) -> p a d", d=D),
                     kk[:, 0:n * D].rearrange("p (a d) -> p a d", d=D),
                     wt[:, h * D:(h + 1) * D].unsqueeze(1).broadcast_to([128, n, D]),
                     ALU.mult),
                 reads=["kk", "wt"], writes=["cprod"], tag="c_mul")
            p.op("vector",
                 lambda e, h=h, n=n: e.tensor_reduce(
                     call[:, h * H:h * H + n],
                     cprod[:, 0:n * D].rearrange("p (a d) -> p a d", d=D),
                     AX.X, ALU.add),
                 reads=["cprod"], writes=["call"], tag="c_red")
        for h in range(H):
            n = h + 1
            p.op("vector",
                 lambda e, h=h, n=n: e.tensor_tensor(
                     yprod[:, 0:D * n].rearrange("p (z a) -> p z a", a=n),
                     vemv[:, :, 0:n],
                     call[:, h * H:h * H + n].unsqueeze(1).broadcast_to([128, D, n]),
                     ALU.mult),
                 reads=["vem", "call"], writes=["yprod"], tag="y_mul")
            p.op("vector",
                 lambda e, h=h, n=n: e.tensor_reduce(
                     rD[:, h * D:(h + 1) * D],
                     yprod[:, 0:D * n].rearrange("p (z a) -> p z a", a=n),
                     AX.X, ALU.add),
                 reads=["yprod"], writes=["rD"], tag="y_red")
        p.op("vector", lambda e: e.tensor_copy(atttok[:, :], rD[:, :]),
             reads=["rD"], writes=["atttok"], tag="y_cast")
        for c in range(CH):
            half = c % 2
            p.op("tensor",
                 lambda e, c=c, half=half: e.transpose(
                     psT[half][:, 0:128], atttok[:, c * 128:(c + 1) * 128], identb[:, :]),
                 reads=["atttok", "identb"], writes=[f"psT{half}"], tag="tr_a")
            p.op("scalar",
                 lambda e, c=c, half=half, tsl=tsl: e.activation(
                     attnres[c][:, tsl], psT[half][:, 0:128], AF.Copy),
                 reads=[f"psT{half}"], writes=[k_("attnres", c)], tag="tr_ae")

    for c in range(CH):
        p.op("gpsimd",
             lambda e, c=c: e.dma_start(out=wq[:, c * C:(c + 1) * C],
                                        in_=wod[c * 128:(c + 1) * 128, :]),
             writes=[k_("wq", c), k_("wo", c)], dma=True, tag="wout")
    for oc in range(CH):
        xk = "xrot"
        p.op("sync", lambda e, oc=oc: e.dma_start(out=xrot[:, :], in_=xin[oc * 128:(oc + 1) * 128, :]),
             writes=[xk], dma=True, tag="ox_ld")
        for s in range(4):
            sl = slice(s * 512, (s + 1) * 512)
            bank = psA if s % 2 == 0 else psB
            bkey = "psA" if s % 2 == 0 else "psB"
            for i in range(CH):
                p.op("tensor",
                     lambda e, oc=oc, i=i, sl=sl, bank=bank: e.matmul(
                         bank[:, :], wq[:, i * C + oc * 128: i * C + (oc + 1) * 128],
                         attnres[i][:, sl], start=(i == 0), stop=(i == CH - 1)),
                     reads=[k_("wo", i), k_("attnres", i)], writes=[bkey], tag="o_mm")
            p.op("vector",
                 lambda e, oc=oc, sl=sl, bank=bank, bkey=bkey, xk=xk: e.scalar_tensor_tensor(
                     xrot[:, sl], bank[:, :], bias_o[:, oc:oc + 1], xrot[:, sl],
                     ALU.add, ALU.add),
                 reads=[bkey, "bias_o", xk], writes=[xk], tag="o_add")
        p.op("sync",
             lambda e, oc=oc, xk=xk: e.dma_start(out=xout[oc * 128:(oc + 1) * 128, :], in_=xrot[:, :]),
             reads=[xk], dma=True, tag="ox_st")

    return p.build()




QMAX16 = 32767.0


def _emit_gscale(p, b, src0, dst, tag):
    mrow = b["mrow"]
    sc, rc = 28 + 2 * dst, 29 + 2 * dst
    p.op("vector",
         lambda e: e.tensor_reduce(mrow[0:1, 24 + dst:25 + dst],
                                   mrow[0:1, src0:src0 + NCORES].rearrange("p (a n) -> p a n", a=1),
                                   AX.X, ALU.max),
         reads=["mrow"], writes=["mrow"], tag=tag + "_gm")
    p.op("vector",
         lambda e: e.tensor_scalar(mrow[0:1, sc:sc + 1], mrow[0:1, 24 + dst:25 + dst],
                                   1.0 / QMAX16, 1e-12, ALU.mult, ALU.max),
         reads=["mrow"], writes=["mrow"], tag=tag + "_s")
    p.op("vector",
         lambda e: e.reciprocal(mrow[0:1, rc:rc + 1], mrow[0:1, sc:sc + 1]),
         reads=["mrow"], writes=["mrow"], tag=tag + "_rs")
    ps = b["psS"]
    p.op("tensor",
         lambda e: e.matmul(ps[:, 0:2], b["ones_k1"][:, :], mrow[0:1, sc:sc + 2],
                            start=True, stop=True),
         reads=["mrow", "ones_k1"], writes=["psS"], tag=tag + "_bc")
    p.op("vector",
         lambda e: e.tensor_copy(b["scol2"][:, 2 * dst:2 * dst + 2], ps[:, 0:2]),
         reads=["psS"], writes=["scol2"], tag=tag + "_ev")


def _emit_qfix(p, src, srckey, n, sc, qi, dq, g1q, qf, out, outkey, tag,
               rs_ap, s_ap, rskey=("scol2",), deq_f32=None, clip_hi=None):
    rk = list(rskey)
    if sc == "t":
        p.op("vector", lambda e: e.tensor_mul(qf[:, 0:n], src[:, 0:n], rs_ap[:, 0:n]),
             reads=[srckey] + rk, writes=["qf"], tag=tag + "_xs")
        p.op("vector", lambda e: e.tensor_copy(qi[:, 0:n], qf[:, 0:n]),
             reads=["qf"], writes=["qi"], tag=tag + "_qi")
        p.op("vector", lambda e: e.tensor_sub(dq[:, 0:n], qf[:, 0:n], qi[:, 0:n]),
             reads=["qf", "qi"], writes=["dq"], tag=tag + "_d")
    else:
        p.op("vector", lambda e: e.tensor_scalar_mul(qi[:, 0:n], src[:, 0:n], rs_ap),
             reads=[srckey] + rk, writes=["qi"], tag=tag + "_qi")
        p.op("vector",
             lambda e: e.scalar_tensor_tensor(dq[:, 0:n], src[:, 0:n], rs_ap,
                                              qi[:, 0:n], ALU.mult, ALU.subtract),
             reads=[srckey, "qi"] + rk, writes=["dq"], tag=tag + "_d")
    p.op("vector",
         lambda e: e.tensor_single_scalar(g1q[:, 0:n], dq[:, 0:n], 0.5, ALU.is_ge),
         reads=["dq"], writes=["g1q"], tag=tag + "_g1")
    p.op("vector",
         lambda e: e.tensor_scalar(dq[:, 0:n], dq[:, 0:n], -0.5, -1.0,
                                   ALU.is_le, ALU.mult),
         reads=["dq"], writes=["dq"], tag=tag + "_g2")
    p.op("vector",
         lambda e: e.tensor_add(g1q[:, 0:n], g1q[:, 0:n], dq[:, 0:n]),
         reads=["g1q", "dq"], writes=["g1q"], tag=tag + "_gg")
    p.op("vector",
         lambda e: e.tensor_add(qf[:, 0:n], qi[:, 0:n], g1q[:, 0:n]),
         reads=["qi", "g1q"], writes=["qf"], tag=tag + "_q")
    if clip_hi is not None:
        p.op("vector",
             lambda e: e.tensor_single_scalar(qf[:, 0:n], qf[:, 0:n], clip_hi, ALU.min),
             reads=["qf"], writes=["qf"], tag=tag + "_cl")
    if sc == "t":
        p.op("vector", lambda e: e.tensor_mul(out[:, 0:n], qf[:, 0:n], s_ap[:, 0:n]),
             reads=["qf"] + rk, writes=[outkey], tag=tag + "_dqo")
    else:
        p.op("vector", lambda e: e.tensor_scalar_mul(out[:, 0:n], qf[:, 0:n], s_ap),
             reads=["qf"] + rk, writes=[outkey], tag=tag + "_dqo")
        if deq_f32 is not None:
            t2, k2 = deq_f32
            p.op("vector", lambda e: e.tensor_scalar_mul(t2[:, 0:n], qf[:, 0:n], s_ap),
                 reads=["qf"] + rk, writes=[k2], tag=tag + "_dqf")


def build_stage_l2ffnA():
    ctx = Ctx("l2ffnA")
    p, b = ctx.p, ctx.b
    cb = b["cb"]
    ones_f, ones_k1 = b["ones_f"], b["ones_k1"]

    xin = p.dram("xin", [C, T], F32)
    xnout = p.dram("xnout", [C, T], F32, kind="ExternalOutput")
    m0d = p.dram("m0", [1, 1], F32, kind="ExternalOutput")
    identfd = p.dram("identf", [128, 128], F32)

    xrot = [p.sb(f"xrot_{i}", [128, T], F32) for i in range(2)]
    xo = p.sb("xo", [128, T], F32)
    scr = p.sb("scr", [128, 512], F32)
    scr2 = p.sb("scr2", [128, 512], F32)
    brow = p.sb("brow", [1, T], F32)
    brow2 = p.sb("brow2", [1, T], F32)
    mcol = p.sb("mcol", [128, 2], F32)
    identf = p.sb("identf_s", [128, 128], F32)
    psA = p.ps("psA", [128, 512], F32)
    psB = p.ps("psB", [128, 512], F32)
    psD = p.ps("psD", [128, 1024], F32)

    p.op("sync", lambda e: e.dma_start(out=identf[:, :], in_=identfd[:, :]), writes=["identf"], dma=True)
    p.op("vector", lambda e: e.memset(mcol[:, :], 0.0), writes=["mcol"])

    _slots = [(psA, "psA", 0), (psA, "psA", 32), (psA, "psA", 64),
              (psB, "psB", 0), (psB, "psB", 32), (psB, "psB", 64),
              (psD, "psD", 0), (psD, "psD", 32)]
    for c in range(CH):
        xr = xrot[c % 2]
        xk = f"xrot:{c % 2}"
        p.op("sync", lambda e, c=c, xr=xr: e.dma_start(out=xr[:, :], in_=xin[c * 128:(c + 1) * 128, :]),
             writes=[xk], dma=True, tag="lnx")
        for s in range(4):
            sl = slice(s * 512, (s + 1) * 512)
            bank, bkey, row = _slots[s * 2]
            p.op("tensor",
                 lambda e, sl=sl, c=c, bank=bank, row=row, xr=xr: e.matmul(
                     bank[row:row + 1, 0:512], ones_f[:, 0:1], xr[:, sl],
                     start=(c == 0), stop=(c == CH - 1)),
                 reads=[xk, "ones_f"], writes=[bkey], tag="ln_mm_s")
            p.op("scalar", lambda e, sl=sl, xr=xr: e.activation(scr[:, :], xr[:, sl], AF.Square),
                 reads=[xk], writes=["scr"], tag="ln_sq")
            bank, bkey, row = _slots[s * 2 + 1]
            p.op("tensor",
                 lambda e, c=c, bank=bank, row=row: e.matmul(
                     bank[row:row + 1, 0:512], ones_f[:, 0:1], scr[:, :],
                     start=(c == 0), stop=(c == CH - 1)),
                 reads=["scr", "ones_f"], writes=[bkey], tag="ln_mm_q")
    for s in range(4):
        sl = slice(s * 512, (s + 1) * 512)
        bank, bkey, row = _slots[s * 2]
        p.op("vector", lambda e, sl=sl, bank=bank, row=row: e.tensor_scalar_mul(
                 brow[0:1, sl], bank[row:row + 1, 0:512], 1.0 / C),
             reads=[bkey], writes=["brow"], tag="ln_mean")
        p.op("vector", lambda e, sl=sl: e.tensor_mul(scr2[0:1, :], brow[0:1, sl], brow[0:1, sl]),
             reads=["brow"], writes=["scr2"], tag="ln_m2")
        bank, bkey, row = _slots[s * 2 + 1]
        p.op("vector",
             lambda e, bank=bank, row=row: e.scalar_tensor_tensor(
                 scr[0:1, :], bank[row:row + 1, 0:512], 1.0 / C,
                 scr2[0:1, :], ALU.mult, ALU.subtract),
             reads=[bkey, "scr2"], writes=["scr"], tag="ln_var")
        p.op("scalar", lambda e: e.activation(scr[0:1, :], scr[0:1, :], AF.Ln, bias=cb[0:1, 0:1]),
             reads=["scr", "cb"], writes=["scr"], tag="ln_ln")
        p.op("scalar", lambda e: e.activation(scr[0:1, :], scr[0:1, :], AF.Exp, scale=-0.5),
             reads=["scr"], writes=["scr"], tag="ln_exp")
        p.op("vector", lambda e, sl=sl: e.tensor_copy(brow2[0:1, sl], scr[0:1, :]),
             reads=["scr"], writes=["brow2"], tag="ln_rst")
    for c in range(CH):
        xr = xrot[c % 2]
        xk = f"xrot:{c % 2}"
        p.op("sync", lambda e, c=c, xr=xr: e.dma_start(out=xr[:, :], in_=xin[c * 128:(c + 1) * 128, :]),
             writes=[xk], dma=True, tag="lnx2")
        for s in range(4):
            sl = slice(s * 512, (s + 1) * 512)
            p.op("tensor",
                 lambda e, sl=sl: e.matmul(psA[:, :], ones_k1[:, :], brow[0:1, sl],
                                           start=True, stop=True),
                 reads=["brow", "ones_k1"], writes=["psA"], tag="ln_bM")
            p.op("tensor",
                 lambda e, sl=sl: e.matmul(psB[:, :], ones_k1[:, :], brow2[0:1, sl],
                                           start=True, stop=True),
                 reads=["brow2", "ones_k1"], writes=["psB"], tag="ln_bR")
            p.op("vector",
                 lambda e, sl=sl, xr=xr: e.tensor_sub(scr[:, :], xr[:, sl], psA[:, :]),
                 reads=[xk, "psA"], writes=["scr"], tag="ln_sub")
            p.op("vector",
                 lambda e, sl=sl: e.tensor_mul(xo[:, sl], scr[:, :], psB[:, :]),
                 reads=["scr", "psB"], writes=["xo"], tag="ln_nrm")
            p.op("vector",
                 lambda e, sl=sl: e.tensor_reduce(
                     scr2[:, 0:1], xo[:, sl].rearrange("p (a n) -> p a n", a=1),
                     AX.X, ALU.max, apply_absolute_value=True),
                 reads=["xo"], writes=["scr2"], tag="amx_r")
            p.op("vector",
                 lambda e: e.tensor_max(mcol[:, 0:1], mcol[:, 0:1], scr2[:, 0:1]),
                 reads=["scr2", "mcol"], writes=["mcol"], tag="amx_a")
        p.op("sync",
             lambda e, c=c: e.dma_start(out=xnout[c * 128:(c + 1) * 128, :], in_=xo[:, :]),
             reads=["xo"], dma=True, tag="stxn")
    p.op("tensor",
         lambda e: e.transpose(psA[0:1, 0:128], mcol[:, 0:1], identf[:, :]),
         reads=["mcol", "identf"], writes=["psA"], tag="amx_t")
    p.op("vector",
         lambda e: e.tensor_reduce(mcol[0:1, 1:2], psA[0:1, 0:128].rearrange("p (a n) -> p a n", a=1),
                                   AX.X, ALU.max),
         reads=["psA"], writes=["mcol"], tag="amx_f")
    p.op("sync", lambda e: e.dma_start(out=m0d[:, :], in_=mcol[0:1, 1:2]),
         reads=["mcol"], dma=True, tag="stm")
    return p.build()


def _emit_wdeq(p, b, wN, wT, selsb, selkey, nblk, nchunk, wdq, wdqkeys, tag):
    O = wN.shape[0]
    I = wN.shape[1]
    noch = O // 128
    nblocks = I // 32
    wnat, sT, rT = b["wnat"], b["sT"], b["rT"]
    rsx, ssx = b["rsx"], b["ssx"]
    qi, dq, g1q, qf, xs = b["qi"], b["dq"], b["g1q"], b["qf"], b["xs"]
    wrot = b["wrot"]
    psE, psF = b["psE"], b["psF"]
    for oc in range(noch):
        p.op("sync",
             lambda e, oc=oc: e.dma_start(out=wnat[:, 0:I], in_=wN[oc * 128:(oc + 1) * 128, :]),
             writes=["wnat"], dma=True, tag=tag + "_ldn")
        p.op("vector",
             lambda e: e.tensor_reduce(qf[:, 0:nblocks],
                                       wnat[:, 0:I].rearrange("p (a n) -> p a n", n=32),
                                       AX.X, ALU.max, apply_absolute_value=True),
             reads=["wnat"], writes=["qf"], tag=tag + "_bm")
        p.op("vector",
             lambda e: e.tensor_single_scalar(qf[:, 0:nblocks], qf[:, 0:nblocks],
                                              1e-12 * 7.0, ALU.max),
             reads=["qf"], writes=["qf"], tag=tag + "_bc")
        p.op("vector",
             lambda e: e.tensor_scalar_mul(ssx[:, 0:nblocks], qf[:, 0:nblocks], 1.0 / 7.0),
             reads=["qf"], writes=["ssx"], tag=tag + "_bs")
        p.op("vector",
             lambda e: e.reciprocal(rsx[:, 0:nblocks], ssx[:, 0:nblocks]),
             reads=["ssx"], writes=["rsx"], tag=tag + "_br")
        p.op("tensor",
             lambda e: e.transpose(psE[0:nblocks, 0:128], ssx[:, 0:nblocks], b["identf"][:, :]),
             reads=["ssx", "identf"], writes=["psE"], tag=tag + "_ts")
        p.op("vector",
             lambda e, oc=oc: e.tensor_copy(sT[0:nblocks, oc * 128:(oc + 1) * 128],
                                            psE[0:nblocks, 0:128]),
             reads=["psE"], writes=["sT"], tag=tag + "_tse")
        p.op("tensor",
             lambda e: e.transpose(psF[0:nblocks, 0:128], rsx[:, 0:nblocks], b["identf"][:, :]),
             reads=["rsx", "identf"], writes=["psF"], tag=tag + "_tr")
        p.op("vector",
             lambda e, oc=oc: e.tensor_copy(rT[0:nblocks, oc * 128:(oc + 1) * 128],
                                            psF[0:nblocks, 0:128]),
             reads=["psF"], writes=["rT"], tag=tag + "_tre")
    for c in range(nchunk):
        widx = (c % 2) if wrot[0] is not wrot[1] else 0
        wr = wrot[widx]
        wk = f"wrot:{widx}"
        p.op("sync",
             lambda e, c=c, wr=wr: e.dma_start(out=wr[:, 0:O], in_=wT[c * 128:(c + 1) * 128, :]),
             writes=[wk], dma=True, tag=tag + "_ldt")
        nog = (O + 511) // 512
        for og in range(nog):
            gn = min(512, O - og * 512)
            p.op("tensor",
                 lambda e, c=c, og=og, gn=gn: e.matmul(
                     psE[:, 0:gn], selsb[0:nblk, c * 128:(c + 1) * 128],
                     rT[0:nblk, og * 512:og * 512 + gn], start=True, stop=True),
                 reads=[selkey, "rT"], writes=["psE"], tag=tag + "_xr")
            p.op("vector",
                 lambda e, og=og, gn=gn: e.tensor_copy(rsx[:, og * 512:og * 512 + gn],
                                                       psE[:, 0:gn]),
                 reads=["psE"], writes=["rsx"], tag=tag + "_xre")
            p.op("tensor",
                 lambda e, c=c, og=og, gn=gn: e.matmul(
                     psF[:, 0:gn], selsb[0:nblk, c * 128:(c + 1) * 128],
                     sT[0:nblk, og * 512:og * 512 + gn], start=True, stop=True),
                 reads=[selkey, "sT"], writes=["psF"], tag=tag + "_xs")
            p.op("vector",
                 lambda e, og=og, gn=gn: e.tensor_copy(ssx[:, og * 512:og * 512 + gn],
                                                       psF[:, 0:gn]),
                 reads=["psF"], writes=["ssx"], tag=tag + "_xse")
        p.op("vector", lambda e, wr=wr: e.tensor_mul(xs[:, 0:O], wr[:, 0:O], rsx[:, 0:O]),
             reads=[wk, "rsx"], writes=["xs"], tag=tag + "_ws")
        p.op("vector", lambda e: e.tensor_copy(qi[:, 0:O], xs[:, 0:O]),
             reads=["xs"], writes=["qi"], tag=tag + "_qi")
        p.op("vector", lambda e: e.tensor_sub(dq[:, 0:O], xs[:, 0:O], qi[:, 0:O]),
             reads=["xs", "qi"], writes=["dq"], tag=tag + "_d")
        p.op("vector",
             lambda e: e.tensor_single_scalar(g1q[:, 0:O], dq[:, 0:O], 0.5, ALU.is_ge),
             reads=["dq"], writes=["g1q"], tag=tag + "_g1")
        p.op("vector",
             lambda e: e.tensor_scalar(dq[:, 0:O], dq[:, 0:O], -0.5, -1.0,
                                       ALU.is_le, ALU.mult),
             reads=["dq"], writes=["dq"], tag=tag + "_g2")
        p.op("vector",
             lambda e: e.tensor_add(g1q[:, 0:O], g1q[:, 0:O], dq[:, 0:O]),
             reads=["g1q", "dq"], writes=["g1q"], tag=tag + "_gg")
        p.op("vector",
             lambda e: e.tensor_add(xs[:, 0:O], qi[:, 0:O], g1q[:, 0:O]),
             reads=["qi", "g1q"], writes=["xs"], tag=tag + "_q")
        p.op("vector",
             lambda e, c=c: e.tensor_mul(wdq[c][:, 0:O], xs[:, 0:O], ssx[:, 0:O]),
             reads=["xs", "ssx"], writes=[k_(kk, c) for kk in wdqkeys], tag=tag + "_dq")


def build_stage_l2ffnB():
    ctx = Ctx("l2ffnB")
    p, b = ctx.p, ctx.b
    H3 = 3072

    xnin = p.dram("xnin", [C, T], F32)
    mvec = p.dram("mvec", [1, NCORES], F32)
    wupT = p.dram("wupT", [C, H3], F32)
    wupN = p.dram("wupN", [H3, C], F32)
    wgT = p.dram("wgT", [C, H3], F32)
    wgN = p.dram("wgN", [H3, C], F32)
    seld = p.dram("selu", [32, CH * 128], F32)
    identfd = p.dram("identf", [128, 128], F32)
    up_d = p.dram("up_d", [T, H3], BF16, kind="ExternalOutput")
    gate_d = p.dram("gate_d", [T, H3], BF16, kind="ExternalOutput")
    m2d = p.dram("m2", [2, 1], F32, kind="ExternalOutput")

    xq = [p.sb(f"xq_{c}", [128, T], BF16) for c in range(CH)]
    wdq = [p.sb(f"wdq_{c}", [128, H3], BF16) for c in range(CH)]
    xrot = [p.sb("xrot_0", [128, T], F32)] * 2
    b["wnat"] = p.sb("wnat", [128, C], F32)
    b["sT"] = p.sb("sT", [32, H3], F32)
    b["rT"] = p.sb("rT", [32, H3], F32)
    b["rsx"] = p.sb("rsx", [128, H3], F32)
    b["ssx"] = p.sb("ssx", [128, H3], F32)
    b["qi"] = p.sb("qi", [128, H3], I32)
    b["dq"] = p.sb("dq", [128, H3], F32)
    b["g1q"] = p.sb("g1q", [128, H3], BF16)
    b["qf"] = p.sb("qf", [128, 96], F32)
    b["xs"] = p.sb("xs", [128, H3], F32)
    b["wrot"] = [p.sb("wrot_0", [128, H3], F32)] * 2
    b["identf"] = p.sb("identf_s", [128, 128], F32)
    b["mrow"] = p.sb("mrow", [1, 40], F32)
    b["scol2"] = p.sb("scol2", [128, 8], F32)
    selsb = p.sb("selsb", [32, CH * 128], F32)
    b["mrow2"] = p.sb("mrow2", [2, 1], F32)
    spb = p.sb("spb", [128, 512], BF16)
    mcol = p.sb("mcol", [128, 2], F32)
    psA = p.ps("psA", [128, 512], F32)
    psB = p.ps("psB", [128, 512], F32)
    b["psE"] = p.ps("psE", [128, 512], F32)
    b["psF"] = p.ps("psF", [128, 512], F32)
    b["psS"] = p.ps("psS", [128, 512], F32)

    p.op("sync", lambda e: e.dma_start(out=b["identf"][:, :], in_=identfd[:, :]),
         writes=["identf"], dma=True)
    p.op("sync", lambda e: e.dma_start(out=selsb[:, :], in_=seld[:, :]),
         writes=["selsb"], dma=True)
    p.op("sync", lambda e: e.dma_start(out=b["mrow"][0:1, 0:NCORES], in_=mvec[:, :]),
         writes=["mrow"], dma=True)
    p.op("vector", lambda e: e.memset(mcol[:, :], 0.0), writes=["mcol"])
    _emit_gscale(p, b, 0, 0, "gs")

    for c in range(CH):
        xidx = (c % 2) if xrot[0] is not xrot[1] else 0
        xr = xrot[xidx]
        xk = f"xrot:{xidx}"
        p.op("sync", lambda e, c=c, xr=xr: e.dma_start(out=xr[:, :], in_=xnin[c * 128:(c + 1) * 128, :]),
             writes=[xk], dma=True, tag="ldx")
        _emit_qfix(p, xr, xk, T, "c", b["qi"], b["dq"], b["g1q"], b["xs"],
                   xq[c], k_("xq", c), f"xq{c}",
                   b["scol2"][:, 1:2], b["scol2"][:, 0:1])

    for wi, (wT_d, wN_d, out_d, mc) in enumerate(
            [(wupT, wupN, up_d, 0), (wgT, wgN, gate_d, 1)]):
        _emit_wdeq(p, b, wN_d, wT_d, selsb, "selsb", 32, CH, wdq,
                   ["wdq"] if wi == 0 else ["wdq", "wdqg"], f"wd{wi}")
        for t in range(NT):
            tsl = slice(t * 128, (t + 1) * 128)
            for og in range(6):
                bank = psA if og % 2 == 0 else psB
                bkey = "psA" if og % 2 == 0 else "psB"
                for c in range(CH):
                    p.op("tensor",
                         lambda e, t=t, og=og, c=c, bank=bank, tsl=tsl, wi=wi: e.matmul(
                             bank[:, :], xq[c][:, tsl],
                             wdq[c][:, og * 512:(og + 1) * 512],
                             start=(c == 0), stop=(c == CH - 1)),
                         reads=[k_("xq", c), k_("wdq" if wi == 0 else "wdqg", c)],
                         writes=[bkey], tag="up_mm")
                p.op("vector",
                     lambda e, bank=bank, mc=mc: e.tensor_reduce(
                         b["g1q"][:, 0:1], bank[:, :].rearrange("p (a n) -> p a n", a=1),
                         AX.X, ALU.max, apply_absolute_value=True),
                     reads=[bkey], writes=["g1q"], tag="up_mx")
                p.op("vector",
                     lambda e, mc=mc: e.tensor_max(mcol[:, mc:mc + 1], mcol[:, mc:mc + 1],
                                                   b["g1q"][:, 0:1]),
                     reads=["g1q", "mcol"], writes=["mcol"], tag="up_mxa")
                p.op("vector",
                     lambda e, bank=bank: e.tensor_copy(spb[:, :], bank[:, :]),
                     reads=[bkey], writes=["spb"], tag="up_ev")
                p.op("sync",
                     lambda e, t=t, og=og, out_d=out_d: e.dma_start(
                         out=out_d[t * 128:(t + 1) * 128, og * 512:(og + 1) * 512],
                         in_=spb[:, :]),
                     reads=["spb"], dma=True, tag="up_st")
    p.op("tensor",
         lambda e: e.transpose(b["psE"][0:2, 0:128], mcol[:, 0:2], b["identf"][:, :]),
         reads=["mcol", "identf"], writes=["psE"], tag="mx_t0")
    p.op("vector",
         lambda e: e.tensor_reduce(b["mrow2"][0:2, 0:1],
                                   b["psE"][0:2, 0:128].rearrange("p (a n) -> p a n", a=1),
                                   AX.X, ALU.max),
         reads=["psE"], writes=["mrow2"], tag="mx_r0")
    p.op("sync", lambda e: e.dma_start(out=m2d[:, :], in_=b["mrow2"][0:2, 0:1]),
         reads=["mrow2"], dma=True, tag="stm")
    if _DBG_L2F:
        dwd = p.dram("dwd", [128, H3], BF16, kind="ExternalOutput")
        dxq = p.dram("dxq", [128, T], BF16, kind="ExternalOutput")
        dsx = p.dram("dsx", [128, H3], F32, kind="ExternalOutput")
        drx = p.dram("drx", [128, H3], F32, kind="ExternalOutput")
        p.op("sync", lambda e: e.dma_start(out=dwd[:, :], in_=wdq[0][:, :]),
             reads=[k_("wdqg", 0)], dma=True, tag="dbg")
        p.op("sync", lambda e: e.dma_start(out=dxq[:, :], in_=xq[0][:, :]),
             reads=[k_("xq", 0)], dma=True, tag="dbg")
        p.op("sync", lambda e: e.dma_start(out=dsx[:, :], in_=b["ssx"][:, :]),
             reads=["ssx"], dma=True, tag="dbg")
        p.op("sync", lambda e: e.dma_start(out=drx[:, :], in_=b["rsx"][:, :]),
             reads=["rsx"], dma=True, tag="dbg")
    return p.build()




def build_stage_l2ffnC():
    ctx = Ctx("l2ffnC")
    p, b = ctx.p, ctx.b
    H3 = 3072

    gate_d = p.dram("gate_d", [T, H3], BF16)
    mgd = p.dram("mg8", [1, NCORES], F32)
    identfd = p.dram("identf", [128, 128], F32)
    g2_d = p.dram("g2_d", [T, H3], BF16, kind="ExternalOutput")
    m1d = p.dram("m1", [1, 1], F32, kind="ExternalOutput")

    grot = [p.sb(f"grot_{i}", [128, 512], BF16) for i in range(2)]
    b["qi"] = p.sb("qi", [128, 512], I32)
    b["dq"] = p.sb("dq", [128, 512], F32)
    b["g1q"] = p.sb("g1q", [128, 512], F32)
    b["qf"] = p.sb("qf", [128, 512], F32)
    gdq = p.sb("gdq", [128, 512], F32)
    sgf = p.sb("sgf", [128, 512], F32)
    sdq = p.sb("sdq", [128, 512], F32)
    pf = p.sb("pf", [128, 512], F32)
    spb = p.sb("spb", [128, 512], BF16)
    mcol = p.sb("mcol", [128, 1], F32)
    amc = p.sb("amc", [128, 1], F32)
    b["identf"] = p.sb("identf_s", [128, 128], F32)
    b["mrow"] = p.sb("mrow", [1, 40], F32)
    b["mrow2"] = p.sb("mrow2", [2, 1], F32)
    b["scol2"] = p.sb("scol2", [128, 8], F32)
    b["psS"] = p.ps("psS", [128, 512], F32)
    psE = p.ps("psE", [128, 512], F32)

    p.op("sync", lambda e: e.dma_start(out=b["identf"][:, :], in_=identfd[:, :]),
         writes=["identf"], dma=True)
    p.op("sync", lambda e: e.dma_start(out=b["mrow"][0:1, 0:NCORES], in_=mgd[:, :]),
         writes=["mrow"], dma=True)
    p.op("vector", lambda e: e.memset(mcol[:, :], 0.0), writes=["mcol"])
    _emit_gscale(p, b, 0, 0, "gs")

    for t in range(NT):
        for og in range(6):
            gr = grot[(t * 6 + og) % 2]
            gk = f"grot:{(t * 6 + og) % 2}"
            p.op("sync",
                 lambda e, t=t, og=og, gr=gr: e.dma_start(
                     out=gr[:, :], in_=gate_d[t * 128:(t + 1) * 128, og * 512:(og + 1) * 512]),
                 writes=[gk], dma=True, tag="ldg")
            _emit_qfix(p, gr, gk, 512, "c", b["qi"], b["dq"], b["g1q"], b["qf"],
                       gdq, "gdq", "gq", b["scol2"][:, 1:2], b["scol2"][:, 0:1])
            p.op("scalar", lambda e: e.activation(sgf[:, :], gdq[:, :], AF.Sigmoid),
                 reads=["gdq"], writes=["sgf"], tag="sig")
            _emit_qfix(p, sgf, "sgf", 512, "c", b["qi"], b["dq"], b["g1q"], b["qf"],
                       sdq, "sdq", "uq", 65536.0, 1.0 / 65536.0, rskey=(),
                       clip_hi=65535.0)
            p.op("vector", lambda e: e.tensor_mul(pf[:, :], gdq[:, :], sdq[:, :]),
                 reads=["gdq", "sdq"], writes=["pf"], tag="g2p")
            p.op("vector",
                 lambda e: e.tensor_reduce(amc[:, 0:1], pf[:, :].rearrange("p (a n) -> p a n", a=1),
                                           AX.X, ALU.max, apply_absolute_value=True),
                 reads=["pf"], writes=["amc"], tag="g2mx")
            p.op("vector", lambda e: e.tensor_max(mcol[:, 0:1], mcol[:, 0:1], amc[:, 0:1]),
                 reads=["amc", "mcol"], writes=["mcol"], tag="g2mxa")
            p.op("vector", lambda e: e.tensor_copy(spb[:, :], pf[:, :]),
                 reads=["pf"], writes=["spb"], tag="g2bf")
            p.op("sync",
                 lambda e, t=t, og=og: e.dma_start(
                     out=g2_d[t * 128:(t + 1) * 128, og * 512:(og + 1) * 512], in_=spb[:, :]),
                 reads=["spb"], dma=True, tag="stg2")
    p.op("tensor",
         lambda e: e.transpose(psE[0:1, 0:128], mcol[:, 0:1], b["identf"][:, :]),
         reads=["mcol", "identf"], writes=["psE"], tag="mx_t")
    p.op("vector",
         lambda e: e.tensor_reduce(b["mrow2"][0:1, 0:1],
                                   psE[0:1, 0:128].rearrange("p (a n) -> p a n", a=1),
                                   AX.X, ALU.max),
         reads=["psE"], writes=["mrow2"], tag="mx_r")
    p.op("sync", lambda e: e.dma_start(out=m1d[:, :], in_=b["mrow2"][0:1, 0:1]),
         reads=["mrow2"], dma=True, tag="stm")
    return p.build()


def build_stage_l2ffnD():
    ctx = Ctx("l2ffnD")
    p, b = ctx.p, ctx.b
    H3 = 3072

    up_d = p.dram("up_d", [T, H3], BF16)
    g2_d = p.dram("g2_d", [T, H3], BF16)
    mud = p.dram("mu8", [1, NCORES], F32)
    mg2d = p.dram("mg28", [1, NCORES], F32)
    identfd = p.dram("identf", [128, 128], F32)
    identbd = p.dram("identb", [128, 128], BF16)
    oT_d = p.dram("oT_d", [H3, T], BF16, kind="ExternalOutput")
    m1d = p.dram("m1", [1, 1], F32, kind="ExternalOutput")

    urot = [p.sb(f"urot_{i}", [128, 512], BF16) for i in range(2)]
    grot = [p.sb(f"grot_{i}", [128, 512], BF16) for i in range(2)]
    b["qi"] = p.sb("qi", [128, 512], I32)
    b["dq"] = p.sb("dq", [128, 512], F32)
    b["g1q"] = p.sb("g1q", [128, 512], F32)
    b["qf"] = p.sb("qf", [128, 512], F32)
    udq = p.sb("udq", [128, 512], F32)
    g2dq = p.sb("g2dq", [128, 512], F32)
    pf = p.sb("pf", [128, 512], F32)
    pb = p.sb("pb", [128, 512], BF16)
    trb = p.sb("trb", [128, 128], BF16)
    mcol = p.sb("mcol", [128, 1], F32)
    amc = p.sb("amc", [128, 1], F32)
    b["identf"] = p.sb("identf_s", [128, 128], F32)
    identb = p.sb("identb_s", [128, 128], BF16)
    b["mrow"] = p.sb("mrow", [1, 40], F32)
    b["mrow2"] = p.sb("mrow2", [2, 1], F32)
    b["scol2"] = p.sb("scol2", [128, 8], F32)
    b["psS"] = p.ps("psS", [128, 512], F32)
    psE = p.ps("psE", [128, 512], F32)
    psT = [p.ps("psT0", [128, 128], BF16), p.ps("psT1", [128, 128], BF16)]

    p.op("sync", lambda e: e.dma_start(out=b["identf"][:, :], in_=identfd[:, :]),
         writes=["identf"], dma=True)
    p.op("sync", lambda e: e.dma_start(out=identb[:, :], in_=identbd[:, :]),
         writes=["identb"], dma=True)
    p.op("sync", lambda e: e.dma_start(out=b["mrow"][0:1, 0:NCORES], in_=mud[:, :]),
         writes=["mrow"], dma=True)
    p.op("sync", lambda e: e.dma_start(out=b["mrow"][0:1, 8:8 + NCORES], in_=mg2d[:, :]),
         writes=["mrow"], dma=True)
    p.op("vector", lambda e: e.memset(mcol[:, :], 0.0), writes=["mcol"])
    _emit_gscale(p, b, 0, 0, "gsu")
    _emit_gscale(p, b, 8, 1, "gsg")

    for t in range(NT):
        for og in range(6):
            i2 = (t * 6 + og) % 2
            ur, gr = urot[i2], grot[i2]
            uk, gk = f"urot:{i2}", f"grot:{i2}"
            p.op("sync",
                 lambda e, t=t, og=og, ur=ur: e.dma_start(
                     out=ur[:, :], in_=up_d[t * 128:(t + 1) * 128, og * 512:(og + 1) * 512]),
                 writes=[uk], dma=True, tag="ldu")
            p.op("sync",
                 lambda e, t=t, og=og, gr=gr: e.dma_start(
                     out=gr[:, :], in_=g2_d[t * 128:(t + 1) * 128, og * 512:(og + 1) * 512]),
                 writes=[gk], dma=True, tag="ldg")
            _emit_qfix(p, ur, uk, 512, "c", b["qi"], b["dq"], b["g1q"], b["qf"],
                       udq, "udq", "uq", b["scol2"][:, 1:2], b["scol2"][:, 0:1])
            _emit_qfix(p, gr, gk, 512, "c", b["qi"], b["dq"], b["g1q"], b["qf"],
                       g2dq, "g2dq", "gq", b["scol2"][:, 3:4], b["scol2"][:, 2:3])
            p.op("vector", lambda e: e.tensor_mul(pf[:, :], udq[:, :], g2dq[:, :]),
                 reads=["udq", "g2dq"], writes=["pf"], tag="opre")
            p.op("vector",
                 lambda e: e.tensor_reduce(amc[:, 0:1], pf[:, :].rearrange("p (a n) -> p a n", a=1),
                                           AX.X, ALU.max, apply_absolute_value=True),
                 reads=["pf"], writes=["amc"], tag="omx")
            p.op("vector", lambda e: e.tensor_max(mcol[:, 0:1], mcol[:, 0:1], amc[:, 0:1]),
                 reads=["amc", "mcol"], writes=["mcol"], tag="omxa")
            p.op("vector", lambda e: e.tensor_copy(pb[:, :], pf[:, :]),
                 reads=["pf"], writes=["pb"], tag="obf")
            for j in range(4):
                half = j % 2
                p.op("tensor",
                     lambda e, j=j, half=half: e.transpose(
                         psT[half][:, 0:128], pb[:, j * 128:(j + 1) * 128], identb[:, :]),
                     reads=["pb", "identb"], writes=[f"psT{half}"], tag="otr")
                p.op("vector",
                     lambda e, half=half: e.tensor_copy(trb[:, :], psT[half][:, 0:128]),
                     reads=[f"psT{half}"], writes=["trb"], tag="otre")
                p.op("sync",
                     lambda e, t=t, og=og, j=j: e.dma_start(
                         out=oT_d[og * 512 + j * 128:og * 512 + (j + 1) * 128,
                                  t * 128:(t + 1) * 128],
                         in_=trb[:, :]),
                     reads=["trb"], dma=True, tag="sto")
    p.op("tensor",
         lambda e: e.transpose(psE[0:1, 0:128], mcol[:, 0:1], b["identf"][:, :]),
         reads=["mcol", "identf"], writes=["psE"], tag="mx_t")
    p.op("vector",
         lambda e: e.tensor_reduce(b["mrow2"][0:1, 0:1],
                                   psE[0:1, 0:128].rearrange("p (a n) -> p a n", a=1),
                                   AX.X, ALU.max),
         reads=["psE"], writes=["mrow2"], tag="mx_r")
    p.op("sync", lambda e: e.dma_start(out=m1d[:, :], in_=b["mrow2"][0:1, 0:1]),
         reads=["mrow2"], dma=True, tag="stm")
    return p.build()


def build_stage_l2ffnE():
    ctx = Ctx("l2ffnE")
    p, b = ctx.p, ctx.b
    H3 = 3072
    NCH3 = H3 // 128

    oT_d = p.dram("oT_d", [H3, T], BF16)
    mod_ = p.dram("mo8", [1, NCORES], F32)
    wdnT = p.dram("wdnT", [H3, C], F32)
    wdnN = p.dram("wdnN", [C, H3], F32)
    seld = p.dram("selw", [96, NCH3 * 128], F32)
    identfd = p.dram("identf", [128, 128], F32)
    xin = p.dram("xin", [C, T], F32)
    xout = p.dram("xout", [C, T], F32, kind="ExternalOutput")

    wdq = [p.sb(f"wdq_{c}", [128, C], BF16) for c in range(NCH3)]
    oqT = [p.sb(f"oqT_{c}", [128, T // 2], BF16) for c in range(NCH3)]
    b["wnat"] = p.sb("wnat", [128, H3], F32)
    b["sT"] = p.sb("sT", [96, C], F32)
    b["rT"] = p.sb("rT", [96, C], F32)
    b["rsx"] = p.sb("rsx", [128, C], F32)
    b["ssx"] = p.sb("ssx", [128, C], F32)
    b["qi"] = p.sb("qi", [128, C], I32)
    b["dq"] = p.sb("dq", [128, C], F32)
    b["g1q"] = p.sb("g1q", [128, C], F32)
    b["qf"] = p.sb("qf", [128, C], F32)
    b["xs"] = p.sb("xs", [128, C], F32)
    b["wrot"] = [p.sb(f"wrot_{i}", [128, C], F32) for i in range(2)]
    orot = [p.sb(f"orot_{i}", [128, T // 2], BF16) for i in range(2)]
    xrot = [p.sb(f"xrot_{i}", [128, T // 2], F32) for i in range(2)]
    xob = p.sb("xob", [128, 512], F32)
    b["identf"] = p.sb("identf_s", [128, 128], F32)
    b["mrow"] = p.sb("mrow", [1, 40], F32)
    b["scol2"] = p.sb("scol2", [128, 8], F32)
    selsb = p.sb("selsb", [96, NCH3 * 128], F32)
    psA = p.ps("psA", [128, 512], F32)
    psB = p.ps("psB", [128, 512], F32)
    b["psE"] = p.ps("psE", [128, 512], F32)
    b["psF"] = p.ps("psF", [128, 512], F32)
    b["psS"] = p.ps("psS", [128, 512], F32)

    p.op("sync", lambda e: e.dma_start(out=b["identf"][:, :], in_=identfd[:, :]),
         writes=["identf"], dma=True)
    p.op("sync", lambda e: e.dma_start(out=selsb[:, :], in_=seld[:, :]),
         writes=["selsb"], dma=True)
    p.op("sync", lambda e: e.dma_start(out=b["mrow"][0:1, 0:NCORES], in_=mod_[:, :]),
         writes=["mrow"], dma=True)
    _emit_gscale(p, b, 0, 0, "gs")

    _emit_wdeq(p, b, wdnN, wdnT, selsb, "selsb", 96, NCH3, wdq, ["wdq"], "wd")

    for half in range(2):
        hsl = slice(half * (T // 2), (half + 1) * (T // 2))
        for c in range(NCH3):
            orr = orot[c % 2]
            ok = f"orot:{c % 2}"
            p.op("sync",
                 lambda e, c=c, orr=orr, hsl=hsl: e.dma_start(
                     out=orr[:, :], in_=oT_d[c * 128:(c + 1) * 128, hsl]),
                 writes=[ok], dma=True, tag="ldo")
            _emit_qfix(p, orr, ok, T // 2, "c", b["qi"], b["dq"], b["g1q"], b["qf"],
                       oqT[c], k_("oqT", c), f"oq{c}",
                       b["scol2"][:, 1:2], b["scol2"][:, 0:1])
        for oc in range(CH):
            xr = xrot[oc % 2]
            xk = f"xrot:{oc % 2}"
            p.op("sync",
                 lambda e, oc=oc, xr=xr, hsl=hsl: e.dma_start(
                     out=xr[:, :], in_=xin[oc * 128:(oc + 1) * 128, hsl]),
                 writes=[xk], dma=True, tag="ldx")
            for tg in range(2):
                bank = psA if tg == 0 else psB
                bkey = "psA" if tg == 0 else "psB"
                tsl = slice(tg * 512, (tg + 1) * 512)
                for c in range(NCH3):
                    p.op("tensor",
                         lambda e, oc=oc, c=c, tsl=tsl, bank=bank: e.matmul(
                             bank[:, :], wdq[c][:, oc * 128:(oc + 1) * 128],
                             oqT[c][:, tsl], start=(c == 0), stop=(c == NCH3 - 1)),
                         reads=[k_("wdq", c), k_("oqT", c)], writes=[bkey], tag="dn_mm")
                p.op("vector",
                     lambda e, bank=bank, tsl=tsl, xr=xr: e.tensor_add(
                         xob[:, :], bank[:, :], xr[:, tsl]),
                     reads=[bkey, xk], writes=["xob"], tag="dn_add")
                p.op("sync",
                     lambda e, oc=oc, half=half, tg=tg: e.dma_start(
                         out=xout[oc * 128:(oc + 1) * 128,
                                  half * (T // 2) + tg * 512:half * (T // 2) + (tg + 1) * 512],
                         in_=xob[:, :]),
                     reads=["xob"], dma=True, tag="dn_st")
    return p.build()





def build_stage_l3attn():
    H, D = 16, 64
    ctx = Ctx("l3attn")
    p, b = ctx.p, ctx.b
    cb = b["cb"]
    ones_f, ones_k1 = b["ones_f"], b["ones_k1"]

    xin = p.dram("xin", [C, T], F32)
    xout = p.dram("xout", [C, T], F32, kind="ExternalOutput")
    xn7_d = p.dram("xn7", [C, T], F32, kind="ExternalOutput")
    top1_d = p.dram("top1", [128, NT], F32, kind="ExternalOutput")
    rank_d = p.dram("rank", [128, NT], F32, kind="ExternalOutput")
    comb_d = p.dram("comb", [128, NT], F32, kind="ExternalOutput")
    cnt_d = p.dram("cnt", [1, 8], F32, kind="ExternalOutput")
    wqcT_d = p.dram("wqcT", [C, C], F32)
    wkcT_d = p.dram("wkcT", [C, C], F32)
    wvcT_d = p.dram("wvcT", [C, C], F32)
    wocT_d = p.dram("wocT", [C, C], F32)
    wqlT_d = p.dram("wqlT", [C, C], F32)
    wklT_d = p.dram("wklT", [C, C], F32)
    wvlT_d = p.dram("wvlT", [C, C], F32)
    wolT_d = p.dram("wolT", [C, C], F32)
    bqc_d = p.dram("bqc2", [128, CH], F32)
    bvc_d = p.dram("bvc2", [128, CH], F32)
    boc_d = p.dram("boc2", [1, C], F32)
    bql_d = p.dram("bql2", [64, H], F32)
    bvl_d = p.dram("bvl2", [128, H], F32)
    bol_d = p.dram("bol2", [128, CH], F32)
    masks_d = p.dram("masks7", [7, 128, 512], BF16)
    identfd = p.dram("identf", [128, 128], F32)
    identbd = p.dram("identb", [128, 128], BF16)
    wg_d = p.dram("wgsb", [128, 64], F32)
    stri_d = p.dram("stri", [128, 128], BF16)

    xrot = [p.sb("xrot_0", [128, T], F32)] * 2
    xn = [p.sb(f"xn_{c}", [128, T], BF16) for c in range(CH)]
    wA = p.sb("wA", [128, CH * C], BF16)
    wB = p.sb("wB", [128, CH * C], BF16)
    wC = p.sb("wC", [128, CH * C], BF16)
    v65 = [p.sb(f"v65_{t}", [128, H * 65], BF16) for t in range(NT)]
    attnres = [p.sb(f"attnres_{i}", [128, T], BF16) for i in range(H // 2)]
    qh = p.sb("qh", [64, T], BF16)
    kh = p.sb("kh", [64, T], BF16)
    pt = p.sb("pt", [128, 512], BF16)
    mask_t = p.sb("mask_t", [128, 7 * 512], BF16)
    scr = p.sb("scr", [128, 512], F32)
    scr2 = p.sb("scr2", [128, 512], F32)
    brow = p.sb("brow", [1, T], F32)
    browb = p.sb("browb", [1, T], BF16)
    nrow = p.sb("nrow", [128, 512], F32)
    Bsb = p.sb("Bsb", [128, 512], F32)
    attf = p.sb("attf", [16, T], F32)
    qb = p.sb("qb", [128, C], BF16)
    qc = p.sb("qc", [128, CH], F32)
    ctxc = p.sb("ctxc", [128, CH], BF16)
    gcol = p.sb("gcol", [128, CH], F32)
    grow = p.sb("grow", [1, C], BF16)
    vc = p.sb("vc", [128, C], BF16)
    att_tok = p.sb("att_tok", [128, 16], BF16)
    sc_tok = p.sb("sc_tok", [128, 16], F32)
    tiny = p.sb("tiny", [16, 4], F32)
    bias_qc = p.sb("bias_qc", [128, CH], F32)
    bias_vc = p.sb("bias_vc", [128, CH], F32)
    bias_oc = p.sb("bias_oc", [1, C], BF16)
    bias_ql = p.sb("bias_ql", [64, H], F32)
    bias_vl = p.sb("bias_vl", [128, H], F32)
    bias_ol = p.sb("bias_ol", [128, CH], F32)
    identf = p.sb("identf_s", [128, 128], F32)
    identb = p.sb("identb_s", [128, 128], BF16)
    wgsbb = p.sb("wg_b", [128, 64], BF16)
    stri = p.sb("stri_s", [128, 128], BF16)
    iot = p.sb("iot", [128, 8], F32)
    rr = p.sb("rr", [128, 8 * 8], F32)
    ohb = p.sb("ohb", [128, 8], BF16)
    crow = p.sb("crow", [1, 8], F32)
    t1v = p.sb("t1v", [128, NT], F32)
    rkv = p.sb("rkv", [128, NT], F32)
    cbv = p.sb("cbv", [128, NT], F32)
    ones128b = p.sb("ones128b", [128, 1], BF16)

    ps_s = [p.ps(f"ps_s{i}", [128, 512], F32) for i in range(4)]
    ps_a = [p.ps(f"ps_a{i}", [128, 512], F32) for i in range(4)]

    onesr = p.sb("onesr", [128, 64], F32)
    b["onesr"] = onesr
    p.op("vector", lambda e: e.memset(onesr[:, :], 1.0), writes=["onesr"])
    p.op("vector", lambda e: e.memset(ones128b[:, :], 1.0), writes=["ones128b"])
    for nm, dst, dk, srcd in (("bqc", bias_qc, "bias_qc", bqc_d), ("bvc", bias_vc, "bias_vc", bvc_d),
                              ("boc", bias_oc, "bias_oc", boc_d), ("bql", bias_ql, "bias_ql", bql_d),
                              ("bvl", bias_vl, "bias_vl", bvl_d), ("bol", bias_ol, "bias_ol", bol_d)):
        p.op("gpsimd", lambda e, dst=dst, srcd=srcd: e.dma_start(out=dst[:, :], in_=srcd[:, :]),
             writes=[dk], dma=True, tag="bld")
    p.op("sync", lambda e: e.dma_start(out=identf[:, :], in_=identfd[:, :]), writes=["identf"], dma=True)
    p.op("sync", lambda e: e.dma_start(out=identb[:, :], in_=identbd[:, :]), writes=["identb"], dma=True)
    p.op("gpsimd", lambda e: e.dma_start(out=wgsbb[:, :], in_=wg_d[:, :]), writes=["wgsbb"], dma=True)
    p.op("sync", lambda e: e.dma_start(out=stri[:, :], in_=stri_d[:, :]), writes=["stri"], dma=True)
    for r in range(7):
        p.op("sync", lambda e, r=r: e.dma_start(out=mask_t[:, r * 512:(r + 1) * 512], in_=masks_d[r, :, :]),
             writes=["mask_t"], dma=True)
    for e8 in range(8):
        p.op("vector", lambda e, e8=e8: e.memset(iot[:, e8:e8 + 1], float(e8)), writes=["iot"])
    p.op("vector", lambda e: e.memset(crow[0:1, :], 0.0), writes=["crow"])
    p.op("vector", lambda e: e.tensor_scalar_mul(bias_ql[:, :], bias_ql[:, :], 0.125),
         reads=["bias_ql"], writes=["bias_ql"], tag="bscale")

    def load_w(dst, dstkeys, srcd, tag):
        for c in range(CH):
            p.op("gpsimd",
                 lambda e, c=c, dst=dst, srcd=srcd: e.dma_start(
                     out=dst[:, c * C:(c + 1) * C], in_=srcd[c * 128:(c + 1) * 128, :]),
                 writes=[k_(kk, c) for kk in dstkeys], dma=True, tag=tag)

    def emit_streamln(src_d, dst, dstprefix, outf_d=None, tag="", alias=None):
        def wkeys(c):
            ks = [k_(dstprefix, c)]
            if alias:
                ks.append(k_(alias, c))
            return ks
        for c in range(CH):
            xr = xrot[0]
            xk = "xrot:0"
            p.op("sync", lambda e, c=c, xr=xr, src_d=src_d: e.dma_start(
                     out=xr[:, :], in_=src_d[c * 128:(c + 1) * 128, :]),
                 writes=[xk], dma=True, tag=tag + "lnx")
            for s in range(4):
                sl = slice(s * 512, (s + 1) * 512)
                p.op("tensor",
                     lambda e, s=s, sl=sl, c=c, xr=xr: e.matmul(
                         ps_s[s][0:1, :], ones_f[:, 0:1], xr[:, sl],
                         start=(c == 0), stop=(c == CH - 1)),
                     reads=[xk, "ones_f"], writes=[f"ps_s{s}"], tag=tag + "ln_s")
                p.op("scalar", lambda e, sl=sl, xr=xr: e.activation(scr[:, :], xr[:, sl], AF.Square),
                     reads=[xk], writes=["scr"], tag=tag + "ln_sq")
                p.op("tensor",
                     lambda e, s=s, c=c: e.matmul(
                         ps_s[s][32:33, :], ones_f[:, 0:1], scr[:, :],
                         start=(c == 0), stop=(c == CH - 1)),
                     reads=["scr", "ones_f"], writes=[f"ps_s{s}"], tag=tag + "ln_q")
        for s in range(4):
            sl = slice(s * 512, (s + 1) * 512)
            p.op("vector", lambda e, s=s, sl=sl: e.tensor_scalar_mul(brow[0:1, sl], ps_s[s][0:1, :], 1.0 / C),
                 reads=[f"ps_s{s}"], writes=["brow"], tag=tag + "ln_mean")
            p.op("vector", lambda e, sl=sl: e.tensor_mul(scr2[0:1, :], brow[0:1, sl], brow[0:1, sl]),
                 reads=["brow"], writes=["scr2"], tag=tag + "ln_m2")
            p.op("vector",
                 lambda e, s=s: e.scalar_tensor_tensor(scr[0:1, :], ps_s[s][32:33, :], 1.0 / C,
                                                       scr2[0:1, :], ALU.mult, ALU.subtract),
                 reads=[f"ps_s{s}", "scr2"], writes=["scr"], tag=tag + "ln_var")
            p.op("scalar", lambda e: e.activation(scr[0:1, :], scr[0:1, :], AF.Ln, bias=cb[0:1, 0:1]),
                 reads=["scr", "cb"], writes=["scr"], tag=tag + "ln_ln")
            p.op("scalar", lambda e: e.activation(scr[0:1, :], scr[0:1, :], AF.Exp, scale=-0.5),
                 reads=["scr"], writes=["scr"], tag=tag + "ln_exp")
            p.op("vector", lambda e, sl=sl: e.tensor_copy(browb[0:1, sl], scr[0:1, :]),
                 reads=["scr"], writes=["browb"], tag=tag + "ln_rbf")
        for c in range(CH):
            xr = xrot[0]
            xk = "xrot:0"
            p.op("sync", lambda e, c=c, xr=xr, src_d=src_d: e.dma_start(
                     out=xr[:, :], in_=src_d[c * 128:(c + 1) * 128, :]),
                 writes=[xk], dma=True, tag=tag + "lnx2")
            for s in range(4):
                sl = slice(s * 512, (s + 1) * 512)
                p.op("tensor",
                     lambda e, sl=sl: e.matmul(ps_s[2][:, :], ones_k1[:, :], brow[0:1, sl],
                                               start=True, stop=True),
                     reads=["brow", "ones_k1"], writes=["ps_s2"], tag=tag + "ln_bM")
                p.op("tensor",
                     lambda e, sl=sl: e.matmul(ps_s[3][:, :], b["ones_k1b"][:, :], browb[0:1, sl],
                                               start=True, stop=True),
                     reads=["browb", "ones_k1b"], writes=["ps_s3"], tag=tag + "ln_bR")
                p.op("vector",
                     lambda e, sl=sl, xr=xr: e.tensor_sub(scr[:, :], xr[:, sl], ps_s[2][:, :]),
                     reads=[xk, "ps_s2"], writes=["scr"], tag=tag + "ln_sub")
                if outf_d is None:
                    p.op("vector",
                         lambda e, c=c, sl=sl: e.tensor_mul(dst[c][:, sl], scr[:, :], ps_s[3][:, :]),
                         reads=["scr", "ps_s3"], writes=wkeys(c), tag=tag + "ln_nrm")
                else:
                    p.op("vector",
                         lambda e, sl=sl: e.tensor_mul(scr2[:, :], scr[:, :], ps_s[3][:, :]),
                         reads=["scr", "ps_s3"], writes=["scr2"], tag=tag + "ln_nrm")
                    p.op("vector",
                         lambda e, c=c, sl=sl: e.tensor_copy(dst[c][:, sl], scr2[:, :]),
                         reads=["scr2"], writes=wkeys(c), tag=tag + "ln_bf")
                    p.op("sync",
                         lambda e, c=c, sl=sl, outf_d=outf_d: e.dma_start(
                             out=outf_d[c * 128:(c + 1) * 128, sl], in_=scr2[:, :]),
                         reads=["scr2"], dma=True, tag=tag + "ln_st")

    emit_streamln(xin, xn, "xn")

    load_w(wA, ["wA", "wqc"], wqcT_d, "wqc")
    for oc in range(CH):
        for c in range(CH):
            p.op("tensor",
                 lambda e, oc=oc, c=c: e.matmul(
                     ps_a[3][:, oc:oc + 1], wA[:, c * C + oc * 128:c * C + (oc + 1) * 128],
                     xn[c][:, 0:1], start=(c == 0), stop=(c == CH - 1)),
                 reads=[k_("wqc", c), k_("xn", c)], writes=["ps_a3"], tag="qc_mm")
    p.op("vector",
         lambda e: e.scalar_tensor_tensor(qc[:, :], ps_a[3][:, 0:CH], 1.0,
                                          bias_qc[:, :], ALU.mult, ALU.add),
         reads=["ps_a3", "bias_qc"], writes=["qc"], tag="qc_ev")
    p.op("vector", lambda e: e.tensor_scalar_mul(qc[:, :], qc[:, :], 0.125),
         reads=["qc"], writes=["qc"], tag="qc_sc")
    p.op("vector", lambda e: e.tensor_copy(ctxc[:, :], qc[:, :]),
         reads=["qc"], writes=["ctxc"], tag="qc_bf")
    for c in range(CH):
        p.op("tensor",
             lambda e, c=c: e.matmul(ps_s[0][0:1, c * 128 % 512:(c * 128 % 512) + 128],
                                     ctxc[:, c:c + 1], identb[:, :],
                                     start=True, stop=True) if False else e.matmul(
                 ps_s[c // 4][0:1, (c % 4) * 128:((c % 4) + 1) * 128],
                 ctxc[:, c:c + 1], identb[:, :], start=True, stop=True),
             reads=["ctxc", "identb"], writes=[f"ps_s{c // 4}"], tag="qr_mm")
    for half in range(2):
        p.op("vector",
             lambda e, half=half: e.tensor_copy(grow[0:1, half * 512:(half + 1) * 512],
                                                ps_s[half][0:1, 0:512]),
             reads=[f"ps_s{half}"], writes=["grow"], tag="qr_ev")
    for half in range(2):
        p.op("tensor",
             lambda e, half=half: e.matmul(ps_a[2][:, :], b["ones_k1b"][:, :],
                                           grow[0:1, half * 512:(half + 1) * 512],
                                           start=True, stop=True),
             reads=["grow", "ones_k1b"], writes=["ps_a2"], tag="qb_mm")
        p.op("vector",
             lambda e, half=half: e.tensor_copy(qb[:, half * 512:(half + 1) * 512], ps_a[2][:, :]),
             reads=["ps_a2"], writes=["qb"], tag="qb_ev")

    load_w(wB, ["wB", "wkc"], wkcT_d, "wkc")
    for t in range(NT):
        tsl = slice(t * 128, (t + 1) * 128)
        for g in range(2):
            bank = ps_a[g]
            bkey = f"ps_a{g}"
            for c in range(CH):
                p.op("tensor",
                     lambda e, t=t, g=g, c=c, bank=bank, tsl=tsl: e.matmul(
                         bank[:, :], xn[c][:, tsl],
                         wB[:, c * C + g * 512:c * C + (g + 1) * 512],
                         start=(c == 0), stop=(c == CH - 1)),
                     reads=[k_("xn", c), k_("wkc", c)], writes=[bkey], tag="kc_mm")
            p.op("vector",
                 lambda e, g=g, bank=bank: e.tensor_mul(scr[:, :], bank[:, :],
                                                        qb[:, g * 512:(g + 1) * 512]),
                 reads=[bkey, "qb"], writes=["scr"], tag="kc_mul")
            p.op("vector",
                 lambda e, g=g: e.tensor_reduce(
                     sc_tok[:, g * 8:(g + 1) * 8],
                     scr[:, :].rearrange("p (h u) -> p h u", u=64),
                     AX.X, ALU.add),
                 reads=["scr"], writes=["sc_tok"], tag="kc_red")
        p.op("tensor",
             lambda e: e.transpose(ps_a[2][0:16, 0:128], sc_tok[:, :], identf[:, :]),
             reads=["sc_tok", "identf"], writes=["ps_a2"], tag="sc_tr")
        p.op("vector",
             lambda e, tsl=tsl: e.tensor_copy(attf[0:16, tsl], ps_a[2][0:16, 0:128]),
             reads=["ps_a2"], writes=["attf"], tag="sc_tre")
    p.op("vector",
         lambda e: e.tensor_reduce(tiny[0:16, 0:1], attf[0:16, :].rearrange("p (a n) -> p a n", a=1),
                                   AX.X, ALU.max),
         reads=["attf"], writes=["tiny"], tag="gm_max")
    p.op("vector", lambda e: e.tensor_scalar_mul(tiny[0:16, 1:2], tiny[0:16, 0:1], -1.0),
         reads=["tiny"], writes=["tiny"], tag="gm_neg")
    p.op("scalar",
         lambda e: e.activation(attf[0:16, :], attf[0:16, :], AF.Exp, bias=tiny[0:16, 1:2]),
         reads=["attf", "tiny"], writes=["attf"], tag="gm_exp")
    p.op("vector",
         lambda e: e.tensor_reduce(tiny[0:16, 2:3], attf[0:16, :].rearrange("p (a n) -> p a n", a=1),
                                   AX.X, ALU.add),
         reads=["attf"], writes=["tiny"], tag="gm_sum")
    p.op("vector", lambda e: e.reciprocal(tiny[0:16, 3:4], tiny[0:16, 2:3]),
         reads=["tiny"], writes=["tiny"], tag="gm_rc")
    p.op("vector",
         lambda e: e.tensor_scalar_mul(attf[0:16, :], attf[0:16, :], tiny[0:16, 3:4]),
         reads=["attf", "tiny"], writes=["attf"], tag="gm_nrm")

    load_w(wC, ["wC", "wvc"], wvcT_d, "wvc")
    for t in range(NT):
        tsl = slice(t * 128, (t + 1) * 128)
        for g in range(2):
            bank = ps_a[g]
            bkey = f"ps_a{g}"
            for c in range(CH):
                p.op("tensor",
                     lambda e, t=t, g=g, c=c, bank=bank, tsl=tsl: e.matmul(
                         bank[:, :], xn[c][:, tsl],
                         wC[:, c * C + g * 512:c * C + (g + 1) * 512],
                         start=(c == 0), stop=(c == CH - 1)),
                     reads=[k_("xn", c), k_("wvc", c)], writes=[bkey], tag="vc_mm")
            p.op("vector",
                 lambda e, g=g, bank=bank: e.tensor_copy(vc[:, g * 512:(g + 1) * 512], bank[:, :]),
                 reads=[bkey], writes=["vc"], tag="vc_ev")
        p.op("tensor",
             lambda e, tsl=tsl: e.transpose(ps_a[2][0:128, 0:16], attf[0:16, tsl], identf[0:16, 0:16]),
             reads=["attf", "identf"], writes=["ps_a2"], tag="at_tr")
        p.op("vector",
             lambda e: e.tensor_copy(att_tok[:, :], ps_a[2][0:128, 0:16]),
             reads=["ps_a2"], writes=["att_tok"], tag="at_tre")
        for h in range(H):
            pb = (h % 2) * 64
            p.op("tensor",
                 lambda e, h=h, pb=pb, t=t: e.matmul(
                     ps_s[0][pb:pb + 64, h // 2:h // 2 + 1], vc[:, h * 64:(h + 1) * 64],
                     att_tok[:, h:h + 1],
                     start=(t == 0), stop=(t == NT - 1)),
                 reads=["vc", "att_tok"], writes=["ps_s0"], tag="ctx_mm")
    p.op("vector",
         lambda e: e.scalar_tensor_tensor(ctxc[:, :], ps_s[0][:, 0:CH], 1.0,
                                          bias_vc[:, :], ALU.mult, ALU.add),
         reads=["ps_s0", "bias_vc"], writes=["ctxc"], tag="ctx_ev")
    load_w(wA, ["wA", "wqc", "woc"], wocT_d, "woc")
    for half in range(2):
        for c in range(CH):
            p.op("tensor",
                 lambda e, half=half, c=c: e.matmul(
                     ps_a[3][0:1, :], ctxc[:, c:c + 1],
                     wA[:, c * C + half * 512:c * C + (half + 1) * 512],
                     start=(c == 0), stop=(c == CH - 1)),
                 reads=["ctxc", k_("woc", c)], writes=["ps_a3"], tag="go_mm")
        p.op("vector",
             lambda e, half=half: e.tensor_add(grow[0:1, half * 512:(half + 1) * 512],
                                               ps_a[3][0:1, :],
                                               bias_oc[0:1, half * 512:(half + 1) * 512]),
             reads=["ps_a3", "bias_oc"], writes=["grow"], tag="go_ev")
    p.op("vector", lambda e: e.memset(scr[:, 0:128], 0.0), writes=["scr"], tag="gt_z")
    for c in range(CH):
        p.op("vector",
             lambda e, c=c: e.tensor_copy(scr[0:1, 0:128], grow[0:1, c * 128:(c + 1) * 128]),
             reads=["grow"], writes=["scr"], tag="gc_cp")
        p.op("tensor",
             lambda e: e.transpose(ps_a[2][:, 0:128], scr[:, 0:128], identf[:, :]),
             reads=["scr", "identf"], writes=["ps_a2"], tag="gc_tr")
        p.op("vector",
             lambda e, c=c: e.tensor_copy(gcol[:, c:c + 1], ps_a[2][:, 0:1]),
             reads=["ps_a2"], writes=["gcol"], tag="gc_ev")

    load_w(wA, ["wA", "wqc", "woc", "wql"], wqlT_d, "wql")
    load_w(wB, ["wB", "wkc", "wkl"], wklT_d, "wkl")
    load_w(wC, ["wC", "wvc", "wvl"], wvlT_d, "wvl")
    for t in range(NT):
        p.op("vector",
             lambda e, t=t: e.memset(
                 v65[t][:, :].rearrange("p (h u) -> p h u", u=65)[:, :, 64:65], 1.0),
             writes=[k_("v65", t)], tag="v_ones")
        for g in range(2):
            bank = ps_a[g]
            bkey = f"ps_a{g}"
            for c in range(CH):
                p.op("tensor",
                     lambda e, t=t, g=g, c=c, bank=bank: e.matmul(
                         bank[:, :], xn[c][:, t * 128:(t + 1) * 128],
                         wC[:, c * C + g * 512:(c * C) + (g + 1) * 512],
                         start=(c == 0), stop=(c == CH - 1)),
                     reads=[k_("xn", c), k_("wvl", c)], writes=[bkey], tag="vl_mm")
            p.op("vector",
                 lambda e, t=t, g=g, bank=bank: e.tensor_copy(
                     v65[t][:, :].rearrange("p (h u) -> p h u", u=65)[:, g * 8:(g + 1) * 8, 0:64],
                     bank[:, :].rearrange("p (h u) -> p h u", u=64)),
                 reads=[bkey], writes=[k_("v65", t)], tag="vl_ev")

    for i in range(H // 2):
        p.op("vector", lambda e, i=i: e.memset(attnres[i][:, :], 0.0),
             writes=[k_("attnres", i)], tag="at_z")
    RVALS = (-128, 0, 128, 256, 384, 512)
    for h in (range(H) if _DBG_HEADS is None else _DBG_HEADS):
        hq = h * 64
        par = h % 2
        at = attnres[h // 2]
        atk = k_("attnres", h // 2)
        for s in range(4):
            sl = slice(s * 512, (s + 1) * 512)
            for c in range(CH):
                p.op("tensor",
                     lambda e, c=c, sl=sl, hq=hq: e.matmul(
                         ps_a[0][0:64, :], wA[:, c * C + hq: c * C + hq + 64],
                         xn[c][:, sl], start=(c == 0), stop=(c == CH - 1)),
                     reads=[k_("wql", c), k_("xn", c)], writes=["ps_a0"], tag="q_mm")
            p.op("vector",
                 lambda e, sl=sl, h=h: e.tensor_scalar(
                     qh[0:64, sl], ps_a[0][0:64, :], 0.125, bias_ql[:, h:h + 1],
                     ALU.mult, ALU.add),
                 reads=["ps_a0", "bias_ql"], writes=["qh"], tag="q_ev")
            for c in range(CH):
                p.op("tensor",
                     lambda e, c=c, sl=sl, hq=hq: e.matmul(
                         ps_a[1][0:64, :], wB[:, c * C + hq: c * C + hq + 64],
                         xn[c][:, sl], start=(c == 0), stop=(c == CH - 1)),
                     reads=[k_("wkl", c), k_("xn", c)], writes=["ps_a1"], tag="k_mm")
            p.op("vector",
                 lambda e, sl=sl: e.tensor_copy(kh[0:64, sl], ps_a[1][0:64, :]),
                 reads=["ps_a1"], writes=["kh"], tag="k_ev")
        for g in range(4):
            kts = [kt for kt in range(NT) if 4 * g - 1 <= kt <= 4 * g + 4]
            for idx, kt in enumerate(kts):
                r = kt * 128 - g * 512
                ri = 6 if (g == 0 and kt == 0) else RVALS.index(r)
                p.op("tensor",
                     lambda e, kt=kt, g=g: e.matmul(
                         ps_s[g][:, :], kh[0:64, kt * 128:(kt + 1) * 128],
                         qh[0:64, g * 512:(g + 1) * 512], start=True, stop=True),
                     reads=["kh", "qh"], writes=[f"ps_s{g}"], tag="sc_mm")
                p.op("scalar",
                     lambda e, g=g: e.activation(
                         pt[:, :], ps_s[g][:, :], AF.Exp),
                     reads=[f"ps_s{g}"], writes=["pt"], tag="sc_exp")
                p.op("vector",
                     lambda e, g=g, ri=ri: e.tensor_mul(
                         pt[:, :], pt[:, :],
                         mask_t[:, ri * 512:(ri + 1) * 512]),
                     reads=["pt", "mask_t"], writes=["pt"], tag="sc_msk")
                if par == 0:
                    p.op("tensor",
                         lambda e, kt=kt, g=g, h=h, idx=idx, last=(idx == len(kts) - 1): e.matmul(
                             ps_a[g][0:65, :], v65[kt][:, h * 65:(h + 1) * 65],
                             pt[:, :],
                             start=(idx == 0), stop=last),
                         reads=[k_("v65", kt), "pt"], writes=[f"ps_a{g}"], tag="av_mm")
                else:
                    p.op("tensor",
                         lambda e, kt=kt, g=g, h=h, idx=idx, last=(idx == len(kts) - 1): e.matmul(
                             ps_a[g][64:128, :], v65[kt][:, h * 65:h * 65 + 64],
                             pt[:, :],
                             start=(idx == 0), stop=last),
                         reads=[k_("v65", kt), "pt"], writes=[f"ps_a{g}"], tag="av_mmo")
                    p.op("tensor",
                         lambda e, kt=kt, g=g, h=h, idx=idx, last=(idx == len(kts) - 1): e.matmul(
                             ps_a[g][0:1, :], v65[kt][:, h * 65 + 64:h * 65 + 65],
                             pt[:, :],
                             start=(idx == 0), stop=last),
                         reads=[k_("v65", kt), "pt"], writes=[f"ps_a{g}"], tag="av_den")
        for g in range(4):
            gsl = slice(g * 512, (g + 1) * 512)
            if par == 0:
                p.op("vector",
                     lambda e, g=g: e.reciprocal(nrow[64:65, :], ps_a[g][64:65, :]),
                     reads=[f"ps_a{g}"], writes=["nrow"], tag="nm_rc")
                p.op("tensor",
                     lambda e: e.matmul(ps_s[0][0:64, :], b["onesr"][64:65, 0:64], nrow[64:65, :],
                                        start=True, stop=True),
                     reads=["nrow", "onesr"], writes=["ps_s0"], tag="nm_bc")
                p.op("vector",
                     lambda e: e.tensor_copy(Bsb[0:64, :], ps_s[0][0:64, :]),
                     reads=["ps_s0"], writes=["Bsb"], tag="nm_ev")
                p.op("vector",
                     lambda e, g=g: e.tensor_mul(scr[0:64, :], ps_a[g][0:64, :], Bsb[0:64, :]),
                     reads=[f"ps_a{g}", "Bsb"], writes=["scr"], tag="nm_mul")
                p.op("vector",
                     lambda e, gsl=gsl, h=h, at=at: e.tensor_scalar_add(
                         at[0:64, gsl], scr[0:64, :], bias_vl[0:64, h:h + 1]),
                     reads=["scr", "bias_vl"], writes=[atk], tag="nm_out")
            else:
                p.op("vector",
                     lambda e, g=g: e.reciprocal(nrow[0:1, :], ps_a[g][0:1, :]),
                     reads=[f"ps_a{g}"], writes=["nrow"], tag="nm_rc")
                p.op("tensor",
                     lambda e: e.matmul(ps_s[0][64:128, :], b["onesr"][0:1, 0:64], nrow[0:1, :],
                                        start=True, stop=True),
                     reads=["nrow", "onesr"], writes=["ps_s0"], tag="nm_bc")
                p.op("vector",
                     lambda e: e.tensor_copy(Bsb[64:128, :], ps_s[0][64:128, :]),
                     reads=["ps_s0"], writes=["Bsb"], tag="nm_ev")
                p.op("vector",
                     lambda e, g=g: e.tensor_mul(scr[64:128, :], ps_a[g][64:128, :], Bsb[64:128, :]),
                     reads=[f"ps_a{g}", "Bsb"], writes=["scr"], tag="nm_mul")
                p.op("vector",
                     lambda e, gsl=gsl, h=h, at=at: e.tensor_scalar_add(
                         at[64:128, gsl], scr[64:128, :], bias_vl[64:128, h:h + 1]),
                     reads=["scr", "bias_vl"], writes=[atk], tag="nm_out")

    load_w(wA, ["wA", "wqc", "woc", "wql", "wol"], wolT_d, "wol")
    for oc in range(CH):
        xr = xrot[0]
        xk = "xrot:0"
        p.op("sync", lambda e, oc=oc, xr=xr: e.dma_start(out=xr[:, :], in_=xin[oc * 128:(oc + 1) * 128, :]),
             writes=[xk], dma=True, tag="ox_ld")
        p.op("vector", lambda e, xr=xr: e.tensor_copy(scr[:, 1:2], xr[:, 0:1]),
             reads=[xk], writes=["scr"], tag="gz_sv")
        for s in range(4):
            sl = slice(s * 512, (s + 1) * 512)
            bank = ps_a[s % 4]
            bkey = f"ps_a{s % 4}"
            for i in range(CH):
                p.op("tensor",
                     lambda e, oc=oc, i=i, sl=sl, bank=bank: e.matmul(
                         bank[:, :], wA[:, i * C + oc * 128: i * C + (oc + 1) * 128],
                         attnres[i][:, sl], start=(i == 0), stop=(i == CH - 1)),
                     reads=[k_("wol", i), k_("attnres", i)], writes=[bkey], tag="o_mm")
            p.op("vector",
                 lambda e, oc=oc, sl=sl, bank=bank, xr=xr, xk=xk: e.scalar_tensor_tensor(
                     xr[:, sl], bank[:, :], bias_ol[:, oc:oc + 1], xr[:, sl],
                     ALU.add, ALU.add),
                 reads=[bkey, "bias_ol", xk], writes=[xk], tag="o_add")
        p.op("vector",
             lambda e, oc=oc, xr=xr, xk=xk: e.tensor_add(xr[:, 0:1], scr[:, 1:2], gcol[:, oc:oc + 1]),
             reads=["scr", "gcol"], writes=[xk], tag="gz_fix")
        p.op("sync",
             lambda e, oc=oc, xr=xr, xk=xk: e.dma_start(out=xout[oc * 128:(oc + 1) * 128, :], in_=xr[:, :]),
             reads=[xk], writes=["xout"], dma=True, tag="ox_st")

    emit_streamln(xout, xn, "xn7", outf_d=xn7_d, tag="r_", alias="xn")
    for t in range(NT):
        tsl = slice(t * 128, (t + 1) * 128)
        for c in range(CH):
            p.op("tensor",
                 lambda e, t=t, c=c, tsl=tsl: e.matmul(
                     ps_a[0][:, 0:8], xn[c][:, tsl], wgsbb[:, c * 8:(c + 1) * 8],
                     start=(c == 0), stop=(c == CH - 1)),
                 reads=[k_("xn7", c), "wgsbb"], writes=["ps_a0"], tag="lg_mm")
        p.op("vector",
             lambda e: e.tensor_reduce(rr[:, 0:1], ps_a[0][:, 0:8].rearrange("p (a n) -> p a n", a=1),
                                       AX.X, ALU.max),
             reads=["ps_a0"], writes=["rr"], tag="rt_max")
        p.op("vector", lambda e: e.tensor_scalar_mul(rr[:, 1:2], rr[:, 0:1], -1.0),
             reads=["rr"], writes=["rr"], tag="rt_neg")
        p.op("scalar",
             lambda e: e.activation(rr[:, 8:16], ps_a[0][:, 0:8], AF.Exp, bias=rr[:, 1:2]),
             reads=["ps_a0", "rr"], writes=["rr"], tag="rt_exp")
        p.op("vector",
             lambda e: e.tensor_reduce(rr[:, 2:3], rr[:, 8:16].rearrange("p (a n) -> p a n", a=1),
                                       AX.X, ALU.add),
             reads=["rr"], writes=["rr"], tag="rt_sum")
        p.op("vector", lambda e, t=t: e.reciprocal(cbv[:, t:t + 1], rr[:, 2:3]),
             reads=["rr"], writes=["cbv"], tag="rt_cmb")
        p.op("vector",
             lambda e: e.tensor_tensor(rr[:, 16:24], ps_a[0][:, 0:8],
                                       rr[:, 0:1].broadcast_to([128, 8]), ALU.is_ge),
             reads=["ps_a0", "rr"], writes=["rr"], tag="rt_eq")
        p.op("vector",
             lambda e: e.tensor_scalar(rr[:, 24:32], rr[:, 16:24], -1000.0, 1000.0,
                                       ALU.mult, ALU.add),
             reads=["rr"], writes=["rr"], tag="rt_pen")
        p.op("vector",
             lambda e: e.tensor_add(rr[:, 24:32], rr[:, 24:32], iot[:, :]),
             reads=["rr", "iot"], writes=["rr"], tag="rt_cnd")
        p.op("vector",
             lambda e, t=t: e.tensor_reduce(t1v[:, t:t + 1],
                                            rr[:, 24:32].rearrange("p (a n) -> p a n", a=1),
                                            AX.X, ALU.min),
             reads=["rr"], writes=["t1v"], tag="rt_t1")
        p.op("vector",
             lambda e, t=t: e.tensor_tensor(rr[:, 32:40], iot[:, :],
                                            t1v[:, t:t + 1].broadcast_to([128, 8]), ALU.is_equal),
             reads=["iot", "t1v"], writes=["rr"], tag="rt_oh")
        p.op("vector", lambda e: e.tensor_copy(ohb[:, :], rr[:, 32:40]),
             reads=["rr"], writes=["ohb"], tag="rt_ohb")
        p.op("tensor",
             lambda e: e.matmul(ps_a[1][:, 0:8], stri[:, :], ohb[:, :], start=True, stop=False),
             reads=["stri", "ohb"], writes=["ps_a1"], tag="rt_pre")
        p.op("tensor",
             lambda e: e.matmul(ps_a[1][:, 0:8], ones_k1[:, :], crow[0:1, :],
                                start=False, stop=True),
             reads=["ones_k1", "crow"], writes=["ps_a1"], tag="rt_car")
        p.op("vector",
             lambda e: e.tensor_mul(rr[:, 40:48], ps_a[1][:, 0:8], rr[:, 32:40]),
             reads=["ps_a1", "rr"], writes=["rr"], tag="rt_rm")
        p.op("vector",
             lambda e, t=t: e.tensor_reduce(rkv[:, t:t + 1],
                                            rr[:, 40:48].rearrange("p (a n) -> p a n", a=1),
                                            AX.X, ALU.add),
             reads=["rr"], writes=["rkv"], tag="rt_rk")
        p.op("tensor",
             lambda e: e.matmul(ps_a[2][0:1, 0:8], ones128b[:, 0:1], ohb[:, :],
                                start=True, stop=True),
             reads=["ones128b", "ohb"], writes=["ps_a2"], tag="rt_cs")
        p.op("vector",
             lambda e: e.tensor_add(crow[0:1, :], crow[0:1, :], ps_a[2][0:1, 0:8]),
             reads=["ps_a2", "crow"], writes=["crow"], tag="rt_cup")
    p.op("sync", lambda e: e.dma_start(out=top1_d[:, :], in_=t1v[:, :]), reads=["t1v"], dma=True, tag="st_t1")
    p.op("sync", lambda e: e.dma_start(out=rank_d[:, :], in_=rkv[:, :]), reads=["rkv"], dma=True, tag="st_rk")
    p.op("sync", lambda e: e.dma_start(out=comb_d[:, :], in_=cbv[:, :]), reads=["cbv"], dma=True, tag="st_cb")
    p.op("sync", lambda e: e.dma_start(out=cnt_d[:, :], in_=crow[0:1, :]), reads=["crow"], dma=True, tag="st_ct")
    if _DBG_L3:
        dat0 = p.dram("dat0", [128, T], BF16, kind="ExternalOutput")
        dqh = p.dram("dqh", [64, T], BF16, kind="ExternalOutput")
        dkh = p.dram("dkh", [64, T], BF16, kind="ExternalOutput")
        dgrow = p.dram("dgrow", [1, C], BF16, kind="ExternalOutput")
        dgcol = p.dram("dgcol", [128, CH], F32, kind="ExternalOutput")
        dattf = p.dram("dattf", [16, T], F32, kind="ExternalOutput")
        dv65 = p.dram("dv65", [128, H * 65], BF16, kind="ExternalOutput")
        dctx = p.dram("dctx", [128, CH], BF16, kind="ExternalOutput")
        for nm, dd, srct, kk in (("dat0", dat0, attnres[0], k_("attnres", 0)),
                                 ("dqh", dqh, qh, "qh"), ("dkh", dkh, kh, "kh"),
                                 ("dgrow", dgrow, grow, "grow"), ("dgcol", dgcol, gcol, "gcol"),
                                 ("dattf", dattf, attf, "attf"), ("dv65", dv65, v65[0], k_("v65", 0)),
                                 ("dctx", dctx, ctxc, "ctxc")):
            p.op("sync", lambda e, dd=dd, srct=srct: e.dma_start(out=dd[:, :], in_=srct[0:dd.shape[0], :]),
                 reads=[kk], dma=True, tag="dbg")
    return p.build()





ECAP = 2561
ETRASH = 8 * ECAP


def build_stage_moeR2():
    ctx = Ctx("moeR2")
    p, b = ctx.p, ctx.b
    cprev_d = p.dram("cprev", [8, 8], F32)
    top1_d = p.dram("top1", [128, NT], F32)
    rank_d = p.dram("rank", [128, NT], F32)
    comb_d = p.dram("comb", [128, NT], F32)
    ss_d = p.dram("slot_s", [128, NT], I32, kind="ExternalOutput")
    sg_d = p.dram("slot_g", [128, NT], I32, kind="ExternalOutput")
    vc_d = p.dram("vc", [128, NT], F32, kind="ExternalOutput")

    cprev = p.sb("cprev_s", [8, 8], F32)
    ones8 = p.sb("ones8", [8, 1], F32)
    t1v = p.sb("t1v", [128, NT], F32)
    rkv = p.sb("rkv", [128, NT], F32)
    cbv = p.sb("cbv", [128, NT], F32)
    iot = p.sb("iot", [128, 8], F32)
    offb = p.sb("offb", [128, 8], F32)
    sc1 = p.sb("sc1", [128, 8], F32)
    rg = p.sb("rg", [128, 1], F32)
    vld = p.sb("vld", [128, 1], F32)
    slt = p.sb("slt", [128, 1], F32)
    si = p.sb("si", [128, NT], I32)
    sgi = p.sb("sgi", [128, NT], I32)
    vcv = p.sb("vcv", [128, NT], F32)
    ps = p.ps("ps", [128, 512], F32)

    p.op("sync", lambda e: e.dma_start(out=cprev[:, :], in_=cprev_d[:, :]), writes=["cprev"], dma=True)
    p.op("sync", lambda e: e.dma_start(out=t1v[:, :], in_=top1_d[:, :]), writes=["t1v"], dma=True)
    p.op("sync", lambda e: e.dma_start(out=rkv[:, :], in_=rank_d[:, :]), writes=["rkv"], dma=True)
    p.op("sync", lambda e: e.dma_start(out=cbv[:, :], in_=comb_d[:, :]), writes=["cbv"], dma=True)
    p.op("vector", lambda e: e.memset(ones8[:, :], 1.0), writes=["ones8"])
    for e8 in range(8):
        p.op("vector", lambda e, e8=e8: e.memset(iot[:, e8:e8 + 1], float(e8)), writes=["iot"])
    p.op("tensor", lambda e: e.matmul(ps[0:1, 0:8], ones8[:, 0:1], cprev[:, :], start=True, stop=True),
         reads=["ones8", "cprev"], writes=["ps"], tag="off_cs")
    p.op("vector", lambda e: e.tensor_copy(sc1[0:1, :], ps[0:1, 0:8]),
         reads=["ps"], writes=["sc1"], tag="off_ev")
    p.op("tensor", lambda e: e.matmul(ps[:, 0:8], b["ones_k1"][:, :], sc1[0:1, 0:8],
                                      start=True, stop=True),
         reads=["sc1", "ones_k1"], writes=["ps"], tag="off_bc")
    p.op("vector", lambda e: e.tensor_copy(offb[:, :], ps[:, 0:8]),
         reads=["ps"], writes=["offb"], tag="off_ev2")
    for t in range(NT):
        p.op("vector",
             lambda e, t=t: e.tensor_tensor(sc1[:, :], iot[:, :],
                                            t1v[:, t:t + 1].broadcast_to([128, 8]), ALU.is_equal),
             reads=["iot", "t1v"], writes=["sc1"], tag="oh")
        p.op("vector", lambda e: e.tensor_mul(sc1[:, :], sc1[:, :], offb[:, :]),
             reads=["sc1", "offb"], writes=["sc1"], tag="ohm")
        p.op("vector",
             lambda e: e.tensor_reduce(rg[:, 0:1], sc1[:, :].rearrange("p (a n) -> p a n", a=1),
                                       AX.X, ALU.add),
             reads=["sc1"], writes=["rg"], tag="offt")
        p.op("vector", lambda e, t=t: e.tensor_add(rg[:, 0:1], rg[:, 0:1], rkv[:, t:t + 1]),
             reads=["rg", "rkv"], writes=["rg"], tag="rgl")
        p.op("vector",
             lambda e: e.tensor_single_scalar(vld[:, 0:1], rg[:, 0:1], float(ECAP), ALU.is_lt),
             reads=["rg"], writes=["vld"], tag="vld")
        p.op("vector",
             lambda e, t=t: e.scalar_tensor_tensor(slt[:, 0:1], t1v[:, t:t + 1], float(ECAP),
                                                   rg[:, 0:1], ALU.mult, ALU.add),
             reads=["t1v", "rg"], writes=["slt"], tag="slt")
        p.op("vector",
             lambda e: e.scalar_tensor_tensor(slt[:, 0:1], slt[:, 0:1], float(ETRASH),
                                              vld[:, 0:1], ALU.subtract, ALU.mult),
             reads=["slt", "vld"], writes=["slt"], tag="slt2")
        p.op("vector",
             lambda e: e.tensor_single_scalar(slt[:, 0:1], slt[:, 0:1], float(ETRASH), ALU.add),
             reads=["slt"], writes=["slt"], tag="slt3")
        p.op("vector", lambda e, t=t: e.tensor_copy(si[:, t:t + 1], slt[:, 0:1]),
             reads=["slt"], writes=["si"], tag="sic")
        p.op("vector",
             lambda e: e.tensor_single_scalar(slt[:, 0:1], slt[:, 0:1], float(ETRASH - 1), ALU.min),
             reads=["slt"], writes=["slt"], tag="sgm")
        p.op("vector", lambda e, t=t: e.tensor_copy(sgi[:, t:t + 1], slt[:, 0:1]),
             reads=["slt"], writes=["sgi"], tag="sgc")
        p.op("vector", lambda e, t=t: e.tensor_mul(vcv[:, t:t + 1], vld[:, 0:1], cbv[:, t:t + 1]),
             reads=["vld", "cbv"], writes=["vcv"], tag="vcm")
    p.op("sync", lambda e: e.dma_start(out=ss_d[:, :], in_=si[:, :]), reads=["si"], dma=True, tag="st1")
    p.op("sync", lambda e: e.dma_start(out=sg_d[:, :], in_=sgi[:, :]), reads=["sgi"], dma=True, tag="st2")
    p.op("sync", lambda e: e.dma_start(out=vc_d[:, :], in_=vcv[:, :]), reads=["vcv"], dma=True, tag="st3")
    return p.build()


def build_stage_moeE1():
    ctx = Ctx("moeE1")
    p, b = ctx.p, ctx.b
    F4 = 4096
    NFS = 32
    ebT_d = p.dram("ebT", [C, ECAP], F32)
    w1r_d = p.dram("w1r", [NFS, C, 128], F32)
    w2r_d = p.dram("w2r", [CH, NFS, 128, 128], F32)
    b1_d = p.dram("b1c", [128, NFS], F32)
    b2_d = p.dram("b2c", [128, CH], F32)
    y_d = p.dram("y", [C, ECAP], BF16, kind="ExternalOutput")

    eb = [p.sb(f"eb_{c}", [128, ECAP], BF16) for c in range(CH)]
    hcg = p.sb("hcg", [128, NFS * 1024], BF16)
    w1sb = [p.sb(f"w1sb_{i}", [128, CH * 128], BF16) for i in range(2)]
    w2sb = [p.sb(f"w2sb_{i}", [128, 128], BF16) for i in range(2)]
    b1c = p.sb("b1c_s", [128, NFS], F32)
    b2c = p.sb("b2c_s", [128, CH], F32)
    yev = p.sb("yev", [128, 512], BF16)
    psA = p.ps("psA", [128, 512], F32)
    psB = p.ps("psB", [128, 512], F32)
    psY = [p.ps(f"psY{i}", [128, 512], F32) for i in range(2)]

    p.op("sync", lambda e: e.dma_start(out=b1c[:, :], in_=b1_d[:, :]), writes=["b1c"], dma=True)
    p.op("sync", lambda e: e.dma_start(out=b2c[:, :], in_=b2_d[:, :]), writes=["b2c"], dma=True)
    for c in range(CH):
        p.op("gpsimd",
             lambda e, c=c: e.dma_start(out=eb[c][:, :], in_=ebT_d[c * 128:(c + 1) * 128, :]),
             writes=[k_("eb", c)], dma=True, tag="ldeb")

    CGS = [(0, 1024), (1024, 1024), (2048, ECAP - 2048)]
    for cg, (c0, cn) in enumerate(CGS):
        ntg = (cn + 511) // 512
        for fs in range(NFS):
            w1 = w1sb[fs % 2]
            w1k = f"w1sb:{fs % 2}"
            p.op("gpsimd",
                 lambda e, fs=fs, w1=w1: e.dma_start(out=w1[:, :].rearrange("p (c n) -> p c n", n=128),
                                                     in_=w1r_d[fs, :, :].rearrange("(c p) n -> p c n", p=128)),
                 writes=[w1k], dma=True, tag="ldw1")
            for tg in range(ntg):
                gn = min(512, cn - tg * 512)
                bank = psA if tg % 2 == 0 else psB
                bkey = "psA" if tg % 2 == 0 else "psB"
                for c in range(CH):
                    p.op("tensor",
                         lambda e, fs=fs, c=c, tg=tg, gn=gn, c0=c0, bank=bank, w1=w1: e.matmul(
                             bank[:, 0:gn], w1[:, c * 128:(c + 1) * 128],
                             eb[c][:, c0 + tg * 512:c0 + tg * 512 + gn],
                             start=(c == 0), stop=(c == CH - 1)),
                         reads=[w1k, k_("eb", c)], writes=[bkey], tag="mm1")
                p.op("scalar",
                     lambda e, fs=fs, tg=tg, gn=gn, bank=bank: e.activation(
                         hcg[:, fs * 1024 + tg * 512:fs * 1024 + tg * 512 + gn],
                         bank[:, 0:gn], AF.Gelu, bias=b1c[:, fs:fs + 1]),
                     reads=[bkey, "b1c"], writes=["hcg"], tag="gelu")
        for oc in range(CH):
            for fs in range(NFS):
                w2 = w2sb[fs % 2]
                w2k = f"w2sb:{fs % 2}"
                p.op("gpsimd",
                     lambda e, oc=oc, fs=fs, w2=w2: e.dma_start(out=w2[:, :], in_=w2r_d[oc, fs, :, :]),
                     writes=[w2k], dma=True, tag="ldw2")
                for tg in range(ntg):
                    gn = min(512, cn - tg * 512)
                    p.op("tensor",
                         lambda e, oc=oc, fs=fs, tg=tg, gn=gn, w2=w2: e.matmul(
                             psY[tg % 2][:, 0:gn], w2[:, :],
                             hcg[:, fs * 1024 + tg * 512:fs * 1024 + tg * 512 + gn],
                             start=(fs == 0), stop=(fs == NFS - 1)),
                         reads=[w2k, "hcg"], writes=[f"psY{tg % 2}"], tag="mm2")
            for tg in range(ntg):
                gn = min(512, cn - tg * 512)
                p.op("vector",
                     lambda e, oc=oc, tg=tg, gn=gn: e.tensor_scalar_add(
                         yev[:, 0:gn], psY[tg % 2][:, 0:gn], b2c[:, oc:oc + 1]),
                     reads=[f"psY{tg % 2}", "b2c"], writes=["yev"], tag="yb")
                p.op("sync",
                     lambda e, oc=oc, tg=tg, gn=gn, c0=c0: e.dma_start(
                         out=y_d[oc * 128:(oc + 1) * 128, c0 + tg * 512:c0 + tg * 512 + gn],
                         in_=yev[:, 0:gn], allow_slow_non_contiguous=(gn < 16)),
                     reads=["yev"], dma=True, tag="sty")
    return p.build()


def build_stage_moeF():
    ctx = Ctx("moeF")
    p, b = ctx.p, ctx.b
    x7_d = p.dram("x7t", [T, C], F32)
    g_d = p.dram("gt", [T, C], BF16)
    vc_d = p.dram("vc", [128, NT], F32)
    fin_d = p.dram("fin", [T, C], F32, kind="ExternalOutput")
    xr = [p.sb(f"xr_{i}", [128, C], F32) for i in range(2)]
    gr = [p.sb(f"gr_{i}", [128, C], F32) for i in range(2)]
    vcv = p.sb("vcv", [128, NT], F32)
    ob = [p.sb(f"ob_{i}", [128, C], F32) for i in range(2)]
    p.op("sync", lambda e: e.dma_start(out=vcv[:, :], in_=vc_d[:, :]), writes=["vcv"], dma=True)
    for t in range(NT):
        i2 = t % 2
        p.op("sync", lambda e, t=t, i2=i2: e.dma_start(out=xr[i2][:, :], in_=x7_d[t * 128:(t + 1) * 128, :]),
             writes=[f"xr:{i2}"], dma=True, tag="ldx")
        p.op("gpsimd", lambda e, t=t, i2=i2: e.dma_start(out=gr[i2][:, :], in_=g_d[t * 128:(t + 1) * 128, :]),
             writes=[f"gr:{i2}"], dma=True, tag="ldg")
        p.op("vector",
             lambda e, t=t, i2=i2: e.scalar_tensor_tensor(
                 ob[i2][:, :], gr[i2][:, :], vcv[:, t:t + 1], xr[i2][:, :],
                 ALU.mult, ALU.add),
             reads=[f"gr:{i2}", f"xr:{i2}", "vcv"], writes=[f"ob:{i2}"], tag="cmb")
        p.op("sync", lambda e, t=t, i2=i2: e.dma_start(out=fin_d[t * 128:(t + 1) * 128, :], in_=ob[i2][:, :]),
             reads=[f"ob:{i2}"], dma=True, tag="st")
    return p.build()




_STAGE_CACHE = {}


def _stage(name):
    if name not in _STAGE_CACHE:
        _STAGE_CACHE[name] = globals()["build_stage_" + name]()
    return _STAGE_CACHE[name]


def _run(nc, in_maps):
    return run_bass_kernel_spmd(nc, in_maps, core_ids=list(range(NCORES))).results


def _l2attn_consts():
    H, T_ = 16, T
    j = np.arange(T_)
    jhi = (j // 128).astype(np.float64)
    jlo = (j % 128).astype(np.float64)
    ones = np.ones(T_)
    zer = np.zeros(T_)
    qac = np.zeros((H, 6, T_), np.float64)
    kacA = np.zeros((H, 7, T_), np.float64)
    kacB = np.zeros((H, 7, T_), np.float64)
    for h in range(H):
        s = 0.5 ** h
        qac[h] = np.stack([ones, ones, -s * 128 * jhi, -s * jlo, s * 128 * jhi, s * jlo])
        kacA[h] = np.stack([ones, s * 128 * jhi, s * jlo, ones, ones, zer, zer])
        kacB[h] = np.stack([ones, -s * 128 * jhi, -s * jlo, zer, zer, ones, ones])
    diagp = np.zeros((4, 128, 512), np.float32)
    a = np.arange(128)[:, None]
    bq = np.arange(512)[None, :]
    for ri, r in enumerate((0, 128, 256, 384)):
        diagp[ri] = np.maximum((r + a) - bq, 0.0)
    return (qac.astype(BF), kacA.astype(BF), kacB.astype(BF), diagp)


def _sel_const(nchunk, nblk):
    sel = np.zeros((nblk, nchunk * 128), dtype=np.float32)
    for c in range(nchunk):
        for pp in range(128):
            sel[4 * c + pp // 32, c * 128 + pp] = 1
    return sel


def _ct(a):
    return np.ascontiguousarray(a)


def kernel(**I):
    B = NCORES
    identb = np.eye(128, dtype=BF)
    identf = np.eye(128, dtype=np.float32)

    ob2 = _ct(I["L0_attn_bo"].reshape(8, 128).T)
    ins = [dict(xin=_ct(I["L0_attn_x"][b].T), Wq=I["L0_attn_Wq"], Wk=I["L0_attn_Wk"],
                Wv=I["L0_attn_Wv"], Wo=I["L0_attn_Wo"], ob2=ob2, ident=identb)
           for b in range(B)]
    r = _run(_stage("l0attn"), ins)
    x1 = [r[b]["xout"] for b in range(B)]

    ins = [dict(xin=x1[b], W_rT=_ct(I["L0_ffn_W_r"].T), W_kT=_ct(I["L0_ffn_W_k"].T),
                W_vT=_ct(I["L0_ffn_W_v"].T), mu_r=I["L0_ffn_mu_r"], mu_k=I["L0_ffn_mu_k"])
           for b in range(B)]
    r = _run(_stage("l0ffn"), ins)
    x2 = [r[b]["xout"] for b in range(B)]

    masks = np.zeros((6, 128, 512), np.float64)
    for ri, rv in enumerate((-256, -128, 0, 128, 256, 384)):
        a = np.arange(128)[:, None] + rv
        b_ = np.arange(512)[None, :]
        masks[ri] = ((a <= b_) & (a > b_ - 256)).astype(np.float64)
    qb2 = _ct(I["L1_attn_bq"].reshape(16, 64).T)
    kb2 = _ct(I["L1_attn_bk"].reshape(16, 64).T)
    vb1 = _ct(I["L1_attn_bv"].reshape(16, 64).T)
    vb2 = np.concatenate([vb1, vb1], axis=0)
    ob2 = _ct(I["L1_attn_bo"].reshape(8, 128).T)
    ins = [dict(xin=x2[b], wq=I["L1_attn_wq"], wk=I["L1_attn_wk"], wv=I["L1_attn_wv"],
                wo=I["L1_attn_wo"], qb2=qb2, kb2=kb2, vb2=vb2, ob2=ob2,
                masks=masks.astype(BF)) for b in range(B)]
    r = _run(_stage("l1attn"), ins)
    x3 = [r[b]["xout"] for b in range(B)]

    ins = [dict(xin=_ct(x3[b].T), w_gateT=_ct(I["L1_ffn_w_gate"].T),
                w_upT=_ct(I["L1_ffn_w_up"].T), w_downT=_ct(I["L1_ffn_w_down"].T),
                ident=identb) for b in range(B)]
    r = _run(_stage("l1ffn"), ins)
    x4 = [r[b]["xout"] for b in range(B)]

    bb = I["L2_attn_qkv_b"].reshape(16, 3, 64)
    qb2 = _ct(bb[:, 0, :].T)
    kb2 = _ct(bb[:, 1, :].T)
    vb1 = _ct(bb[:, 2, :].T)
    vb2 = np.concatenate([vb1, vb1], axis=0)
    ob2 = _ct(I["L2_attn_out_b"].reshape(8, 128).T)
    qac, kacA, kacB, diagp = _l2attn_consts()
    ins = [dict(xin=_ct(x4[b].T), qkvT=_ct(I["L2_attn_qkv_w"].T), outT=_ct(I["L2_attn_out_w"].T),
                qb2=qb2, kb2=kb2, vb2=vb2, ob2=ob2, qac=qac, kacA=kacA, kacB=kacB,
                diagp=diagp) for b in range(B)]
    r = _run(_stage("l2attn"), ins)
    x5 = [r[b]["xout"] for b in range(B)]

    ins = [dict(xin=x5[b], identf=identf) for b in range(B)]
    rA = _run(_stage("l2ffnA"), ins)
    m0 = np.array([[rA[b]["m0"][0, 0] for b in range(B)]], dtype=np.float32)
    wup, wg, wdn = I["L2_ffn_w_up"], I["L2_ffn_w_gate"], I["L2_ffn_w_down"]
    selu = _sel_const(8, 32)
    ins = [dict(xnin=rA[b]["xnout"], mvec=m0, wupT=_ct(wup.T), wupN=wup,
                wgT=_ct(wg.T), wgN=wg, selu=selu, identf=identf) for b in range(B)]
    rB = _run(_stage("l2ffnB"), ins)
    mu8 = np.array([[rB[b]["m2"][0, 0] for b in range(B)]], dtype=np.float32)
    mg8 = np.array([[rB[b]["m2"][1, 0] for b in range(B)]], dtype=np.float32)
    ins = [dict(gate_d=rB[b]["gate_d"], mg8=mg8, identf=identf) for b in range(B)]
    rC = _run(_stage("l2ffnC"), ins)
    mg28 = np.array([[rC[b]["m1"][0, 0] for b in range(B)]], dtype=np.float32)
    ins = [dict(up_d=rB[b]["up_d"], g2_d=rC[b]["g2_d"], mu8=mu8, mg28=mg28,
                identf=identf, identb=identb) for b in range(B)]
    rD = _run(_stage("l2ffnD"), ins)
    mo8 = np.array([[rD[b]["m1"][0, 0] for b in range(B)]], dtype=np.float32)
    selw = _sel_const(24, 96)
    ins = [dict(oT_d=rD[b]["oT_d"], mo8=mo8, wdnT=_ct(wdn.T), wdnN=wdn, selw=selw,
                identf=identf, xin=x5[b]) for b in range(B)]
    rE = _run(_stage("l2ffnE"), ins)
    x6 = [rE[b]["xout"] for b in range(B)]

    iwc, ibc = I["L3_attn_in_w_cls"], I["L3_attn_in_b_cls"]
    owc, obc = I["L3_attn_out_w_cls"], I["L3_attn_out_b_cls"]
    iwl, ibl = I["L3_attn_in_w_loc"], I["L3_attn_in_b_loc"]
    owl, obl = I["L3_attn_out_w_loc"], I["L3_attn_out_b_loc"]
    wgr = I["L3_ffn_wg"]
    RVALS = (-128, 0, 128, 256, 384, 512)
    masks7 = np.zeros((7, 128, 512), dtype=BF)
    jj = np.arange(128)[:, None]
    ii = np.arange(512)[None, :]
    for ri, rv in enumerate(RVALS):
        masks7[ri] = (np.abs(ii - (rv + jj)) <= 16).astype(BF)
    mz = (np.abs(ii - jj) <= 16).astype(np.float32)
    mz[0, :] = 0.0
    mz[:, 0] = 0.0
    mz[0, 0] = 1.0
    masks7[6] = mz.astype(BF)
    wgsb = np.concatenate([wgr[c * 128:(c + 1) * 128] for c in range(8)], axis=1).astype(np.float32)
    stri = np.triu(np.ones((128, 128), dtype=np.float32), 1).astype(BF)
    ins = [dict(
        xin=x6[b],
        wqcT=_ct(iwc[0:1024].T), wkcT=_ct(iwc[1024:2048].T), wvcT=_ct(iwc[2048:3072].T),
        wocT=_ct(owc.T),
        wqlT=_ct(iwl[0:1024].T), wklT=_ct(iwl[1024:2048].T), wvlT=_ct(iwl[2048:3072].T),
        wolT=_ct(owl.T),
        bqc2=_ct(ibc[0:1024].reshape(8, 128).T), bvc2=_ct(ibc[2048:3072].reshape(8, 128).T),
        boc2=obc.reshape(1, 1024),
        bql2=_ct(ibl[0:1024].reshape(16, 64).T),
        bvl2=_ct(np.concatenate([ibl[2048:3072].reshape(16, 64).T] * 2, axis=0)),
        bol2=_ct(obl.reshape(8, 128).T),
        masks7=masks7, identf=identf, identb=identb, wgsb=wgsb, stri=stri,
    ) for b in range(B)]
    rL3 = _run(_stage("l3attn"), ins)

    counts_all = np.stack([np.asarray(rL3[b]["cnt"])[0] for b in range(B)])
    ins = []
    for b in range(B):
        cprev = np.zeros((8, 8), dtype=np.float32)
        cprev[:b] = counts_all[:b]
        ins.append(dict(cprev=cprev, top1=rL3[b]["top1"], rank=rL3[b]["rank"],
                        comb=rL3[b]["comb"]))
    rR2 = _run(_stage("moeR2"), ins)

    slot_all = np.concatenate([np.asarray(rR2[b]["slot_s"]).T.flatten() for b in range(B)])
    slotg_all = np.concatenate([np.asarray(rR2[b]["slot_g"]).T.flatten() for b in range(B)])
    xf = np.concatenate([np.asarray(rL3[b]["xn7"]).T for b in range(B)], axis=0)
    buf = np.zeros((ETRASH + 1, C), dtype=np.float32)
    buf[slot_all] = xf
    W1, b1 = I["L3_ffn_W1"], I["L3_ffn_b1"]
    W2, b2 = I["L3_ffn_W2"], I["L3_ffn_b2"]
    ins = []
    for e in range(B):
        ins.append(dict(ebT=_ct(buf[e * ECAP:(e + 1) * ECAP].T),
                        w1r=_ct(W1[e].reshape(1024, 32, 128).transpose(1, 0, 2)),
                        w2r=_ct(W2[e].reshape(32, 128, 8, 128).transpose(2, 0, 1, 3)),
                        b1c=_ct(b1[e].reshape(32, 128).T),
                        b2c=_ct(b2[e].reshape(8, 128).T)))
    rE1 = _run(_stage("moeE1"), ins)

    y_all = np.concatenate([np.asarray(rE1[e]["y"]).T for e in range(B)], axis=0)
    gathered = y_all[slotg_all]
    ins = [dict(x7t=_ct(np.asarray(rL3[b]["xout"]).T),
                gt=_ct(gathered[b * T:(b + 1) * T]),
                vc=rR2[b]["vc"]) for b in range(B)]
    rF = _run(_stage("moeF"), ins)
    out = np.stack([np.asarray(rF[b]["fin"]) for b in range(B)]).astype(np.float32)
    return out


# version 03
# speedup vs baseline: 1.0004x; 1.0004x over previous
import contextlib
import numpy as np
import ml_dtypes
import concourse.bass as bass
import concourse.mybir as mybir
from concourse.bass_utils import run_bass_kernel_spmd
from concourse.bass_core_rust import IndirectOffsetOnAxis

F32 = mybir.dt.float32
BF16 = mybir.dt.bfloat16
I32 = mybir.dt.int32
AF = mybir.ActivationFunctionType
ALU = mybir.AluOpType
AX = mybir.AxisListType
BF = ml_dtypes.bfloat16

NCORES = 8
_DBG_HEADS = None
_DBG_L0 = False
_DBG_L2F = False
_DBG_L3 = False
_DBG_L2 = None
T = 2048
C = 1024
CH = C // 128
NT = T // 128
ROFF = 65536.0



_ALL_STREAMS = ("tensor", "vector", "scalar", "gpsimd", "sync")


class _Op:
    __slots__ = ("fn", "waits", "inc_sem", "inc_val", "tag")

    def __init__(self, fn, waits, inc_sem, inc_val, tag):
        self.fn = fn
        self.waits = waits
        self.inc_sem = inc_sem
        self.inc_val = inc_val
        self.tag = tag


class Prog:
    def __init__(self, name="prog"):
        self.name = name
        self.nc = bass.Bass("TRN2", target_bir_lowering=False)
        self.stack = contextlib.ExitStack()
        self.ops = {e: [] for e in _ALL_STREAMS}
        self.sem_count = {}
        self.last_writer = {}
        self.readers = {}
        self.hiwater = {e: {} for e in _ALL_STREAMS}
        self.finalized = False

    def dram(self, name, shape, dtype, kind="ExternalInput"):
        return self.nc.dram_tensor(name, list(shape), dtype, kind=kind).ap()

    def sb(self, name, shape, dtype):
        return self.stack.enter_context(self.nc.sbuf_tensor(name, list(shape), dtype))

    def ps(self, name, shape, dtype=F32):
        return self.stack.enter_context(self.nc.psum_tensor(name, list(shape), dtype))

    DMA_POOL = 8

    def op(self, stream, fn, reads=(), writes=(), dma=False, tag=""):
        assert not self.finalized
        if dma:
            idx = getattr(self, "dma_idx", None)
            if idx is None:
                idx = self.dma_idx = {}
            i = idx.get(stream, 0)
            idx[stream] = i + 1
            semkey = f"d{stream}{i % self.DMA_POOL}"
        else:
            semkey = stream
        inc = 16 if dma else 1
        cur = self.sem_count.get(semkey, 0)
        new = cur + inc
        self.sem_count[semkey] = new

        deps = {}

        def add_dep(sk, v):
            if deps.get(sk, 0) < v:
                deps[sk] = v

        if dma and cur > 0:
            add_dep(semkey, cur)

        for k in reads:
            for sk, v in self.last_writer.get(k, {}).items():
                add_dep(sk, v)
        for k in writes:
            for sk, v in self.last_writer.get(k, {}).items():
                add_dep(sk, v)
            for sk, v in self.readers.get(k, {}).items():
                add_dep(sk, v)

        hw = self.hiwater[stream]
        waits = []
        for sk, v in sorted(deps.items()):
            if sk == semkey and v >= new:
                raise RuntimeError(f"self dependency on {semkey} at {v} (op {tag})")
            if hw.get(sk, 0) >= v:
                continue
            waits.append((sk, v))
            hw[sk] = v

        self.ops[stream].append(_Op(fn, waits, semkey, inc, tag))

        for k in reads:
            self.readers.setdefault(k, {})[semkey] = new
        for k in writes:
            self.last_writer.setdefault(k, {})[semkey] = new
            self.readers[k] = {}

    def build(self):
        assert not self.finalized
        self.finalized = True
        nc = self.nc
        sems = {}
        with self.stack:
            for key in sorted(self.sem_count):
                sems[key] = self.stack.enter_context(nc.semaphore("s_" + key))
            final_deps = dict(self.sem_count)
            with nc.Block() as block:
                def make_runner(stream):
                    def run(eng):
                        for o in self.ops[stream]:
                            for sk, v in o.waits:
                                eng.wait_ge(sems[sk], v)
                            inst = o.fn(eng)
                            inst.then_inc(sems[o.inc_sem], o.inc_val)
                        if stream == "sync":
                            for sk, v in sorted(final_deps.items()):
                                eng.wait_ge(sems[sk], v)
                    return run
                block.tensor(make_runner("tensor"))
                block.vector(make_runner("vector"))
                block.scalar(make_runner("scalar"))
                block.gpsimd(make_runner("gpsimd"))
                block.sync(make_runner("sync"))
        return nc


def k_(name, i=None):
    return name if i is None else f"{name}:{i}"



class Ctx:

    def __init__(self, name):
        self.p = Prog(name)
        self.b = {}
        p, b = self.p, self.b
        b["ones_f"] = p.sb("ones_f", [128, 1], F32)
        b["ones_k1"] = p.sb("ones_k1", [1, 128], F32)
        b["ones_k1b"] = p.sb("ones_k1b", [1, 128], BF16)
        b["cb"] = p.sb("cb", [128, 3], F32)
        p.op("vector", lambda e: e.memset(b["cb"][:, 0:1], 1e-5), writes=["cb"])
        p.op("vector", lambda e: e.memset(b["cb"][:, 1:2], 1.0), writes=["cb"])
        p.op("vector", lambda e: e.memset(b["cb"][:, 2:3], 1e-6), writes=["cb"])
        p.op("vector", lambda e: e.memset(b["ones_f"][:, :], 1.0), writes=["ones_f"])
        p.op("vector", lambda e: e.memset(b["ones_k1"][:, :], 1.0), writes=["ones_k1"])
        p.op("vector", lambda e: e.memset(b["ones_k1b"][:, :], 1.0), writes=["ones_k1b"])

    def resid_in(self, name="xin"):
        p, b = self.p, self.b
        xin = p.dram(name, [C, T], F32)
        xc = []
        for c in range(CH):
            t = p.sb(f"x_{c}", [128, T], F32)
            xc.append(t)
            p.op("sync",
                 lambda e, c=c, t=t: e.dma_start(out=t[:, :], in_=xin[c * 128:(c + 1) * 128, :]),
                 writes=[k_("x", c)], dma=True, tag="ldx")
        b["x"] = xc
        return xc

    def resid_out(self, name="xout"):
        p, b = self.p, self.b
        xout = p.dram(name, [C, T], F32, kind="ExternalOutput")
        for c in range(CH):
            p.op("sync",
                 lambda e, c=c: e.dma_start(out=xout[c * 128:(c + 1) * 128, :], in_=b["x"][c][:, :]),
                 reads=[k_("x", c)], dma=True, tag="stx")

def emit_ln(ctx, src, dstbuf, dstkey):
    p, b = ctx.p, ctx.b
    x = b[src]
    xsq, scr = b["xsq"], b["scr"]
    rowA, rowB, rrow_bf = b["rowA"], b["rowB"], b["rrow_bf"]
    ps_sum, ps_sq, ps_bM, ps_bR = b["ps_sum"], b["ps_sq"], b["ps_bM"], b["ps_bR"]
    ones_f, ones_k1, ones_k1b = b["ones_f"], b["ones_k1"], b["ones_k1b"]
    cb = b["cb"]

    for s in range(4):
        sl = slice(s * 512, (s + 1) * 512)
        for c in range(CH):
            p.op("tensor",
                 lambda e, c=c, sl=sl: e.matmul(
                     ps_sum[0:1, :], ones_f[:, 0:1], x[c][:, sl],
                     start=(c == 0), stop=(c == CH - 1)),
                 reads=[k_(src, c), "ones_f"], writes=["ps_sum"], tag="ln_mm_s")
        for c in range(CH):
            p.op("scalar",
                 lambda e, c=c, sl=sl: e.activation(xsq[:, :], x[c][:, sl], AF.Square),
                 reads=[k_(src, c)], writes=["xsq"], tag="ln_sq")
            p.op("tensor",
                 lambda e, c=c: e.matmul(
                     ps_sq[0:1, :], ones_f[:, 0:1], xsq[:, :],
                     start=(c == 0), stop=(c == CH - 1)),
                 reads=["xsq", "ones_f"], writes=["ps_sq"], tag="ln_mm_q")
        p.op("vector",
             lambda e, sl=sl: e.tensor_scalar_mul(rowA[0:1, sl], ps_sum[0:1, :], 1.0 / C),
             reads=["ps_sum"], writes=["rowA"], tag="ln_mean")
        p.op("vector",
             lambda e, sl=sl: e.tensor_scalar_mul(rowB[0:1, sl], ps_sq[0:1, :], 1.0 / C),
             reads=["ps_sq"], writes=["rowB"], tag="ln_msq")

    for s in range(4):
        sl = slice(s * 512, (s + 1) * 512)
        p.op("vector",
             lambda e, sl=sl: e.tensor_mul(scr[0:1, :], rowA[0:1, sl], rowA[0:1, sl]),
             reads=["rowA"], writes=["scr"], tag="ln_m2")
        p.op("vector",
             lambda e, sl=sl: e.tensor_sub(rowB[0:1, sl], rowB[0:1, sl], scr[0:1, :]),
             reads=["rowB", "scr"], writes=["rowB"], tag="ln_var")
        p.op("scalar",
             lambda e, sl=sl: e.activation(scr[0:1, :], rowB[0:1, sl], AF.Ln, bias=cb[0:1, 0:1]),
             reads=["rowB", "cb"], writes=["scr"], tag="ln_ln")
        p.op("scalar",
             lambda e, sl=sl: e.activation(rowB[0:1, sl], scr[0:1, :], AF.Exp, scale=-0.5),
             reads=["scr"], writes=["rowB"], tag="ln_exp")
    p.op("vector",
         lambda e: e.tensor_copy(rrow_bf[0:1, :], rowB[0:1, :]),
         reads=["rowB"], writes=["rrow_bf"], tag="ln_rbf")

    for s in range(4):
        sl = slice(s * 512, (s + 1) * 512)
        p.op("tensor",
             lambda e, sl=sl: e.matmul(ps_bM[:, :], ones_k1[:, :], rowA[0:1, sl],
                                       start=True, stop=True),
             reads=["rowA", "ones_k1"], writes=["ps_bM"], tag="ln_bM")
        p.op("tensor",
             lambda e, sl=sl: e.matmul(ps_bR[:, :], ones_k1b[:, :], rrow_bf[0:1, sl],
                                       start=True, stop=True),
             reads=["rrow_bf", "ones_k1b"], writes=["ps_bR"], tag="ln_bR")
        for c in range(CH):
            p.op("vector",
                 lambda e, c=c, sl=sl: e.tensor_sub(scr[:, :], x[c][:, sl], ps_bM[:, :]),
                 reads=[k_(src, c), "ps_bM"], writes=["scr"], tag="ln_sub")
            p.op("vector",
                 lambda e, c=c, sl=sl: e.tensor_mul(dstbuf[c][:, sl], scr[:, :], ps_bR[:, :]),
                 reads=["scr", "ps_bR"], writes=[k_(dstkey, c)], tag="ln_nrm")


def load_weight_bf16(ctx, dram_ap, sbuf_t, key, rows, cols, tag="w"):
    p = ctx.p
    for c in range(rows // 128):
        p.op("gpsimd",
             lambda e, c=c: e.dma_start(
                 out=sbuf_t[:, c * cols:(c + 1) * cols],
                 in_=dram_ap[c * 128:(c + 1) * 128, :]),
             writes=[k_(key, c)], dma=True, tag=tag)



def build_stage_l0ffn():
    ctx = Ctx("l0ffn")
    p, b = ctx.p, ctx.b
    ctx.resid_in()
    ctx.ln_scratch()

    wrT = p.dram("W_rT", [C, C], F32)
    wkT = p.dram("W_kT", [C, C], F32)
    wvT = p.dram("W_vT", [C, C], F32)
    mur = p.dram("mu_r", [C], F32)
    muk = p.dram("mu_k", [C], F32)

    nbuf = [p.sb(f"nbuf_{c}", [128, T], BF16) for c in range(CH)]
    mix = [p.sb(f"mix_{c}", [128, T], BF16) for c in range(CH)]
    r_t = [p.sb(f"r_{c}", [128, T], BF16) for c in range(CH)]
    wbuf = p.sb("wbuf", [128, CH * C], BF16)
    scr = b["scr"]
    scr2 = p.sb("scr2", [128, 512], F32)
    mixscr = p.sb("mixscr", [128, 512], BF16)
    ps_mm = [p.ps(f"ps_mm{i}", [128, 512], F32) for i in range(4)]

    emit_ln(ctx, "x", nbuf, "nbuf")



    mur_t = p.sb("mur_t", [128, CH], F32)
    muk_t = p.sb("muk_t", [128, CH], F32)
    mu1_t = p.sb("mu1_t", [128, 2 * CH], F32)
    for c in range(CH):
        p.op("gpsimd",
             lambda e, c=c: e.dma_start(out=mur_t[:, c:c + 1],
                                        in_=mur[c * 128:(c + 1) * 128].unsqueeze(1)),
             writes=["mur_t"], dma=True)
        p.op("gpsimd",
             lambda e, c=c: e.dma_start(out=muk_t[:, c:c + 1],
                                        in_=muk[c * 128:(c + 1) * 128].unsqueeze(1)),
             writes=["muk_t"], dma=True)
    p.op("vector", lambda e: e.tensor_scalar(mu1_t[:, 0:CH], mur_t[:, :], -1.0, 1.0, ALU.mult, ALU.add),
         reads=["mur_t"], writes=["mu1_t"])
    p.op("vector", lambda e: e.tensor_scalar(mu1_t[:, CH:2 * CH], muk_t[:, :], -1.0, 1.0, ALU.mult, ALU.add),
         reads=["muk_t"], writes=["mu1_t"])

    def emit_mix(mu_tile, mukey, mu1col, c):
        for s in range(4):
            lo = max(1, s * 512)
            hi = (s + 1) * 512
            n = hi - lo
            p.op("vector",
                 lambda e, c=c, lo=lo, hi=hi, n=n: e.tensor_scalar_mul(
                     mixscr[:, 0:n], nbuf[c][:, lo:hi], mu_tile[:, c:c + 1]),
                 reads=[k_("nbuf", c), mukey], writes=["mixscr"], tag="mixa")
            p.op("vector",
                 lambda e, c=c, lo=lo, hi=hi, n=n: e.scalar_tensor_tensor(
                     mix[c][:, lo:hi], nbuf[c][:, lo - 1:hi - 1],
                     mu1_t[:, mu1col + c:mu1col + c + 1],
                     mixscr[:, 0:n], ALU.mult, ALU.add),
                 reads=[k_("nbuf", c), "mu1_t", "mixscr"], writes=[k_("mix", c)], tag="mixb")
        p.op("vector",
             lambda e, c=c: e.tensor_scalar_mul(mix[c][:, 0:1], nbuf[c][:, 0:1], mu_tile[:, c:c + 1]),
             reads=[k_("nbuf", c), mukey], writes=[k_("mix", c)], tag="mixc")

    def emit_proj(w_tile, w_key, src_key, post, tag):
        for oc in range(CH):
            for ts in range(4):
                bank = ps_mm[ts % 4]
                bkey = f"ps_mm{ts % 4}"
                sl = slice(ts * 512, (ts + 1) * 512)
                for kc in range(CH):
                    p.op("tensor",
                         lambda e, oc=oc, kc=kc, sl=sl, bank=bank, w_tile=w_tile: e.matmul(
                             bank[:, :],
                             w_tile[:, kc * C + oc * 128: kc * C + (oc + 1) * 128],
                             (mix if src_key == "mix" else nbuf)[kc][:, sl],
                             start=(kc == 0), stop=(kc == CH - 1)),
                         reads=[k_(w_key, kc), k_(src_key, kc)],
                         writes=[bkey], tag="mm_" + tag)
                post(oc, ts, sl, bank, bkey)

    def post_r(oc, ts, sl, bank, bkey):
        p.op("scalar",
             lambda e, bank=bank: e.activation(scr[:, :], bank[:, :], AF.Exp, scale=-1.0),
             reads=[bkey], writes=["scr"], tag="sig_e")
        p.op("scalar",
             lambda e: e.activation(scr2[:, :], scr[:, :], AF.Ln, bias=b["cb"][:, 1:2]),
             reads=["scr", "cb"], writes=["scr2"], tag="sig_l")
        p.op("scalar",
             lambda e, oc=oc, sl=sl: e.activation(r_t[oc][:, sl], scr2[:, :], AF.Exp, scale=-1.0),
             reads=["scr2"], writes=[k_("r", oc)], tag="sig_s")

    def post_k(oc, ts, sl, bank, bkey):
        p.op("scalar",
             lambda e, bank=bank: e.activation(scr[:, :], bank[:, :], AF.Relu),
             reads=[bkey], writes=["scr"], tag="k_relu")
        p.op("vector",
             lambda e, oc=oc, sl=sl: e.tensor_mul(nbuf[oc][:, sl], scr[:, :], scr[:, :]),
             reads=["scr"], writes=[k_("nbuf", oc)], tag="k_sq")

    load_weight_bf16(ctx, wrT, wbuf, "wbuf", C, C, tag="wr")
    for c in range(CH):
        emit_mix(mur_t, "mur_t", 0, c)
    emit_proj(wbuf, "wbuf", "mix", post_r, "r")
    load_weight_bf16(ctx, wkT, wbuf, "wbuf", C, C, tag="wk")
    for c in range(CH):
        emit_mix(muk_t, "muk_t", CH, c)
    emit_proj(wbuf, "wbuf", "mix", post_k, "k")
    load_weight_bf16(ctx, wvT, wbuf, "wbuf", C, C, tag="wv")
    for oc in range(CH):
        for ts in range(4):
            bank = ps_mm[ts % 4]
            bkey = f"ps_mm{ts % 4}"
            sl = slice(ts * 512, (ts + 1) * 512)
            for kc in range(CH):
                p.op("tensor",
                     lambda e, oc=oc, kc=kc, sl=sl, bank=bank: e.matmul(
                         bank[:, :],
                         wbuf[:, kc * C + oc * 128: kc * C + (oc + 1) * 128],
                         nbuf[kc][:, sl],
                         start=(kc == 0), stop=(kc == CH - 1)),
                     reads=[k_("wbuf", kc), k_("nbuf", kc)],
                     writes=[bkey], tag="mm_v")
            p.op("vector",
                 lambda e, oc=oc, sl=sl, bank=bank: e.tensor_mul(
                     scr2[:, :], r_t[oc][:, sl], bank[:, :]),
                 reads=[k_("r", oc), bkey], writes=["scr2"], tag="v_mul")
            p.op("vector",
                 lambda e, oc=oc, sl=sl: e.tensor_add(
                     b["x"][oc][:, sl], b["x"][oc][:, sl], scr2[:, :]),
                 reads=["scr2", k_("x", oc)], writes=[k_("x", oc)], tag="v_add")

    ctx.resid_out()
    return p.build()


def _ctx_ln_scratch(self):
    p, b = self.p, self.b
    b["xsq"] = p.sb("xsq", [128, 512], F32)
    b["rowA"] = p.sb("rowA", [1, T], F32)
    b["rowB"] = p.sb("rowB", [1, T], F32)
    b["rrow_bf"] = p.sb("rrow_bf", [1, T], BF16)
    b["scr"] = p.sb("scr", [128, 512], F32)
    b["ps_sum"] = p.ps("ps_sum", [128, 512], F32)
    b["ps_sq"] = p.ps("ps_sq", [128, 512], F32)
    b["ps_bM"] = p.ps("ps_bM", [128, 512], F32)
    b["ps_bR"] = p.ps("ps_bR", [128, 512], F32)


Ctx.ln_scratch = _ctx_ln_scratch




def build_stage_l1ffn():
    INNER = 4096
    ctx = Ctx("l1ffn")
    p, b = ctx.p, ctx.b
    cb = b["cb"]
    ones_f, ones_k1 = b["ones_f"], b["ones_k1"]

    xin = p.dram("xin", [T, C], F32)
    xout = p.dram("xout", [T, C], F32, kind="ExternalOutput")
    wgT = p.dram("w_gateT", [C, INNER], F32)
    wuT = p.dram("w_upT", [C, INNER], F32)
    wdT = p.dram("w_downT", [INNER, C], F32)
    ident = p.dram("ident", [128, 128], BF16)
    hspill = p.dram("hspill", [T, INNER], F32, kind="Internal")

    idt = p.sb("idt", [128, 128], BF16)
    p.op("sync", lambda e: e.dma_start(out=idt[:, :], in_=ident[:, :]),
         writes=["idt"], dma=True)

    wdq = p.sb("wdq", [128, 32 * C], BF16)
    xqf = [p.sb(f"xqf_{c}", [128, T], BF16) for c in range(CH)]
    h32 = [p.sb("h32_0", [128, INNER], F32)]
    wst = [p.sb(f"wst_{i}", [128, 2048], F32) for i in range(2)]
    wqg = p.sb("wqg", [128, CH * 512], BF16)
    wqu = p.sb("wqu", [128, CH * 512], BF16)
    hfeat = p.sb("hfeat", [128, 32 * 128], BF16)
    xres = [p.sb(f"xres_{i}", [128, C], F32) for i in range(2)]
    xtok = [p.sb(f"xtok_{i}", [128, C], F32) for i in range(2)]
    xnf = p.sb("xnf", [128, C], F32)
    xqi = p.sb("xqi", [128, C], I32)
    xqb = p.sb("xqb", [128, C], BF16)
    wqi = p.sb("wqi", [128, 1024], I32)
    rscr = p.sb("rscr", [128, 1536], F32)
    g32 = p.sb("g32", [128, 512], F32)
    u32 = p.sb("u32", [128, 512], F32)
    sg32 = p.sb("sg32", [128, 512], F32)
    hcur = [p.sb(f"hcur_{i}", [128, 512], F32) for i in range(2)]
    hqb = p.sb("hqb", [128, INNER], BF16)
    rows = p.sb("rows", [128, 24], F32)
    hmax = p.sb("hmax", [128, NT], F32)
    rst_all = p.sb("rst_all", [128, NT], F32)
    mrow = p.sb("mrow", [128, 4], F32)
    tiny = p.sb("tiny", [1, 8], F32)
    wsum = p.sb("wsum", [128, 4], F32)

    ps_mm = [p.ps(f"ps_mm{i}", [128, 512], F32) for i in range(4)]
    ps_tr = [p.ps("ps_tr0", [128, 512], BF16), p.ps("ps_tr1", [128, 512], BF16)]
    ps_sc = p.ps("ps_sc", [128, 512], F32)

    def prepass2(dram, rows_, cols_, col, subw):
        p.op("vector", lambda e, col=col: e.memset(wsum[:, col:col + 1], 0.0),
             writes=["wsum"], tag="wz")
        nch = rows_ // 128
        nsub = cols_ // subw
        idx = 0
        for cc in range(nch):
            for s in range(nsub):
                st = wst[idx % 2]
                stk = f"wst:{idx % 2}"
                idx += 1
                p.op("gpsimd",
                     lambda e, cc=cc, s=s, st=st, dram=dram, subw=subw: e.dma_start(
                         out=st[:, 0:subw],
                         in_=dram[cc * 128:(cc + 1) * 128, s * subw:(s + 1) * subw]),
                     writes=[stk], dma=True, tag="wpre_ld")
                p.op("vector",
                     lambda e, st=st, subw=subw: e.tensor_reduce(
                         wsum[:, 3:4], st[:, 0:subw], AX.X, ALU.add,
                         apply_absolute_value=True),
                     reads=[stk], writes=["wsum"], tag="wpre_red")
                p.op("vector",
                     lambda e, col=col: e.tensor_add(wsum[:, col:col + 1],
                                                     wsum[:, col:col + 1], wsum[:, 3:4]),
                     reads=["wsum"], writes=["wsum"], tag="wpre_acc")
        p.op("tensor",
             lambda e, col=col: e.matmul(ps_sc[0:1, 0:1], wsum[:, col:col + 1],
                                         ones_f[:, 0:1], start=True, stop=True),
             reads=["wsum", "ones_f"], writes=["ps_sc"], tag="wpre_tot")
        p.op("vector",
             lambda e, col=col, rows_=rows_, cols_=cols_: e.tensor_scalar(
                 tiny[0:1, col:col + 1], ps_sc[0:1, 0:1], 1.0 / (rows_ * cols_), 1e-5,
                 ALU.mult, ALU.max),
             reads=["ps_sc"], writes=["tiny"], tag="wpre_mean")

    prepass2(wgT, C, INNER, 0, 2048)
    prepass2(wuT, C, INNER, 1, 2048)
    prepass2(wdT, INNER, C, 2, 1024)
    p.op("vector", lambda e: e.reciprocal(tiny[0:1, 4:7], tiny[0:1, 0:3]),
         reads=["tiny"], writes=["tiny"], tag="wpre_recip")
    for j in range(3):
        p.op("tensor",
             lambda e, j=j: e.matmul(ps_sc[:, 0:1], ones_k1[:, :], tiny[0:1, j:j + 1],
                                     start=True, stop=True),
             reads=["tiny", "ones_k1"], writes=["ps_sc"], tag="minv_b")
        p.op("vector",
             lambda e, j=j: e.tensor_copy(mrow[:, j:j + 1], ps_sc[:, 0:1]),
             reads=["ps_sc"], writes=["mrow"], tag="minv_c")

    p.op("tensor",
         lambda e: e.matmul(ps_sc[:, 0:1], ones_k1[:, :], tiny[0:1, 6:7],
                            start=True, stop=True),
         reads=["tiny", "ones_k1"], writes=["ps_sc"], tag="wsd_b")
    p.op("vector",
         lambda e: e.tensor_copy(mrow[:, 3:4], ps_sc[:, 0:1]),
         reads=["ps_sc"], writes=["mrow"], tag="wsd_c")
    for cc in range(32):
        st = wst[cc % 2]
        stk = f"wst:{cc % 2}"
        p.op("gpsimd",
             lambda e, cc=cc, st=st: e.dma_start(
                 out=st[:, 0:1024], in_=wdT[cc * 128:(cc + 1) * 128, :]),
             writes=[stk], dma=True, tag="wd_ld")
        p.op("vector",
             lambda e, st=st: e.tensor_scalar(st[:, 1024:2048], st[:, 0:1024],
                                              mrow[:, 3:4], 0.5, ALU.mult, ALU.is_ge),
             reads=[stk, "mrow"], writes=[stk], tag="wd_q1")
        p.op("vector",
             lambda e, st=st: e.tensor_scalar(st[:, 0:1024], st[:, 0:1024],
                                              mrow[:, 3:4], -0.5, ALU.mult, ALU.is_le),
             reads=[stk, "mrow"], writes=[stk], tag="wd_q2")
        p.op("vector",
             lambda e, cc=cc, st=st: e.tensor_sub(
                 wdq[:, cc * C:(cc + 1) * C], st[:, 1024:2048], st[:, 0:1024]),
             reads=[stk], writes=[k_("wdq", cc)], tag="wd_q3")

    for t in range(NT):
        xt = xtok[t % 2]
        xtk = f"xtok:{t % 2}"
        p.op("sync",
             lambda e, t=t, xt=xt: e.dma_start(out=xt[:, :], in_=xin[t * 128:(t + 1) * 128, :]),
             writes=[xtk], dma=True, tag="ln_ldx")
        p.op("vector",
             lambda e, xt=xt: e.tensor_reduce(rows[:, 0:1], xt[:, :], AX.X, ALU.add),
             reads=[xtk], writes=["rows"], tag="ln_sum")
        p.op("scalar",
             lambda e, xt=xt: e.activation(xnf[:, :], xt[:, :], AF.Square),
             reads=[xtk], writes=["xnf"], tag="ln_sq")
        p.op("vector",
             lambda e: e.tensor_reduce(rows[:, 1:2], xnf[:, :], AX.X, ALU.add),
             reads=["xnf"], writes=["rows"], tag="ln_sumsq")
        p.op("vector",
             lambda e: e.tensor_scalar_mul(rows[:, 2:3], rows[:, 0:1], 1.0 / C),
             reads=["rows"], writes=["rows"], tag="ln_mean")
        p.op("vector",
             lambda e: e.tensor_mul(rows[:, 3:4], rows[:, 2:3], rows[:, 2:3]),
             reads=["rows"], writes=["rows"], tag="ln_m2")
        p.op("vector",
             lambda e: e.scalar_tensor_tensor(rows[:, 3:4], rows[:, 1:2], 1.0 / C,
                                              rows[:, 3:4], ALU.mult, ALU.subtract),
             reads=["rows"], writes=["rows"], tag="ln_var")
        p.op("scalar",
             lambda e: e.activation(rows[:, 4:5], rows[:, 3:4], AF.Ln, bias=cb[:, 0:1]),
             reads=["rows", "cb"], writes=["rows"], tag="ln_ln")
        p.op("scalar",
             lambda e: e.activation(rows[:, 4:5], rows[:, 4:5], AF.Exp, scale=-0.5),
             reads=["rows"], writes=["rows"], tag="ln_exp")
        p.op("vector",
             lambda e, xt=xt: e.tensor_scalar(xnf[:, :], xt[:, :], rows[:, 2:3],
                                              rows[:, 4:5], ALU.subtract, ALU.mult),
             reads=[xtk, "rows"], writes=["xnf"], tag="ln_nrm")
        p.op("vector",
             lambda e: e.tensor_reduce(rows[:, 5:6], xnf[:, :], AX.X, ALU.max,
                                       apply_absolute_value=True),
             reads=["xnf"], writes=["rows"], tag="aq_max")
        p.op("vector",
             lambda e: e.tensor_scalar_max(rows[:, 5:6], rows[:, 5:6], 1e-5),
             reads=["rows"], writes=["rows"], tag="aq_clip")
        p.op("vector",
             lambda e: e.reciprocal(rows[:, 6:7], rows[:, 5:6]),
             reads=["rows"], writes=["rows"], tag="aq_rec")
        p.op("vector",
             lambda e: e.tensor_scalar_mul(rows[:, 6:7], rows[:, 6:7], 127.0),
             reads=["rows"], writes=["rows"], tag="aq_st")
        p.op("vector",
             lambda e, t=t: e.tensor_scalar_mul(rst_all[:, t:t + 1], rows[:, 5:6], 1.0 / 127.0),
             reads=["rows"], writes=["rst_all"], tag="aq_rst")
        p.op("vector",
             lambda e: e.tensor_scalar_mul(xqi[:, :], xnf[:, :], rows[:, 6:7]),
             reads=["xnf", "rows"], writes=["xqi"], tag="aq_q")
        for qs in range(2):
            qsl = slice(qs * 512, (qs + 1) * 512)
            p.op("vector",
                 lambda e, qsl=qsl: e.scalar_tensor_tensor(
                     rscr[:, 0:512], xnf[:, qsl], rows[:, 6:7], xqi[:, qsl],
                     ALU.mult, ALU.subtract),
                 reads=["xnf", "rows", "xqi"], writes=["rscr"], tag="aq_d")
            p.op("vector",
                 lambda e: e.tensor_single_scalar(rscr[:, 512:1024], rscr[:, 0:512],
                                                  0.5, ALU.is_ge),
                 reads=["rscr"], writes=["rscr"], tag="aq_g1")
            p.op("gpsimd",
                 lambda e: e.tensor_single_scalar(rscr[:, 1024:1536], rscr[:, 0:512],
                                                  -0.5, ALU.is_le),
                 reads=["rscr"], writes=["rscr"], tag="aq_g2")
            p.op("vector",
                 lambda e: e.tensor_sub(rscr[:, 512:1024], rscr[:, 512:1024],
                                        rscr[:, 1024:1536]),
                 reads=["rscr"], writes=["rscr"], tag="aq_gg")
            p.op("vector",
                 lambda e, qsl=qsl: e.tensor_add(xqb[:, qsl], xqi[:, qsl],
                                                 rscr[:, 512:1024]),
                 reads=["xqi", "rscr"], writes=["xqb"], tag="aq_bf")
        for cc in range(CH):
            half = cc % 2
            p.op("tensor",
                 lambda e, cc=cc, half=half: e.transpose(
                     ps_tr[half][:, 0:128],
                     xqb[:, cc * 128:(cc + 1) * 128], idt[:, :]),
                 reads=["xqb", "idt"], writes=[f"ps_tr:{half}"], tag="tr_x")
            p.op("vector",
                 lambda e, cc=cc, t=t, half=half: e.tensor_copy(
                     xqf[cc][:, t * 128:(t + 1) * 128],
                     ps_tr[half][:, 0:128]),
                 reads=[f"ps_tr:{half}"], writes=[k_("xqf", cc)], tag="tr_ev")

    NO = INNER // 512
    for o in range(NO):
        for (dram, dstt, dstk, mcol) in ((wgT, wqg, "wqg", 0), (wuT, wqu, "wqu", 1)):
            p.op("tensor",
                 lambda e, mcol=mcol: e.matmul(ps_sc[:, 0:1], ones_k1[:, :],
                                               tiny[0:1, 4 + mcol:5 + mcol],
                                               start=True, stop=True),
                 reads=["tiny", "ones_k1"], writes=["ps_sc"], tag="ws_b")
            p.op("vector",
                 lambda e: e.tensor_copy(mrow[:, 3:4], ps_sc[:, 0:1]),
                 reads=["ps_sc"], writes=["mrow"], tag="ws_c")
            for cc in range(CH):
                st = wst[cc % 2]
                stk = f"wst:{cc % 2}"
                p.op("gpsimd",
                     lambda e, cc=cc, o=o, st=st, dram=dram: e.dma_start(
                         out=st[:, 0:512],
                         in_=dram[cc * 128:(cc + 1) * 128, o * 512:(o + 1) * 512]),
                     writes=[stk], dma=True, tag="wgu_ld")
                p.op("vector",
                     lambda e, st=st: e.tensor_scalar(st[:, 512:1024], st[:, 0:512],
                                                      mrow[:, 3:4], 0.5, ALU.mult, ALU.is_ge),
                     reads=[stk, "mrow"], writes=[stk], tag="wgu_q1")
                p.op("vector",
                     lambda e, st=st: e.tensor_scalar(
                         st[:, 0:512], st[:, 0:512],
                         mrow[:, 3:4], -0.5, ALU.mult, ALU.is_le),
                     reads=[stk, "mrow"], writes=[stk], tag="wgu_q2")
                p.op("vector",
                     lambda e, cc=cc, st=st, dstt=dstt, dstk=dstk: e.tensor_sub(
                         dstt[:, cc * 512:(cc + 1) * 512], st[:, 512:1024],
                         st[:, 0:512]),
                     reads=[stk], writes=[k_(dstk, cc)], tag="wgu_q3")
        for t in range(NT):
            bg = ps_mm[0]
            bu = ps_mm[1]
            for cc in range(CH):
                p.op("tensor",
                     lambda e, cc=cc, t=t, bg=bg: e.matmul(
                         bg[:, :], xqf[cc][:, t * 128:(t + 1) * 128],
                         wqg[:, cc * 512:(cc + 1) * 512],
                         start=(cc == 0), stop=(cc == CH - 1)),
                     reads=[k_("xqf", cc), k_("wqg", cc)], writes=["ps_mm0"], tag="mm_g")
            for cc in range(CH):
                p.op("tensor",
                     lambda e, cc=cc, t=t, bu=bu: e.matmul(
                         bu[:, :], xqf[cc][:, t * 128:(t + 1) * 128],
                         wqu[:, cc * 512:(cc + 1) * 512],
                         start=(cc == 0), stop=(cc == CH - 1)),
                     reads=[k_("xqf", cc), k_("wqu", cc)], writes=["ps_mm1"], tag="mm_u")
            p.op("vector",
                 lambda e, t=t: e.tensor_mul(rows[:, 8:9], rst_all[:, t:t + 1], mrow[:, 0:1]),
                 reads=["rst_all", "mrow"], writes=["rows"], tag="dsc_g")
            p.op("vector",
                 lambda e, t=t: e.tensor_mul(rows[:, 9:10], rst_all[:, t:t + 1], mrow[:, 1:2]),
                 reads=["rst_all", "mrow"], writes=["rows"], tag="dsc_u")
            p.op("vector",
                 lambda e: e.tensor_scalar_mul(g32[:, :], ps_mm[0][:, :], rows[:, 8:9]),
                 reads=["ps_mm0", "rows"], writes=["g32"], tag="deq_g")
            p.op("vector",
                 lambda e: e.tensor_scalar_mul(u32[:, :], ps_mm[1][:, :], rows[:, 9:10]),
                 reads=["ps_mm1", "rows"], writes=["u32"], tag="deq_u")
            p.op("scalar",
                 lambda e: e.activation(sg32[:, :], g32[:, :], AF.Exp, scale=-1.0),
                 reads=["g32"], writes=["sg32"], tag="sil_e")
            p.op("scalar",
                 lambda e: e.activation(sg32[:, :], sg32[:, :], AF.Ln, bias=cb[:, 1:2]),
                 reads=["sg32", "cb"], writes=["sg32"], tag="sil_l")
            p.op("scalar",
                 lambda e: e.activation(sg32[:, :], sg32[:, :], AF.Exp, scale=-1.0),
                 reads=["sg32"], writes=["sg32"], tag="sil_s")
            hc = hcur[t % 2]
            hck = f"hcur:{t % 2}"
            p.op("vector",
                 lambda e, hc=hc: e.tensor_mul(hc[:, :], g32[:, :], sg32[:, :]),
                 reads=["g32", "sg32"], writes=[hck], tag="h_m1")
            p.op("vector",
                 lambda e, hc=hc: e.tensor_mul(hc[:, :], hc[:, :], u32[:, :]),
                 reads=["u32", hck], writes=[hck], tag="h_m2")
            p.op("vector",
                 lambda e, hc=hc: e.tensor_reduce(rows[:, 10:11], hc[:, :], AX.X,
                                                  ALU.max, apply_absolute_value=True),
                 reads=[hck], writes=["rows"], tag="h_max")
            if o == 0:
                p.op("vector",
                     lambda e, t=t: e.tensor_copy(hmax[:, t:t + 1], rows[:, 10:11]),
                     reads=["rows"], writes=["hmax"], tag="h_max0")
            else:
                p.op("vector",
                     lambda e, t=t: e.tensor_max(hmax[:, t:t + 1], hmax[:, t:t + 1],
                                                 rows[:, 10:11]),
                     reads=["rows", "hmax"], writes=["hmax"], tag="h_maxa")
            p.op("sync",
                 lambda e, t=t, o=o, hc=hc: e.dma_start(
                     out=hspill[t * 128:(t + 1) * 128, o * 512:(o + 1) * 512],
                     in_=hc[:, :]),
                 reads=[hck], dma=True, tag="h_sp")

    for t in range(NT):
        ht = h32[0]
        htk = "h32:0"
        p.op("sync",
             lambda e, t=t, ht=ht: e.dma_start(
                 out=ht[:, :], in_=hspill[t * 128:(t + 1) * 128, :]),
             writes=[htk], dma=True, tag="h_rl")
        xr = xres[t % 2]
        xrk = f"xres:{t % 2}"
        p.op("sync",
             lambda e, t=t, xr=xr: e.dma_start(out=xr[:, :], in_=xin[t * 128:(t + 1) * 128, :]),
             writes=[xrk], dma=True, tag="x_rl")
        p.op("vector",
             lambda e, t=t: e.tensor_scalar_max(rows[:, 11:12], hmax[:, t:t + 1], 1e-5),
             reads=["hmax"], writes=["rows"], tag="h_clip")
        p.op("vector",
             lambda e: e.reciprocal(rows[:, 12:13], rows[:, 11:12]),
             reads=["rows"], writes=["rows"], tag="h_rec")
        p.op("vector",
             lambda e: e.tensor_scalar_mul(rows[:, 12:13], rows[:, 12:13], 127.0),
             reads=["rows"], writes=["rows"], tag="h_s2")
        p.op("vector",
             lambda e: e.scalar_tensor_tensor(rows[:, 13:14], rows[:, 11:12], 1.0 / 127.0,
                                              mrow[:, 2:3], ALU.mult, ALU.mult),
             reads=["rows", "mrow"], writes=["rows"], tag="h_dscd")
        for hs in range(INNER // 512):
            hsl = slice(hs * 512, (hs + 1) * 512)
            p.op("vector",
                 lambda e, ht=ht, hsl=hsl: e.tensor_scalar_mul(
                     wqi[:, 0:512], ht[:, hsl], rows[:, 12:13]),
                 reads=[htk, "rows"], writes=["wqi"], tag="h_q")
            p.op("vector",
                 lambda e, ht=ht, hsl=hsl: e.scalar_tensor_tensor(
                     rscr[:, 0:512], ht[:, hsl], rows[:, 12:13], wqi[:, 0:512],
                     ALU.mult, ALU.subtract),
                 reads=[htk, "rows", "wqi"], writes=["rscr"], tag="h_d")
            p.op("vector",
                 lambda e: e.tensor_single_scalar(rscr[:, 512:1024], rscr[:, 0:512],
                                                  0.5, ALU.is_ge),
                 reads=["rscr"], writes=["rscr"], tag="h_g1")
            p.op("gpsimd",
                 lambda e: e.tensor_single_scalar(rscr[:, 1024:1536], rscr[:, 0:512],
                                                  -0.5, ALU.is_le),
                 reads=["rscr"], writes=["rscr"], tag="h_g2")
            p.op("vector",
                 lambda e: e.tensor_sub(rscr[:, 512:1024], rscr[:, 512:1024],
                                        rscr[:, 1024:1536]),
                 reads=["rscr"], writes=["rscr"], tag="h_gg")
            p.op("vector",
                 lambda e, hsl=hsl: e.tensor_add(hqb[:, hsl], wqi[:, 0:512],
                                                 rscr[:, 512:1024]),
                 reads=["wqi", "rscr"], writes=["hqb"], tag="h_bf")
        for cc in range(32):
            half = cc % 2
            p.op("tensor",
                 lambda e, cc=cc, half=half: e.transpose(
                     ps_tr[half][:, 0:128],
                     hqb[:, cc * 128:(cc + 1) * 128], idt[:, :]),
                 reads=["hqb", "idt"], writes=[f"ps_tr:{half}"], tag="tr_h")
            p.op("vector",
                 lambda e, cc=cc, half=half: e.tensor_copy(
                     hfeat[:, cc * 128:(cc + 1) * 128],
                     ps_tr[half][:, 0:128]),
                 reads=[f"ps_tr:{half}"], writes=[k_("hfeat", cc)], tag="tr_hev")
        for o2 in range(2):
            bank = ps_mm[2 + o2]
            bkey = f"ps_mm{2 + o2}"
            for cc in range(32):
                p.op("tensor",
                     lambda e, cc=cc, o2=o2, bank=bank: e.matmul(
                         bank[:, :], hfeat[:, cc * 128:(cc + 1) * 128],
                         wdq[:, cc * C + o2 * 512: cc * C + (o2 + 1) * 512],
                         start=(cc == 0), stop=(cc == 31)),
                     reads=[k_("hfeat", cc), k_("wdq", cc)], writes=[bkey], tag="mm_d")
            p.op("vector",
                 lambda e, o2=o2, xr=xr, bank=bank: e.scalar_tensor_tensor(
                     xr[:, o2 * 512:(o2 + 1) * 512], bank[:, :], rows[:, 13:14],
                     xr[:, o2 * 512:(o2 + 1) * 512], ALU.mult, ALU.add),
                 reads=[bkey, "rows", xrk], writes=[xrk], tag="d_add")
        p.op("sync",
             lambda e, t=t, xr=xr: e.dma_start(
                 out=xout[t * 128:(t + 1) * 128, :], in_=xr[:, :]),
             reads=[xrk], dma=True, tag="d_st")

    return p.build()




def build_stage_l2attn():
    H, D = 16, 64
    ctx = Ctx("l2attn")
    p, b = ctx.p, ctx.b
    cb = b["cb"]
    ones_f, ones_k1, ones_k1b = b["ones_f"], b["ones_k1"], b["ones_k1b"]

    xin = p.dram("xin", [C, T], F32)
    xout = p.dram("xout", [C, T], F32, kind="ExternalOutput")
    qkvT = p.dram("qkvT", [C, 3 * C], F32)
    outT = p.dram("outT", [C, C], F32)
    qb2 = p.dram("qb2", [64, H], F32)
    kb2 = p.dram("kb2", [64, H], F32)
    vb2 = p.dram("vb2", [128, H], F32)
    ob2 = p.dram("ob2", [128, CH], F32)
    qac = p.dram("qac", [H, 6, T], BF16)
    kacA = p.dram("kacA", [H, 7, T], BF16)
    kacB = p.dram("kacB", [H, 7, T], BF16)
    diagp = p.dram("diagp", [4, 128, 512], F32)

    xrot = [p.sb(f"xrot_{i}", [128, T], F32) for i in range(2)]
    xn = [p.sb(f"xn_{c}", [128, T], BF16) for c in range(CH)]
    wq = p.sb("wq", [128, CH * 3072], BF16)
    wo = wq
    v65 = [p.sb(f"v65_{t}", [128, H * 65], BF16) for t in range(NT)]
    qaug = p.sb("qaug", [71, T], BF16)
    kaugA = p.sb("kaugA", [71, T], BF16)
    kaugB = p.sb("kaugB", [71, T], BF16)
    pt = p.sb("pt", [128, T], BF16)
    attnres = [p.sb(f"attnres_{i}", [128, T], BF16) for i in range(H // 2)]
    qsq = p.sb("qsq", [64, 512], BF16)
    brow = p.sb("brow", [1, T], F32)
    browb = p.sb("browb", [1, T], BF16)
    nrow = p.sb("nrow", [128, 512], F32)
    Bsb = p.sb("Bsb", [128, 512], F32)
    bias_q = p.sb("bias_q", [64, H], F32)
    bias_k = p.sb("bias_k", [64, H], F32)
    bias_v = p.sb("bias_v", [128, H], F32)
    bias_o = p.sb("bias_o", [128, CH], F32)
    diag_t = p.sb("diag_t", [128, 4 * 512], F32)
    scr = p.sb("scr", [128, 512], F32)
    scr2 = p.sb("scr2", [128, 512], F32)
    tinyr = p.sb("tinyr", [1, 8], F32)

    ps_s = [p.ps(f"ps_s{i}", [128, 512], F32) for i in range(4)]
    ps_a = [p.ps(f"ps_a{i}", [128, 512], F32) for i in range(4)]
    if _DBG_L2 is not None:
        dbg_dram = p.dram("dbg_sc", [128, 512], F32, kind="ExternalOutput")
        dbg2_dram = p.dram("dbg_sc2", [128, 512], F32, kind="ExternalOutput")
    ones64b = p.sb("ones64b", [64, 1], BF16)
    p.op("vector", lambda e: e.memset(ones64b[:, :], 1.0), writes=["ones64b"])
    onesr = p.sb("onesr", [128, 128], F32)
    p.op("vector", lambda e: e.memset(onesr[:, :], 1.0), writes=["onesr"])

    p.op("sync", lambda e: e.dma_start(out=bias_q[:, :], in_=qb2[:, :]), writes=["bias_q"], dma=True)
    p.op("sync", lambda e: e.dma_start(out=bias_k[:, :], in_=kb2[:, :]), writes=["bias_k"], dma=True)
    p.op("sync", lambda e: e.dma_start(out=bias_v[:, :], in_=vb2[:, :]), writes=["bias_v"], dma=True)
    p.op("sync", lambda e: e.dma_start(out=bias_o[:, :], in_=ob2[:, :]), writes=["bias_o"], dma=True)
    for r in range(4):
        p.op("sync", lambda e, r=r: e.dma_start(out=diag_t[:, r * 512:(r + 1) * 512], in_=diagp[r, :, :]),
             writes=["diag_t"], dma=True)
    for c in range(CH):
        p.op("gpsimd",
             lambda e, c=c: e.dma_start(out=wq[:, c * 3072:(c + 1) * 3072],
                                        in_=qkvT[c * 128:(c + 1) * 128, :]),
             writes=[k_("wq", c)], dma=True, tag="wqkv")


    for c in range(CH):
        xr = xrot[c % 2]
        xk = f"xrot:{c % 2}"
        p.op("sync", lambda e, c=c, xr=xr: e.dma_start(out=xr[:, :], in_=xin[c * 128:(c + 1) * 128, :]),
             writes=[xk], dma=True, tag="lnx")
        for s in range(4):
            sl = slice(s * 512, (s + 1) * 512)
            p.op("tensor",
                 lambda e, s=s, sl=sl, c=c, xr=xr: e.matmul(
                     ps_s[s][0:1, :], ones_f[:, 0:1], xr[:, sl],
                     start=(c == 0), stop=(c == CH - 1)),
                 reads=[xk, "ones_f"], writes=[f"ps_s{s}"], tag="ln_mm_s")
            p.op("scalar", lambda e, sl=sl, xr=xr: e.activation(scr[:, :], xr[:, sl], AF.Square),
                 reads=[xk], writes=["scr"], tag="ln_sq")
            p.op("tensor",
                 lambda e, s=s, c=c: e.matmul(
                     ps_s[s][32:33, :], ones_f[:, 0:1], scr[:, :],
                     start=(c == 0), stop=(c == CH - 1)),
                 reads=["scr", "ones_f"], writes=[f"ps_s{s}"], tag="ln_mm_q")
    for s in range(4):
        sl = slice(s * 512, (s + 1) * 512)
        p.op("vector", lambda e, s=s, sl=sl: e.tensor_scalar_mul(brow[0:1, sl], ps_s[s][0:1, :], 1.0 / C),
             reads=[f"ps_s{s}"], writes=["brow"], tag="ln_mean")
        p.op("vector", lambda e, sl=sl: e.tensor_mul(scr2[0:1, :], brow[0:1, sl], brow[0:1, sl]),
             reads=["brow"], writes=["scr2"], tag="ln_m2")
        p.op("vector",
             lambda e, s=s: e.scalar_tensor_tensor(scr[0:1, :], ps_s[s][32:33, :], 1.0 / C,
                                                   scr2[0:1, :], ALU.mult, ALU.subtract),
             reads=[f"ps_s{s}", "scr2"], writes=["scr"], tag="ln_var")
        p.op("scalar", lambda e: e.activation(scr[0:1, :], scr[0:1, :], AF.Ln, bias=cb[0:1, 0:1]),
             reads=["scr", "cb"], writes=["scr"], tag="ln_ln")
        p.op("scalar", lambda e: e.activation(scr[0:1, :], scr[0:1, :], AF.Exp, scale=-0.5),
             reads=["scr"], writes=["scr"], tag="ln_exp")
        p.op("vector", lambda e, sl=sl: e.tensor_copy(browb[0:1, sl], scr[0:1, :]),
             reads=["scr"], writes=["browb"], tag="ln_rbf")
    for c in range(CH):
        xr = xrot[c % 2]
        xk = f"xrot:{c % 2}"
        p.op("sync", lambda e, c=c, xr=xr: e.dma_start(out=xr[:, :], in_=xin[c * 128:(c + 1) * 128, :]),
             writes=[xk], dma=True, tag="lnx2")
        for s in range(4):
            sl = slice(s * 512, (s + 1) * 512)
            p.op("tensor",
                 lambda e, sl=sl: e.matmul(ps_s[2][:, :], ones_k1[:, :], brow[0:1, sl],
                                           start=True, stop=True),
                 reads=["brow", "ones_k1"], writes=["ps_s2"], tag="ln_bM")
            p.op("tensor",
                 lambda e, sl=sl: e.matmul(ps_s[3][:, :], ones_k1b[:, :], browb[0:1, sl],
                                           start=True, stop=True),
                 reads=["browb", "ones_k1b"], writes=["ps_s3"], tag="ln_bR")
            p.op("vector",
                 lambda e, sl=sl, xr=xr: e.tensor_sub(scr[:, :], xr[:, sl], ps_s[2][:, :]),
                 reads=[xk, "ps_s2"], writes=["scr"], tag="ln_sub")
            p.op("vector",
                 lambda e, c=c, sl=sl: e.tensor_mul(xn[c][:, sl], scr[:, :], ps_s[3][:, :]),
                 reads=["scr", "ps_s3"], writes=[k_("xn", c)], tag="ln_nrm")

    for t in range(NT):
        p.op("vector",
             lambda e, t=t: e.memset(
                 v65[t][:, :].rearrange("p (h u) -> p h u", u=65)[:, :, 64:65], 1.0),
             writes=[k_("v65", t)], tag="v_ones")
        for g in range(2):
            bank = ps_a[g]
            bkey = f"ps_a{g}"
            for c in range(CH):
                p.op("tensor",
                     lambda e, t=t, g=g, c=c, bank=bank: e.matmul(
                         bank[:, :],
                         xn[c][:, t * 128:(t + 1) * 128],
                         wq[:, c * 3072:(c + 1) * 3072].rearrange(
                             "p (h u) -> p h u", u=192)[:, g * 8:(g + 1) * 8, 128:192],
                         start=(c == 0), stop=(c == CH - 1)),
                     reads=[k_("xn", c), k_("wq", c)], writes=[bkey], tag="v_mm")
            p.op("vector",
                 lambda e, t=t, g=g, bank=bank: e.tensor_copy(
                     v65[t][:, :].rearrange("p (h u) -> p h u", u=65)[:, g * 8:(g + 1) * 8, 0:64],
                     bank[:, :].rearrange("p (h u) -> p h u", u=64)),
                 reads=[bkey], writes=[k_("v65", t)], tag="v_ev")

    for i in range(H // 2):
        p.op("vector", lambda e, i=i: e.memset(attnres[i][:, :], 0.0),
             writes=[k_("attnres", i)], tag="at_z")
    for h in (range(H) if _DBG_HEADS is None else _DBG_HEADS):
        slope = 0.5 ** h
        hq = h * 192
        par = h % 2
        at = attnres[h // 2]
        atk = k_("attnres", h // 2)

        for s in range(4):
            sl = slice(s * 512, (s + 1) * 512)
            for c in range(CH):
                p.op("tensor",
                     lambda e, s=s, c=c, sl=sl, hq=hq: e.matmul(
                         ps_a[0][0:64, :], wq[:, c * 3072 + hq: c * 3072 + hq + 64],
                         xn[c][:, sl], start=(c == 0), stop=(c == CH - 1)),
                     reads=[k_("wq", c), k_("xn", c)], writes=["ps_a0"], tag="q_mm")
            p.op("vector",
                 lambda e, s=s, sl=sl, h=h: e.tensor_scalar(
                     qaug[0:64, sl], ps_a[0][0:64, :], bias_q[:, h:h + 1], 0.125,
                     ALU.add, ALU.mult),
                 reads=["ps_a0", "bias_q"], writes=["qaug"], tag="q_ev")
            for c in range(CH):
                p.op("tensor",
                     lambda e, s=s, c=c, sl=sl, hq=hq: e.matmul(
                         ps_a[1][0:64, :], wq[:, c * 3072 + hq + 64: c * 3072 + hq + 128],
                         xn[c][:, sl], start=(c == 0), stop=(c == CH - 1)),
                     reads=[k_("wq", c), k_("xn", c)], writes=["ps_a1"], tag="k_mm")
            p.op("vector",
                 lambda e, s=s, sl=sl, h=h: e.tensor_scalar_add(
                     kaugA[0:64, sl], ps_a[1][0:64, :], bias_k[:, h:h + 1]),
                 reads=["ps_a1", "bias_k"], writes=["kaugA"], tag="k_evA")
            p.op("vector",
                 lambda e, sl=sl: e.tensor_copy(kaugB[0:64, sl], kaugA[0:64, sl]),
                 reads=["kaugA"], writes=["kaugB"], tag="k_evB")

        for (srct, srck) in ((kaugA, "kaugA"), (qaug, "qaug")):
            for s in range(4):
                sl = slice(s * 512, (s + 1) * 512)
                p.op("scalar",
                     lambda e, sl=sl, srct=srct: e.activation(qsq[0:64, 0:512], srct[0:64, sl], AF.Square),
                     reads=[srck], writes=["qsq"], tag="nsq")
                p.op("tensor",
                     lambda e: e.matmul(ps_a[2][0:1, :], ones64b[:, 0:1],
                                        qsq[0:64, 0:512], start=True, stop=True),
                     reads=["qsq", "ones64b"], writes=["ps_a2"], tag="nmm")
                p.op("vector",
                     lambda e, sl=sl: e.tensor_copy(brow[0:1, sl], ps_a[2][0:1, :]),
                     reads=["ps_a2"], writes=["brow"], tag="nev")
                if _DBG_L2 is not None and h == _DBG_L2[0] and s == 0 and srck == "kaugA":
                    p.op("vector",
                         lambda e: e.tensor_copy(Bsb[0:64, :], qsq[0:64, 0:512]),
                         reads=["qsq"], writes=["Bsb"], tag="dbg_qsq")
                    p.op("sync",
                         lambda e: e.dma_start(out=dbg_dram[64:128, 0:512], in_=Bsb[0:64, :]),
                         reads=["Bsb"], writes=["dbgout"], dma=True, tag="dbg_qsqd")
                    p.op("vector",
                         lambda e: e.tensor_copy(scr2[0:1, :], ps_a[2][0:1, :]),
                         reads=["ps_a2"], writes=["scr2"], tag="dbg_ps")
                    p.op("sync",
                         lambda e: e.dma_start(out=dbg_dram[6:7, 0:512], in_=scr2[0:1, :]),
                         reads=["scr2"], writes=["dbgout"], dma=True, tag="dbg_psd")
                    p.op("vector",
                         lambda e: e.tensor_copy(scr[0:32, 0:64], kaugA[0:32, 0:64]),
                         reads=["kaugA"], writes=["scr"], tag="dbg_kacp")
                    p.op("gpsimd",
                         lambda e: e.dma_start(out=dbg_dram[8:40, 0:64], in_=scr[0:32, 0:64]),
                         reads=["scr"], writes=["dbgout"], dma=True, tag="dbg_ka")
            if srck == "kaugA":
                p.op("vector",
                     lambda e: e.tensor_reduce(tinyr[0:1, 0:1], brow[0:1, :], AX.X, ALU.max),
                     reads=["brow"], writes=["tinyr"], tag="nk2max")
                if _DBG_L2 is not None and h == _DBG_L2[0]:
                    for ss in range(4):
                        p.op("sync",
                             lambda e, ss=ss: e.dma_start(
                                 out=dbg_dram[2 + ss:3 + ss, 0:512],
                                 in_=brow[0:1, ss * 512:(ss + 1) * 512]),
                             reads=["brow"], writes=["dbgout"], dma=True, tag="dbg_nk")
                    p.op("sync",
                         lambda e: e.dma_start(out=dbg_dram[1:2, 0:1], in_=tinyr[0:1, 0:1]),
                         reads=["tinyr"], writes=["dbgout"], dma=True, tag="dbg_ti")
        p.op("vector",
             lambda e: e.tensor_scalar_mul(brow[0:1, :], brow[0:1, :], tinyr[0:1, 0:1]),
             reads=["brow", "tinyr"], writes=["brow"], tag="bq_t")
        p.op("scalar",
             lambda e: e.activation(brow[0:1, :], brow[0:1, :], AF.Ln, bias=cb[0:1, 0:1]),
             reads=["brow", "cb"], writes=["brow"], tag="bq_ln")
        p.op("scalar",
             lambda e: e.activation(brow[0:1, :], brow[0:1, :], AF.Exp, scale=0.5),
             reads=["brow"], writes=["brow"], tag="bq_exp")
        p.op("vector",
             lambda e: e.tensor_scalar_mul(browb[0:1, :], brow[0:1, :], -1.0),
             reads=["brow"], writes=["browb"], tag="bq_neg")
        p.op("gpsimd",
             lambda e: e.dma_start(out=qaug[64:65, :], in_=browb[0:1, :]),
             reads=["browb"], writes=["qaug"], dma=True, tag="aug_b")
        p.op("sync",
             lambda e, h=h: e.dma_start(out=qaug[65:71, :], in_=qac[h, :, :]),
             writes=["qaug"], dma=True, tag="aug_q")
        p.op("sync",
             lambda e, h=h: e.dma_start(out=kaugA[64:71, :], in_=kacA[h, :, :]),
             writes=["kaugA"], dma=True, tag="aug_kA")
        p.op("sync",
             lambda e, h=h: e.dma_start(out=kaugB[64:71, :], in_=kacB[h, :, :]),
             writes=["kaugB"], dma=True, tag="aug_kB")

        for kt in range(NT):
            for g in range(4):
                if kt < 4 * g:
                    kv, kvk = kaugA, "kaugA"
                elif kt >= 4 * (g + 1):
                    kv, kvk = kaugB, "kaugB"
                else:
                    kv, kvk = kaugA, "kaugA"
                p.op("tensor",
                     lambda e, kt=kt, g=g, kv=kv: e.matmul(
                         ps_s[g][:, :], kv[0:71, kt * 128:(kt + 1) * 128],
                         qaug[0:71, g * 512:(g + 1) * 512], start=True, stop=True),
                     reads=[kvk, "qaug"], writes=[f"ps_s{g}"], tag="sc_mm")
                if 4 * g <= kt < 4 * (g + 1):
                    ri = kt - 4 * g
                    p.op("vector",
                         lambda e, g=g, ri=ri, slope=slope: e.scalar_tensor_tensor(
                             ps_s[g][:, :], diag_t[:, ri * 512:(ri + 1) * 512],
                             -2.0 * slope, ps_s[g][:, :], ALU.mult, ALU.add),
                         reads=["diag_t", f"ps_s{g}"], writes=[f"ps_s{g}"], tag="sc_diag")
                p.op("scalar",
                     lambda e, kt=kt, g=g: e.activation(
                         pt[:, g * 512:(g + 1) * 512], ps_s[g][:, :], AF.Exp),
                     reads=[f"ps_s{g}"], writes=["pt"], tag="sc_exp")
                if _DBG_L2 is not None and (h, kt, g) == _DBG_L2:
                    p.op("vector",
                         lambda e, g=g: e.tensor_copy(Bsb[:, :], ps_s[g][:, :]),
                         reads=[f"ps_s{g}"], writes=["Bsb"], tag="dbg_cp")
                    p.op("sync",
                         lambda e: e.dma_start(out=dbg2_dram[:, :], in_=Bsb[:, :]),
                         reads=["Bsb"], writes=["dbgout2"], dma=True, tag="dbg_st")
                if par == 0:
                    p.op("tensor",
                         lambda e, kt=kt, g=g, h=h: e.matmul(
                             ps_a[g][0:65, :], v65[kt][:, h * 65:(h + 1) * 65],
                             pt[:, g * 512:(g + 1) * 512],
                             start=(kt == 0), stop=(kt == NT - 1)),
                         reads=[k_("v65", kt), "pt"], writes=[f"ps_a{g}"], tag="av_mm")
                else:
                    p.op("tensor",
                         lambda e, kt=kt, g=g, h=h: e.matmul(
                             ps_a[g][64:128, :], v65[kt][:, h * 65:h * 65 + 64],
                             pt[:, g * 512:(g + 1) * 512],
                             start=(kt == 0), stop=(kt == NT - 1)),
                         reads=[k_("v65", kt), "pt"], writes=[f"ps_a{g}"], tag="av_mmo")
                    p.op("tensor",
                         lambda e, kt=kt, g=g, h=h: e.matmul(
                             ps_a[g][0:1, :], v65[kt][:, h * 65 + 64:h * 65 + 65],
                             pt[:, g * 512:(g + 1) * 512],
                             start=(kt == 0), stop=(kt == NT - 1)),
                         reads=[k_("v65", kt), "pt"], writes=[f"ps_a{g}"], tag="av_den")

        for g in range(4):
            gsl = slice(g * 512, (g + 1) * 512)
            if par == 0:
                p.op("vector",
                     lambda e, g=g: e.reciprocal(nrow[64:65, :], ps_a[g][64:65, :]),
                     reads=[f"ps_a{g}"], writes=["nrow"], tag="nm_rc")
                p.op("tensor",
                     lambda e: e.matmul(ps_s[0][0:64, :], onesr[64:65, 0:64], nrow[64:65, :],
                                        start=True, stop=True),
                     reads=["nrow", "onesr"], writes=["ps_s0"], tag="nm_bc")
                p.op("vector",
                     lambda e: e.tensor_copy(Bsb[0:64, :], ps_s[0][0:64, :]),
                     reads=["ps_s0"], writes=["Bsb"], tag="nm_ev")
                p.op("vector",
                     lambda e, g=g: e.tensor_mul(scr[0:64, :], ps_a[g][0:64, :], Bsb[0:64, :]),
                     reads=[f"ps_a{g}", "Bsb"], writes=["scr"], tag="nm_mul")
                p.op("vector",
                     lambda e, g=g, gsl=gsl, h=h, at=at: e.tensor_scalar_add(
                         at[0:64, gsl], scr[0:64, :], bias_v[0:64, h:h + 1]),
                     reads=["scr", "bias_v"], writes=[atk], tag="nm_out")
            else:
                p.op("vector",
                     lambda e, g=g: e.reciprocal(nrow[0:1, :], ps_a[g][0:1, :]),
                     reads=[f"ps_a{g}"], writes=["nrow"], tag="nm_rc")
                p.op("tensor",
                     lambda e: e.matmul(ps_s[0][64:128, :], onesr[0:1, 0:64], nrow[0:1, :],
                                        start=True, stop=True),
                     reads=["nrow", "onesr"], writes=["ps_s0"], tag="nm_bc")
                p.op("vector",
                     lambda e: e.tensor_copy(Bsb[64:128, :], ps_s[0][64:128, :]),
                     reads=["ps_s0"], writes=["Bsb"], tag="nm_ev")
                p.op("vector",
                     lambda e, g=g: e.tensor_mul(scr[64:128, :], ps_a[g][64:128, :], Bsb[64:128, :]),
                     reads=[f"ps_a{g}", "Bsb"], writes=["scr"], tag="nm_mul")
                p.op("vector",
                     lambda e, g=g, gsl=gsl, h=h, at=at: e.tensor_scalar_add(
                         at[64:128, gsl], scr[64:128, :], bias_v[64:128, h:h + 1]),
                     reads=["scr", "bias_v"], writes=[atk], tag="nm_out")

    for c in range(CH):
        wqk = k_("wq", (c * 1024) // 3072)
        wqk2 = k_("wq", ((c + 1) * 1024 - 1) // 3072)
        p.op("gpsimd",
             lambda e, c=c: e.dma_start(out=wo[:, c * 1024:(c + 1) * 1024],
                                        in_=outT[c * 128:(c + 1) * 128, :]),
             writes=list({wqk, wqk2, k_("wo", c)}), dma=True, tag="wout")
    for oc in range(CH):
        xr = xrot[oc % 2]
        xk = f"xrot:{oc % 2}"
        p.op("sync", lambda e, oc=oc, xr=xr: e.dma_start(out=xr[:, :], in_=xin[oc * 128:(oc + 1) * 128, :]),
             writes=[xk], dma=True, tag="ox_ld")
        p.op("vector", lambda e, xr=xr: e.tensor_copy(scr[:, 1:2], xr[:, 0:1]),
             reads=[xk], writes=["scr"], tag="gz_sv")
        for s in range(4):
            sl = slice(s * 512, (s + 1) * 512)
            bank = ps_a[s % 4]
            bkey = f"ps_a{s % 4}"
            for i in range(CH):
                p.op("tensor",
                     lambda e, oc=oc, i=i, sl=sl, bank=bank: e.matmul(
                         bank[:, :], wo[:, i * 1024 + oc * 128: i * 1024 + (oc + 1) * 128],
                         attnres[i][:, sl], start=(i == 0), stop=(i == CH - 1)),
                     reads=[k_("wo", i), k_("attnres", i)], writes=[bkey], tag="o_mm")
            p.op("vector",
                 lambda e, oc=oc, sl=sl, bank=bank, xr=xr: e.scalar_tensor_tensor(
                     xr[:, sl], bank[:, :], bias_o[:, oc:oc + 1], xr[:, sl],
                     ALU.add, ALU.add),
                 reads=[bkey, "bias_o", xk], writes=[xk], tag="o_add")
        p.op("sync",
             lambda e, oc=oc, xr=xr: e.dma_start(out=xout[oc * 128:(oc + 1) * 128, :], in_=xr[:, :]),
             reads=[xk], dma=True, tag="ox_st")

    return p.build()




def build_stage_l1attn():
    H, D = 16, 64
    ctx = Ctx("l1attn")
    p, b = ctx.p, ctx.b
    cb = b["cb"]
    ones_f, ones_k1 = b["ones_f"], b["ones_k1"]

    xin = p.dram("xin", [C, T], F32)
    xout = p.dram("xout", [C, T], F32, kind="ExternalOutput")
    wqd = p.dram("wq", [C, C], F32)
    wkd = p.dram("wk", [C, C], F32)
    wvd = p.dram("wv", [C, C], F32)
    wod = p.dram("wo", [C, C], F32)
    qb2 = p.dram("qb2", [64, H], F32)
    kb2 = p.dram("kb2", [64, H], F32)
    vb2 = p.dram("vb2", [128, H], F32)
    ob2 = p.dram("ob2", [128, CH], F32)
    masks = p.dram("masks", [6, 128, 512], BF16)

    xrot = [p.sb(f"xrot_{i}", [128, T], F32) for i in range(2)]
    xn = [p.sb(f"xn_{c}", [128, T], BF16) for c in range(CH)]
    wqt = p.sb("wqt", [128, CH * C], BF16)
    wkt = p.sb("wkt", [128, CH * C], BF16)
    wvt = p.sb("wvt", [128, CH * C], BF16)
    wot = wqt
    v65 = [p.sb(f"v65_{t}", [128, H * 65], BF16) for t in range(NT)]
    qaug = p.sb("qaug", [65, T], BF16)
    kaug = p.sb("kaug", [65, T], BF16)
    pt = p.sb("pt", [128, T], BF16)
    attnres = [p.sb(f"attnres_{i}", [128, T], BF16) for i in range(H // 2)]
    qsq = p.sb("qsq", [64, 512], BF16)
    brow = p.sb("brow", [1, T], F32)
    browb = p.sb("browb", [1, T], BF16)
    nrow = p.sb("nrow", [128, 512], F32)
    Bsb = p.sb("Bsb", [128, 512], F32)
    bias_q = p.sb("bias_q", [64, H], F32)
    bias_k = p.sb("bias_k", [64, H], F32)
    bias_v = p.sb("bias_v", [128, H], F32)
    bias_o = p.sb("bias_o", [128, CH], F32)
    mask_t = p.sb("mask_t", [128, 6 * 512], BF16)
    scr = p.sb("scr", [128, 512], F32)
    scr2 = p.sb("scr2", [128, 512], F32)
    tinyr = p.sb("tinyr", [1, 8], F32)

    ps_s = [p.ps(f"ps_s{i}", [128, 512], F32) for i in range(4)]
    ps_a = [p.ps(f"ps_a{i}", [128, 512], F32) for i in range(4)]
    ones64b = p.sb("ones64b", [64, 1], BF16)
    p.op("vector", lambda e: e.memset(ones64b[:, :], 1.0), writes=["ones64b"])
    onesr = p.sb("onesr", [128, 128], F32)
    p.op("vector", lambda e: e.memset(onesr[:, :], 1.0), writes=["onesr"])

    p.op("sync", lambda e: e.dma_start(out=bias_q[:, :], in_=qb2[:, :]), writes=["bias_q"], dma=True)
    p.op("sync", lambda e: e.dma_start(out=bias_k[:, :], in_=kb2[:, :]), writes=["bias_k"], dma=True)
    p.op("sync", lambda e: e.dma_start(out=bias_v[:, :], in_=vb2[:, :]), writes=["bias_v"], dma=True)
    p.op("sync", lambda e: e.dma_start(out=bias_o[:, :], in_=ob2[:, :]), writes=["bias_o"], dma=True)
    for r in range(6):
        p.op("sync", lambda e, r=r: e.dma_start(out=mask_t[:, r * 512:(r + 1) * 512], in_=masks[r, :, :]),
             writes=["mask_t"], dma=True)
    for c in range(CH):
        p.op("gpsimd",
             lambda e, c=c: e.dma_start(out=wqt[:, c * C:(c + 1) * C], in_=wqd[c * 128:(c + 1) * 128, :]),
             writes=[k_("wqt", c)], dma=True, tag="wld")
        p.op("gpsimd",
             lambda e, c=c: e.dma_start(out=wkt[:, c * C:(c + 1) * C], in_=wkd[c * 128:(c + 1) * 128, :]),
             writes=[k_("wkt", c)], dma=True, tag="wld")
        p.op("gpsimd",
             lambda e, c=c: e.dma_start(out=wvt[:, c * C:(c + 1) * C], in_=wvd[c * 128:(c + 1) * 128, :]),
             writes=[k_("wvt", c)], dma=True, tag="wld")

    for c in range(CH):
        xr = xrot[c % 2]
        xk = f"xrot:{c % 2}"
        p.op("sync", lambda e, c=c, xr=xr: e.dma_start(out=xr[:, :], in_=xin[c * 128:(c + 1) * 128, :]),
             writes=[xk], dma=True, tag="lnx")
        for s in range(4):
            sl = slice(s * 512, (s + 1) * 512)
            p.op("tensor",
                 lambda e, s=s, sl=sl, c=c, xr=xr: e.matmul(
                     ps_s[s][0:1, :], ones_f[:, 0:1], xr[:, sl],
                     start=(c == 0), stop=(c == CH - 1)),
                 reads=[xk, "ones_f"], writes=[f"ps_s{s}"], tag="ln_mm_s")
            p.op("scalar", lambda e, sl=sl, xr=xr: e.activation(scr[:, :], xr[:, sl], AF.Square),
                 reads=[xk], writes=["scr"], tag="ln_sq")
            p.op("tensor",
                 lambda e, s=s, c=c: e.matmul(
                     ps_s[s][32:33, :], ones_f[:, 0:1], scr[:, :],
                     start=(c == 0), stop=(c == CH - 1)),
                 reads=["scr", "ones_f"], writes=[f"ps_s{s}"], tag="ln_mm_q")
    for s in range(4):
        sl = slice(s * 512, (s + 1) * 512)
        p.op("vector", lambda e, s=s, sl=sl: e.tensor_scalar_mul(brow[0:1, sl], ps_s[s][0:1, :], 1.0 / C),
             reads=[f"ps_s{s}"], writes=["brow"], tag="ln_mean")
        p.op("vector", lambda e, sl=sl: e.tensor_mul(scr2[0:1, :], brow[0:1, sl], brow[0:1, sl]),
             reads=["brow"], writes=["scr2"], tag="ln_m2")
        p.op("vector",
             lambda e, s=s: e.scalar_tensor_tensor(scr[0:1, :], ps_s[s][32:33, :], 1.0 / C,
                                                   scr2[0:1, :], ALU.mult, ALU.subtract),
             reads=[f"ps_s{s}", "scr2"], writes=["scr"], tag="ln_var")
        p.op("scalar", lambda e: e.activation(scr[0:1, :], scr[0:1, :], AF.Ln, bias=cb[0:1, 0:1]),
             reads=["scr", "cb"], writes=["scr"], tag="ln_ln")
        p.op("scalar", lambda e: e.activation(scr[0:1, :], scr[0:1, :], AF.Exp, scale=-0.5),
             reads=["scr"], writes=["scr"], tag="ln_exp")
        p.op("vector", lambda e, sl=sl: e.tensor_copy(browb[0:1, sl], scr[0:1, :]),
             reads=["scr"], writes=["browb"], tag="ln_rbf")
    for c in range(CH):
        xr = xrot[c % 2]
        xk = f"xrot:{c % 2}"
        p.op("sync", lambda e, c=c, xr=xr: e.dma_start(out=xr[:, :], in_=xin[c * 128:(c + 1) * 128, :]),
             writes=[xk], dma=True, tag="lnx2")
        for s in range(4):
            sl = slice(s * 512, (s + 1) * 512)
            p.op("tensor",
                 lambda e, sl=sl: e.matmul(ps_s[2][:, :], ones_k1[:, :], brow[0:1, sl],
                                           start=True, stop=True),
                 reads=["brow", "ones_k1"], writes=["ps_s2"], tag="ln_bM")
            p.op("tensor",
                 lambda e, sl=sl: e.matmul(ps_s[3][:, :], b["ones_k1b"][:, :], browb[0:1, sl],
                                           start=True, stop=True),
                 reads=["browb", "ones_k1b"], writes=["ps_s3"], tag="ln_bR")
            p.op("vector",
                 lambda e, sl=sl, xr=xr: e.tensor_sub(scr[:, :], xr[:, sl], ps_s[2][:, :]),
                 reads=[xk, "ps_s2"], writes=["scr"], tag="ln_sub")
            p.op("vector",
                 lambda e, c=c, sl=sl: e.tensor_mul(xn[c][:, sl], scr[:, :], ps_s[3][:, :]),
                 reads=["scr", "ps_s3"], writes=[k_("xn", c)], tag="ln_nrm")

    for t in range(NT):
        p.op("vector",
             lambda e, t=t: e.memset(
                 v65[t][:, :].rearrange("p (h u) -> p h u", u=65)[:, :, 64:65], 1.0),
             writes=[k_("v65", t)], tag="v_ones")
        for g in range(2):
            bank = ps_a[g]
            bkey = f"ps_a{g}"
            for c in range(CH):
                p.op("tensor",
                     lambda e, t=t, g=g, c=c, bank=bank: e.matmul(
                         bank[:, :], xn[c][:, t * 128:(t + 1) * 128],
                         wvt[:, c * C + g * 512:(c * C) + (g + 1) * 512],
                         start=(c == 0), stop=(c == CH - 1)),
                     reads=[k_("xn", c), k_("wvt", c)], writes=[bkey], tag="v_mm")
            p.op("vector",
                 lambda e, t=t, g=g, bank=bank: e.tensor_copy(
                     v65[t][:, :].rearrange("p (h u) -> p h u", u=65)[:, g * 8:(g + 1) * 8, 0:64],
                     bank[:, :].rearrange("p (h u) -> p h u", u=64)),
                 reads=[bkey], writes=[k_("v65", t)], tag="v_ev")

    for i in range(H // 2):
        p.op("vector", lambda e, i=i: e.memset(attnres[i][:, :], 0.0),
             writes=[k_("attnres", i)], tag="at_z")
    p.op("vector", lambda e: e.memset(kaug[64:65, :], 1.0), writes=["kaug"], tag="k_one")
    for h in (range(H) if _DBG_HEADS is None else _DBG_HEADS):
        hq = h * 64
        par = h % 2
        at = attnres[h // 2]
        atk = k_("attnres", h // 2)
        for s in range(4):
            sl = slice(s * 512, (s + 1) * 512)
            for c in range(CH):
                p.op("tensor",
                     lambda e, c=c, sl=sl, hq=hq: e.matmul(
                         ps_a[0][0:64, :], wqt[:, c * C + hq: c * C + hq + 64],
                         xn[c][:, sl], start=(c == 0), stop=(c == CH - 1)),
                     reads=[k_("wqt", c), k_("xn", c)], writes=["ps_a0"], tag="q_mm")
            p.op("vector",
                 lambda e, sl=sl, h=h: e.tensor_scalar(
                     qaug[0:64, sl], ps_a[0][0:64, :], bias_q[:, h:h + 1], 0.125,
                     ALU.add, ALU.mult),
                 reads=["ps_a0", "bias_q"], writes=["qaug"], tag="q_ev")
            for c in range(CH):
                p.op("tensor",
                     lambda e, c=c, sl=sl, hq=hq: e.matmul(
                         ps_a[1][0:64, :], wkt[:, c * C + hq: c * C + hq + 64],
                         xn[c][:, sl], start=(c == 0), stop=(c == CH - 1)),
                     reads=[k_("wkt", c), k_("xn", c)], writes=["ps_a1"], tag="k_mm")
            p.op("vector",
                 lambda e, sl=sl, h=h: e.tensor_scalar_add(
                     kaug[0:64, sl], ps_a[1][0:64, :], bias_k[:, h:h + 1]),
                 reads=["ps_a1", "bias_k"], writes=["kaug"], tag="k_ev")

        for (srck, tname) in (("kaug", kaug), ("qaug", qaug)):
            for s in range(4):
                sl = slice(s * 512, (s + 1) * 512)
                p.op("scalar",
                     lambda e, sl=sl, tname=tname: e.activation(qsq[0:64, 0:512], tname[0:64, sl], AF.Square),
                     reads=[srck], writes=["qsq"], tag="nsq")
                p.op("tensor",
                     lambda e: e.matmul(ps_a[2][0:1, :], ones64b[:, 0:1],
                                        qsq[0:64, 0:512], start=True, stop=True),
                     reads=["qsq", "ones64b"], writes=["ps_a2"], tag="nmm")
                p.op("vector",
                     lambda e, sl=sl: e.tensor_copy(brow[0:1, sl], ps_a[2][0:1, :]),
                     reads=["ps_a2"], writes=["brow"], tag="nev")
            if srck == "kaug":
                p.op("vector",
                     lambda e: e.tensor_reduce(tinyr[0:1, 0:1], brow[0:1, :], AX.X, ALU.max),
                     reads=["brow"], writes=["tinyr"], tag="nk2max")
        p.op("vector",
             lambda e: e.tensor_scalar_mul(brow[0:1, :], brow[0:1, :], tinyr[0:1, 0:1]),
             reads=["brow", "tinyr"], writes=["brow"], tag="bq_t")
        p.op("scalar",
             lambda e: e.activation(brow[0:1, :], brow[0:1, :], AF.Ln, bias=cb[0:1, 0:1]),
             reads=["brow", "cb"], writes=["brow"], tag="bq_ln")
        p.op("scalar",
             lambda e: e.activation(brow[0:1, :], brow[0:1, :], AF.Exp, scale=0.5),
             reads=["brow"], writes=["brow"], tag="bq_exp")
        p.op("vector",
             lambda e: e.tensor_scalar_mul(browb[0:1, :], brow[0:1, :], -1.0),
             reads=["brow"], writes=["browb"], tag="bq_neg")
        p.op("gpsimd",
             lambda e: e.dma_start(out=qaug[64:65, :], in_=browb[0:1, :]),
             reads=["browb"], writes=["qaug"], dma=True, tag="aug_b")

        RVALS = (-256, -128, 0, 128, 256, 384)
        for g in range(4):
            kts = [kt for kt in range(NT) if 4 * g - 2 <= kt <= 4 * g + 3]
            for idx, kt in enumerate(kts):
                r = kt * 128 - g * 512
                ri = RVALS.index(r)
                p.op("tensor",
                     lambda e, kt=kt, g=g: e.matmul(
                         ps_s[g][:, :], kaug[0:65, kt * 128:(kt + 1) * 128],
                         qaug[0:65, g * 512:(g + 1) * 512], start=True, stop=True),
                     reads=["kaug", "qaug"], writes=[f"ps_s{g}"], tag="sc_mm")
                p.op("scalar",
                     lambda e, g=g: e.activation(
                         pt[:, g * 512:(g + 1) * 512], ps_s[g][:, :], AF.Exp),
                     reads=[f"ps_s{g}"], writes=["pt"], tag="sc_exp")
                p.op("vector",
                     lambda e, g=g, ri=ri: e.tensor_mul(
                         pt[:, g * 512:(g + 1) * 512], pt[:, g * 512:(g + 1) * 512],
                         mask_t[:, ri * 512:(ri + 1) * 512]),
                     reads=["pt", "mask_t"], writes=["pt"], tag="sc_msk")
                if par == 0:
                    p.op("tensor",
                         lambda e, kt=kt, g=g, h=h, idx=idx, last=(idx == len(kts) - 1): e.matmul(
                             ps_a[g][0:65, :], v65[kt][:, h * 65:(h + 1) * 65],
                             pt[:, g * 512:(g + 1) * 512],
                             start=(idx == 0), stop=last),
                         reads=[k_("v65", kt), "pt"], writes=[f"ps_a{g}"], tag="av_mm")
                else:
                    p.op("tensor",
                         lambda e, kt=kt, g=g, h=h, idx=idx, last=(idx == len(kts) - 1): e.matmul(
                             ps_a[g][64:128, :], v65[kt][:, h * 65:h * 65 + 64],
                             pt[:, g * 512:(g + 1) * 512],
                             start=(idx == 0), stop=last),
                         reads=[k_("v65", kt), "pt"], writes=[f"ps_a{g}"], tag="av_mmo")
                    p.op("tensor",
                         lambda e, kt=kt, g=g, h=h, idx=idx, last=(idx == len(kts) - 1): e.matmul(
                             ps_a[g][0:1, :], v65[kt][:, h * 65 + 64:h * 65 + 65],
                             pt[:, g * 512:(g + 1) * 512],
                             start=(idx == 0), stop=last),
                         reads=[k_("v65", kt), "pt"], writes=[f"ps_a{g}"], tag="av_den")

        for g in range(4):
            gsl = slice(g * 512, (g + 1) * 512)
            if par == 0:
                p.op("vector",
                     lambda e, g=g: e.reciprocal(nrow[64:65, :], ps_a[g][64:65, :]),
                     reads=[f"ps_a{g}"], writes=["nrow"], tag="nm_rc")
                p.op("tensor",
                     lambda e: e.matmul(ps_s[0][0:64, :], onesr[64:65, 0:64], nrow[64:65, :],
                                        start=True, stop=True),
                     reads=["nrow", "onesr"], writes=["ps_s0"], tag="nm_bc")
                p.op("vector",
                     lambda e: e.tensor_copy(Bsb[0:64, :], ps_s[0][0:64, :]),
                     reads=["ps_s0"], writes=["Bsb"], tag="nm_ev")
                p.op("vector",
                     lambda e, g=g: e.tensor_mul(scr[0:64, :], ps_a[g][0:64, :], Bsb[0:64, :]),
                     reads=[f"ps_a{g}", "Bsb"], writes=["scr"], tag="nm_mul")
                p.op("vector",
                     lambda e, gsl=gsl, h=h, at=at: e.tensor_scalar_add(
                         at[0:64, gsl], scr[0:64, :], bias_v[0:64, h:h + 1]),
                     reads=["scr", "bias_v"], writes=[atk], tag="nm_out")
            else:
                p.op("vector",
                     lambda e, g=g: e.reciprocal(nrow[0:1, :], ps_a[g][0:1, :]),
                     reads=[f"ps_a{g}"], writes=["nrow"], tag="nm_rc")
                p.op("tensor",
                     lambda e: e.matmul(ps_s[0][64:128, :], onesr[0:1, 0:64], nrow[0:1, :],
                                        start=True, stop=True),
                     reads=["nrow", "onesr"], writes=["ps_s0"], tag="nm_bc")
                p.op("vector",
                     lambda e: e.tensor_copy(Bsb[64:128, :], ps_s[0][64:128, :]),
                     reads=["ps_s0"], writes=["Bsb"], tag="nm_ev")
                p.op("vector",
                     lambda e, g=g: e.tensor_mul(scr[64:128, :], ps_a[g][64:128, :], Bsb[64:128, :]),
                     reads=[f"ps_a{g}", "Bsb"], writes=["scr"], tag="nm_mul")
                p.op("vector",
                     lambda e, gsl=gsl, h=h, at=at: e.tensor_scalar_add(
                         at[64:128, gsl], scr[64:128, :], bias_v[64:128, h:h + 1]),
                     reads=["scr", "bias_v"], writes=[atk], tag="nm_out")

    for c in range(CH):
        p.op("gpsimd",
             lambda e, c=c: e.dma_start(out=wot[:, c * C:(c + 1) * C],
                                        in_=wod[c * 128:(c + 1) * 128, :]),
             writes=[k_("wqt", c), k_("wo", c)], dma=True, tag="wout")
    for oc in range(CH):
        xr = xrot[oc % 2]
        xk = f"xrot:{oc % 2}"
        p.op("sync", lambda e, oc=oc, xr=xr: e.dma_start(out=xr[:, :], in_=xin[oc * 128:(oc + 1) * 128, :]),
             writes=[xk], dma=True, tag="ox_ld")
        p.op("vector", lambda e, xr=xr: e.tensor_copy(scr[:, 1:2], xr[:, 0:1]),
             reads=[xk], writes=["scr"], tag="gz_sv")
        for s in range(4):
            sl = slice(s * 512, (s + 1) * 512)
            bank = ps_a[s % 4]
            bkey = f"ps_a{s % 4}"
            for i in range(CH):
                p.op("tensor",
                     lambda e, oc=oc, i=i, sl=sl, bank=bank: e.matmul(
                         bank[:, :], wot[:, i * C + oc * 128: i * C + (oc + 1) * 128],
                         attnres[i][:, sl], start=(i == 0), stop=(i == CH - 1)),
                     reads=[k_("wo", i), k_("attnres", i)], writes=[bkey], tag="o_mm")
            p.op("vector",
                 lambda e, oc=oc, sl=sl, bank=bank, xr=xr, xk=xk: e.scalar_tensor_tensor(
                     xr[:, sl], bank[:, :], bias_o[:, oc:oc + 1], xr[:, sl],
                     ALU.add, ALU.add),
                 reads=[bkey, "bias_o", xk], writes=[xk], tag="o_add")
        p.op("sync",
             lambda e, oc=oc, xr=xr, xk=xk: e.dma_start(out=xout[oc * 128:(oc + 1) * 128, :], in_=xr[:, :]),
             reads=[xk], dma=True, tag="ox_st")

    return p.build()




def build_stage_l0attn():
    H, D = 16, 64
    ctx = Ctx("l0attn")
    p, b = ctx.p, ctx.b
    cb = b["cb"]
    ones_f, ones_k1 = b["ones_f"], b["ones_k1"]

    xin = p.dram("xin", [C, T], F32)
    xout = p.dram("xout", [C, T], F32, kind="ExternalOutput")
    wqd = p.dram("Wq", [C, C], F32)
    wkd = p.dram("Wk", [C, C], F32)
    wvd = p.dram("Wv", [C, C], F32)
    wod = p.dram("Wo", [C, C], F32)
    ob2 = p.dram("ob2", [128, CH], F32)
    identd = p.dram("ident", [128, 128], BF16)

    xrot = p.sb("xrot", [128, T], F32)
    xn = [p.sb(f"xn_{c}", [128, T], BF16) for c in range(CH)]
    attnres = [p.sb(f"attnres_{i}", [128, T], BF16) for i in range(CH)]
    wq = p.sb("wq", [128, CH * C], BF16)
    wk = p.sb("wk", [128, CH * C], BF16)
    wv = p.sb("wv", [128, CH * C], BF16)
    f1 = p.sb("f1", [128, C], F32)
    f2 = p.sb("f2", [128, C], F32)
    kk = p.sb("kk", [128, C], BF16)
    vv = p.sb("vv", [128, C], BF16)
    vem = p.sb("vem", [128, C], BF16)
    wt = p.sb("wt", [128, C], BF16)
    Dten = p.sb("Dten", [128, C], F32)
    rD = p.sb("rD", [128, C], F32)
    call = p.sb("call", [128, H * H], F32)
    cprod = p.sb("cprod", [128, C], BF16)
    yprod = p.sb("yprod", [128, C], BF16)
    atttok = p.sb("atttok", [128, C], BF16)
    segr = p.sb("segr", [128, 3 * H], F32)
    scr = p.sb("scr", [128, 512], F32)
    scr2 = p.sb("scr2", [128, 512], F32)
    brow = p.sb("brow", [1, T], F32)
    browb = p.sb("browb", [1, T], BF16)
    identb = p.sb("identb", [128, 128], BF16)
    bias_o = p.sb("bias_o", [128, CH], F32)

    psA = p.ps("psA", [128, 512], F32)
    psB = p.ps("psB", [128, 512], F32)
    psD = p.ps("psD", [128, 1024], F32)
    psT = [p.ps("psT0", [128, 128], BF16), p.ps("psT1", [128, 128], BF16)]

    p.op("sync", lambda e: e.dma_start(out=bias_o[:, :], in_=ob2[:, :]), writes=["bias_o"], dma=True)
    p.op("sync", lambda e: e.dma_start(out=identb[:, :], in_=identd[:, :]), writes=["identb"], dma=True)
    for c in range(CH):
        p.op("gpsimd",
             lambda e, c=c: e.dma_start(out=wq[:, c * C:(c + 1) * C], in_=wqd[c * 128:(c + 1) * 128, :]),
             writes=[k_("wq", c)], dma=True, tag="wqld")
        p.op("gpsimd",
             lambda e, c=c: e.dma_start(out=wk[:, c * C:(c + 1) * C], in_=wkd[c * 128:(c + 1) * 128, :]),
             writes=[k_("wk", c)], dma=True, tag="wkld")
        p.op("gpsimd",
             lambda e, c=c: e.dma_start(out=wv[:, c * C:(c + 1) * C], in_=wvd[c * 128:(c + 1) * 128, :]),
             writes=[k_("wv", c)], dma=True, tag="wvld")

    _slots = [(psA, "psA", 0), (psA, "psA", 32), (psA, "psA", 64),
              (psB, "psB", 0), (psB, "psB", 32), (psB, "psB", 64),
              (psD, "psD", 0), (psD, "psD", 32)]

    def _slot(s, kind):
        return _slots[s * 2 + kind]

    for c in range(CH):
        xk = "xrot"
        p.op("sync", lambda e, c=c: e.dma_start(out=xrot[:, :], in_=xin[c * 128:(c + 1) * 128, :]),
             writes=[xk], dma=True, tag="lnx")
        for s in range(4):
            sl = slice(s * 512, (s + 1) * 512)
            bank, bkey, row = _slot(s, 0)
            p.op("tensor",
                 lambda e, sl=sl, c=c, bank=bank, row=row: e.matmul(
                     bank[row:row + 1, 0:512], ones_f[:, 0:1], xrot[:, sl],
                     start=(c == 0), stop=(c == CH - 1)),
                 reads=[xk, "ones_f"], writes=[bkey], tag="ln_mm_s")
            p.op("scalar", lambda e, sl=sl: e.activation(scr[:, :], xrot[:, sl], AF.Square),
                 reads=[xk], writes=["scr"], tag="ln_sq")
            bank, bkey, row = _slot(s, 1)
            p.op("tensor",
                 lambda e, c=c, bank=bank, row=row: e.matmul(
                     bank[row:row + 1, 0:512], ones_f[:, 0:1], scr[:, :],
                     start=(c == 0), stop=(c == CH - 1)),
                 reads=["scr", "ones_f"], writes=[bkey], tag="ln_mm_q")
    for s in range(4):
        sl = slice(s * 512, (s + 1) * 512)
        bank, bkey, row = _slot(s, 0)
        p.op("vector", lambda e, sl=sl, bank=bank, row=row: e.tensor_scalar_mul(
                 brow[0:1, sl], bank[row:row + 1, 0:512], 1.0 / C),
             reads=[bkey], writes=["brow"], tag="ln_mean")
        p.op("vector", lambda e, sl=sl: e.tensor_mul(scr2[0:1, :], brow[0:1, sl], brow[0:1, sl]),
             reads=["brow"], writes=["scr2"], tag="ln_m2")
        bank, bkey, row = _slot(s, 1)
        p.op("vector",
             lambda e, bank=bank, row=row: e.scalar_tensor_tensor(
                 scr[0:1, :], bank[row:row + 1, 0:512], 1.0 / C,
                 scr2[0:1, :], ALU.mult, ALU.subtract),
             reads=[bkey, "scr2"], writes=["scr"], tag="ln_var")
        p.op("scalar", lambda e: e.activation(scr[0:1, :], scr[0:1, :], AF.Ln, bias=cb[0:1, 0:1]),
             reads=["scr", "cb"], writes=["scr"], tag="ln_ln")
        p.op("scalar", lambda e: e.activation(scr[0:1, :], scr[0:1, :], AF.Exp, scale=-0.5),
             reads=["scr"], writes=["scr"], tag="ln_exp")
        p.op("vector", lambda e, sl=sl: e.tensor_copy(browb[0:1, sl], scr[0:1, :]),
             reads=["scr"], writes=["browb"], tag="ln_rbf")
    for c in range(CH):
        xk = "xrot"
        p.op("sync", lambda e, c=c: e.dma_start(out=xrot[:, :], in_=xin[c * 128:(c + 1) * 128, :]),
             writes=[xk], dma=True, tag="lnx2")
        for s in range(4):
            sl = slice(s * 512, (s + 1) * 512)
            p.op("tensor",
                 lambda e, sl=sl: e.matmul(psA[:, :], ones_k1[:, :], brow[0:1, sl],
                                           start=True, stop=True),
                 reads=["brow", "ones_k1"], writes=["psA"], tag="ln_bM")
            p.op("tensor",
                 lambda e, sl=sl: e.matmul(psB[:, :], b["ones_k1b"][:, :], browb[0:1, sl],
                                           start=True, stop=True),
                 reads=["browb", "ones_k1b"], writes=["psB"], tag="ln_bR")
            p.op("vector",
                 lambda e, sl=sl: e.tensor_sub(scr[:, :], xrot[:, sl], psA[:, :]),
                 reads=[xk, "psA"], writes=["scr"], tag="ln_sub")
            p.op("vector",
                 lambda e, c=c, sl=sl: e.tensor_mul(xn[c][:, sl], scr[:, :], psB[:, :]),
                 reads=["scr", "psB"], writes=[k_("xn", c)], tag="ln_nrm")

    segv = segr[:, :].rearrange("p (a h) -> p a h", h=H)
    kkv = kk[:, :].rearrange("p (h d) -> p h d", d=D)
    vemv = vem[:, :].rearrange("p (e h) -> p e h", h=H)

    for t in range(NT):
        tsl = slice(t * 128, (t + 1) * 128)
        for g in range(2):
            bank = psA if g == 0 else psB
            bkey = "psA" if g == 0 else "psB"
            gsl = slice(g * 512, (g + 1) * 512)
            for c in range(CH):
                p.op("tensor",
                     lambda e, g=g, c=c, bank=bank, tsl=tsl: e.matmul(
                         bank[:, :], xn[c][:, tsl],
                         wq[:, c * C + g * 512: c * C + (g + 1) * 512],
                         start=(c == 0), stop=(c == CH - 1)),
                     reads=[k_("xn", c), k_("wq", c)], writes=[bkey], tag="q_mm")
            p.op("scalar",
                 lambda e, gsl=gsl, bank=bank: e.activation(f1[:, gsl], bank[:, :], AF.Copy),
                 reads=[bkey], writes=["f1"], tag="q_ev")
            for c in range(CH):
                p.op("tensor",
                     lambda e, g=g, c=c, bank=bank, tsl=tsl: e.matmul(
                         bank[:, :], xn[c][:, tsl],
                         wk[:, c * C + g * 512: c * C + (g + 1) * 512],
                         start=(c == 0), stop=(c == CH - 1)),
                     reads=[k_("xn", c), k_("wk", c)], writes=[bkey], tag="k_mm")
            p.op("vector",
                 lambda e, bank=bank: e.tensor_scalar_min(scr[:, :], bank[:, :], 0.0),
                 reads=[bkey], writes=["scr"], tag="k_min")
            p.op("scalar",
                 lambda e: e.activation(scr2[:, :], scr[:, :], AF.Exp),
                 reads=["scr"], writes=["scr2"], tag="k_exp")
            p.op("vector",
                 lambda e, gsl=gsl, bank=bank: e.scalar_tensor_tensor(
                     kk[:, gsl], bank[:, :], 0.0, scr2[:, :], ALU.max, ALU.add),
                 reads=[bkey, "scr2"], writes=["kk"], tag="k_elu")
            for c in range(CH):
                p.op("tensor",
                     lambda e, g=g, c=c, bank=bank, tsl=tsl: e.matmul(
                         bank[:, :], xn[c][:, tsl],
                         wv[:, c * C + g * 512: c * C + (g + 1) * 512],
                         start=(c == 0), stop=(c == CH - 1)),
                     reads=[k_("xn", c), k_("wv", c)], writes=[bkey], tag="v_mm")
            p.op("scalar",
                 lambda e, gsl=gsl, bank=bank: e.activation(vv[:, gsl], bank[:, :], AF.Copy),
                 reads=[bkey], writes=["vv"], tag="v_ev")
        p.op("vector",
             lambda e: e.tensor_copy(vemv, vv[:, :].rearrange("p (h e) -> p e h", e=D)),
             reads=["vv"], writes=["vem"], tag="vem")
        p.op("vector", lambda e: e.tensor_copy(Dten[:, 0:D], kk[:, 0:D]),
             reads=["kk"], writes=["Dten"], tag="dc0")
        for h in range(1, H):
            p.op("vector",
                 lambda e, h=h: e.tensor_add(Dten[:, h * D:(h + 1) * D],
                                             Dten[:, (h - 1) * D:h * D], kk[:, h * D:(h + 1) * D]),
                 reads=["Dten", "kk"], writes=["Dten"], tag="dcs")
        p.op("scalar", lambda e: e.activation(rD[:, :], Dten[:, :], AF.Ln, bias=cb[:, 2:3]),
             reads=["Dten", "cb"], writes=["rD"], tag="rd_ln")
        p.op("scalar", lambda e: e.activation(rD[:, :], rD[:, :], AF.Exp, scale=-1.0),
             reads=["rD"], writes=["rD"], tag="rd_exp")
        p.op("vector",
             lambda e: e.tensor_reduce(segv[:, 0, :], f1[:, :].rearrange("p (h u) -> p h u", u=D),
                                       AX.X, ALU.max),
             reads=["f1"], writes=["segr"], tag="q_smax")
        p.op("vector",
             lambda e: e.tensor_tensor(
                 f2[:, :].rearrange("p (h u) -> p h u", u=D),
                 f1[:, :].rearrange("p (h u) -> p h u", u=D),
                 segv[:, 0, :].unsqueeze(2).broadcast_to([128, H, D]),
                 ALU.subtract),
             reads=["f1", "segr"], writes=["f2"], tag="q_sub")
        p.op("scalar", lambda e: e.activation(f2[:, :], f2[:, :], AF.Exp),
             reads=["f2"], writes=["f2"], tag="q_exp")
        p.op("vector",
             lambda e: e.tensor_reduce(segv[:, 1, :], f2[:, :].rearrange("p (h u) -> p h u", u=D),
                                       AX.X, ALU.add),
             reads=["f2"], writes=["segr"], tag="q_ssum")
        p.op("vector",
             lambda e: e.reciprocal(segv[:, 2, :], segv[:, 1, :]),
             reads=["segr"], writes=["segr"], tag="q_rseg")
        p.op("vector",
             lambda e: e.tensor_mul(rD[:, :], f2[:, :], rD[:, :]),
             reads=["f2", "rD"], writes=["rD"], tag="w_m1")
        p.op("vector",
             lambda e: e.tensor_tensor(
                 wt[:, :].rearrange("p (h u) -> p h u", u=D),
                 rD[:, :].rearrange("p (h u) -> p h u", u=D),
                 segv[:, 2, :].unsqueeze(2).broadcast_to([128, H, D]),
                 ALU.mult),
             reads=["rD", "segr"], writes=["wt"], tag="w_m2")
        for h in range(H):
            n = h + 1
            p.op("vector",
                 lambda e, h=h, n=n: e.tensor_tensor(
                     cprod[:, 0:n * D].rearrange("p (a d) -> p a d", d=D),
                     kk[:, 0:n * D].rearrange("p (a d) -> p a d", d=D),
                     wt[:, h * D:(h + 1) * D].unsqueeze(1).broadcast_to([128, n, D]),
                     ALU.mult),
                 reads=["kk", "wt"], writes=["cprod"], tag="c_mul")
            p.op("vector",
                 lambda e, h=h, n=n: e.tensor_reduce(
                     call[:, h * H:h * H + n],
                     cprod[:, 0:n * D].rearrange("p (a d) -> p a d", d=D),
                     AX.X, ALU.add),
                 reads=["cprod"], writes=["call"], tag="c_red")
        for h in range(H):
            n = h + 1
            p.op("vector",
                 lambda e, h=h, n=n: e.tensor_tensor(
                     yprod[:, 0:D * n].rearrange("p (z a) -> p z a", a=n),
                     vemv[:, :, 0:n],
                     call[:, h * H:h * H + n].unsqueeze(1).broadcast_to([128, D, n]),
                     ALU.mult),
                 reads=["vem", "call"], writes=["yprod"], tag="y_mul")
            p.op("vector",
                 lambda e, h=h, n=n: e.tensor_reduce(
                     rD[:, h * D:(h + 1) * D],
                     yprod[:, 0:D * n].rearrange("p (z a) -> p z a", a=n),
                     AX.X, ALU.add),
                 reads=["yprod"], writes=["rD"], tag="y_red")
        p.op("vector", lambda e: e.tensor_copy(atttok[:, :], rD[:, :]),
             reads=["rD"], writes=["atttok"], tag="y_cast")
        for c in range(CH):
            half = c % 2
            p.op("tensor",
                 lambda e, c=c, half=half: e.transpose(
                     psT[half][:, 0:128], atttok[:, c * 128:(c + 1) * 128], identb[:, :]),
                 reads=["atttok", "identb"], writes=[f"psT{half}"], tag="tr_a")
            p.op("scalar",
                 lambda e, c=c, half=half, tsl=tsl: e.activation(
                     attnres[c][:, tsl], psT[half][:, 0:128], AF.Copy),
                 reads=[f"psT{half}"], writes=[k_("attnres", c)], tag="tr_ae")

    for c in range(CH):
        p.op("gpsimd",
             lambda e, c=c: e.dma_start(out=wq[:, c * C:(c + 1) * C],
                                        in_=wod[c * 128:(c + 1) * 128, :]),
             writes=[k_("wq", c), k_("wo", c)], dma=True, tag="wout")
    for oc in range(CH):
        xk = "xrot"
        p.op("sync", lambda e, oc=oc: e.dma_start(out=xrot[:, :], in_=xin[oc * 128:(oc + 1) * 128, :]),
             writes=[xk], dma=True, tag="ox_ld")
        for s in range(4):
            sl = slice(s * 512, (s + 1) * 512)
            bank = psA if s % 2 == 0 else psB
            bkey = "psA" if s % 2 == 0 else "psB"
            for i in range(CH):
                p.op("tensor",
                     lambda e, oc=oc, i=i, sl=sl, bank=bank: e.matmul(
                         bank[:, :], wq[:, i * C + oc * 128: i * C + (oc + 1) * 128],
                         attnres[i][:, sl], start=(i == 0), stop=(i == CH - 1)),
                     reads=[k_("wo", i), k_("attnres", i)], writes=[bkey], tag="o_mm")
            p.op("vector",
                 lambda e, oc=oc, sl=sl, bank=bank, bkey=bkey, xk=xk: e.scalar_tensor_tensor(
                     xrot[:, sl], bank[:, :], bias_o[:, oc:oc + 1], xrot[:, sl],
                     ALU.add, ALU.add),
                 reads=[bkey, "bias_o", xk], writes=[xk], tag="o_add")
        p.op("sync",
             lambda e, oc=oc, xk=xk: e.dma_start(out=xout[oc * 128:(oc + 1) * 128, :], in_=xrot[:, :]),
             reads=[xk], dma=True, tag="ox_st")

    return p.build()




QMAX16 = 32767.0


def _emit_gscale(p, b, src0, dst, tag):
    mrow = b["mrow"]
    sc, rc = 28 + 2 * dst, 29 + 2 * dst
    p.op("vector",
         lambda e: e.tensor_reduce(mrow[0:1, 24 + dst:25 + dst],
                                   mrow[0:1, src0:src0 + NCORES].rearrange("p (a n) -> p a n", a=1),
                                   AX.X, ALU.max),
         reads=["mrow"], writes=["mrow"], tag=tag + "_gm")
    p.op("vector",
         lambda e: e.tensor_scalar(mrow[0:1, sc:sc + 1], mrow[0:1, 24 + dst:25 + dst],
                                   1.0 / QMAX16, 1e-12, ALU.mult, ALU.max),
         reads=["mrow"], writes=["mrow"], tag=tag + "_s")
    p.op("vector",
         lambda e: e.reciprocal(mrow[0:1, rc:rc + 1], mrow[0:1, sc:sc + 1]),
         reads=["mrow"], writes=["mrow"], tag=tag + "_rs")
    ps = b["psS"]
    p.op("tensor",
         lambda e: e.matmul(ps[:, 0:2], b["ones_k1"][:, :], mrow[0:1, sc:sc + 2],
                            start=True, stop=True),
         reads=["mrow", "ones_k1"], writes=["psS"], tag=tag + "_bc")
    p.op("vector",
         lambda e: e.tensor_copy(b["scol2"][:, 2 * dst:2 * dst + 2], ps[:, 0:2]),
         reads=["psS"], writes=["scol2"], tag=tag + "_ev")


def _emit_qfix(p, src, srckey, n, sc, qi, dq, g1q, qf, out, outkey, tag,
               rs_ap, s_ap, rskey=("scol2",), deq_f32=None, clip_hi=None):
    rk = list(rskey)
    if sc == "t":
        p.op("vector", lambda e: e.tensor_mul(qf[:, 0:n], src[:, 0:n], rs_ap[:, 0:n]),
             reads=[srckey] + rk, writes=["qf"], tag=tag + "_xs")
        p.op("vector", lambda e: e.tensor_copy(qi[:, 0:n], qf[:, 0:n]),
             reads=["qf"], writes=["qi"], tag=tag + "_qi")
        p.op("vector", lambda e: e.tensor_sub(dq[:, 0:n], qf[:, 0:n], qi[:, 0:n]),
             reads=["qf", "qi"], writes=["dq"], tag=tag + "_d")
    else:
        p.op("vector", lambda e: e.tensor_scalar_mul(qi[:, 0:n], src[:, 0:n], rs_ap),
             reads=[srckey] + rk, writes=["qi"], tag=tag + "_qi")
        p.op("vector",
             lambda e: e.scalar_tensor_tensor(dq[:, 0:n], src[:, 0:n], rs_ap,
                                              qi[:, 0:n], ALU.mult, ALU.subtract),
             reads=[srckey, "qi"] + rk, writes=["dq"], tag=tag + "_d")
    p.op("vector",
         lambda e: e.tensor_single_scalar(g1q[:, 0:n], dq[:, 0:n], 0.5, ALU.is_ge),
         reads=["dq"], writes=["g1q"], tag=tag + "_g1")
    p.op("vector",
         lambda e: e.tensor_scalar(dq[:, 0:n], dq[:, 0:n], -0.5, -1.0,
                                   ALU.is_le, ALU.mult),
         reads=["dq"], writes=["dq"], tag=tag + "_g2")
    p.op("vector",
         lambda e: e.tensor_add(g1q[:, 0:n], g1q[:, 0:n], dq[:, 0:n]),
         reads=["g1q", "dq"], writes=["g1q"], tag=tag + "_gg")
    p.op("vector",
         lambda e: e.tensor_add(qf[:, 0:n], qi[:, 0:n], g1q[:, 0:n]),
         reads=["qi", "g1q"], writes=["qf"], tag=tag + "_q")
    if clip_hi is not None:
        p.op("vector",
             lambda e: e.tensor_single_scalar(qf[:, 0:n], qf[:, 0:n], clip_hi, ALU.min),
             reads=["qf"], writes=["qf"], tag=tag + "_cl")
    if sc == "t":
        p.op("vector", lambda e: e.tensor_mul(out[:, 0:n], qf[:, 0:n], s_ap[:, 0:n]),
             reads=["qf"] + rk, writes=[outkey], tag=tag + "_dqo")
    else:
        p.op("vector", lambda e: e.tensor_scalar_mul(out[:, 0:n], qf[:, 0:n], s_ap),
             reads=["qf"] + rk, writes=[outkey], tag=tag + "_dqo")
        if deq_f32 is not None:
            t2, k2 = deq_f32
            p.op("vector", lambda e: e.tensor_scalar_mul(t2[:, 0:n], qf[:, 0:n], s_ap),
                 reads=["qf"] + rk, writes=[k2], tag=tag + "_dqf")


def build_stage_l2ffnA():
    ctx = Ctx("l2ffnA")
    p, b = ctx.p, ctx.b
    cb = b["cb"]
    ones_f, ones_k1 = b["ones_f"], b["ones_k1"]

    xin = p.dram("xin", [C, T], F32)
    xnout = p.dram("xnout", [C, T], F32, kind="ExternalOutput")
    m0d = p.dram("m0", [1, 1], F32, kind="ExternalOutput")
    identfd = p.dram("identf", [128, 128], F32)

    xrot = [p.sb(f"xrot_{i}", [128, T], F32) for i in range(2)]
    xo = p.sb("xo", [128, T], F32)
    scr = p.sb("scr", [128, 512], F32)
    scr2 = p.sb("scr2", [128, 512], F32)
    brow = p.sb("brow", [1, T], F32)
    brow2 = p.sb("brow2", [1, T], F32)
    mcol = p.sb("mcol", [128, 2], F32)
    identf = p.sb("identf_s", [128, 128], F32)
    psA = p.ps("psA", [128, 512], F32)
    psB = p.ps("psB", [128, 512], F32)
    psD = p.ps("psD", [128, 1024], F32)

    p.op("sync", lambda e: e.dma_start(out=identf[:, :], in_=identfd[:, :]), writes=["identf"], dma=True)
    p.op("vector", lambda e: e.memset(mcol[:, :], 0.0), writes=["mcol"])

    _slots = [(psA, "psA", 0), (psA, "psA", 32), (psA, "psA", 64),
              (psB, "psB", 0), (psB, "psB", 32), (psB, "psB", 64),
              (psD, "psD", 0), (psD, "psD", 32)]
    for c in range(CH):
        xr = xrot[c % 2]
        xk = f"xrot:{c % 2}"
        p.op("sync", lambda e, c=c, xr=xr: e.dma_start(out=xr[:, :], in_=xin[c * 128:(c + 1) * 128, :]),
             writes=[xk], dma=True, tag="lnx")
        for s in range(4):
            sl = slice(s * 512, (s + 1) * 512)
            bank, bkey, row = _slots[s * 2]
            p.op("tensor",
                 lambda e, sl=sl, c=c, bank=bank, row=row, xr=xr: e.matmul(
                     bank[row:row + 1, 0:512], ones_f[:, 0:1], xr[:, sl],
                     start=(c == 0), stop=(c == CH - 1)),
                 reads=[xk, "ones_f"], writes=[bkey], tag="ln_mm_s")
            p.op("scalar", lambda e, sl=sl, xr=xr: e.activation(scr[:, :], xr[:, sl], AF.Square),
                 reads=[xk], writes=["scr"], tag="ln_sq")
            bank, bkey, row = _slots[s * 2 + 1]
            p.op("tensor",
                 lambda e, c=c, bank=bank, row=row: e.matmul(
                     bank[row:row + 1, 0:512], ones_f[:, 0:1], scr[:, :],
                     start=(c == 0), stop=(c == CH - 1)),
                 reads=["scr", "ones_f"], writes=[bkey], tag="ln_mm_q")
    for s in range(4):
        sl = slice(s * 512, (s + 1) * 512)
        bank, bkey, row = _slots[s * 2]
        p.op("vector", lambda e, sl=sl, bank=bank, row=row: e.tensor_scalar_mul(
                 brow[0:1, sl], bank[row:row + 1, 0:512], 1.0 / C),
             reads=[bkey], writes=["brow"], tag="ln_mean")
        p.op("vector", lambda e, sl=sl: e.tensor_mul(scr2[0:1, :], brow[0:1, sl], brow[0:1, sl]),
             reads=["brow"], writes=["scr2"], tag="ln_m2")
        bank, bkey, row = _slots[s * 2 + 1]
        p.op("vector",
             lambda e, bank=bank, row=row: e.scalar_tensor_tensor(
                 scr[0:1, :], bank[row:row + 1, 0:512], 1.0 / C,
                 scr2[0:1, :], ALU.mult, ALU.subtract),
             reads=[bkey, "scr2"], writes=["scr"], tag="ln_var")
        p.op("scalar", lambda e: e.activation(scr[0:1, :], scr[0:1, :], AF.Ln, bias=cb[0:1, 0:1]),
             reads=["scr", "cb"], writes=["scr"], tag="ln_ln")
        p.op("scalar", lambda e: e.activation(scr[0:1, :], scr[0:1, :], AF.Exp, scale=-0.5),
             reads=["scr"], writes=["scr"], tag="ln_exp")
        p.op("vector", lambda e, sl=sl: e.tensor_copy(brow2[0:1, sl], scr[0:1, :]),
             reads=["scr"], writes=["brow2"], tag="ln_rst")
    for c in range(CH):
        xr = xrot[c % 2]
        xk = f"xrot:{c % 2}"
        p.op("sync", lambda e, c=c, xr=xr: e.dma_start(out=xr[:, :], in_=xin[c * 128:(c + 1) * 128, :]),
             writes=[xk], dma=True, tag="lnx2")
        for s in range(4):
            sl = slice(s * 512, (s + 1) * 512)
            p.op("tensor",
                 lambda e, sl=sl: e.matmul(psA[:, :], ones_k1[:, :], brow[0:1, sl],
                                           start=True, stop=True),
                 reads=["brow", "ones_k1"], writes=["psA"], tag="ln_bM")
            p.op("tensor",
                 lambda e, sl=sl: e.matmul(psB[:, :], ones_k1[:, :], brow2[0:1, sl],
                                           start=True, stop=True),
                 reads=["brow2", "ones_k1"], writes=["psB"], tag="ln_bR")
            p.op("vector",
                 lambda e, sl=sl, xr=xr: e.tensor_sub(scr[:, :], xr[:, sl], psA[:, :]),
                 reads=[xk, "psA"], writes=["scr"], tag="ln_sub")
            p.op("vector",
                 lambda e, sl=sl: e.tensor_mul(xo[:, sl], scr[:, :], psB[:, :]),
                 reads=["scr", "psB"], writes=["xo"], tag="ln_nrm")
            p.op("vector",
                 lambda e, sl=sl: e.tensor_reduce(
                     scr2[:, 0:1], xo[:, sl].rearrange("p (a n) -> p a n", a=1),
                     AX.X, ALU.max, apply_absolute_value=True),
                 reads=["xo"], writes=["scr2"], tag="amx_r")
            p.op("vector",
                 lambda e: e.tensor_max(mcol[:, 0:1], mcol[:, 0:1], scr2[:, 0:1]),
                 reads=["scr2", "mcol"], writes=["mcol"], tag="amx_a")
        p.op("sync",
             lambda e, c=c: e.dma_start(out=xnout[c * 128:(c + 1) * 128, :], in_=xo[:, :]),
             reads=["xo"], dma=True, tag="stxn")
    p.op("tensor",
         lambda e: e.transpose(psA[0:1, 0:128], mcol[:, 0:1], identf[:, :]),
         reads=["mcol", "identf"], writes=["psA"], tag="amx_t")
    p.op("vector",
         lambda e: e.tensor_reduce(mcol[0:1, 1:2], psA[0:1, 0:128].rearrange("p (a n) -> p a n", a=1),
                                   AX.X, ALU.max),
         reads=["psA"], writes=["mcol"], tag="amx_f")
    p.op("sync", lambda e: e.dma_start(out=m0d[:, :], in_=mcol[0:1, 1:2]),
         reads=["mcol"], dma=True, tag="stm")
    return p.build()


def _emit_wdeq(p, b, wN, wT, selsb, selkey, nblk, nchunk, wdq, wdqkeys, tag):
    O = wN.shape[0]
    I = wN.shape[1]
    noch = O // 128
    nblocks = I // 32
    wnat, sT, rT = b["wnat"], b["sT"], b["rT"]
    rsx, ssx = b["rsx"], b["ssx"]
    qi, dq, g1q, qf, xs = b["qi"], b["dq"], b["g1q"], b["qf"], b["xs"]
    wrot = b["wrot"]
    psE, psF = b["psE"], b["psF"]
    for oc in range(noch):
        p.op("sync",
             lambda e, oc=oc: e.dma_start(out=wnat[:, 0:I], in_=wN[oc * 128:(oc + 1) * 128, :]),
             writes=["wnat"], dma=True, tag=tag + "_ldn")
        p.op("vector",
             lambda e: e.tensor_reduce(qf[:, 0:nblocks],
                                       wnat[:, 0:I].rearrange("p (a n) -> p a n", n=32),
                                       AX.X, ALU.max, apply_absolute_value=True),
             reads=["wnat"], writes=["qf"], tag=tag + "_bm")
        p.op("vector",
             lambda e: e.tensor_single_scalar(qf[:, 0:nblocks], qf[:, 0:nblocks],
                                              1e-12 * 7.0, ALU.max),
             reads=["qf"], writes=["qf"], tag=tag + "_bc")
        p.op("vector",
             lambda e: e.tensor_scalar_mul(ssx[:, 0:nblocks], qf[:, 0:nblocks], 1.0 / 7.0),
             reads=["qf"], writes=["ssx"], tag=tag + "_bs")
        p.op("vector",
             lambda e: e.reciprocal(rsx[:, 0:nblocks], ssx[:, 0:nblocks]),
             reads=["ssx"], writes=["rsx"], tag=tag + "_br")
        p.op("tensor",
             lambda e: e.transpose(psE[0:nblocks, 0:128], ssx[:, 0:nblocks], b["identf"][:, :]),
             reads=["ssx", "identf"], writes=["psE"], tag=tag + "_ts")
        p.op("vector",
             lambda e, oc=oc: e.tensor_copy(sT[0:nblocks, oc * 128:(oc + 1) * 128],
                                            psE[0:nblocks, 0:128]),
             reads=["psE"], writes=["sT"], tag=tag + "_tse")
        p.op("tensor",
             lambda e: e.transpose(psF[0:nblocks, 0:128], rsx[:, 0:nblocks], b["identf"][:, :]),
             reads=["rsx", "identf"], writes=["psF"], tag=tag + "_tr")
        p.op("vector",
             lambda e, oc=oc: e.tensor_copy(rT[0:nblocks, oc * 128:(oc + 1) * 128],
                                            psF[0:nblocks, 0:128]),
             reads=["psF"], writes=["rT"], tag=tag + "_tre")
    for c in range(nchunk):
        widx = (c % 2) if wrot[0] is not wrot[1] else 0
        wr = wrot[widx]
        wk = f"wrot:{widx}"
        p.op("sync",
             lambda e, c=c, wr=wr: e.dma_start(out=wr[:, 0:O], in_=wT[c * 128:(c + 1) * 128, :]),
             writes=[wk], dma=True, tag=tag + "_ldt")
        nog = (O + 511) // 512
        for og in range(nog):
            gn = min(512, O - og * 512)
            p.op("tensor",
                 lambda e, c=c, og=og, gn=gn: e.matmul(
                     psE[:, 0:gn], selsb[0:nblk, c * 128:(c + 1) * 128],
                     rT[0:nblk, og * 512:og * 512 + gn], start=True, stop=True),
                 reads=[selkey, "rT"], writes=["psE"], tag=tag + "_xr")
            p.op("vector",
                 lambda e, og=og, gn=gn: e.tensor_copy(rsx[:, og * 512:og * 512 + gn],
                                                       psE[:, 0:gn]),
                 reads=["psE"], writes=["rsx"], tag=tag + "_xre")
            p.op("tensor",
                 lambda e, c=c, og=og, gn=gn: e.matmul(
                     psF[:, 0:gn], selsb[0:nblk, c * 128:(c + 1) * 128],
                     sT[0:nblk, og * 512:og * 512 + gn], start=True, stop=True),
                 reads=[selkey, "sT"], writes=["psF"], tag=tag + "_xs")
            p.op("vector",
                 lambda e, og=og, gn=gn: e.tensor_copy(ssx[:, og * 512:og * 512 + gn],
                                                       psF[:, 0:gn]),
                 reads=["psF"], writes=["ssx"], tag=tag + "_xse")
        p.op("vector", lambda e, wr=wr: e.tensor_mul(xs[:, 0:O], wr[:, 0:O], rsx[:, 0:O]),
             reads=[wk, "rsx"], writes=["xs"], tag=tag + "_ws")
        p.op("vector", lambda e: e.tensor_copy(qi[:, 0:O], xs[:, 0:O]),
             reads=["xs"], writes=["qi"], tag=tag + "_qi")
        p.op("vector", lambda e: e.tensor_sub(dq[:, 0:O], xs[:, 0:O], qi[:, 0:O]),
             reads=["xs", "qi"], writes=["dq"], tag=tag + "_d")
        p.op("vector",
             lambda e: e.tensor_single_scalar(g1q[:, 0:O], dq[:, 0:O], 0.5, ALU.is_ge),
             reads=["dq"], writes=["g1q"], tag=tag + "_g1")
        p.op("vector",
             lambda e: e.tensor_scalar(dq[:, 0:O], dq[:, 0:O], -0.5, -1.0,
                                       ALU.is_le, ALU.mult),
             reads=["dq"], writes=["dq"], tag=tag + "_g2")
        p.op("vector",
             lambda e: e.tensor_add(g1q[:, 0:O], g1q[:, 0:O], dq[:, 0:O]),
             reads=["g1q", "dq"], writes=["g1q"], tag=tag + "_gg")
        p.op("vector",
             lambda e: e.tensor_add(xs[:, 0:O], qi[:, 0:O], g1q[:, 0:O]),
             reads=["qi", "g1q"], writes=["xs"], tag=tag + "_q")
        p.op("vector",
             lambda e, c=c: e.tensor_mul(wdq[c][:, 0:O], xs[:, 0:O], ssx[:, 0:O]),
             reads=["xs", "ssx"], writes=[k_(kk, c) for kk in wdqkeys], tag=tag + "_dq")


def build_stage_l2ffnB():
    ctx = Ctx("l2ffnB")
    p, b = ctx.p, ctx.b
    H3 = 3072

    xnin = p.dram("xnin", [C, T], F32)
    mvec = p.dram("mvec", [1, NCORES], F32)
    wupT = p.dram("wupT", [C, H3], F32)
    wupN = p.dram("wupN", [H3, C], F32)
    wgT = p.dram("wgT", [C, H3], F32)
    wgN = p.dram("wgN", [H3, C], F32)
    seld = p.dram("selu", [32, CH * 128], F32)
    identfd = p.dram("identf", [128, 128], F32)
    up_d = p.dram("up_d", [T, H3], BF16, kind="ExternalOutput")
    gate_d = p.dram("gate_d", [T, H3], BF16, kind="ExternalOutput")
    m2d = p.dram("m2", [2, 1], F32, kind="ExternalOutput")

    xq = [p.sb(f"xq_{c}", [128, T], BF16) for c in range(CH)]
    wdq = [p.sb(f"wdq_{c}", [128, H3], BF16) for c in range(CH)]
    xrot = [p.sb("xrot_0", [128, T], F32)] * 2
    b["wnat"] = p.sb("wnat", [128, C], F32)
    b["sT"] = p.sb("sT", [32, H3], F32)
    b["rT"] = p.sb("rT", [32, H3], F32)
    b["rsx"] = p.sb("rsx", [128, H3], F32)
    b["ssx"] = p.sb("ssx", [128, H3], F32)
    b["qi"] = p.sb("qi", [128, H3], I32)
    b["dq"] = p.sb("dq", [128, H3], F32)
    b["g1q"] = p.sb("g1q", [128, H3], BF16)
    b["qf"] = p.sb("qf", [128, 96], F32)
    b["xs"] = p.sb("xs", [128, H3], F32)
    b["wrot"] = [p.sb("wrot_0", [128, H3], F32)] * 2
    b["identf"] = p.sb("identf_s", [128, 128], F32)
    b["mrow"] = p.sb("mrow", [1, 40], F32)
    b["scol2"] = p.sb("scol2", [128, 8], F32)
    selsb = p.sb("selsb", [32, CH * 128], F32)
    b["mrow2"] = p.sb("mrow2", [2, 1], F32)
    spb = p.sb("spb", [128, 512], BF16)
    mcol = p.sb("mcol", [128, 2], F32)
    psA = p.ps("psA", [128, 512], F32)
    psB = p.ps("psB", [128, 512], F32)
    b["psE"] = p.ps("psE", [128, 512], F32)
    b["psF"] = p.ps("psF", [128, 512], F32)
    b["psS"] = p.ps("psS", [128, 512], F32)

    p.op("sync", lambda e: e.dma_start(out=b["identf"][:, :], in_=identfd[:, :]),
         writes=["identf"], dma=True)
    p.op("sync", lambda e: e.dma_start(out=selsb[:, :], in_=seld[:, :]),
         writes=["selsb"], dma=True)
    p.op("sync", lambda e: e.dma_start(out=b["mrow"][0:1, 0:NCORES], in_=mvec[:, :]),
         writes=["mrow"], dma=True)
    p.op("vector", lambda e: e.memset(mcol[:, :], 0.0), writes=["mcol"])
    _emit_gscale(p, b, 0, 0, "gs")

    for c in range(CH):
        xidx = (c % 2) if xrot[0] is not xrot[1] else 0
        xr = xrot[xidx]
        xk = f"xrot:{xidx}"
        p.op("sync", lambda e, c=c, xr=xr: e.dma_start(out=xr[:, :], in_=xnin[c * 128:(c + 1) * 128, :]),
             writes=[xk], dma=True, tag="ldx")
        _emit_qfix(p, xr, xk, T, "c", b["qi"], b["dq"], b["g1q"], b["xs"],
                   xq[c], k_("xq", c), f"xq{c}",
                   b["scol2"][:, 1:2], b["scol2"][:, 0:1])

    for wi, (wT_d, wN_d, out_d, mc) in enumerate(
            [(wupT, wupN, up_d, 0), (wgT, wgN, gate_d, 1)]):
        _emit_wdeq(p, b, wN_d, wT_d, selsb, "selsb", 32, CH, wdq,
                   ["wdq"] if wi == 0 else ["wdq", "wdqg"], f"wd{wi}")
        for t in range(NT):
            tsl = slice(t * 128, (t + 1) * 128)
            for og in range(6):
                bank = psA if og % 2 == 0 else psB
                bkey = "psA" if og % 2 == 0 else "psB"
                for c in range(CH):
                    p.op("tensor",
                         lambda e, t=t, og=og, c=c, bank=bank, tsl=tsl, wi=wi: e.matmul(
                             bank[:, :], xq[c][:, tsl],
                             wdq[c][:, og * 512:(og + 1) * 512],
                             start=(c == 0), stop=(c == CH - 1)),
                         reads=[k_("xq", c), k_("wdq" if wi == 0 else "wdqg", c)],
                         writes=[bkey], tag="up_mm")
                p.op("vector",
                     lambda e, bank=bank, mc=mc: e.tensor_reduce(
                         b["g1q"][:, 0:1], bank[:, :].rearrange("p (a n) -> p a n", a=1),
                         AX.X, ALU.max, apply_absolute_value=True),
                     reads=[bkey], writes=["g1q"], tag="up_mx")
                p.op("vector",
                     lambda e, mc=mc: e.tensor_max(mcol[:, mc:mc + 1], mcol[:, mc:mc + 1],
                                                   b["g1q"][:, 0:1]),
                     reads=["g1q", "mcol"], writes=["mcol"], tag="up_mxa")
                p.op("vector",
                     lambda e, bank=bank: e.tensor_copy(spb[:, :], bank[:, :]),
                     reads=[bkey], writes=["spb"], tag="up_ev")
                p.op("sync",
                     lambda e, t=t, og=og, out_d=out_d: e.dma_start(
                         out=out_d[t * 128:(t + 1) * 128, og * 512:(og + 1) * 512],
                         in_=spb[:, :]),
                     reads=["spb"], dma=True, tag="up_st")
    p.op("tensor",
         lambda e: e.transpose(b["psE"][0:2, 0:128], mcol[:, 0:2], b["identf"][:, :]),
         reads=["mcol", "identf"], writes=["psE"], tag="mx_t0")
    p.op("vector",
         lambda e: e.tensor_reduce(b["mrow2"][0:2, 0:1],
                                   b["psE"][0:2, 0:128].rearrange("p (a n) -> p a n", a=1),
                                   AX.X, ALU.max),
         reads=["psE"], writes=["mrow2"], tag="mx_r0")
    p.op("sync", lambda e: e.dma_start(out=m2d[:, :], in_=b["mrow2"][0:2, 0:1]),
         reads=["mrow2"], dma=True, tag="stm")
    if _DBG_L2F:
        dwd = p.dram("dwd", [128, H3], BF16, kind="ExternalOutput")
        dxq = p.dram("dxq", [128, T], BF16, kind="ExternalOutput")
        dsx = p.dram("dsx", [128, H3], F32, kind="ExternalOutput")
        drx = p.dram("drx", [128, H3], F32, kind="ExternalOutput")
        p.op("sync", lambda e: e.dma_start(out=dwd[:, :], in_=wdq[0][:, :]),
             reads=[k_("wdqg", 0)], dma=True, tag="dbg")
        p.op("sync", lambda e: e.dma_start(out=dxq[:, :], in_=xq[0][:, :]),
             reads=[k_("xq", 0)], dma=True, tag="dbg")
        p.op("sync", lambda e: e.dma_start(out=dsx[:, :], in_=b["ssx"][:, :]),
             reads=["ssx"], dma=True, tag="dbg")
        p.op("sync", lambda e: e.dma_start(out=drx[:, :], in_=b["rsx"][:, :]),
             reads=["rsx"], dma=True, tag="dbg")
    return p.build()




def build_stage_l2ffnC():
    ctx = Ctx("l2ffnC")
    p, b = ctx.p, ctx.b
    H3 = 3072

    gate_d = p.dram("gate_d", [T, H3], BF16)
    mgd = p.dram("mg8", [1, NCORES], F32)
    identfd = p.dram("identf", [128, 128], F32)
    g2_d = p.dram("g2_d", [T, H3], BF16, kind="ExternalOutput")
    m1d = p.dram("m1", [1, 1], F32, kind="ExternalOutput")

    grot = [p.sb(f"grot_{i}", [128, 512], BF16) for i in range(2)]
    b["qi"] = p.sb("qi", [128, 512], I32)
    b["dq"] = p.sb("dq", [128, 512], F32)
    b["g1q"] = p.sb("g1q", [128, 512], F32)
    b["qf"] = p.sb("qf", [128, 512], F32)
    gdq = p.sb("gdq", [128, 512], F32)
    sgf = p.sb("sgf", [128, 512], F32)
    sdq = p.sb("sdq", [128, 512], F32)
    pf = p.sb("pf", [128, 512], F32)
    spb = p.sb("spb", [128, 512], BF16)
    mcol = p.sb("mcol", [128, 1], F32)
    amc = p.sb("amc", [128, 1], F32)
    b["identf"] = p.sb("identf_s", [128, 128], F32)
    b["mrow"] = p.sb("mrow", [1, 40], F32)
    b["mrow2"] = p.sb("mrow2", [2, 1], F32)
    b["scol2"] = p.sb("scol2", [128, 8], F32)
    b["psS"] = p.ps("psS", [128, 512], F32)
    psE = p.ps("psE", [128, 512], F32)

    p.op("sync", lambda e: e.dma_start(out=b["identf"][:, :], in_=identfd[:, :]),
         writes=["identf"], dma=True)
    p.op("sync", lambda e: e.dma_start(out=b["mrow"][0:1, 0:NCORES], in_=mgd[:, :]),
         writes=["mrow"], dma=True)
    p.op("vector", lambda e: e.memset(mcol[:, :], 0.0), writes=["mcol"])
    _emit_gscale(p, b, 0, 0, "gs")

    for t in range(NT):
        for og in range(6):
            gr = grot[(t * 6 + og) % 2]
            gk = f"grot:{(t * 6 + og) % 2}"
            p.op("sync",
                 lambda e, t=t, og=og, gr=gr: e.dma_start(
                     out=gr[:, :], in_=gate_d[t * 128:(t + 1) * 128, og * 512:(og + 1) * 512]),
                 writes=[gk], dma=True, tag="ldg")
            _emit_qfix(p, gr, gk, 512, "c", b["qi"], b["dq"], b["g1q"], b["qf"],
                       gdq, "gdq", "gq", b["scol2"][:, 1:2], b["scol2"][:, 0:1])
            p.op("scalar", lambda e: e.activation(sgf[:, :], gdq[:, :], AF.Sigmoid),
                 reads=["gdq"], writes=["sgf"], tag="sig")
            _emit_qfix(p, sgf, "sgf", 512, "c", b["qi"], b["dq"], b["g1q"], b["qf"],
                       sdq, "sdq", "uq", 65536.0, 1.0 / 65536.0, rskey=(),
                       clip_hi=65535.0)
            p.op("vector", lambda e: e.tensor_mul(pf[:, :], gdq[:, :], sdq[:, :]),
                 reads=["gdq", "sdq"], writes=["pf"], tag="g2p")
            p.op("vector",
                 lambda e: e.tensor_reduce(amc[:, 0:1], pf[:, :].rearrange("p (a n) -> p a n", a=1),
                                           AX.X, ALU.max, apply_absolute_value=True),
                 reads=["pf"], writes=["amc"], tag="g2mx")
            p.op("vector", lambda e: e.tensor_max(mcol[:, 0:1], mcol[:, 0:1], amc[:, 0:1]),
                 reads=["amc", "mcol"], writes=["mcol"], tag="g2mxa")
            p.op("vector", lambda e: e.tensor_copy(spb[:, :], pf[:, :]),
                 reads=["pf"], writes=["spb"], tag="g2bf")
            p.op("sync",
                 lambda e, t=t, og=og: e.dma_start(
                     out=g2_d[t * 128:(t + 1) * 128, og * 512:(og + 1) * 512], in_=spb[:, :]),
                 reads=["spb"], dma=True, tag="stg2")
    p.op("tensor",
         lambda e: e.transpose(psE[0:1, 0:128], mcol[:, 0:1], b["identf"][:, :]),
         reads=["mcol", "identf"], writes=["psE"], tag="mx_t")
    p.op("vector",
         lambda e: e.tensor_reduce(b["mrow2"][0:1, 0:1],
                                   psE[0:1, 0:128].rearrange("p (a n) -> p a n", a=1),
                                   AX.X, ALU.max),
         reads=["psE"], writes=["mrow2"], tag="mx_r")
    p.op("sync", lambda e: e.dma_start(out=m1d[:, :], in_=b["mrow2"][0:1, 0:1]),
         reads=["mrow2"], dma=True, tag="stm")
    return p.build()


def build_stage_l2ffnD():
    ctx = Ctx("l2ffnD")
    p, b = ctx.p, ctx.b
    H3 = 3072

    up_d = p.dram("up_d", [T, H3], BF16)
    g2_d = p.dram("g2_d", [T, H3], BF16)
    mud = p.dram("mu8", [1, NCORES], F32)
    mg2d = p.dram("mg28", [1, NCORES], F32)
    identfd = p.dram("identf", [128, 128], F32)
    identbd = p.dram("identb", [128, 128], BF16)
    oT_d = p.dram("oT_d", [H3, T], BF16, kind="ExternalOutput")
    m1d = p.dram("m1", [1, 1], F32, kind="ExternalOutput")

    urot = [p.sb(f"urot_{i}", [128, 512], BF16) for i in range(2)]
    grot = [p.sb(f"grot_{i}", [128, 512], BF16) for i in range(2)]
    b["qi"] = p.sb("qi", [128, 512], I32)
    b["dq"] = p.sb("dq", [128, 512], F32)
    b["g1q"] = p.sb("g1q", [128, 512], F32)
    b["qf"] = p.sb("qf", [128, 512], F32)
    udq = p.sb("udq", [128, 512], F32)
    g2dq = p.sb("g2dq", [128, 512], F32)
    pf = p.sb("pf", [128, 512], F32)
    pb = p.sb("pb", [128, 512], BF16)
    trb = p.sb("trb", [128, 128], BF16)
    mcol = p.sb("mcol", [128, 1], F32)
    amc = p.sb("amc", [128, 1], F32)
    b["identf"] = p.sb("identf_s", [128, 128], F32)
    identb = p.sb("identb_s", [128, 128], BF16)
    b["mrow"] = p.sb("mrow", [1, 40], F32)
    b["mrow2"] = p.sb("mrow2", [2, 1], F32)
    b["scol2"] = p.sb("scol2", [128, 8], F32)
    b["psS"] = p.ps("psS", [128, 512], F32)
    psE = p.ps("psE", [128, 512], F32)
    psT = [p.ps("psT0", [128, 128], BF16), p.ps("psT1", [128, 128], BF16)]

    p.op("sync", lambda e: e.dma_start(out=b["identf"][:, :], in_=identfd[:, :]),
         writes=["identf"], dma=True)
    p.op("sync", lambda e: e.dma_start(out=identb[:, :], in_=identbd[:, :]),
         writes=["identb"], dma=True)
    p.op("sync", lambda e: e.dma_start(out=b["mrow"][0:1, 0:NCORES], in_=mud[:, :]),
         writes=["mrow"], dma=True)
    p.op("sync", lambda e: e.dma_start(out=b["mrow"][0:1, 8:8 + NCORES], in_=mg2d[:, :]),
         writes=["mrow"], dma=True)
    p.op("vector", lambda e: e.memset(mcol[:, :], 0.0), writes=["mcol"])
    _emit_gscale(p, b, 0, 0, "gsu")
    _emit_gscale(p, b, 8, 1, "gsg")

    for t in range(NT):
        for og in range(6):
            i2 = (t * 6 + og) % 2
            ur, gr = urot[i2], grot[i2]
            uk, gk = f"urot:{i2}", f"grot:{i2}"
            p.op("sync",
                 lambda e, t=t, og=og, ur=ur: e.dma_start(
                     out=ur[:, :], in_=up_d[t * 128:(t + 1) * 128, og * 512:(og + 1) * 512]),
                 writes=[uk], dma=True, tag="ldu")
            p.op("sync",
                 lambda e, t=t, og=og, gr=gr: e.dma_start(
                     out=gr[:, :], in_=g2_d[t * 128:(t + 1) * 128, og * 512:(og + 1) * 512]),
                 writes=[gk], dma=True, tag="ldg")
            _emit_qfix(p, ur, uk, 512, "c", b["qi"], b["dq"], b["g1q"], b["qf"],
                       udq, "udq", "uq", b["scol2"][:, 1:2], b["scol2"][:, 0:1])
            _emit_qfix(p, gr, gk, 512, "c", b["qi"], b["dq"], b["g1q"], b["qf"],
                       g2dq, "g2dq", "gq", b["scol2"][:, 3:4], b["scol2"][:, 2:3])
            p.op("vector", lambda e: e.tensor_mul(pf[:, :], udq[:, :], g2dq[:, :]),
                 reads=["udq", "g2dq"], writes=["pf"], tag="opre")
            p.op("vector",
                 lambda e: e.tensor_reduce(amc[:, 0:1], pf[:, :].rearrange("p (a n) -> p a n", a=1),
                                           AX.X, ALU.max, apply_absolute_value=True),
                 reads=["pf"], writes=["amc"], tag="omx")
            p.op("vector", lambda e: e.tensor_max(mcol[:, 0:1], mcol[:, 0:1], amc[:, 0:1]),
                 reads=["amc", "mcol"], writes=["mcol"], tag="omxa")
            p.op("vector", lambda e: e.tensor_copy(pb[:, :], pf[:, :]),
                 reads=["pf"], writes=["pb"], tag="obf")
            for j in range(4):
                half = j % 2
                p.op("tensor",
                     lambda e, j=j, half=half: e.transpose(
                         psT[half][:, 0:128], pb[:, j * 128:(j + 1) * 128], identb[:, :]),
                     reads=["pb", "identb"], writes=[f"psT{half}"], tag="otr")
                p.op("vector",
                     lambda e, half=half: e.tensor_copy(trb[:, :], psT[half][:, 0:128]),
                     reads=[f"psT{half}"], writes=["trb"], tag="otre")
                p.op("sync",
                     lambda e, t=t, og=og, j=j: e.dma_start(
                         out=oT_d[og * 512 + j * 128:og * 512 + (j + 1) * 128,
                                  t * 128:(t + 1) * 128],
                         in_=trb[:, :]),
                     reads=["trb"], dma=True, tag="sto")
    p.op("tensor",
         lambda e: e.transpose(psE[0:1, 0:128], mcol[:, 0:1], b["identf"][:, :]),
         reads=["mcol", "identf"], writes=["psE"], tag="mx_t")
    p.op("vector",
         lambda e: e.tensor_reduce(b["mrow2"][0:1, 0:1],
                                   psE[0:1, 0:128].rearrange("p (a n) -> p a n", a=1),
                                   AX.X, ALU.max),
         reads=["psE"], writes=["mrow2"], tag="mx_r")
    p.op("sync", lambda e: e.dma_start(out=m1d[:, :], in_=b["mrow2"][0:1, 0:1]),
         reads=["mrow2"], dma=True, tag="stm")
    return p.build()


def build_stage_l2ffnE():
    ctx = Ctx("l2ffnE")
    p, b = ctx.p, ctx.b
    H3 = 3072
    NCH3 = H3 // 128

    oT_d = p.dram("oT_d", [H3, T], BF16)
    mod_ = p.dram("mo8", [1, NCORES], F32)
    wdnT = p.dram("wdnT", [H3, C], F32)
    wdnN = p.dram("wdnN", [C, H3], F32)
    seld = p.dram("selw", [96, NCH3 * 128], F32)
    identfd = p.dram("identf", [128, 128], F32)
    xin = p.dram("xin", [C, T], F32)
    xout = p.dram("xout", [C, T], F32, kind="ExternalOutput")

    wdq = [p.sb(f"wdq_{c}", [128, C], BF16) for c in range(NCH3)]
    oqT = [p.sb(f"oqT_{c}", [128, T // 2], BF16) for c in range(NCH3)]
    b["wnat"] = p.sb("wnat", [128, H3], F32)
    b["sT"] = p.sb("sT", [96, C], F32)
    b["rT"] = p.sb("rT", [96, C], F32)
    b["rsx"] = p.sb("rsx", [128, C], F32)
    b["ssx"] = p.sb("ssx", [128, C], F32)
    b["qi"] = p.sb("qi", [128, C], I32)
    b["dq"] = p.sb("dq", [128, C], F32)
    b["g1q"] = p.sb("g1q", [128, C], F32)
    b["qf"] = p.sb("qf", [128, C], F32)
    b["xs"] = p.sb("xs", [128, C], F32)
    b["wrot"] = [p.sb(f"wrot_{i}", [128, C], F32) for i in range(2)]
    orot = [p.sb(f"orot_{i}", [128, T // 2], BF16) for i in range(2)]
    xrot = [p.sb(f"xrot_{i}", [128, T // 2], F32) for i in range(2)]
    xob = p.sb("xob", [128, 512], F32)
    b["identf"] = p.sb("identf_s", [128, 128], F32)
    b["mrow"] = p.sb("mrow", [1, 40], F32)
    b["scol2"] = p.sb("scol2", [128, 8], F32)
    selsb = p.sb("selsb", [96, NCH3 * 128], F32)
    psA = p.ps("psA", [128, 512], F32)
    psB = p.ps("psB", [128, 512], F32)
    b["psE"] = p.ps("psE", [128, 512], F32)
    b["psF"] = p.ps("psF", [128, 512], F32)
    b["psS"] = p.ps("psS", [128, 512], F32)

    p.op("sync", lambda e: e.dma_start(out=b["identf"][:, :], in_=identfd[:, :]),
         writes=["identf"], dma=True)
    p.op("sync", lambda e: e.dma_start(out=selsb[:, :], in_=seld[:, :]),
         writes=["selsb"], dma=True)
    p.op("sync", lambda e: e.dma_start(out=b["mrow"][0:1, 0:NCORES], in_=mod_[:, :]),
         writes=["mrow"], dma=True)
    _emit_gscale(p, b, 0, 0, "gs")

    _emit_wdeq(p, b, wdnN, wdnT, selsb, "selsb", 96, NCH3, wdq, ["wdq"], "wd")

    for half in range(2):
        hsl = slice(half * (T // 2), (half + 1) * (T // 2))
        for c in range(NCH3):
            orr = orot[c % 2]
            ok = f"orot:{c % 2}"
            p.op("sync",
                 lambda e, c=c, orr=orr, hsl=hsl: e.dma_start(
                     out=orr[:, :], in_=oT_d[c * 128:(c + 1) * 128, hsl]),
                 writes=[ok], dma=True, tag="ldo")
            _emit_qfix(p, orr, ok, T // 2, "c", b["qi"], b["dq"], b["g1q"], b["qf"],
                       oqT[c], k_("oqT", c), f"oq{c}",
                       b["scol2"][:, 1:2], b["scol2"][:, 0:1])
        for oc in range(CH):
            xr = xrot[oc % 2]
            xk = f"xrot:{oc % 2}"
            p.op("sync",
                 lambda e, oc=oc, xr=xr, hsl=hsl: e.dma_start(
                     out=xr[:, :], in_=xin[oc * 128:(oc + 1) * 128, hsl]),
                 writes=[xk], dma=True, tag="ldx")
            for tg in range(2):
                bank = psA if tg == 0 else psB
                bkey = "psA" if tg == 0 else "psB"
                tsl = slice(tg * 512, (tg + 1) * 512)
                for c in range(NCH3):
                    p.op("tensor",
                         lambda e, oc=oc, c=c, tsl=tsl, bank=bank: e.matmul(
                             bank[:, :], wdq[c][:, oc * 128:(oc + 1) * 128],
                             oqT[c][:, tsl], start=(c == 0), stop=(c == NCH3 - 1)),
                         reads=[k_("wdq", c), k_("oqT", c)], writes=[bkey], tag="dn_mm")
                p.op("vector",
                     lambda e, bank=bank, tsl=tsl, xr=xr: e.tensor_add(
                         xob[:, :], bank[:, :], xr[:, tsl]),
                     reads=[bkey, xk], writes=["xob"], tag="dn_add")
                p.op("sync",
                     lambda e, oc=oc, half=half, tg=tg: e.dma_start(
                         out=xout[oc * 128:(oc + 1) * 128,
                                  half * (T // 2) + tg * 512:half * (T // 2) + (tg + 1) * 512],
                         in_=xob[:, :]),
                     reads=["xob"], dma=True, tag="dn_st")
    return p.build()





def build_stage_l3attn():
    H, D = 16, 64
    ctx = Ctx("l3attn")
    p, b = ctx.p, ctx.b
    cb = b["cb"]
    ones_f, ones_k1 = b["ones_f"], b["ones_k1"]

    xin = p.dram("xin", [C, T], F32)
    xout = p.dram("xout", [C, T], F32, kind="ExternalOutput")
    xn7_d = p.dram("xn7", [C, T], F32, kind="ExternalOutput")
    top1_d = p.dram("top1", [128, NT], F32, kind="ExternalOutput")
    rank_d = p.dram("rank", [128, NT], F32, kind="ExternalOutput")
    comb_d = p.dram("comb", [128, NT], F32, kind="ExternalOutput")
    cnt_d = p.dram("cnt", [1, 8], F32, kind="ExternalOutput")
    wqcT_d = p.dram("wqcT", [C, C], F32)
    wkcT_d = p.dram("wkcT", [C, C], F32)
    wvcT_d = p.dram("wvcT", [C, C], F32)
    wocT_d = p.dram("wocT", [C, C], F32)
    wqlT_d = p.dram("wqlT", [C, C], F32)
    wklT_d = p.dram("wklT", [C, C], F32)
    wvlT_d = p.dram("wvlT", [C, C], F32)
    wolT_d = p.dram("wolT", [C, C], F32)
    bqc_d = p.dram("bqc2", [128, CH], F32)
    bvc_d = p.dram("bvc2", [128, CH], F32)
    boc_d = p.dram("boc2", [1, C], F32)
    bql_d = p.dram("bql2", [64, H], F32)
    bvl_d = p.dram("bvl2", [128, H], F32)
    bol_d = p.dram("bol2", [128, CH], F32)
    masks_d = p.dram("masks7", [7, 128, 512], BF16)
    identfd = p.dram("identf", [128, 128], F32)
    identbd = p.dram("identb", [128, 128], BF16)
    wg_d = p.dram("wgsb", [128, 64], F32)
    stri_d = p.dram("stri", [128, 128], BF16)

    xrot = [p.sb("xrot_0", [128, T], F32)] * 2
    xn = [p.sb(f"xn_{c}", [128, T], BF16) for c in range(CH)]
    wA = p.sb("wA", [128, CH * C], BF16)
    wB = p.sb("wB", [128, CH * C], BF16)
    wC = p.sb("wC", [128, CH * C], BF16)
    v65 = [p.sb(f"v65_{t}", [128, H * 65], BF16) for t in range(NT)]
    attnres = [p.sb(f"attnres_{i}", [128, T], BF16) for i in range(H // 2)]
    qh = p.sb("qh", [64, T], BF16)
    kh = p.sb("kh", [64, T], BF16)
    pt = p.sb("pt", [128, 512], BF16)
    mask_t = p.sb("mask_t", [128, 7 * 512], BF16)
    scr = p.sb("scr", [128, 512], F32)
    scr2 = p.sb("scr2", [128, 512], F32)
    brow = p.sb("brow", [1, T], F32)
    browb = p.sb("browb", [1, T], BF16)
    nrow = p.sb("nrow", [128, 512], F32)
    Bsb = p.sb("Bsb", [128, 512], F32)
    attf = p.sb("attf", [16, T], F32)
    qb = p.sb("qb", [128, C], BF16)
    qc = p.sb("qc", [128, CH], F32)
    ctxc = p.sb("ctxc", [128, CH], BF16)
    ctxf = p.sb("ctxf", [128, CH], F32)
    gcol = p.sb("gcol", [128, CH], F32)
    grow = p.sb("grow", [1, C], BF16)
    vc = p.sb("vc", [128, C], BF16)
    att_tok = p.sb("att_tok", [128, 16], BF16)
    sc_tok = p.sb("sc_tok", [128, 16], F32)
    tiny = p.sb("tiny", [16, 4], F32)
    bias_qc = p.sb("bias_qc", [128, CH], F32)
    bias_vc = p.sb("bias_vc", [128, CH], F32)
    bias_oc = p.sb("bias_oc", [1, C], BF16)
    bias_ql = p.sb("bias_ql", [64, H], F32)
    bias_vl = p.sb("bias_vl", [128, H], F32)
    bias_ol = p.sb("bias_ol", [128, CH], F32)
    identf = p.sb("identf_s", [128, 128], F32)
    identb = p.sb("identb_s", [128, 128], BF16)
    wgsbb = p.sb("wg_b", [128, 64], BF16)
    stri = p.sb("stri_s", [128, 128], BF16)
    iot = p.sb("iot", [128, 8], F32)
    rr = p.sb("rr", [128, 8 * 8], F32)
    ohb = p.sb("ohb", [128, 8], BF16)
    crow = p.sb("crow", [1, 8], F32)
    t1v = p.sb("t1v", [128, NT], F32)
    rkv = p.sb("rkv", [128, NT], F32)
    cbv = p.sb("cbv", [128, NT], F32)
    ones128b = p.sb("ones128b", [128, 1], BF16)

    ps_s = [p.ps(f"ps_s{i}", [128, 512], F32) for i in range(4)]
    ps_a = [p.ps(f"ps_a{i}", [128, 512], F32) for i in range(4)]

    onesr = p.sb("onesr", [128, 64], F32)
    b["onesr"] = onesr
    p.op("vector", lambda e: e.memset(onesr[:, :], 1.0), writes=["onesr"])
    p.op("vector", lambda e: e.memset(ones128b[:, :], 1.0), writes=["ones128b"])
    for nm, dst, dk, srcd in (("bqc", bias_qc, "bias_qc", bqc_d), ("bvc", bias_vc, "bias_vc", bvc_d),
                              ("boc", bias_oc, "bias_oc", boc_d), ("bql", bias_ql, "bias_ql", bql_d),
                              ("bvl", bias_vl, "bias_vl", bvl_d), ("bol", bias_ol, "bias_ol", bol_d)):
        p.op("gpsimd", lambda e, dst=dst, srcd=srcd: e.dma_start(out=dst[:, :], in_=srcd[:, :]),
             writes=[dk], dma=True, tag="bld")
    p.op("sync", lambda e: e.dma_start(out=identf[:, :], in_=identfd[:, :]), writes=["identf"], dma=True)
    p.op("sync", lambda e: e.dma_start(out=identb[:, :], in_=identbd[:, :]), writes=["identb"], dma=True)
    p.op("gpsimd", lambda e: e.dma_start(out=wgsbb[:, :], in_=wg_d[:, :]), writes=["wgsbb"], dma=True)
    p.op("sync", lambda e: e.dma_start(out=stri[:, :], in_=stri_d[:, :]), writes=["stri"], dma=True)
    for r in range(7):
        p.op("sync", lambda e, r=r: e.dma_start(out=mask_t[:, r * 512:(r + 1) * 512], in_=masks_d[r, :, :]),
             writes=["mask_t"], dma=True)
    for e8 in range(8):
        p.op("vector", lambda e, e8=e8: e.memset(iot[:, e8:e8 + 1], float(e8)), writes=["iot"])
    p.op("vector", lambda e: e.memset(crow[0:1, :], 0.0), writes=["crow"])
    p.op("vector", lambda e: e.tensor_scalar_mul(bias_ql[:, :], bias_ql[:, :], 0.125),
         reads=["bias_ql"], writes=["bias_ql"], tag="bscale")

    def load_w(dst, dstkeys, srcd, tag):
        for c in range(CH):
            p.op("gpsimd",
                 lambda e, c=c, dst=dst, srcd=srcd: e.dma_start(
                     out=dst[:, c * C:(c + 1) * C], in_=srcd[c * 128:(c + 1) * 128, :]),
                 writes=[k_(kk, c) for kk in dstkeys], dma=True, tag=tag)

    def emit_streamln(src_d, dst, dstprefix, outf_d=None, tag="", alias=None):
        def wkeys(c):
            ks = [k_(dstprefix, c)]
            if alias:
                ks.append(k_(alias, c))
            return ks
        for c in range(CH):
            xr = xrot[0]
            xk = "xrot:0"
            p.op("sync", lambda e, c=c, xr=xr, src_d=src_d: e.dma_start(
                     out=xr[:, :], in_=src_d[c * 128:(c + 1) * 128, :]),
                 writes=[xk], dma=True, tag=tag + "lnx")
            for s in range(4):
                sl = slice(s * 512, (s + 1) * 512)
                p.op("tensor",
                     lambda e, s=s, sl=sl, c=c, xr=xr: e.matmul(
                         ps_s[s][0:1, :], ones_f[:, 0:1], xr[:, sl],
                         start=(c == 0), stop=(c == CH - 1)),
                     reads=[xk, "ones_f"], writes=[f"ps_s{s}"], tag=tag + "ln_s")
                p.op("scalar", lambda e, sl=sl, xr=xr: e.activation(scr[:, :], xr[:, sl], AF.Square),
                     reads=[xk], writes=["scr"], tag=tag + "ln_sq")
                p.op("tensor",
                     lambda e, s=s, c=c: e.matmul(
                         ps_s[s][32:33, :], ones_f[:, 0:1], scr[:, :],
                         start=(c == 0), stop=(c == CH - 1)),
                     reads=["scr", "ones_f"], writes=[f"ps_s{s}"], tag=tag + "ln_q")
        for s in range(4):
            sl = slice(s * 512, (s + 1) * 512)
            p.op("vector", lambda e, s=s, sl=sl: e.tensor_scalar_mul(brow[0:1, sl], ps_s[s][0:1, :], 1.0 / C),
                 reads=[f"ps_s{s}"], writes=["brow"], tag=tag + "ln_mean")
            p.op("vector", lambda e, sl=sl: e.tensor_mul(scr2[0:1, :], brow[0:1, sl], brow[0:1, sl]),
                 reads=["brow"], writes=["scr2"], tag=tag + "ln_m2")
            p.op("vector",
                 lambda e, s=s: e.scalar_tensor_tensor(scr[0:1, :], ps_s[s][32:33, :], 1.0 / C,
                                                       scr2[0:1, :], ALU.mult, ALU.subtract),
                 reads=[f"ps_s{s}", "scr2"], writes=["scr"], tag=tag + "ln_var")
            p.op("scalar", lambda e: e.activation(scr[0:1, :], scr[0:1, :], AF.Ln, bias=cb[0:1, 0:1]),
                 reads=["scr", "cb"], writes=["scr"], tag=tag + "ln_ln")
            p.op("scalar", lambda e: e.activation(scr[0:1, :], scr[0:1, :], AF.Exp, scale=-0.5),
                 reads=["scr"], writes=["scr"], tag=tag + "ln_exp")
            p.op("vector", lambda e, sl=sl: e.tensor_copy(browb[0:1, sl], scr[0:1, :]),
                 reads=["scr"], writes=["browb"], tag=tag + "ln_rbf")
        for c in range(CH):
            xr = xrot[0]
            xk = "xrot:0"
            p.op("sync", lambda e, c=c, xr=xr, src_d=src_d: e.dma_start(
                     out=xr[:, :], in_=src_d[c * 128:(c + 1) * 128, :]),
                 writes=[xk], dma=True, tag=tag + "lnx2")
            for s in range(4):
                sl = slice(s * 512, (s + 1) * 512)
                p.op("tensor",
                     lambda e, sl=sl: e.matmul(ps_s[2][:, :], ones_k1[:, :], brow[0:1, sl],
                                               start=True, stop=True),
                     reads=["brow", "ones_k1"], writes=["ps_s2"], tag=tag + "ln_bM")
                p.op("tensor",
                     lambda e, sl=sl: e.matmul(ps_s[3][:, :], b["ones_k1b"][:, :], browb[0:1, sl],
                                               start=True, stop=True),
                     reads=["browb", "ones_k1b"], writes=["ps_s3"], tag=tag + "ln_bR")
                p.op("vector",
                     lambda e, sl=sl, xr=xr: e.tensor_sub(scr[:, :], xr[:, sl], ps_s[2][:, :]),
                     reads=[xk, "ps_s2"], writes=["scr"], tag=tag + "ln_sub")
                if outf_d is None:
                    p.op("vector",
                         lambda e, c=c, sl=sl: e.tensor_mul(dst[c][:, sl], scr[:, :], ps_s[3][:, :]),
                         reads=["scr", "ps_s3"], writes=wkeys(c), tag=tag + "ln_nrm")
                else:
                    p.op("vector",
                         lambda e, sl=sl: e.tensor_mul(scr2[:, :], scr[:, :], ps_s[3][:, :]),
                         reads=["scr", "ps_s3"], writes=["scr2"], tag=tag + "ln_nrm")
                    p.op("vector",
                         lambda e, c=c, sl=sl: e.tensor_copy(dst[c][:, sl], scr2[:, :]),
                         reads=["scr2"], writes=wkeys(c), tag=tag + "ln_bf")
                    p.op("sync",
                         lambda e, c=c, sl=sl, outf_d=outf_d: e.dma_start(
                             out=outf_d[c * 128:(c + 1) * 128, sl], in_=scr2[:, :]),
                         reads=["scr2"], dma=True, tag=tag + "ln_st")

    emit_streamln(xin, xn, "xn")

    load_w(wA, ["wA", "wqc"], wqcT_d, "wqc")
    for oc in range(CH):
        for c in range(CH):
            p.op("tensor",
                 lambda e, oc=oc, c=c: e.matmul(
                     ps_a[3][:, oc:oc + 1], wA[:, c * C + oc * 128:c * C + (oc + 1) * 128],
                     xn[c][:, 0:1], start=(c == 0), stop=(c == CH - 1)),
                 reads=[k_("wqc", c), k_("xn", c)], writes=["ps_a3"], tag="qc_mm")
    p.op("vector",
         lambda e: e.scalar_tensor_tensor(qc[:, :], ps_a[3][:, 0:CH], 1.0,
                                          bias_qc[:, :], ALU.mult, ALU.add),
         reads=["ps_a3", "bias_qc"], writes=["qc"], tag="qc_ev")
    p.op("vector", lambda e: e.tensor_scalar_mul(qc[:, :], qc[:, :], 0.125),
         reads=["qc"], writes=["qc"], tag="qc_sc")
    p.op("vector", lambda e: e.tensor_copy(ctxc[:, :], qc[:, :]),
         reads=["qc"], writes=["ctxc"], tag="qc_bf")
    for c in range(CH):
        p.op("tensor",
             lambda e, c=c: e.matmul(ps_s[0][0:1, c * 128 % 512:(c * 128 % 512) + 128],
                                     ctxc[:, c:c + 1], identb[:, :],
                                     start=True, stop=True) if False else e.matmul(
                 ps_s[c // 4][0:1, (c % 4) * 128:((c % 4) + 1) * 128],
                 ctxc[:, c:c + 1], identb[:, :], start=True, stop=True),
             reads=["ctxc", "identb"], writes=[f"ps_s{c // 4}"], tag="qr_mm")
    for half in range(2):
        p.op("vector",
             lambda e, half=half: e.tensor_copy(grow[0:1, half * 512:(half + 1) * 512],
                                                ps_s[half][0:1, 0:512]),
             reads=[f"ps_s{half}"], writes=["grow"], tag="qr_ev")
    for half in range(2):
        p.op("tensor",
             lambda e, half=half: e.matmul(ps_a[2][:, :], b["ones_k1b"][:, :],
                                           grow[0:1, half * 512:(half + 1) * 512],
                                           start=True, stop=True),
             reads=["grow", "ones_k1b"], writes=["ps_a2"], tag="qb_mm")
        p.op("vector",
             lambda e, half=half: e.tensor_copy(qb[:, half * 512:(half + 1) * 512], ps_a[2][:, :]),
             reads=["ps_a2"], writes=["qb"], tag="qb_ev")

    load_w(wB, ["wB", "wkc"], wkcT_d, "wkc")
    for t in range(NT):
        tsl = slice(t * 128, (t + 1) * 128)
        for g in range(2):
            bank = ps_a[g]
            bkey = f"ps_a{g}"
            for c in range(CH):
                p.op("tensor",
                     lambda e, t=t, g=g, c=c, bank=bank, tsl=tsl: e.matmul(
                         bank[:, :], xn[c][:, tsl],
                         wB[:, c * C + g * 512:c * C + (g + 1) * 512],
                         start=(c == 0), stop=(c == CH - 1)),
                     reads=[k_("xn", c), k_("wkc", c)], writes=[bkey], tag="kc_mm")
            p.op("vector",
                 lambda e, g=g, bank=bank: e.tensor_mul(scr[:, :], bank[:, :],
                                                        qb[:, g * 512:(g + 1) * 512]),
                 reads=[bkey, "qb"], writes=["scr"], tag="kc_mul")
            p.op("vector",
                 lambda e, g=g: e.tensor_reduce(
                     sc_tok[:, g * 8:(g + 1) * 8],
                     scr[:, :].rearrange("p (h u) -> p h u", u=64),
                     AX.X, ALU.add),
                 reads=["scr"], writes=["sc_tok"], tag="kc_red")
        p.op("tensor",
             lambda e: e.transpose(ps_a[2][0:16, 0:128], sc_tok[:, :], identf[:, :]),
             reads=["sc_tok", "identf"], writes=["ps_a2"], tag="sc_tr")
        p.op("vector",
             lambda e, tsl=tsl: e.tensor_copy(attf[0:16, tsl], ps_a[2][0:16, 0:128]),
             reads=["ps_a2"], writes=["attf"], tag="sc_tre")
    p.op("vector",
         lambda e: e.tensor_reduce(tiny[0:16, 0:1], attf[0:16, :].rearrange("p (a n) -> p a n", a=1),
                                   AX.X, ALU.max),
         reads=["attf"], writes=["tiny"], tag="gm_max")
    p.op("vector", lambda e: e.tensor_scalar_mul(tiny[0:16, 1:2], tiny[0:16, 0:1], -1.0),
         reads=["tiny"], writes=["tiny"], tag="gm_neg")
    p.op("scalar",
         lambda e: e.activation(attf[0:16, :], attf[0:16, :], AF.Exp, bias=tiny[0:16, 1:2]),
         reads=["attf", "tiny"], writes=["attf"], tag="gm_exp")
    p.op("vector",
         lambda e: e.tensor_reduce(tiny[0:16, 2:3], attf[0:16, :].rearrange("p (a n) -> p a n", a=1),
                                   AX.X, ALU.add),
         reads=["attf"], writes=["tiny"], tag="gm_sum")
    p.op("vector", lambda e: e.reciprocal(tiny[0:16, 3:4], tiny[0:16, 2:3]),
         reads=["tiny"], writes=["tiny"], tag="gm_rc")
    p.op("vector",
         lambda e: e.tensor_scalar_mul(attf[0:16, :], attf[0:16, :], tiny[0:16, 3:4]),
         reads=["attf", "tiny"], writes=["attf"], tag="gm_nrm")

    load_w(wC, ["wC", "wvc"], wvcT_d, "wvc")
    p.op("vector", lambda e: e.memset(ctxf[:, :], 0.0), writes=["ctxf"], tag="ctx_z")
    for t in range(NT):
        tsl = slice(t * 128, (t + 1) * 128)
        for g in range(2):
            bank = ps_a[g]
            bkey = f"ps_a{g}"
            for c in range(CH):
                p.op("tensor",
                     lambda e, t=t, g=g, c=c, bank=bank, tsl=tsl: e.matmul(
                         bank[:, :], xn[c][:, tsl],
                         wC[:, c * C + g * 512:c * C + (g + 1) * 512],
                         start=(c == 0), stop=(c == CH - 1)),
                     reads=[k_("xn", c), k_("wvc", c)], writes=[bkey], tag="vc_mm")
            p.op("vector",
                 lambda e, g=g, bank=bank: e.tensor_copy(vc[:, g * 512:(g + 1) * 512], bank[:, :]),
                 reads=[bkey], writes=["vc"], tag="vc_ev")
        p.op("tensor",
             lambda e, tsl=tsl: e.transpose(ps_a[2][0:128, 0:16], attf[0:16, tsl], identf[0:16, 0:16]),
             reads=["attf", "identf"], writes=["ps_a2"], tag="at_tr")
        p.op("vector",
             lambda e: e.tensor_copy(att_tok[:, :], ps_a[2][0:128, 0:16]),
             reads=["ps_a2"], writes=["att_tok"], tag="at_tre")
        for h in range(H):
            pb = (h % 2) * 64
            p.op("tensor",
                 lambda e, h=h, pb=pb, t=t: e.matmul(
                     ps_s[0][pb:pb + 64, h // 2:h // 2 + 1], vc[:, h * 64:(h + 1) * 64],
                     att_tok[:, h:h + 1],
                     start=True, stop=True),
                 reads=["vc", "att_tok"], writes=["ps_s0"], tag="ctx_mm")
        p.op("vector",
             lambda e: e.tensor_add(ctxf[:, :], ctxf[:, :], ps_s[0][:, 0:CH]),
             reads=["ps_s0", "ctxf"], writes=["ctxf"], tag="ctx_acc")
    p.op("vector",
         lambda e: e.tensor_add(ctxc[:, :], ctxf[:, :], bias_vc[:, :]),
         reads=["ctxf", "bias_vc"], writes=["ctxc"], tag="ctx_ev")
    load_w(wA, ["wA", "wqc", "woc"], wocT_d, "woc")
    for half in range(2):
        for c in range(CH):
            p.op("tensor",
                 lambda e, half=half, c=c: e.matmul(
                     ps_a[3][0:1, :], ctxc[:, c:c + 1],
                     wA[:, c * C + half * 512:c * C + (half + 1) * 512],
                     start=(c == 0), stop=(c == CH - 1)),
                 reads=["ctxc", k_("woc", c)], writes=["ps_a3"], tag="go_mm")
        p.op("vector",
             lambda e, half=half: e.tensor_add(grow[0:1, half * 512:(half + 1) * 512],
                                               ps_a[3][0:1, :],
                                               bias_oc[0:1, half * 512:(half + 1) * 512]),
             reads=["ps_a3", "bias_oc"], writes=["grow"], tag="go_ev")
    p.op("vector", lambda e: e.memset(scr[:, 0:128], 0.0), writes=["scr"], tag="gt_z")
    for c in range(CH):
        p.op("vector",
             lambda e, c=c: e.tensor_copy(scr[0:1, 0:128], grow[0:1, c * 128:(c + 1) * 128]),
             reads=["grow"], writes=["scr"], tag="gc_cp")
        p.op("tensor",
             lambda e: e.transpose(ps_a[2][:, 0:128], scr[:, 0:128], identf[:, :]),
             reads=["scr", "identf"], writes=["ps_a2"], tag="gc_tr")
        p.op("vector",
             lambda e, c=c: e.tensor_copy(gcol[:, c:c + 1], ps_a[2][:, 0:1]),
             reads=["ps_a2"], writes=["gcol"], tag="gc_ev")

    load_w(wA, ["wA", "wqc", "woc", "wql"], wqlT_d, "wql")
    load_w(wB, ["wB", "wkc", "wkl"], wklT_d, "wkl")
    load_w(wC, ["wC", "wvc", "wvl"], wvlT_d, "wvl")
    for t in range(NT):
        p.op("vector",
             lambda e, t=t: e.memset(
                 v65[t][:, :].rearrange("p (h u) -> p h u", u=65)[:, :, 64:65], 1.0),
             writes=[k_("v65", t)], tag="v_ones")
        for g in range(2):
            bank = ps_a[g]
            bkey = f"ps_a{g}"
            for c in range(CH):
                p.op("tensor",
                     lambda e, t=t, g=g, c=c, bank=bank: e.matmul(
                         bank[:, :], xn[c][:, t * 128:(t + 1) * 128],
                         wC[:, c * C + g * 512:(c * C) + (g + 1) * 512],
                         start=(c == 0), stop=(c == CH - 1)),
                     reads=[k_("xn", c), k_("wvl", c)], writes=[bkey], tag="vl_mm")
            p.op("vector",
                 lambda e, t=t, g=g, bank=bank: e.tensor_copy(
                     v65[t][:, :].rearrange("p (h u) -> p h u", u=65)[:, g * 8:(g + 1) * 8, 0:64],
                     bank[:, :].rearrange("p (h u) -> p h u", u=64)),
                 reads=[bkey], writes=[k_("v65", t)], tag="vl_ev")

    for i in range(H // 2):
        p.op("vector", lambda e, i=i: e.memset(attnres[i][:, :], 0.0),
             writes=[k_("attnres", i)], tag="at_z")
    RVALS = (-128, 0, 128, 256, 384, 512)
    for h in (range(H) if _DBG_HEADS is None else _DBG_HEADS):
        hq = h * 64
        par = h % 2
        at = attnres[h // 2]
        atk = k_("attnres", h // 2)
        for s in range(4):
            sl = slice(s * 512, (s + 1) * 512)
            for c in range(CH):
                p.op("tensor",
                     lambda e, c=c, sl=sl, hq=hq: e.matmul(
                         ps_a[0][0:64, :], wA[:, c * C + hq: c * C + hq + 64],
                         xn[c][:, sl], start=(c == 0), stop=(c == CH - 1)),
                     reads=[k_("wql", c), k_("xn", c)], writes=["ps_a0"], tag="q_mm")
            p.op("vector",
                 lambda e, sl=sl, h=h: e.tensor_scalar(
                     qh[0:64, sl], ps_a[0][0:64, :], 0.125, bias_ql[:, h:h + 1],
                     ALU.mult, ALU.add),
                 reads=["ps_a0", "bias_ql"], writes=["qh"], tag="q_ev")
            for c in range(CH):
                p.op("tensor",
                     lambda e, c=c, sl=sl, hq=hq: e.matmul(
                         ps_a[1][0:64, :], wB[:, c * C + hq: c * C + hq + 64],
                         xn[c][:, sl], start=(c == 0), stop=(c == CH - 1)),
                     reads=[k_("wkl", c), k_("xn", c)], writes=["ps_a1"], tag="k_mm")
            p.op("vector",
                 lambda e, sl=sl: e.tensor_copy(kh[0:64, sl], ps_a[1][0:64, :]),
                 reads=["ps_a1"], writes=["kh"], tag="k_ev")
        for g in range(4):
            kts = [kt for kt in range(NT) if 4 * g - 1 <= kt <= 4 * g + 4]
            for idx, kt in enumerate(kts):
                r = kt * 128 - g * 512
                ri = 6 if (g == 0 and kt == 0) else RVALS.index(r)
                p.op("tensor",
                     lambda e, kt=kt, g=g: e.matmul(
                         ps_s[g][:, :], kh[0:64, kt * 128:(kt + 1) * 128],
                         qh[0:64, g * 512:(g + 1) * 512], start=True, stop=True),
                     reads=["kh", "qh"], writes=[f"ps_s{g}"], tag="sc_mm")
                p.op("scalar",
                     lambda e, g=g: e.activation(
                         pt[:, :], ps_s[g][:, :], AF.Exp),
                     reads=[f"ps_s{g}"], writes=["pt"], tag="sc_exp")
                p.op("vector",
                     lambda e, g=g, ri=ri: e.tensor_mul(
                         pt[:, :], pt[:, :],
                         mask_t[:, ri * 512:(ri + 1) * 512]),
                     reads=["pt", "mask_t"], writes=["pt"], tag="sc_msk")
                if par == 0:
                    p.op("tensor",
                         lambda e, kt=kt, g=g, h=h, idx=idx, last=(idx == len(kts) - 1): e.matmul(
                             ps_a[g][0:65, :], v65[kt][:, h * 65:(h + 1) * 65],
                             pt[:, :],
                             start=(idx == 0), stop=last),
                         reads=[k_("v65", kt), "pt"], writes=[f"ps_a{g}"], tag="av_mm")
                else:
                    p.op("tensor",
                         lambda e, kt=kt, g=g, h=h, idx=idx, last=(idx == len(kts) - 1): e.matmul(
                             ps_a[g][64:128, :], v65[kt][:, h * 65:h * 65 + 64],
                             pt[:, :],
                             start=(idx == 0), stop=last),
                         reads=[k_("v65", kt), "pt"], writes=[f"ps_a{g}"], tag="av_mmo")
                    p.op("tensor",
                         lambda e, kt=kt, g=g, h=h, idx=idx, last=(idx == len(kts) - 1): e.matmul(
                             ps_a[g][0:1, :], v65[kt][:, h * 65 + 64:h * 65 + 65],
                             pt[:, :],
                             start=(idx == 0), stop=last),
                         reads=[k_("v65", kt), "pt"], writes=[f"ps_a{g}"], tag="av_den")
        for g in range(4):
            gsl = slice(g * 512, (g + 1) * 512)
            if par == 0:
                p.op("vector",
                     lambda e, g=g: e.reciprocal(nrow[64:65, :], ps_a[g][64:65, :]),
                     reads=[f"ps_a{g}"], writes=["nrow"], tag="nm_rc")
                p.op("tensor",
                     lambda e: e.matmul(ps_s[0][0:64, :], b["onesr"][64:65, 0:64], nrow[64:65, :],
                                        start=True, stop=True),
                     reads=["nrow", "onesr"], writes=["ps_s0"], tag="nm_bc")
                p.op("vector",
                     lambda e: e.tensor_copy(Bsb[0:64, :], ps_s[0][0:64, :]),
                     reads=["ps_s0"], writes=["Bsb"], tag="nm_ev")
                p.op("vector",
                     lambda e, g=g: e.tensor_mul(scr[0:64, :], ps_a[g][0:64, :], Bsb[0:64, :]),
                     reads=[f"ps_a{g}", "Bsb"], writes=["scr"], tag="nm_mul")
                p.op("vector",
                     lambda e, gsl=gsl, h=h, at=at: e.tensor_scalar_add(
                         at[0:64, gsl], scr[0:64, :], bias_vl[0:64, h:h + 1]),
                     reads=["scr", "bias_vl"], writes=[atk], tag="nm_out")
            else:
                p.op("vector",
                     lambda e, g=g: e.reciprocal(nrow[0:1, :], ps_a[g][0:1, :]),
                     reads=[f"ps_a{g}"], writes=["nrow"], tag="nm_rc")
                p.op("tensor",
                     lambda e: e.matmul(ps_s[0][64:128, :], b["onesr"][0:1, 0:64], nrow[0:1, :],
                                        start=True, stop=True),
                     reads=["nrow", "onesr"], writes=["ps_s0"], tag="nm_bc")
                p.op("vector",
                     lambda e: e.tensor_copy(Bsb[64:128, :], ps_s[0][64:128, :]),
                     reads=["ps_s0"], writes=["Bsb"], tag="nm_ev")
                p.op("vector",
                     lambda e, g=g: e.tensor_mul(scr[64:128, :], ps_a[g][64:128, :], Bsb[64:128, :]),
                     reads=[f"ps_a{g}", "Bsb"], writes=["scr"], tag="nm_mul")
                p.op("vector",
                     lambda e, gsl=gsl, h=h, at=at: e.tensor_scalar_add(
                         at[64:128, gsl], scr[64:128, :], bias_vl[64:128, h:h + 1]),
                     reads=["scr", "bias_vl"], writes=[atk], tag="nm_out")

    load_w(wA, ["wA", "wqc", "woc", "wql", "wol"], wolT_d, "wol")
    for oc in range(CH):
        xr = xrot[0]
        xk = "xrot:0"
        p.op("sync", lambda e, oc=oc, xr=xr: e.dma_start(out=xr[:, :], in_=xin[oc * 128:(oc + 1) * 128, :]),
             writes=[xk], dma=True, tag="ox_ld")
        p.op("vector", lambda e, xr=xr: e.tensor_copy(scr[:, 1:2], xr[:, 0:1]),
             reads=[xk], writes=["scr"], tag="gz_sv")
        for s in range(4):
            sl = slice(s * 512, (s + 1) * 512)
            bank = ps_a[s % 4]
            bkey = f"ps_a{s % 4}"
            for i in range(CH):
                p.op("tensor",
                     lambda e, oc=oc, i=i, sl=sl, bank=bank: e.matmul(
                         bank[:, :], wA[:, i * C + oc * 128: i * C + (oc + 1) * 128],
                         attnres[i][:, sl], start=(i == 0), stop=(i == CH - 1)),
                     reads=[k_("wol", i), k_("attnres", i)], writes=[bkey], tag="o_mm")
            p.op("vector",
                 lambda e, oc=oc, sl=sl, bank=bank, xr=xr, xk=xk: e.scalar_tensor_tensor(
                     xr[:, sl], bank[:, :], bias_ol[:, oc:oc + 1], xr[:, sl],
                     ALU.add, ALU.add),
                 reads=[bkey, "bias_ol", xk], writes=[xk], tag="o_add")
        p.op("vector",
             lambda e, oc=oc, xr=xr, xk=xk: e.tensor_add(xr[:, 0:1], scr[:, 1:2], gcol[:, oc:oc + 1]),
             reads=["scr", "gcol"], writes=[xk], tag="gz_fix")
        p.op("sync",
             lambda e, oc=oc, xr=xr, xk=xk: e.dma_start(out=xout[oc * 128:(oc + 1) * 128, :], in_=xr[:, :]),
             reads=[xk], writes=["xout"], dma=True, tag="ox_st")

    emit_streamln(xout, xn, "xn7", outf_d=xn7_d, tag="r_", alias="xn")
    for t in range(NT):
        tsl = slice(t * 128, (t + 1) * 128)
        for c in range(CH):
            p.op("tensor",
                 lambda e, t=t, c=c, tsl=tsl: e.matmul(
                     ps_a[0][:, 0:8], xn[c][:, tsl], wgsbb[:, c * 8:(c + 1) * 8],
                     start=(c == 0), stop=(c == CH - 1)),
                 reads=[k_("xn7", c), "wgsbb"], writes=["ps_a0"], tag="lg_mm")
        p.op("vector",
             lambda e: e.tensor_reduce(rr[:, 0:1], ps_a[0][:, 0:8].rearrange("p (a n) -> p a n", a=1),
                                       AX.X, ALU.max),
             reads=["ps_a0"], writes=["rr"], tag="rt_max")
        p.op("vector", lambda e: e.tensor_scalar_mul(rr[:, 1:2], rr[:, 0:1], -1.0),
             reads=["rr"], writes=["rr"], tag="rt_neg")
        p.op("scalar",
             lambda e: e.activation(rr[:, 8:16], ps_a[0][:, 0:8], AF.Exp, bias=rr[:, 1:2]),
             reads=["ps_a0", "rr"], writes=["rr"], tag="rt_exp")
        p.op("vector",
             lambda e: e.tensor_reduce(rr[:, 2:3], rr[:, 8:16].rearrange("p (a n) -> p a n", a=1),
                                       AX.X, ALU.add),
             reads=["rr"], writes=["rr"], tag="rt_sum")
        p.op("vector", lambda e, t=t: e.reciprocal(cbv[:, t:t + 1], rr[:, 2:3]),
             reads=["rr"], writes=["cbv"], tag="rt_cmb")
        p.op("vector",
             lambda e: e.tensor_tensor(rr[:, 16:24], ps_a[0][:, 0:8],
                                       rr[:, 0:1].broadcast_to([128, 8]), ALU.is_ge),
             reads=["ps_a0", "rr"], writes=["rr"], tag="rt_eq")
        p.op("vector",
             lambda e: e.tensor_scalar(rr[:, 24:32], rr[:, 16:24], -1000.0, 1000.0,
                                       ALU.mult, ALU.add),
             reads=["rr"], writes=["rr"], tag="rt_pen")
        p.op("vector",
             lambda e: e.tensor_add(rr[:, 24:32], rr[:, 24:32], iot[:, :]),
             reads=["rr", "iot"], writes=["rr"], tag="rt_cnd")
        p.op("vector",
             lambda e, t=t: e.tensor_reduce(t1v[:, t:t + 1],
                                            rr[:, 24:32].rearrange("p (a n) -> p a n", a=1),
                                            AX.X, ALU.min),
             reads=["rr"], writes=["t1v"], tag="rt_t1")
        p.op("vector",
             lambda e, t=t: e.tensor_tensor(rr[:, 32:40], iot[:, :],
                                            t1v[:, t:t + 1].broadcast_to([128, 8]), ALU.is_equal),
             reads=["iot", "t1v"], writes=["rr"], tag="rt_oh")
        p.op("vector", lambda e: e.tensor_copy(ohb[:, :], rr[:, 32:40]),
             reads=["rr"], writes=["ohb"], tag="rt_ohb")
        p.op("tensor",
             lambda e: e.matmul(ps_a[1][:, 0:8], stri[:, :], ohb[:, :], start=True, stop=False),
             reads=["stri", "ohb"], writes=["ps_a1"], tag="rt_pre")
        p.op("tensor",
             lambda e: e.matmul(ps_a[1][:, 0:8], ones_k1[:, :], crow[0:1, :],
                                start=False, stop=True),
             reads=["ones_k1", "crow"], writes=["ps_a1"], tag="rt_car")
        p.op("vector",
             lambda e: e.tensor_mul(rr[:, 40:48], ps_a[1][:, 0:8], rr[:, 32:40]),
             reads=["ps_a1", "rr"], writes=["rr"], tag="rt_rm")
        p.op("vector",
             lambda e, t=t: e.tensor_reduce(rkv[:, t:t + 1],
                                            rr[:, 40:48].rearrange("p (a n) -> p a n", a=1),
                                            AX.X, ALU.add),
             reads=["rr"], writes=["rkv"], tag="rt_rk")
        p.op("tensor",
             lambda e: e.matmul(ps_a[2][0:1, 0:8], ones128b[:, 0:1], ohb[:, :],
                                start=True, stop=True),
             reads=["ones128b", "ohb"], writes=["ps_a2"], tag="rt_cs")
        p.op("vector",
             lambda e: e.tensor_add(crow[0:1, :], crow[0:1, :], ps_a[2][0:1, 0:8]),
             reads=["ps_a2", "crow"], writes=["crow"], tag="rt_cup")
    p.op("sync", lambda e: e.dma_start(out=top1_d[:, :], in_=t1v[:, :]), reads=["t1v"], dma=True, tag="st_t1")
    p.op("sync", lambda e: e.dma_start(out=rank_d[:, :], in_=rkv[:, :]), reads=["rkv"], dma=True, tag="st_rk")
    p.op("sync", lambda e: e.dma_start(out=comb_d[:, :], in_=cbv[:, :]), reads=["cbv"], dma=True, tag="st_cb")
    p.op("sync", lambda e: e.dma_start(out=cnt_d[:, :], in_=crow[0:1, :]), reads=["crow"], dma=True, tag="st_ct")
    if _DBG_L3:
        dat0 = p.dram("dat0", [128, T], BF16, kind="ExternalOutput")
        dqh = p.dram("dqh", [64, T], BF16, kind="ExternalOutput")
        dkh = p.dram("dkh", [64, T], BF16, kind="ExternalOutput")
        dgrow = p.dram("dgrow", [1, C], BF16, kind="ExternalOutput")
        dgcol = p.dram("dgcol", [128, CH], F32, kind="ExternalOutput")
        dattf = p.dram("dattf", [16, T], F32, kind="ExternalOutput")
        dv65 = p.dram("dv65", [128, H * 65], BF16, kind="ExternalOutput")
        dctx = p.dram("dctx", [128, CH], BF16, kind="ExternalOutput")
        for nm, dd, srct, kk in (("dat0", dat0, attnres[0], k_("attnres", 0)),
                                 ("dqh", dqh, qh, "qh"), ("dkh", dkh, kh, "kh"),
                                 ("dgrow", dgrow, grow, "grow"), ("dgcol", dgcol, gcol, "gcol"),
                                 ("dattf", dattf, attf, "attf"), ("dv65", dv65, v65[0], k_("v65", 0)),
                                 ("dctx", dctx, ctxc, "ctxc")):
            p.op("sync", lambda e, dd=dd, srct=srct: e.dma_start(out=dd[:, :], in_=srct[0:dd.shape[0], :]),
                 reads=[kk], dma=True, tag="dbg")
    return p.build()





ECAP = 2561
ETRASH = 8 * ECAP


def build_stage_moeR2():
    ctx = Ctx("moeR2")
    p, b = ctx.p, ctx.b
    cprev_d = p.dram("cprev", [8, 8], F32)
    top1_d = p.dram("top1", [128, NT], F32)
    rank_d = p.dram("rank", [128, NT], F32)
    comb_d = p.dram("comb", [128, NT], F32)
    ss_d = p.dram("slot_s", [128, NT], I32, kind="ExternalOutput")
    sg_d = p.dram("slot_g", [128, NT], I32, kind="ExternalOutput")
    vc_d = p.dram("vc", [128, NT], F32, kind="ExternalOutput")

    cprev = p.sb("cprev_s", [8, 8], F32)
    ones8 = p.sb("ones8", [8, 1], F32)
    t1v = p.sb("t1v", [128, NT], F32)
    rkv = p.sb("rkv", [128, NT], F32)
    cbv = p.sb("cbv", [128, NT], F32)
    iot = p.sb("iot", [128, 8], F32)
    offb = p.sb("offb", [128, 8], F32)
    sc1 = p.sb("sc1", [128, 8], F32)
    rg = p.sb("rg", [128, 1], F32)
    vld = p.sb("vld", [128, 1], F32)
    slt = p.sb("slt", [128, 1], F32)
    si = p.sb("si", [128, NT], I32)
    sgi = p.sb("sgi", [128, NT], I32)
    vcv = p.sb("vcv", [128, NT], F32)
    ps = p.ps("ps", [128, 512], F32)

    p.op("sync", lambda e: e.dma_start(out=cprev[:, :], in_=cprev_d[:, :]), writes=["cprev"], dma=True)
    p.op("sync", lambda e: e.dma_start(out=t1v[:, :], in_=top1_d[:, :]), writes=["t1v"], dma=True)
    p.op("sync", lambda e: e.dma_start(out=rkv[:, :], in_=rank_d[:, :]), writes=["rkv"], dma=True)
    p.op("sync", lambda e: e.dma_start(out=cbv[:, :], in_=comb_d[:, :]), writes=["cbv"], dma=True)
    p.op("vector", lambda e: e.memset(ones8[:, :], 1.0), writes=["ones8"])
    for e8 in range(8):
        p.op("vector", lambda e, e8=e8: e.memset(iot[:, e8:e8 + 1], float(e8)), writes=["iot"])
    p.op("tensor", lambda e: e.matmul(ps[0:1, 0:8], ones8[:, 0:1], cprev[:, :], start=True, stop=True),
         reads=["ones8", "cprev"], writes=["ps"], tag="off_cs")
    p.op("vector", lambda e: e.tensor_copy(sc1[0:1, :], ps[0:1, 0:8]),
         reads=["ps"], writes=["sc1"], tag="off_ev")
    p.op("tensor", lambda e: e.matmul(ps[:, 0:8], b["ones_k1"][:, :], sc1[0:1, 0:8],
                                      start=True, stop=True),
         reads=["sc1", "ones_k1"], writes=["ps"], tag="off_bc")
    p.op("vector", lambda e: e.tensor_copy(offb[:, :], ps[:, 0:8]),
         reads=["ps"], writes=["offb"], tag="off_ev2")
    for t in range(NT):
        p.op("vector",
             lambda e, t=t: e.tensor_tensor(sc1[:, :], iot[:, :],
                                            t1v[:, t:t + 1].broadcast_to([128, 8]), ALU.is_equal),
             reads=["iot", "t1v"], writes=["sc1"], tag="oh")
        p.op("vector", lambda e: e.tensor_mul(sc1[:, :], sc1[:, :], offb[:, :]),
             reads=["sc1", "offb"], writes=["sc1"], tag="ohm")
        p.op("vector",
             lambda e: e.tensor_reduce(rg[:, 0:1], sc1[:, :].rearrange("p (a n) -> p a n", a=1),
                                       AX.X, ALU.add),
             reads=["sc1"], writes=["rg"], tag="offt")
        p.op("vector", lambda e, t=t: e.tensor_add(rg[:, 0:1], rg[:, 0:1], rkv[:, t:t + 1]),
             reads=["rg", "rkv"], writes=["rg"], tag="rgl")
        p.op("vector",
             lambda e: e.tensor_single_scalar(vld[:, 0:1], rg[:, 0:1], float(ECAP), ALU.is_lt),
             reads=["rg"], writes=["vld"], tag="vld")
        p.op("vector",
             lambda e, t=t: e.scalar_tensor_tensor(slt[:, 0:1], t1v[:, t:t + 1], float(ECAP),
                                                   rg[:, 0:1], ALU.mult, ALU.add),
             reads=["t1v", "rg"], writes=["slt"], tag="slt")
        p.op("vector",
             lambda e: e.scalar_tensor_tensor(slt[:, 0:1], slt[:, 0:1], float(ETRASH),
                                              vld[:, 0:1], ALU.subtract, ALU.mult),
             reads=["slt", "vld"], writes=["slt"], tag="slt2")
        p.op("vector",
             lambda e: e.tensor_single_scalar(slt[:, 0:1], slt[:, 0:1], float(ETRASH), ALU.add),
             reads=["slt"], writes=["slt"], tag="slt3")
        p.op("vector", lambda e, t=t: e.tensor_copy(si[:, t:t + 1], slt[:, 0:1]),
             reads=["slt"], writes=["si"], tag="sic")
        p.op("vector",
             lambda e: e.tensor_single_scalar(slt[:, 0:1], slt[:, 0:1], float(ETRASH - 1), ALU.min),
             reads=["slt"], writes=["slt"], tag="sgm")
        p.op("vector", lambda e, t=t: e.tensor_copy(sgi[:, t:t + 1], slt[:, 0:1]),
             reads=["slt"], writes=["sgi"], tag="sgc")
        p.op("vector", lambda e, t=t: e.tensor_mul(vcv[:, t:t + 1], vld[:, 0:1], cbv[:, t:t + 1]),
             reads=["vld", "cbv"], writes=["vcv"], tag="vcm")
    p.op("sync", lambda e: e.dma_start(out=ss_d[:, :], in_=si[:, :]), reads=["si"], dma=True, tag="st1")
    p.op("sync", lambda e: e.dma_start(out=sg_d[:, :], in_=sgi[:, :]), reads=["sgi"], dma=True, tag="st2")
    p.op("sync", lambda e: e.dma_start(out=vc_d[:, :], in_=vcv[:, :]), reads=["vcv"], dma=True, tag="st3")
    return p.build()


def build_stage_moeE1():
    ctx = Ctx("moeE1")
    p, b = ctx.p, ctx.b
    F4 = 4096
    NFS = 32
    ebT_d = p.dram("ebT", [C, ECAP], F32)
    w1r_d = p.dram("w1r", [NFS, C, 128], F32)
    w2r_d = p.dram("w2r", [CH, NFS, 128, 128], F32)
    b1_d = p.dram("b1c", [128, NFS], F32)
    b2_d = p.dram("b2c", [128, CH], F32)
    y_d = p.dram("y", [C, ECAP], BF16, kind="ExternalOutput")

    eb = [p.sb(f"eb_{c}", [128, ECAP], BF16) for c in range(CH)]
    hcg = p.sb("hcg", [128, NFS * 1024], BF16)
    w1sb = [p.sb(f"w1sb_{i}", [128, CH * 128], BF16) for i in range(2)]
    w2sb = [p.sb(f"w2sb_{i}", [128, 128], BF16) for i in range(2)]
    b1c = p.sb("b1c_s", [128, NFS], F32)
    b2c = p.sb("b2c_s", [128, CH], F32)
    yev = p.sb("yev", [128, 512], BF16)
    psA = p.ps("psA", [128, 512], F32)
    psB = p.ps("psB", [128, 512], F32)
    psY = [p.ps(f"psY{i}", [128, 512], F32) for i in range(2)]

    p.op("sync", lambda e: e.dma_start(out=b1c[:, :], in_=b1_d[:, :]), writes=["b1c"], dma=True)
    p.op("sync", lambda e: e.dma_start(out=b2c[:, :], in_=b2_d[:, :]), writes=["b2c"], dma=True)
    for c in range(CH):
        p.op("gpsimd",
             lambda e, c=c: e.dma_start(out=eb[c][:, :], in_=ebT_d[c * 128:(c + 1) * 128, :]),
             writes=[k_("eb", c)], dma=True, tag="ldeb")

    CGS = [(0, 1024), (1024, 1024), (2048, ECAP - 2048)]
    for cg, (c0, cn) in enumerate(CGS):
        ntg = (cn + 511) // 512
        for fs in range(NFS):
            w1 = w1sb[fs % 2]
            w1k = f"w1sb:{fs % 2}"
            p.op("gpsimd",
                 lambda e, fs=fs, w1=w1: e.dma_start(out=w1[:, :].rearrange("p (c n) -> p c n", n=128),
                                                     in_=w1r_d[fs, :, :].rearrange("(c p) n -> p c n", p=128)),
                 writes=[w1k], dma=True, tag="ldw1")
            for tg in range(ntg):
                gn = min(512, cn - tg * 512)
                bank = psA if tg % 2 == 0 else psB
                bkey = "psA" if tg % 2 == 0 else "psB"
                for c in range(CH):
                    p.op("tensor",
                         lambda e, fs=fs, c=c, tg=tg, gn=gn, c0=c0, bank=bank, w1=w1: e.matmul(
                             bank[:, 0:gn], w1[:, c * 128:(c + 1) * 128],
                             eb[c][:, c0 + tg * 512:c0 + tg * 512 + gn],
                             start=(c == 0), stop=(c == CH - 1)),
                         reads=[w1k, k_("eb", c)], writes=[bkey], tag="mm1")
                p.op("scalar",
                     lambda e, fs=fs, tg=tg, gn=gn, bank=bank: e.activation(
                         hcg[:, fs * 1024 + tg * 512:fs * 1024 + tg * 512 + gn],
                         bank[:, 0:gn], AF.Gelu, bias=b1c[:, fs:fs + 1]),
                     reads=[bkey, "b1c"], writes=["hcg"], tag="gelu")
        for oc in range(CH):
            for fs in range(NFS):
                w2 = w2sb[fs % 2]
                w2k = f"w2sb:{fs % 2}"
                p.op("gpsimd",
                     lambda e, oc=oc, fs=fs, w2=w2: e.dma_start(out=w2[:, :], in_=w2r_d[oc, fs, :, :]),
                     writes=[w2k], dma=True, tag="ldw2")
                for tg in range(ntg):
                    gn = min(512, cn - tg * 512)
                    p.op("tensor",
                         lambda e, oc=oc, fs=fs, tg=tg, gn=gn, w2=w2: e.matmul(
                             psY[tg % 2][:, 0:gn], w2[:, :],
                             hcg[:, fs * 1024 + tg * 512:fs * 1024 + tg * 512 + gn],
                             start=(fs == 0), stop=(fs == NFS - 1)),
                         reads=[w2k, "hcg"], writes=[f"psY{tg % 2}"], tag="mm2")
            for tg in range(ntg):
                gn = min(512, cn - tg * 512)
                p.op("vector",
                     lambda e, oc=oc, tg=tg, gn=gn: e.tensor_scalar_add(
                         yev[:, 0:gn], psY[tg % 2][:, 0:gn], b2c[:, oc:oc + 1]),
                     reads=[f"psY{tg % 2}", "b2c"], writes=["yev"], tag="yb")
                p.op("sync",
                     lambda e, oc=oc, tg=tg, gn=gn, c0=c0: e.dma_start(
                         out=y_d[oc * 128:(oc + 1) * 128, c0 + tg * 512:c0 + tg * 512 + gn],
                         in_=yev[:, 0:gn], allow_slow_non_contiguous=(gn < 16)),
                     reads=["yev"], dma=True, tag="sty")
    return p.build()


def build_stage_moeF():
    ctx = Ctx("moeF")
    p, b = ctx.p, ctx.b
    x7_d = p.dram("x7t", [T, C], F32)
    g_d = p.dram("gt", [T, C], BF16)
    vc_d = p.dram("vc", [128, NT], F32)
    fin_d = p.dram("fin", [T, C], F32, kind="ExternalOutput")
    xr = [p.sb(f"xr_{i}", [128, C], F32) for i in range(2)]
    gr = [p.sb(f"gr_{i}", [128, C], F32) for i in range(2)]
    vcv = p.sb("vcv", [128, NT], F32)
    ob = [p.sb(f"ob_{i}", [128, C], F32) for i in range(2)]
    p.op("sync", lambda e: e.dma_start(out=vcv[:, :], in_=vc_d[:, :]), writes=["vcv"], dma=True)
    for t in range(NT):
        i2 = t % 2
        p.op("sync", lambda e, t=t, i2=i2: e.dma_start(out=xr[i2][:, :], in_=x7_d[t * 128:(t + 1) * 128, :]),
             writes=[f"xr:{i2}"], dma=True, tag="ldx")
        p.op("gpsimd", lambda e, t=t, i2=i2: e.dma_start(out=gr[i2][:, :], in_=g_d[t * 128:(t + 1) * 128, :]),
             writes=[f"gr:{i2}"], dma=True, tag="ldg")
        p.op("vector",
             lambda e, t=t, i2=i2: e.scalar_tensor_tensor(
                 ob[i2][:, :], gr[i2][:, :], vcv[:, t:t + 1], xr[i2][:, :],
                 ALU.mult, ALU.add),
             reads=[f"gr:{i2}", f"xr:{i2}", "vcv"], writes=[f"ob:{i2}"], tag="cmb")
        p.op("sync", lambda e, t=t, i2=i2: e.dma_start(out=fin_d[t * 128:(t + 1) * 128, :], in_=ob[i2][:, :]),
             reads=[f"ob:{i2}"], dma=True, tag="st")
    return p.build()




_STAGE_CACHE = {}


def _stage(name):
    if name not in _STAGE_CACHE:
        _STAGE_CACHE[name] = globals()["build_stage_" + name]()
    return _STAGE_CACHE[name]


def _run(nc, in_maps):
    return run_bass_kernel_spmd(nc, in_maps, core_ids=list(range(NCORES))).results


def _l2attn_consts():
    H, T_ = 16, T
    j = np.arange(T_)
    jhi = (j // 128).astype(np.float64)
    jlo = (j % 128).astype(np.float64)
    ones = np.ones(T_)
    zer = np.zeros(T_)
    qac = np.zeros((H, 6, T_), np.float64)
    kacA = np.zeros((H, 7, T_), np.float64)
    kacB = np.zeros((H, 7, T_), np.float64)
    for h in range(H):
        s = 0.5 ** h
        qac[h] = np.stack([ones, ones, -s * 128 * jhi, -s * jlo, s * 128 * jhi, s * jlo])
        kacA[h] = np.stack([ones, s * 128 * jhi, s * jlo, ones, ones, zer, zer])
        kacB[h] = np.stack([ones, -s * 128 * jhi, -s * jlo, zer, zer, ones, ones])
    diagp = np.zeros((4, 128, 512), np.float32)
    a = np.arange(128)[:, None]
    bq = np.arange(512)[None, :]
    for ri, r in enumerate((0, 128, 256, 384)):
        diagp[ri] = np.maximum((r + a) - bq, 0.0)
    return (qac.astype(BF), kacA.astype(BF), kacB.astype(BF), diagp)


def _sel_const(nchunk, nblk):
    sel = np.zeros((nblk, nchunk * 128), dtype=np.float32)
    for c in range(nchunk):
        for pp in range(128):
            sel[4 * c + pp // 32, c * 128 + pp] = 1
    return sel


def _ct(a):
    return np.ascontiguousarray(a)


def kernel(**I):
    B = NCORES
    identb = np.eye(128, dtype=BF)
    identf = np.eye(128, dtype=np.float32)

    ob2 = _ct(I["L0_attn_bo"].reshape(8, 128).T)
    ins = [dict(xin=_ct(I["L0_attn_x"][b].T), Wq=I["L0_attn_Wq"], Wk=I["L0_attn_Wk"],
                Wv=I["L0_attn_Wv"], Wo=I["L0_attn_Wo"], ob2=ob2, ident=identb)
           for b in range(B)]
    r = _run(_stage("l0attn"), ins)
    x1 = [r[b]["xout"] for b in range(B)]

    ins = [dict(xin=x1[b], W_rT=_ct(I["L0_ffn_W_r"].T), W_kT=_ct(I["L0_ffn_W_k"].T),
                W_vT=_ct(I["L0_ffn_W_v"].T), mu_r=I["L0_ffn_mu_r"], mu_k=I["L0_ffn_mu_k"])
           for b in range(B)]
    r = _run(_stage("l0ffn"), ins)
    x2 = [r[b]["xout"] for b in range(B)]

    masks = np.zeros((6, 128, 512), np.float64)
    for ri, rv in enumerate((-256, -128, 0, 128, 256, 384)):
        a = np.arange(128)[:, None] + rv
        b_ = np.arange(512)[None, :]
        masks[ri] = ((a <= b_) & (a > b_ - 256)).astype(np.float64)
    qb2 = _ct(I["L1_attn_bq"].reshape(16, 64).T)
    kb2 = _ct(I["L1_attn_bk"].reshape(16, 64).T)
    vb1 = _ct(I["L1_attn_bv"].reshape(16, 64).T)
    vb2 = np.concatenate([vb1, vb1], axis=0)
    ob2 = _ct(I["L1_attn_bo"].reshape(8, 128).T)
    ins = [dict(xin=x2[b], wq=I["L1_attn_wq"], wk=I["L1_attn_wk"], wv=I["L1_attn_wv"],
                wo=I["L1_attn_wo"], qb2=qb2, kb2=kb2, vb2=vb2, ob2=ob2,
                masks=masks.astype(BF)) for b in range(B)]
    r = _run(_stage("l1attn"), ins)
    x3 = [r[b]["xout"] for b in range(B)]

    ins = [dict(xin=_ct(x3[b].T), w_gateT=_ct(I["L1_ffn_w_gate"].T),
                w_upT=_ct(I["L1_ffn_w_up"].T), w_downT=_ct(I["L1_ffn_w_down"].T),
                ident=identb) for b in range(B)]
    r = _run(_stage("l1ffn"), ins)
    x4 = [r[b]["xout"] for b in range(B)]

    bb = I["L2_attn_qkv_b"].reshape(16, 3, 64)
    qb2 = _ct(bb[:, 0, :].T)
    kb2 = _ct(bb[:, 1, :].T)
    vb1 = _ct(bb[:, 2, :].T)
    vb2 = np.concatenate([vb1, vb1], axis=0)
    ob2 = _ct(I["L2_attn_out_b"].reshape(8, 128).T)
    qac, kacA, kacB, diagp = _l2attn_consts()
    ins = [dict(xin=_ct(x4[b].T), qkvT=_ct(I["L2_attn_qkv_w"].T), outT=_ct(I["L2_attn_out_w"].T),
                qb2=qb2, kb2=kb2, vb2=vb2, ob2=ob2, qac=qac, kacA=kacA, kacB=kacB,
                diagp=diagp) for b in range(B)]
    r = _run(_stage("l2attn"), ins)
    x5 = [r[b]["xout"] for b in range(B)]

    ins = [dict(xin=x5[b], identf=identf) for b in range(B)]
    rA = _run(_stage("l2ffnA"), ins)
    m0 = np.array([[rA[b]["m0"][0, 0] for b in range(B)]], dtype=np.float32)
    wup, wg, wdn = I["L2_ffn_w_up"], I["L2_ffn_w_gate"], I["L2_ffn_w_down"]
    selu = _sel_const(8, 32)
    ins = [dict(xnin=rA[b]["xnout"], mvec=m0, wupT=_ct(wup.T), wupN=wup,
                wgT=_ct(wg.T), wgN=wg, selu=selu, identf=identf) for b in range(B)]
    rB = _run(_stage("l2ffnB"), ins)
    mu8 = np.array([[rB[b]["m2"][0, 0] for b in range(B)]], dtype=np.float32)
    mg8 = np.array([[rB[b]["m2"][1, 0] for b in range(B)]], dtype=np.float32)
    ins = [dict(gate_d=rB[b]["gate_d"], mg8=mg8, identf=identf) for b in range(B)]
    rC = _run(_stage("l2ffnC"), ins)
    mg28 = np.array([[rC[b]["m1"][0, 0] for b in range(B)]], dtype=np.float32)
    ins = [dict(up_d=rB[b]["up_d"], g2_d=rC[b]["g2_d"], mu8=mu8, mg28=mg28,
                identf=identf, identb=identb) for b in range(B)]
    rD = _run(_stage("l2ffnD"), ins)
    mo8 = np.array([[rD[b]["m1"][0, 0] for b in range(B)]], dtype=np.float32)
    selw = _sel_const(24, 96)
    ins = [dict(oT_d=rD[b]["oT_d"], mo8=mo8, wdnT=_ct(wdn.T), wdnN=wdn, selw=selw,
                identf=identf, xin=x5[b]) for b in range(B)]
    rE = _run(_stage("l2ffnE"), ins)
    x6 = [rE[b]["xout"] for b in range(B)]

    iwc, ibc = I["L3_attn_in_w_cls"], I["L3_attn_in_b_cls"]
    owc, obc = I["L3_attn_out_w_cls"], I["L3_attn_out_b_cls"]
    iwl, ibl = I["L3_attn_in_w_loc"], I["L3_attn_in_b_loc"]
    owl, obl = I["L3_attn_out_w_loc"], I["L3_attn_out_b_loc"]
    wgr = I["L3_ffn_wg"]
    RVALS = (-128, 0, 128, 256, 384, 512)
    masks7 = np.zeros((7, 128, 512), dtype=BF)
    jj = np.arange(128)[:, None]
    ii = np.arange(512)[None, :]
    for ri, rv in enumerate(RVALS):
        masks7[ri] = (np.abs(ii - (rv + jj)) <= 16).astype(BF)
    mz = (np.abs(ii - jj) <= 16).astype(np.float32)
    mz[0, :] = 0.0
    mz[:, 0] = 0.0
    mz[0, 0] = 1.0
    masks7[6] = mz.astype(BF)
    wgsb = np.concatenate([wgr[c * 128:(c + 1) * 128] for c in range(8)], axis=1).astype(np.float32)
    stri = np.triu(np.ones((128, 128), dtype=np.float32), 1).astype(BF)
    ins = [dict(
        xin=x6[b],
        wqcT=_ct(iwc[0:1024].T), wkcT=_ct(iwc[1024:2048].T), wvcT=_ct(iwc[2048:3072].T),
        wocT=_ct(owc.T),
        wqlT=_ct(iwl[0:1024].T), wklT=_ct(iwl[1024:2048].T), wvlT=_ct(iwl[2048:3072].T),
        wolT=_ct(owl.T),
        bqc2=_ct(ibc[0:1024].reshape(8, 128).T), bvc2=_ct(ibc[2048:3072].reshape(8, 128).T),
        boc2=obc.reshape(1, 1024),
        bql2=_ct(ibl[0:1024].reshape(16, 64).T),
        bvl2=_ct(np.concatenate([ibl[2048:3072].reshape(16, 64).T] * 2, axis=0)),
        bol2=_ct(obl.reshape(8, 128).T),
        masks7=masks7, identf=identf, identb=identb, wgsb=wgsb, stri=stri,
    ) for b in range(B)]
    rL3 = _run(_stage("l3attn"), ins)

    counts_all = np.stack([np.asarray(rL3[b]["cnt"])[0] for b in range(B)])
    ins = []
    for b in range(B):
        cprev = np.zeros((8, 8), dtype=np.float32)
        cprev[:b] = counts_all[:b]
        ins.append(dict(cprev=cprev, top1=rL3[b]["top1"], rank=rL3[b]["rank"],
                        comb=rL3[b]["comb"]))
    rR2 = _run(_stage("moeR2"), ins)

    slot_all = np.concatenate([np.asarray(rR2[b]["slot_s"]).T.flatten() for b in range(B)])
    slotg_all = np.concatenate([np.asarray(rR2[b]["slot_g"]).T.flatten() for b in range(B)])
    xf = np.concatenate([np.asarray(rL3[b]["xn7"]).T for b in range(B)], axis=0)
    buf = np.zeros((ETRASH + 1, C), dtype=np.float32)
    buf[slot_all] = xf
    W1, b1 = I["L3_ffn_W1"], I["L3_ffn_b1"]
    W2, b2 = I["L3_ffn_W2"], I["L3_ffn_b2"]
    ins = []
    for e in range(B):
        ins.append(dict(ebT=_ct(buf[e * ECAP:(e + 1) * ECAP].T),
                        w1r=_ct(W1[e].reshape(1024, 32, 128).transpose(1, 0, 2)),
                        w2r=_ct(W2[e].reshape(32, 128, 8, 128).transpose(2, 0, 1, 3)),
                        b1c=_ct(b1[e].reshape(32, 128).T),
                        b2c=_ct(b2[e].reshape(8, 128).T)))
    rE1 = _run(_stage("moeE1"), ins)

    y_all = np.concatenate([np.asarray(rE1[e]["y"]).T for e in range(B)], axis=0)
    gathered = y_all[slotg_all]
    ins = [dict(x7t=_ct(np.asarray(rL3[b]["xout"]).T),
                gt=_ct(gathered[b * T:(b + 1) * T]),
                vc=rR2[b]["vc"]) for b in range(B)]
    rF = _run(_stage("moeF"), ins)
    out = np.stack([np.asarray(rF[b]["fin"]) for b in range(B)]).astype(np.float32)
    return out
